# version 01
# baseline (speedup 1.0000x reference)
import numpy as np
from contextlib import ExitStack

import concourse.bass as bass
import concourse.mybir as mybir
from concourse.bass_utils import run_bass_kernel_spmd
from concourse.bass import IndirectOffsetOnAxis

F32 = mybir.dt.float32
BF16 = mybir.dt.bfloat16
I32 = mybir.dt.int32
ALU = mybir.AluOpType
AF = mybir.ActivationFunctionType
AX = mybir.AxisListType.X

D = 1024
T = 2048
NCH = 8
NTG = 4
TG = 512
EPS = 1e-5

ESEM = {"tensor": "pe", "scalar": "sc", "vector": "dv", "gpsimd": "gp", "sync": "sy"}
_NOINC = {"wait_ge", "nop", "sem_inc", "wait_op", "load_library"}


class EW:

    def __init__(self, ctx, eng, ename):
        self._ctx = ctx
        self._eng = eng
        self._ename = ename
        self._esem = ESEM[ename]

    def wait_ge(self, sem, val):
        if isinstance(sem, str):
            sem = self._ctx.sem(sem)
        self._eng.wait_ge(sem, val)

    def wait(self, pairs):
        for name, val in pairs:
            self._eng.wait_ge(self._ctx.sem(name), val)

    def dma(self, semname, *args, **kwargs):
        inst = self._eng.dma_start(*args, **kwargs)
        s, _ = self._ctx.bump(semname, 16)
        inst.then_inc(s, 16)
        return inst

    def idma(self, semname, *args, **kwargs):
        inst = self._eng.indirect_dma_start(*args, **kwargs)
        s, _ = self._ctx.bump(semname, 16)
        inst.then_inc(s, 16)
        return inst

    def cc(self, *args, **kwargs):
        inst = self._eng.collective_compute(*args, **kwargs)
        s, _ = self._ctx.bump("cc", 1)
        inst.then_inc(s, 1)
        return inst

    _SELFSYNC = {"scalar", "vector", "gpsimd"}

    def __getattr__(self, op):
        f = getattr(self._eng, op)
        if op in _NOINC or op.startswith("_") or not callable(f):
            return f
        ctx, esem = self._ctx, self._esem

        def g(*a, **k):
            if self._ename in EW._SELFSYNC and ctx.cnt.get(esem, 0) > 0:
                self._eng.wait_ge(ctx.sem(esem), ctx.cnt[esem])
            inst = f(*a, **k)
            s, _ = ctx.bump(esem, 1)
            try:
                inst.then_inc(s, 1)
            except Exception:
                ctx.cnt[esem] -= 1
            return inst
        return g


class Ctx:
    def __init__(self, nc, block, stack):
        self.nc = nc
        self.block = block
        self.stack = stack
        self.sems = {}
        self.cnt = {}

    def sem(self, name):
        if name not in self.sems:
            self.sems[name] = self.stack.enter_context(self.nc.semaphore(name))
            self.cnt[name] = 0
        return self.sems[name]

    def bump(self, name, by=1):
        s = self.sem(name)
        self.cnt[name] += by
        return s, by

    def val(self, name):
        self.sem(name)
        return self.cnt[name]

    def sb(self, name, shape, dtype):
        return self.stack.enter_context(self.nc.sbuf_tensor(name, shape, dtype))

    def on(self, ename):
        def deco(fn):
            @getattr(self.block, ename)
            def _(eng):
                fn(EW(self, eng, ename))
        return deco

    def snapshot(self, names=("pe", "sc", "dv", "gp")):
        return [(n, self.val(n)) for n in names]


def barrier(ctx, extra=()):
    pairs = ctx.snapshot() + [(n, ctx.val(n)) for n in extra]
    for e in ESEM:
        @ctx.on(e)
        def _(eng):
            eng.wait(pairs)



def load_weight_chunks(ctx, dram_ap, tiles, rows=128, sem="wld"):
    @ctx.on("gpsimd")
    def _(g):
        for i, t in enumerate(tiles):
            g.dma(sem, out=t[:, :], in_=dram_ap[i * rows:(i + 1) * rows, :])


def ln_rows(ctx, x_dram, rstd_row, negmr_row, tag="", xbuf=None):
    nc = ctx.nc
    rstd_row = rstd_row[0:1, :]
    negmr_row = negmr_row[0:1, :]
    with ExitStack() as es:
        if xbuf is None:
            xbuf = [es.enter_context(nc.sbuf_tensor(f"lnx{tag}{i}", [128, T], F32)) for i in range(2)]
        sq = es.enter_context(nc.sbuf_tensor(f"lnsq{tag}", [128, T], F32))
        ones = es.enter_context(nc.sbuf_tensor(f"lnones{tag}", [128, 1], F32))
        srow = es.enter_context(nc.sbuf_tensor(f"lnsrow{tag}", [1, T], F32))[0:1, :]
        qrow = es.enter_context(nc.sbuf_tensor(f"lnqrow{tag}", [1, T], F32))[0:1, :]
        epsc = es.enter_context(nc.sbuf_tensor(f"lneps{tag}", [1, 1], F32))
        with nc.psum_tensor(f"ps_s{tag}", [1, T], F32) as ps_s, \
             nc.psum_tensor(f"ps_q{tag}", [1, T], F32) as ps_q:
            _ln_body(ctx, x_dram, rstd_row, negmr_row,
                     xbuf, sq, ones, srow, qrow, epsc, ps_s, ps_q)


def _ln_body(ctx, x_dram, rstd_row, negmr_row,
             xbuf, sq, ones, srow, qrow, epsc, ps_s, ps_q):
    @ctx.on("vector")
    def _(v):
        v.memset(ones[:, :], 1.0)
        v.memset(epsc[:, :], EPS)
    dv0 = ctx.val("dv")

    lda0, ldb0 = ctx.val("lda"), ctx.val("ldb")
    pe0 = ctx.val("pe")
    sc0 = ctx.val("sc")
    sq_done = [sc0 + c + 1 for c in range(NCH)]
    mm_done = [pe0 + 8 * (c + 1) for c in range(NCH)]

    @ctx.on("sync")
    def _(sy):
        for c in range(NCH):
            if c >= 2:
                sy.wait_ge("pe", mm_done[c - 2])
            sy.dma("lda" if c % 2 == 0 else "ldb",
                   out=xbuf[c % 2][:, :], in_=x_dram[c * 128:(c + 1) * 128, :])

    @ctx.on("scalar")
    def _(s):
        for c in range(NCH):
            s.wait_ge("lda" if c % 2 == 0 else "ldb",
                      (lda0 if c % 2 == 0 else ldb0) + 16 * (c // 2 + 1))
            if c >= 1:
                s.wait_ge("pe", mm_done[c - 1])
            s.activation(out=sq[:, :], in_=xbuf[c % 2][:, :], func=AF.Square)

    @ctx.on("tensor")
    def _(t):
        t.wait_ge("dv", dv0)
        for c in range(NCH):
            t.wait_ge("sc", sq_done[c])
            for g in range(NTG):
                t.matmul(ps_s[0:1, g * TG:(g + 1) * TG], ones[:, 0:1],
                         xbuf[c % 2][:, g * TG:(g + 1) * TG],
                         start=(c == 0), stop=(c == NCH - 1))
                t.matmul(ps_q[0:1, g * TG:(g + 1) * TG], ones[:, 0:1],
                         sq[:, g * TG:(g + 1) * TG],
                         start=(c == 0), stop=(c == NCH - 1))
    pe_done = ctx.val("pe")

    @ctx.on("vector")
    def _(v):
        v.wait_ge("pe", pe_done)
        v.tensor_scalar(srow, ps_s[0:1, :], 1.0 / D, None, ALU.mult)
        v.tensor_scalar(qrow, ps_q[0:1, :], 1.0 / D, None, ALU.mult)
        v.tensor_tensor(rstd_row, srow, srow, ALU.mult)
        v.tensor_tensor(qrow, qrow, rstd_row, ALU.subtract)
    dv1 = ctx.val("dv")

    @ctx.on("scalar")
    def _(s):
        s.wait_ge("dv", dv1)
        s.activation(out=rstd_row, in_=qrow, func=AF.Sqrt, bias=epsc[0:1, 0:1])
    sc1 = ctx.val("sc")

    @ctx.on("vector")
    def _(v):
        v.wait_ge("sc", sc1)
        v.reciprocal(rstd_row, rstd_row)
        v.tensor_tensor(negmr_row, srow, rstd_row, ALU.mult)
        v.tensor_scalar(negmr_row, negmr_row, -1.0, None, ALU.mult)

    barrier(ctx, extra=("lda", "ldb"))


def replicate_rows(ctx, rows_and_dsts, ones_row, tag=""):
    nc = ctx.nc
    with nc.psum_tensor(f"repps{tag}", [128, T], F32) as ps, \
         nc.sbuf_tensor(f"repones{tag}", [65, 128], F32) as op:
        @ctx.on("vector")
        def _(v):
            v.memset(op[:, :], 1.0)
        dvo = ctx.val("dv")

        for i, (row, dst) in enumerate(rows_and_dsts):
            sc_gate = ctx.val("sc")
            row = row[0:1, :]
            b_ = getattr(row, "base_partition", 0)
            bp = b_() if callable(b_) else int(b_)

            @ctx.on("tensor")
            def _(t, row=row, bp=bp):
                t.wait_ge("dv", dvo)
                if i > 0:
                    t.wait_ge("sc", sc_gate)
                for g in range(NTG):
                    t.matmul(ps[:, g * TG:(g + 1) * TG], op[bp:bp + 1, 0:128],
                             row[0:1, g * TG:(g + 1) * TG], start=True, stop=True)
            pv = ctx.val("pe")

            @ctx.on("scalar")
            def _(s, dst=dst):
                s.wait_ge("pe", pv)
                s.copy(dst[:, :], ps[:, :])
    barrier(ctx)


def proj_1024(ctx, w_tiles, rhs_tiles, evac, nk=NCH, psname="projps",
              ngroups=NTG, tgw=TG, mchunks=NCH):
    nc = ctx.nc
    with nc.psum_tensor(psname + "0", [128, tgw], F32) as p0, \
         nc.psum_tensor(psname + "1", [128, tgw], F32) as p1:
        banks = [p0, p1]
        bank_free = [None, None]
        bi = 0
        for m in range(mchunks):
            for g in range(ngroups):
                ps = banks[bi]
                gate = bank_free[bi]

                @ctx.on("tensor")
                def _(t, m=m, g=g, ps=ps, gate=gate):
                    if gate is not None:
                        t.wait_ge(gate[0], gate[1])
                    for k in range(nk):
                        t.matmul(ps[:, :], w_tiles[k][:, m * 128:(m + 1) * 128],
                                 rhs_tiles[k][:, g * tgw:(g + 1) * tgw],
                                 start=(k == 0), stop=(k == nk - 1))
                pv = ctx.val("pe")
                bank_free[bi] = evac(m, g, ps, pv)
                bi ^= 1
    barrier(ctx)



def emit_rwkv(ctx, x_in, x_out, WrT, WkT, WvT, mu_r_col, mu_k_col):
    nc = ctx.nc

    wr = [ctx.sb(f"wr{k}", [128, D], BF16) for k in range(NCH)]
    wk = [ctx.sb(f"wk{k}", [128, D], BF16) for k in range(NCH)]
    wv = [ctx.sb(f"wv{k}", [128, D], BF16) for k in range(NCH)]
    murc = ctx.sb("murc", [128, NCH], F32)
    mukc = ctx.sb("mukc", [128, NCH], F32)
    ones_row = ctx.sb("onesrow", [1, 128], F32)
    half = ctx.sb("halfc", [128, 1], F32)
    rstd = ctx.sb("rstd", [1, T], F32)
    negmr = ctx.sb("negmr", [1, T], F32)
    rep_rstd = ctx.sb("rep_rstd", [128, T], BF16)
    rep_negmr = ctx.sb("rep_negmr", [128, T], BF16)

    load_weight_chunks(ctx, WrT, wr)
    load_weight_chunks(ctx, WkT, wk)
    load_weight_chunks(ctx, WvT, wv)

    @ctx.on("gpsimd")
    def _(g):
        g.dma("wld", out=murc[:, :], in_=mu_r_col)
        g.dma("wld", out=mukc[:, :], in_=mu_k_col)

    @ctx.on("vector")
    def _(v):
        v.memset(ones_row[:, :], 1.0)
        v.memset(half[:, :], 0.5)

    ln_rows(ctx, x_in, rstd, negmr)
    replicate_rows(ctx, [(rstd, rep_rstd), (negmr, rep_negmr)], ones_row)

    xr = [ctx.sb(f"xr{k}", [128, T], BF16) for k in range(NCH)]
    xk = [ctx.sb(f"xk{k}", [128, T], BF16) for k in range(NCH)]
    r = [ctx.sb(f"r{k}", [128, T], BF16) for k in range(NCH)]
    ybuf = ctx.sb("ybuf", [128, T], BF16)
    tbuf = ctx.sb("tbuf", [128, T], BF16)
    ktmp = ctx.sb("ktmp", [128, TG], BF16)
    xbuf = [ctx.sb(f"xb{i}", [128, T], F32) for i in range(2)]
    vtmp = ctx.sb("vtmp", [128, TG], F32)
    xres = [ctx.sb(f"xres{i}", [128, TG], F32) for i in range(2)]

    wld_done = ctx.val("wld")

    lda0, ldb0 = ctx.val("lda"), ctx.val("ldb")
    dv0 = ctx.val("dv")
    mix_done = [dv0 + 7 * (c + 1) for c in range(NCH)]

    @ctx.on("sync")
    def _(sy):
        for c in range(NCH):
            if c >= 2:
                sy.wait_ge("dv", mix_done[c - 2])
            sy.dma("lda" if c % 2 == 0 else "ldb",
                   out=xbuf[c % 2][:, :], in_=x_in[c * 128:(c + 1) * 128, :])

    @ctx.on("vector")
    def _(v):
        v.wait_ge("wld", wld_done)
        for c in range(NCH):
            v.wait_ge("lda" if c % 2 == 0 else "ldb",
                      (lda0 if c % 2 == 0 else ldb0) + 16 * (c // 2 + 1))
            v.tensor_tensor(tbuf[:, :], xbuf[c % 2][:, :], rep_rstd[:, :], ALU.mult)
            v.tensor_tensor(ybuf[:, :], tbuf[:, :], rep_negmr[:, :], ALU.add)
            v.tensor_tensor(tbuf[:, 1:T], ybuf[:, 1:T], ybuf[:, 0:T - 1], ALU.subtract)
            v.scalar_tensor_tensor(xr[c][:, 1:T], tbuf[:, 1:T], murc[:, c:c + 1],
                                   ybuf[:, 0:T - 1], ALU.mult, ALU.add)
            v.tensor_scalar(xr[c][:, 0:1], ybuf[:, 0:1], murc[:, c:c + 1], None, ALU.mult)
            v.scalar_tensor_tensor(xk[c][:, 1:T], tbuf[:, 1:T], mukc[:, c:c + 1],
                                   ybuf[:, 0:T - 1], ALU.mult, ALU.add)
            v.tensor_scalar(xk[c][:, 0:1], ybuf[:, 0:1], mukc[:, c:c + 1], None, ALU.mult)
    assert ctx.val("dv") == mix_done[-1], (ctx.val("dv"), mix_done[-1])
    barrier(ctx, extra=("lda", "ldb"))

    def evac_r(m, g, ps, pv):
        @ctx.on("scalar")
        def _(s):
            s.wait_ge("pe", pv)
            s.activation(out=r[m][:, g * TG:(g + 1) * TG], in_=ps[:, :], func=AF.Sigmoid)
        return ("sc", ctx.val("sc"))
    proj_1024(ctx, wr, xr, evac_r, psname="rps")

    kact = xr
    ktmp_read = [0]

    def evac_k(m, g, ps, pv):
        @ctx.on("scalar")
        def _(s):
            s.wait_ge("pe", pv)
            s.wait_ge("dv", ktmp_read[0])
            s.activation(out=ktmp[:, :], in_=ps[:, :], func=AF.Relu)
        sv = ctx.val("sc")

        @ctx.on("vector")
        def _(v):
            v.wait_ge("sc", sv)
            v.tensor_tensor(kact[m][:, g * TG:(g + 1) * TG], ktmp[:, :], ktmp[:, :], ALU.mult)
        ktmp_read[0] = ctx.val("dv")
        return ("dv", ctx.val("dv"))
    proj_1024(ctx, wk, xk, evac_k, psname="kps")

    res_gates = []
    store_gates = []

    def evac_v(m, g, ps, pv):
        idx = m * NTG + g
        rsem = "rla" if idx % 2 == 0 else "rlb"

        ssem = "sta" if idx % 2 == 0 else "stb"

        @ctx.on("sync")
        def _(sy):
            if idx >= 2:
                sy.wait_ge(*store_gates[idx - 2])
            sy.dma(rsem, out=xres[idx % 2][:, :],
                   in_=x_in[m * 128:(m + 1) * 128, g * TG:(g + 1) * TG])
        lv = ctx.val(rsem)

        @ctx.on("vector")
        def _(v):
            v.wait_ge("pe", pv)
            v.wait_ge(rsem, lv)
            v.tensor_tensor(vtmp[:, :], ps[:, :], r[m][:, g * TG:(g + 1) * TG], ALU.mult)
            v.scalar_tensor_tensor(xres[idx % 2][:, :], vtmp[:, :], half[:, 0:1],
                                   xres[idx % 2][:, :], ALU.mult, ALU.add)
        dvv = ctx.val("dv")
        res_gates.append(dvv)

        @ctx.on("gpsimd")
        def _(g2):
            g2.wait_ge("dv", dvv)
            g2.dma(ssem, out=x_out[m * 128:(m + 1) * 128, g * TG:(g + 1) * TG],
                   in_=xres[idx % 2][:, :])
        store_gates.append((ssem, ctx.val(ssem)))
        return ("dv", dvv)
    proj_1024(ctx, wv, kact, evac_v, psname="vps")

    sta_v, stb_v = ctx.val("sta"), ctx.val("stb")

    @ctx.on("gpsimd")
    def _(g):
        g.wait_ge("sta", sta_v)
        g.wait_ge("stb", stb_v)



def build_y(ctx, x_in, rep_rstd, rep_negmr, ytiles, xbuf, per_chunk_extra=None):
    lda0, ldb0 = ctx.val("lda"), ctx.val("ldb")
    dv0 = ctx.val("dv")
    nops = 2
    y_done = [dv0 + nops * (c + 1) for c in range(NCH)]

    @ctx.on("sync")
    def _(sy):
        for c in range(NCH):
            if c >= 2:
                sy.wait_ge("dv", y_done[c - 2])
            sy.dma("lda" if c % 2 == 0 else "ldb",
                   out=xbuf[c % 2][:, :], in_=x_in[c * 128:(c + 1) * 128, :])

    @ctx.on("vector")
    def _(v):
        for c in range(NCH):
            v.wait_ge("lda" if c % 2 == 0 else "ldb",
                      (lda0 if c % 2 == 0 else ldb0) + 16 * (c // 2 + 1))
            v.tensor_tensor(ytiles[c][:, :], xbuf[c % 2][:, :], rep_rstd[:, :], ALU.mult)
            v.tensor_tensor(ytiles[c][:, :], ytiles[c][:, :], rep_negmr[:, :], ALU.add)
    assert ctx.val("dv") == y_done[-1]
    barrier(ctx, extra=("lda", "ldb"))


def residual_evac(ctx, x_in, x_out, m, g, idx, src_ap, scale_col, xres, store_gates, dv_ready):
    rsem = "rla" if idx % 2 == 0 else "rlb"
    ssem = "sta" if idx % 2 == 0 else "stb"

    @ctx.on("sync")
    def _(sy):
        if idx >= 2:
            sy.wait_ge(*store_gates[idx - 2])
        sy.dma(rsem, out=xres[idx % 2][:, :],
               in_=x_in[m * 128:(m + 1) * 128, g * TG:(g + 1) * TG])
    lv = ctx.val(rsem)

    @ctx.on("vector")
    def _(v):
        v.wait_ge("dv", dv_ready)
        v.wait_ge(rsem, lv)
        v.scalar_tensor_tensor(xres[idx % 2][:, :], src_ap, scale_col,
                               xres[idx % 2][:, :], ALU.mult, ALU.add)
    dvv = ctx.val("dv")

    @ctx.on("gpsimd")
    def _(g2):
        g2.wait_ge("dv", dvv)
        g2.dma(ssem, out=x_out[m * 128:(m + 1) * 128, g * TG:(g + 1) * TG],
               in_=xres[idx % 2][:, :])
    store_gates.append((ssem, ctx.val(ssem)))
    return dvv


def wait_stores(ctx):
    sta_v, stb_v = ctx.val("sta"), ctx.val("stb")

    @ctx.on("gpsimd")
    def _(g):
        g.wait_ge("sta", sta_v)
        g.wait_ge("stb", stb_v)



def emit_l2ffn(ctx, x_in, x_out, WfcT_dram, WprojT_dram):
    nc = ctx.nc
    ones_row = ctx.sb("onesrow2", [1, 128], F32)
    tenth = ctx.sb("tenth", [128, 1], F32)
    y = [ctx.sb(f"y2{k}", [128, T], BF16) for k in range(NCH)]

    @ctx.on("vector")
    def _(v):
        v.memset(ones_row[:, :], 1.0)
        v.memset(tenth[:, :], 0.1)

    with ExitStack() as es:
        rstd = es.enter_context(nc.sbuf_tensor("rstd2", [1, T], F32))
        negmr = es.enter_context(nc.sbuf_tensor("negmr2", [1, T], F32))
        rep_rstd = es.enter_context(nc.sbuf_tensor("rep_rstd2", [128, T], BF16))
        rep_negmr = es.enter_context(nc.sbuf_tensor("rep_negmr2", [128, T], BF16))
        xbuf = [es.enter_context(nc.sbuf_tensor(f"xb2{i}", [128, T], F32)) for i in range(2)]
        ln_rows(ctx, x_in, rstd, negmr, tag="f2", xbuf=xbuf)
        replicate_rows(ctx, [(rstd, rep_rstd), (negmr, rep_negmr)], ones_row, tag="f2")
        build_y(ctx, x_in, rep_rstd, rep_negmr, y, xbuf)

    wfc = [ctx.sb(f"wfc{k}", [128, 4 * D], BF16) for k in range(NCH)]
    wproj = [ctx.sb(f"wpj{k}", [128, D], BF16) for k in range(4 * NCH)]
    h = [ctx.sb(f"h{k}", [128, TG], BF16) for k in range(4 * NCH)]
    htmp = ctx.sb("htmp", [128, TG], BF16)
    y2t = ctx.sb("y2t", [128, TG], F32)
    mem = ctx.sb("memt", [128, TG], F32)
    spk = ctx.sb("spkt", [128, TG], F32)
    acc = ctx.sb("acct", [128, TG], F32)
    t1 = ctx.sb("t1t", [128, TG], F32)
    xres = [ctx.sb(f"xres2{i}", [128, TG], F32) for i in range(2)]

    load_weight_chunks(ctx, WfcT_dram, wfc)
    load_weight_chunks(ctx, WprojT_dram, wproj)
    wld_done = ctx.val("wld")

    @ctx.on("tensor")
    def _(t):
        t.wait_ge("wld", wld_done)

    store_gates = []
    htmp_read = [0]
    for g in range(NTG):
        def evac_h(m, _g, ps, pv, g=g):
            @ctx.on("scalar")
            def _(sx):
                sx.wait_ge("pe", pv)
                sx.wait_ge("dv", htmp_read[0])
                sx.activation(out=htmp[:, :], in_=ps[:, :], func=AF.Relu)
            sv = ctx.val("sc")

            @ctx.on("vector")
            def _(v):
                v.wait_ge("sc", sv)
                v.tensor_tensor(h[m][:, :], htmp[:, :], htmp[:, :], ALU.mult)
            htmp_read[0] = ctx.val("dv")
            return ("dv", ctx.val("dv"))
        proj_1024(ctx, wfc, [yy[:, g * TG:(g + 1) * TG] for yy in y], evac_h,
                  psname=f"hps{g}", mchunks=4 * NCH, ngroups=1, tgw=TG)

        def evac_y2(m, _g, ps, pv, g=g):
            idx = g * NCH + m

            @ctx.on("vector")
            def _(v):
                v.wait_ge("pe", pv)
                v.tensor_copy(y2t[:, :], ps[:, :])
                v.tensor_copy(mem[:, :], y2t[:, :])
                v.tensor_scalar(spk[:, :], mem[:, :], 1.0, None, ALU.is_gt)
                v.tensor_copy(acc[:, :], spk[:, :])
                for st in range(1, 5):
                    v.tensor_tensor(t1[:, :], y2t[:, :], spk[:, :], ALU.subtract)
                    v.scalar_tensor_tensor(mem[:, :], mem[:, :], 0.9, t1[:, :],
                                           ALU.mult, ALU.add)
                    v.tensor_scalar(spk[:, :], mem[:, :], 1.0, None, ALU.is_gt)
                    v.tensor_tensor(acc[:, :], acc[:, :], spk[:, :], ALU.add)
            dvv = ctx.val("dv")
            dvv2 = residual_evac(ctx, x_in, x_out, m, g, idx, acc[:, :], tenth[:, 0:1],
                                 xres, store_gates, dvv)
            return ("dv", dvv2)
        proj_1024(ctx, wproj, h, evac_y2, psname=f"y2ps{g}", nk=4 * NCH,
                  mchunks=NCH, ngroups=1, tgw=TG)

    wait_stores(ctx)


def build_l2ffn():
    nc = bass.Bass("TRN2", target_bir_lowering=False, debug=False)
    x_in = nc.dram_tensor("x_in", [D, T], F32, kind="ExternalInput").ap()
    WfcT = nc.dram_tensor("WfcT", [D, 4 * D], F32, kind="ExternalInput").ap()
    WprojT = nc.dram_tensor("WprojT", [4 * D, D], F32, kind="ExternalInput").ap()
    x_out = nc.dram_tensor("x_out", [D, T], F32, kind="ExternalOutput").ap()
    with ExitStack() as stack:
        block = stack.enter_context(nc.Block())
        ctx = Ctx(nc, block, stack)
        emit_l2ffn(ctx, x_in, x_out, WfcT, WprojT)
    return nc


def l2ffn_maps(W_fc, W_proj):
    return {
        "WfcT": np.ascontiguousarray(np.asarray(W_fc).T),
        "WprojT": np.ascontiguousarray(np.asarray(W_proj).T),
    }


def run_l2ffn(xT, W_fc, W_proj):
    nc = build_l2ffn()
    wmap = l2ffn_maps(W_fc, W_proj)
    in_maps = [dict(wmap, x_in=np.ascontiguousarray(xT[b])) for b in range(8)]
    res = run_bass_kernel_spmd(nc, in_maps, core_ids=list(range(8)))
    return np.stack([r["x_out"] for r in res.results])



def emit_l1attn(ctx, x_in, x_out, Wq_d, Wk_d, Wv_d, bq_d, bk_d, bv_d, ident_d):
    nc = ctx.nc
    ones_row = ctx.sb("onesrowA", [1, TG], F32)
    ones_bf = ctx.sb("onesbfA", [1, TG], BF16)
    qscale = ctx.sb("qscaleA", [128, 1], F32)
    ident = ctx.sb("identA", [128, 128], BF16)

    QT = [ctx.sb(f"qt{k}", [128, T], BF16) for k in range(NCH)]
    KT = [ctx.sb(f"kt{k}", [128, T], BF16) for k in range(NCH)]
    V = [ctx.sb(f"vtl{k}", [128, D], BF16) for k in range(16)]

    @ctx.on("vector")
    def _(v):
        v.memset(ones_row[:, :], 1.0)
        v.memset(ones_bf[:, :], 1.0)
        v.memset(qscale[:, :], 1.0 / 32.0)

    @ctx.on("gpsimd")
    def _(g):
        g.dma("wld", out=ident[:, :], in_=ident_d)

    with ExitStack() as es:
        rstd = es.enter_context(nc.sbuf_tensor("rstdA", [1, T], F32))
        negmr = es.enter_context(nc.sbuf_tensor("negmrA", [1, T], F32))
        rep_rstd = es.enter_context(nc.sbuf_tensor("rep_rstdA", [128, T], BF16))
        rep_negmr = es.enter_context(nc.sbuf_tensor("rep_negmrA", [128, T], BF16))
        xbuf = [es.enter_context(nc.sbuf_tensor(f"xbA{i}", [128, T], F32)) for i in range(2)]
        y = [es.enter_context(nc.sbuf_tensor(f"yA{k}", [128, T], BF16)) for k in range(NCH)]
        brow = es.enter_context(nc.sbuf_tensor("browA", [1, D], BF16))

        ln_rows(ctx, x_in, rstd, negmr, tag="A", xbuf=xbuf)
        replicate_rows(ctx, [(rstd, rep_rstd), (negmr, rep_negmr)], ones_row[0:1, 0:128], tag="A")
        build_y(ctx, x_in, rep_rstd, rep_negmr, y, xbuf)

        for wi, (Wd, bd, dst, scale) in enumerate(
                ((Wq_d, bq_d, QT, qscale), (Wk_d, bk_d, KT, None))):
            with ExitStack() as es2:
                wt = [es2.enter_context(nc.sbuf_tensor(f"wA{wi}_{k}", [128, D], BF16))
                      for k in range(NCH)]
                load_weight_chunks(ctx, Wd, wt)

                @ctx.on("gpsimd")
                def _(g, bd=bd):
                    g.dma("wld", out=brow[0:1, :], in_=bd)
                wv_done = ctx.val("wld")

                @ctx.on("tensor")
                def _(t):
                    t.wait_ge("wld", wv_done)

                def evac_qk(m, g, ps, pv, dst=dst, scale=scale):
                    @ctx.on("scalar")
                    def _(sx):
                        sx.wait_ge("pe", pv)
                        if scale is not None:
                            sx.activation(out=dst[m][:, g * TG:(g + 1) * TG], in_=ps[:, :],
                                          func=AF.Identity, scale=scale[:, 0:1])
                        else:
                            sx.copy(dst[m][:, g * TG:(g + 1) * TG], ps[:, :])
                    return ("sc", ctx.val("sc"))

                with nc.psum_tensor(f"qkps0_{wi}", [128, TG], F32) as p0, \
                     nc.psum_tensor(f"qkps1_{wi}", [128, TG], F32) as p1:
                    banks = [p0, p1]
                    bank_free = [None, None]
                    bi = 0
                    for m in range(NCH):
                        for g in range(NTG):
                            ps = banks[bi]
                            gate = bank_free[bi]

                            @ctx.on("tensor")
                            def _(t, m=m, g=g, ps=ps, gate=gate):
                                if gate is not None:
                                    t.wait_ge(gate[0], gate[1])
                                for k in range(NCH):
                                    t.matmul(ps[:, :], wt[k][:, m * 128:(m + 1) * 128],
                                             y[k][:, g * TG:(g + 1) * TG],
                                             start=(k == 0), stop=False)
                                t.matmul(ps[:, :], brow[0:1, m * 128:(m + 1) * 128],
                                         ones_bf[0:1, :], start=False, stop=True)
                            pv = ctx.val("pe")
                            bank_free[bi] = evac_qk(m, g, ps, pv)
                            bi ^= 1
                barrier(ctx)

        with ExitStack() as es2:
            wt = [es2.enter_context(nc.sbuf_tensor(f"wVA{k}", [128, D], BF16))
                  for k in range(NCH)]
            load_weight_chunks(ctx, Wv_d, wt)

            @ctx.on("gpsimd")
            def _(g):
                g.dma("wld", out=brow[0:1, :], in_=bv_d)
            wv_done = ctx.val("wld")

            @ctx.on("tensor")
            def _(t):
                t.wait_ge("wld", wv_done)

            with nc.psum_tensor("vps0", [128, TG], F32) as p0, \
                 nc.psum_tensor("vps1", [128, TG], F32) as p1:
                banks = [p0, p1]
                bank_free = [None, None]
                bi = 0
                for tt in range(16):
                    for ng in range(2):
                        ps = banks[bi]
                        gate = bank_free[bi]

                        @ctx.on("tensor")
                        def _(t, tt=tt, ng=ng, ps=ps, gate=gate):
                            if gate is not None:
                                t.wait_ge(gate[0], gate[1])
                            for k in range(NCH):
                                t.matmul(ps[:, :], y[k][:, tt * 128:(tt + 1) * 128],
                                         wt[k][:, ng * TG:(ng + 1) * TG],
                                         start=(k == 0), stop=False)
                            t.matmul(ps[:, :], ones_bf[0:1, 0:128],
                                     brow[0:1, ng * TG:(ng + 1) * TG],
                                     start=False, stop=True)
                        pv = ctx.val("pe")

                        @ctx.on("scalar")
                        def _(sx, tt=tt, ng=ng, ps=ps):
                            sx.wait_ge("pe", pv)
                            sx.copy(V[tt][:, ng * TG:(ng + 1) * TG], ps[:, :])
                        bank_free[bi] = ("sc", ctx.val("sc"))
                        bi ^= 1
            barrier(ctx)

    ao = [ctx.sb(f"ao{k}", [128, T], BF16) for k in range(NCH)]
    wtile = ctx.sb("wtileA", [128, T], BF16)
    wTt = ctx.sb("wTA", [128, T], BF16)
    mx = ctx.sb("mxA", [128, 1], F32)
    negmx = ctx.sb("negmxA", [128, 1], F32)
    sums = ctx.sb("sumsA", [128, 1], F32)
    rsum = ctx.sb("rsumA", [128, 1], F32)

    with nc.psum_tensor("lgps", [128, T], F32) as lgps, \
         nc.psum_tensor("tpps0", [128, 128], BF16) as tpps0, \
         nc.psum_tensor("tpps1", [128, 128], BF16) as tpps1, \
         nc.psum_tensor("avps", [128, 1024], F32) as avps:
        tpps = [tpps0, tpps1]
        for qt in range(16):
            dv_gate = ctx.val("dv")
            sc_gate = ctx.val("sc")

            @ctx.on("tensor")
            def _(t, qt=qt):
                t.wait_ge("dv", dv_gate)
                t.wait_ge("sc", sc_gate)
                for gk in range(NTG):
                    for kd in range(NCH):
                        t.matmul(lgps[:, gk * TG:(gk + 1) * TG],
                                 QT[kd][:, qt * 128:(qt + 1) * 128],
                                 KT[kd][:, gk * TG:(gk + 1) * TG],
                                 start=(kd == 0), stop=(kd == NCH - 1))
            pv = ctx.val("pe")

            @ctx.on("vector")
            def _(v):
                v.wait_ge("pe", pv)
                v.reduce_max(mx[:, 0:1], lgps[:, :], axis=AX)
                v.tensor_scalar(negmx[:, 0:1], mx[:, 0:1], -1.0, None, ALU.mult)
            dvv = ctx.val("dv")

            @ctx.on("scalar")
            def _(sx):
                sx.wait_ge("dv", dvv)
                sx.activation(out=wtile[:, :], in_=lgps[:, :], func=AF.Exp,
                              bias=negmx[:, 0:1], accum_out=sums[:, 0:1])
            scv = ctx.val("sc")

            @ctx.on("vector")
            def _(v):
                v.wait_ge("sc", scv)
                v.reciprocal(rsum[:, 0:1], sums[:, 0:1])
                v.tensor_scalar(wtile[:, :], wtile[:, :], rsum[:, 0:1], None, ALU.mult)
            dvn = ctx.val("dv")

            tp_evac = [None, None]
            for j in range(16):
                pj = tpps[j % 2]
                gate = tp_evac[j % 2]

                @ctx.on("tensor")
                def _(t, j=j, pj=pj, gate=gate):
                    if j == 0:
                        t.wait_ge("dv", dvn)
                    if gate is not None:
                        t.wait_ge(gate[0], gate[1])
                    t.transpose(pj[:, :], wtile[:, j * 128:(j + 1) * 128], ident[:, :])
                pvj = ctx.val("pe")

                @ctx.on("vector")
                def _(v, j=j, pj=pj):
                    v.wait_ge("pe", pvj)
                    v.tensor_copy(wTt[:, j * 128:(j + 1) * 128], pj[:, :])
                tp_evac[j % 2] = ("dv", ctx.val("dv"))
            dvT = ctx.val("dv")

            @ctx.on("tensor")
            def _(t, qt=qt):
                t.wait_ge("dv", dvT)
                for m in range(NCH):
                    for j in range(16):
                        t.matmul(avps[:, m * 128:(m + 1) * 128],
                                 V[j][:, m * 128:(m + 1) * 128],
                                 wTt[:, j * 128:(j + 1) * 128],
                                 start=(j == 0), stop=(j == 15))
            pva = ctx.val("pe")

            @ctx.on("scalar")
            def _(sx, qt=qt):
                sx.wait_ge("pe", pva)
                for m in range(NCH):
                    sx.copy(ao[m][:, qt * 128:(qt + 1) * 128], avps[:, m * 128:(m + 1) * 128])
    barrier(ctx)

    one_col = ctx.sb("onecolA", [128, 1], F32)
    xres = [ctx.sb(f"xresA{i}", [128, TG], F32) for i in range(2)]

    @ctx.on("vector")
    def _(v):
        v.memset(one_col[:, :], 1.0)

    store_gates = []
    for m in range(NCH):
        for g in range(NTG):
            idx = m * NTG + g
            dvv = ctx.val("dv")
            residual_evac(ctx, x_in, x_out, m, g, idx, ao[m][:, g * TG:(g + 1) * TG],
                          one_col[:, 0:1], xres, store_gates, dvv)
    wait_stores(ctx)


def build_l1attn():
    nc = bass.Bass("TRN2", target_bir_lowering=False, debug=False)
    x_in = nc.dram_tensor("x_in", [D, T], F32, kind="ExternalInput").ap()
    Wq = nc.dram_tensor("Wq", [D, D], F32, kind="ExternalInput").ap()
    Wk = nc.dram_tensor("Wk", [D, D], F32, kind="ExternalInput").ap()
    Wv = nc.dram_tensor("Wv", [D, D], F32, kind="ExternalInput").ap()
    bq = nc.dram_tensor("bq", [1, D], F32, kind="ExternalInput").ap()
    bk = nc.dram_tensor("bk", [1, D], F32, kind="ExternalInput").ap()
    bv = nc.dram_tensor("bv", [1, D], F32, kind="ExternalInput").ap()
    ident = nc.dram_tensor("ident", [128, 128], F32, kind="ExternalInput").ap()
    x_out = nc.dram_tensor("x_out", [D, T], F32, kind="ExternalOutput").ap()
    with ExitStack() as stack:
        block = stack.enter_context(nc.Block())
        ctx = Ctx(nc, block, stack)
        emit_l1attn(ctx, x_in, x_out, Wq, Wk, Wv, bq, bk, bv, ident)
    return nc


def l1attn_maps(Wq, Wk, Wv, bq, bk, bv):
    return {
        "Wq": np.ascontiguousarray(np.asarray(Wq)),
        "Wk": np.ascontiguousarray(np.asarray(Wk)),
        "Wv": np.ascontiguousarray(np.asarray(Wv)),
        "bq": np.asarray(bq).reshape(1, D).copy(),
        "bk": np.asarray(bk).reshape(1, D).copy(),
        "bv": np.asarray(bv).reshape(1, D).copy(),
        "ident": np.eye(128, dtype=np.float32),
    }


def run_l1attn(xT, Wq, Wk, Wv, bq, bk, bv):
    nc = build_l1attn()
    wmap = l1attn_maps(Wq, Wk, Wv, bq, bk, bv)
    in_maps = [dict(wmap, x_in=np.ascontiguousarray(xT[b])) for b in range(8)]
    res = run_bass_kernel_spmd(nc, in_maps, core_ids=list(range(8)))
    return np.stack([r["x_out"] for r in res.results])



def emit_l2attn(ctx, x_in, x_out, Wqkv_d, Wout_d, hs_d, dirs_d, cmask_d, ao_dram):
    nc = ctx.nc
    H = 8
    SC = 1.0 / np.sqrt(128.0)
    ones_row = ctx.sb("onesrowC", [1, 128], F32)
    ones_bf = ctx.sb("onesbfC", [1, TG], BF16)
    onescol_bf = ctx.sb("onescolC", [128, 1], BF16)
    one_col = ctx.sb("onecolC", [128, 1], F32)
    qsc = ctx.sb("qscC", [128, 1], F32)
    dirs = ctx.sb("dirsC", [3, H], BF16)
    hsrow = ctx.sb("hsrowC", [1, H], F32)
    cmask = [ctx.sb(f"cmaskC{j}", [128, TG], BF16) for j in range(4)]

    @ctx.on("vector")
    def _(v):
        v.memset(ones_row[:, :], 1.0)
        v.memset(ones_bf[:, :], 1.0)
        v.memset(onescol_bf[:, :], 1.0)
        v.memset(one_col[:, :], 1.0)
        v.memset(qsc[:, :], SC)

    @ctx.on("gpsimd")
    def _(g):
        g.dma("wld", out=dirs[:, :], in_=dirs_d)
        g.dma("wld", out=hsrow[:, :], in_=hs_d)
        for j in range(4):
            g.dma("wld", out=cmask[j][:, :], in_=cmask_d[j])

    with ExitStack() as esq:
        QT = [esq.enter_context(nc.sbuf_tensor(f"qtC{k}", [128, T], BF16)) for k in range(H)]
        KT = [esq.enter_context(nc.sbuf_tensor(f"ktC{k}", [128, T], BF16)) for k in range(H)]
        V = [esq.enter_context(nc.sbuf_tensor(f"vtlC{k}", [128, D], BF16)) for k in range(16)]

        with ExitStack() as es:
            rstd = es.enter_context(nc.sbuf_tensor("rstdC", [1, T], F32))
            negmr = es.enter_context(nc.sbuf_tensor("negmrC", [1, T], F32))
            rep_rstd = es.enter_context(nc.sbuf_tensor("rep_rstdC", [128, T], BF16))
            rep_negmr = es.enter_context(nc.sbuf_tensor("rep_negmrC", [128, T], BF16))
            xbuf = [es.enter_context(nc.sbuf_tensor(f"xbC{i}", [128, T], F32)) for i in range(2)]
            y = [es.enter_context(nc.sbuf_tensor(f"yC{k}", [128, T], BF16)) for k in range(NCH)]

            ln_rows(ctx, x_in, rstd, negmr, tag="C", xbuf=xbuf)
            replicate_rows(ctx, [(rstd, rep_rstd), (negmr, rep_negmr)], ones_row, tag="C")
            build_y(ctx, x_in, rep_rstd, rep_negmr, y, xbuf)

            with ExitStack() as es2:
                wt = [es2.enter_context(nc.sbuf_tensor(f"wCqk{k}", [128, 2 * D], BF16))
                      for k in range(NCH)]
                load_weight_chunks(ctx, Wqkv_d[:, 0:2 * D], wt)
                wv_done = ctx.val("wld")

                @ctx.on("tensor")
                def _(t):
                    t.wait_ge("wld", wv_done)

                def evac_qk(m, g, ps, pv):
                    @ctx.on("scalar")
                    def _(sx):
                        sx.wait_ge("pe", pv)
                        if m < H:
                            sx.activation(out=QT[m][:, g * TG:(g + 1) * TG], in_=ps[:, :],
                                          func=AF.Identity, scale=qsc[:, 0:1])
                        else:
                            sx.copy(KT[m - H][:, g * TG:(g + 1) * TG], ps[:, :])
                    return ("sc", ctx.val("sc"))
                proj_1024(ctx, wt, y, evac_qk, psname="qkC", mchunks=16)

            with ExitStack() as es2:
                wt = [es2.enter_context(nc.sbuf_tensor(f"wCv{k}", [128, D], BF16))
                      for k in range(NCH)]
                load_weight_chunks(ctx, Wqkv_d[:, 2 * D:3 * D], wt)
                wv_done = ctx.val("wld")

                @ctx.on("tensor")
                def _(t):
                    t.wait_ge("wld", wv_done)

                with nc.psum_tensor("vpsC0", [128, TG], F32) as p0, \
                     nc.psum_tensor("vpsC1", [128, TG], F32) as p1:
                    banks = [p0, p1]
                    bank_free = [None, None]
                    bi = 0
                    for tt in range(16):
                        for ng in range(2):
                            ps = banks[bi]
                            gate = bank_free[bi]

                            @ctx.on("tensor")
                            def _(t, tt=tt, ng=ng, ps=ps, gate=gate):
                                if gate is not None:
                                    t.wait_ge(gate[0], gate[1])
                                for k in range(NCH):
                                    t.matmul(ps[:, :], y[k][:, tt * 128:(tt + 1) * 128],
                                             wt[k][:, ng * TG:(ng + 1) * TG],
                                             start=(k == 0), stop=(k == NCH - 1))
                            pv = ctx.val("pe")

                            @ctx.on("scalar")
                            def _(sx, tt=tt, ng=ng, ps=ps):
                                sx.wait_ge("pe", pv)
                                sx.copy(V[tt][:, ng * TG:(ng + 1) * TG], ps[:, :])
                            bank_free[bi] = ("sc", ctx.val("sc"))
                            bi ^= 1
                barrier(ctx)

        esa = ExitStack()
        qps_r = esa.enter_context(nc.sbuf_tensor("qpsC", [1, T], BF16))
        kp_r = esa.enter_context(nc.sbuf_tensor("kpC", [1, T], BF16))
        negc_r = esa.enter_context(nc.sbuf_tensor("negcC", [1, T], BF16))
        sqt = esa.enter_context(nc.sbuf_tensor("sqtC", [128, T], BF16))
        nq = esa.enter_context(nc.sbuf_tensor("nqC", [1, T], F32))
        nk = esa.enter_context(nc.sbuf_tensor("nkC", [1, T], F32))
        qpf = esa.enter_context(nc.sbuf_tensor("qpfC", [1, T], F32))
        kpf = esa.enter_context(nc.sbuf_tensor("kpfC", [1, T], F32))
        sc2 = esa.enter_context(nc.sbuf_tensor("scal2C", [1, 4], F32))
        crow = esa.enter_context(nc.sbuf_tensor("crowC", [1, T], F32))
        wch = [esa.enter_context(nc.sbuf_tensor(f"wchC{i}", [128, TG], BF16)) for i in range(2)]
        rsum = esa.enter_context(nc.sbuf_tensor("rsumC", [1, TG], F32))
        repn = esa.enter_context(nc.sbuf_tensor("repnC", [128, TG], F32))
        aot = [esa.enter_context(nc.sbuf_tensor(f"aotC{i}", [128, TG], BF16)) for i in range(2)]

        with nc.psum_tensor("wpsC0", [128, TG], F32) as w0, \
             nc.psum_tensor("wpsC1", [128, TG], F32) as w1, \
             nc.psum_tensor("sumpsC", [1, TG], F32) as sps, \
             nc.psum_tensor("avpsC", [128, TG], F32) as avps, \
             nc.psum_tensor("repnpsC", [128, TG], F32) as rps3, \
             nc.psum_tensor("rowpsC", [1, TG], F32) as rps, \
             nc.psum_tensor("rowps2C", [1, TG], F32) as rps2:
            wps = [w0, w1]
            wp_gate = [None, None]
            av_gate = None
            norm_gate = None
            row_gate = None
            aot_gate = [None, None]
            for h in range(H):
                sc_gate = ctx.val("sc")
                dv_gate = ctx.val("dv")
                pe_gate = ctx.val("pe")

                @ctx.on("scalar")
                def _(sx, h=h, row_gate=row_gate):
                    if row_gate is not None:
                        sx.wait_ge(row_gate[0], row_gate[1])
                    sx.activation(out=sqt[:, :], in_=QT[h][:, :], func=AF.Square)
                sc_sq = ctx.val("sc")

                prev_cp = ctx.val("dv")
                for g in range(NTG):
                    @ctx.on("tensor")
                    def _(t, h=h, g=g, prev_cp=prev_cp):
                        if g == 0:
                            t.wait_ge("sc", sc_gate)
                            t.wait_ge("dv", dv_gate)
                        else:
                            t.wait_ge("dv", prev_cp)
                        t.matmul(rps[0:1, :], dirs[0:3, h:h + 1],
                                 QT[h][0:3, g * TG:(g + 1) * TG], start=True, stop=True)
                        t.matmul(rps2[0:1, :], dirs[0:3, h:h + 1],
                                 KT[h][0:3, g * TG:(g + 1) * TG], start=True, stop=True)
                    pv = ctx.val("pe")

                    @ctx.on("vector")
                    def _(v, h=h, g=g):
                        v.wait_ge("pe", pv)
                        v.tensor_copy(qpf[0:1, g * TG:(g + 1) * TG], rps[0:1, :])
                        v.tensor_copy(kpf[0:1, g * TG:(g + 1) * TG], rps2[0:1, :])
                    prev_cp = ctx.val("dv")
                qk_cp = prev_cp
                dv_rows = ctx.val("dv")

                @ctx.on("vector")
                def _(v, h=h):
                    v.wait_ge("dv", dv_rows)
                    v.tensor_copy(kp_r[0:1, :], kpf[0:1, :])
                    v.tensor_scalar(sc2[0:1, 2:3], hsrow[0:1, h:h + 1],
                                    float(np.sqrt(128.0)), None, ALU.mult)
                    v.tensor_scalar(qps_r[0:1, :], qpf[0:1, :], sc2[0:1, 2:3], None, ALU.mult)
                dv1 = ctx.val("dv")

                prev_cp = ctx.val("sc")
                for g in range(NTG):
                    @ctx.on("tensor")
                    def _(t, h=h, g=g, prev_cp=prev_cp):
                        if g == 0:
                            t.wait_ge("sc", sc_sq)
                            t.wait_ge("dv", qk_cp)
                        else:
                            t.wait_ge("sc", prev_cp)
                        t.matmul(rps[0:1, :], onescol_bf[:, 0:1],
                                 sqt[:, g * TG:(g + 1) * TG], start=True, stop=True)
                    pv = ctx.val("pe")

                    @ctx.on("scalar")
                    def _(sx, h=h, g=g):
                        sx.wait_ge("pe", pv)
                        sx.activation(out=nq[0:1, g * TG:(g + 1) * TG], in_=rps[0:1, :],
                                      func=AF.Sqrt)
                    prev_cp = ctx.val("sc")
                sc_nq = ctx.val("sc")

                @ctx.on("scalar")
                def _(sx, h=h):
                    sx.activation(out=sqt[:, :], in_=KT[h][:, :], func=AF.Square)
                sc_sqk = ctx.val("sc")

                prev_cp = ctx.val("sc")
                for g in range(NTG):
                    @ctx.on("tensor")
                    def _(t, h=h, g=g, prev_cp=prev_cp):
                        if g == 0:
                            t.wait_ge("sc", sc_sqk)
                            t.wait_ge("dv", qk_cp)
                        else:
                            t.wait_ge("sc", prev_cp)
                        t.matmul(rps2[0:1, :], onescol_bf[:, 0:1],
                                 sqt[:, g * TG:(g + 1) * TG], start=True, stop=True)
                    pv = ctx.val("pe")

                    @ctx.on("scalar")
                    def _(sx, h=h, g=g):
                        sx.wait_ge("pe", pv)
                        sx.activation(out=nk[0:1, g * TG:(g + 1) * TG], in_=rps2[0:1, :],
                                      func=AF.Sqrt)
                    prev_cp = ctx.val("sc")
                row_gate = ("pe", ctx.val("pe"))
                sc_nk = ctx.val("sc")

                @ctx.on("vector")
                def _(v, h=h):
                    v.wait_ge("sc", sc_nk)
                    v.reduce_max(sc2[0:1, 0:1], nk[0:1, :], axis=AX)
                    v.reduce_max(sc2[0:1, 1:2], kpf[0:1, :], axis=AX,
                                 apply_absolute_value=True)
                    v.tensor_scalar(crow[0:1, :], nq[0:1, :], sc2[0:1, 0:1], None, ALU.mult)
                    v.tensor_scalar(qpf[0:1, :], qps_r[0:1, :], sc2[0:1, 1:2], None, ALU.mult)
                    v.tensor_scalar(kpf[0:1, :], qpf[0:1, :], -1.0, None, ALU.mult)
                    v.tensor_tensor(qpf[0:1, :], qpf[0:1, :], kpf[0:1, :], ALU.max)
                    v.tensor_tensor(crow[0:1, :], crow[0:1, :], qpf[0:1, :], ALU.add)
                    v.tensor_scalar(crow[0:1, :], crow[0:1, :], 1.05, 0.5, ALU.mult, ALU.add)
                    v.tensor_scalar(negc_r[0:1, :], crow[0:1, :], -1.0, None, ALU.mult)
                dv_c = ctx.val("dv")

                for qg in range(NTG):
                    nkc = 4 * (qg + 1)
                    for kc in range(nkc):
                        ps = wps[kc % 2]
                        gate = wp_gate[kc % 2]

                        @ctx.on("tensor")
                        def _(t, h=h, qg=qg, kc=kc, ps=ps, gate=gate):
                            if kc == 0 and qg == 0:
                                t.wait_ge("dv", dv_c)
                            if gate is not None:
                                t.wait_ge(gate[0], gate[1])
                            t.matmul(ps[:, :], KT[h][:, kc * 128:(kc + 1) * 128],
                                     QT[h][:, qg * TG:(qg + 1) * TG], start=True, stop=False)
                            t.matmul(ps[:, :], kp_r[0:1, kc * 128:(kc + 1) * 128],
                                     qps_r[0:1, qg * TG:(qg + 1) * TG], start=False, stop=False)
                            t.matmul(ps[:, :], ones_bf[0:1, 0:128],
                                     negc_r[0:1, qg * TG:(qg + 1) * TG], start=False, stop=True)
                        pv = ctx.val("pe")

                        @ctx.on("scalar")
                        def _(sx, kc=kc, ps=ps):
                            sx.wait_ge("pe", pv)
                            sx.activation(out=wch[kc % 2][:, :], in_=ps[:, :], func=AF.Exp)
                        scv = ctx.val("sc")
                        wgate = ("sc", scv)

                        j = kc - 4 * qg
                        if j >= 0:
                            @ctx.on("vector")
                            def _(v, kc=kc, j=j):
                                v.wait_ge("sc", scv)
                                v.tensor_tensor(wch[kc % 2][:, :], wch[kc % 2][:, :],
                                                cmask[j][:, :], ALU.mult)
                            wgate = ("dv", ctx.val("dv"))

                        @ctx.on("tensor")
                        def _(t, h=h, qg=qg, kc=kc, wgate=wgate, av_gate=av_gate, nkc=nkc):
                            t.wait_ge(wgate[0], wgate[1])
                            if kc == 0 and av_gate is not None:
                                t.wait_ge(av_gate[0], av_gate[1])
                            t.matmul(sps[0:1, :], onescol_bf[:, 0:1], wch[kc % 2][:, :],
                                     start=(kc == 0), stop=(kc == nkc - 1))
                            t.matmul(avps[:, :], V[kc][:, h * 128:(h + 1) * 128],
                                     wch[kc % 2][:, :], start=(kc == 0), stop=(kc == nkc - 1))
                        wp_gate[kc % 2] = ("pe", ctx.val("pe"))
                    pvs = ctx.val("pe")

                    @ctx.on("vector")
                    def _(v, norm_gate=norm_gate):
                        v.wait_ge("pe", pvs)
                        if norm_gate is not None:
                            v.wait_ge(norm_gate[0], norm_gate[1])
                        v.reciprocal(rsum[0:1, :], sps[0:1, :])
                    dvr = ctx.val("dv")

                    @ctx.on("tensor")
                    def _(t):
                        t.wait_ge("dv", dvr)
                        t.matmul(rps3[:, :], ones_row[0:1, 0:128], rsum[0:1, :],
                                 start=True, stop=True)
                    pvr = ctx.val("pe")

                    @ctx.on("scalar")
                    def _(sx):
                        sx.wait_ge("pe", pvr)
                        sx.copy(repn[:, :], rps3[:, :])
                    scr = ctx.val("sc")

                    bi = (h * NTG + qg) % 2

                    @ctx.on("vector")
                    def _(v, bi=bi, gate=aot_gate[bi]):
                        v.wait_ge("sc", scr)
                        if gate is not None:
                            v.wait_ge(gate[0], gate[1])
                        v.tensor_tensor(aot[bi][:, :], avps[:, :], repn[:, :], ALU.mult)
                    av_gate = ("dv", ctx.val("dv"))
                    norm_gate = ("pe", pvr)

                    @ctx.on("gpsimd")
                    def _(g2, h=h, qg=qg, bi=bi, dvv=av_gate[1]):
                        g2.wait_ge("dv", dvv)
                        g2.dma("sta2" if bi == 0 else "stb2",
                               out=ao_dram[h * 128:(h + 1) * 128, qg * TG:(qg + 1) * TG],
                               in_=aot[bi][:, :])
                    aot_gate[bi] = ("sta2" if bi == 0 else "stb2",
                                    ctx.val("sta2" if bi == 0 else "stb2"))
        barrier(ctx, extra=("sta2", "stb2"))
        esa.close()

    ao = [ctx.sb(f"aoC{k}", [128, T], BF16) for k in range(NCH)]
    xres = [ctx.sb(f"xresC{i}", [128, TG], F32) for i in range(2)]
    otmp = ctx.sb("otmpC", [128, TG], F32)
    load_weight_chunks(ctx, ao_dram, ao, sem="wld")
    store_gates = []
    with ExitStack() as es:
        wo = [es.enter_context(nc.sbuf_tensor(f"woC{k}", [128, D], BF16))
              for k in range(NCH)]
        load_weight_chunks(ctx, Wout_d, wo)
        wo_done = ctx.val("wld")

        @ctx.on("tensor")
        def _(t):
            t.wait_ge("wld", wo_done)

        def evac_o(m, g, ps, pv):
            idx = m * NTG + g

            @ctx.on("vector")
            def _(v):
                v.wait_ge("pe", pv)
                v.tensor_copy(otmp[:, :], ps[:, :])
            dvv = ctx.val("dv")
            dvv2 = residual_evac(ctx, x_in, x_out, m, g, idx, otmp[:, :],
                                 one_col[:, 0:1], xres, store_gates, dvv)
            return ("dv", dvv2)
        proj_1024(ctx, wo, ao, evac_o, psname="oC")
    wait_stores(ctx)


def build_l2attn():
    nc = bass.Bass("TRN2", target_bir_lowering=False, debug=False)
    x_in = nc.dram_tensor("x_in", [D, T], F32, kind="ExternalInput").ap()
    Wqkv = nc.dram_tensor("Wqkv", [D, 3 * D], F32, kind="ExternalInput").ap()
    Wout = nc.dram_tensor("Wout", [D, D], F32, kind="ExternalInput").ap()
    hs = nc.dram_tensor("hs", [1, 8], F32, kind="ExternalInput").ap()
    dirs = nc.dram_tensor("dirs", [3, 8], F32, kind="ExternalInput").ap()
    cmask = nc.dram_tensor("cmask", [4, 128, TG], F32, kind="ExternalInput").ap()
    x_out = nc.dram_tensor("x_out", [D, T], F32, kind="ExternalOutput").ap()
    ao_dram = nc.dram_tensor("ao_scr", [D, T], BF16).ap()
    with ExitStack() as stack:
        block = stack.enter_context(nc.Block())
        ctx = Ctx(nc, block, stack)
        emit_l2attn(ctx, x_in, x_out, Wqkv, Wout, hs, dirs, cmask, ao_dram)
    return nc


def l2attn_maps(Wqkv, Wout, head_scales, head_dirs):
    j = np.arange(4)[:, None, None] * 128
    kp = np.arange(128)[None, :, None]
    qf = np.arange(TG)[None, None, :]
    cmask = ((j + kp) <= qf).astype(np.float32)
    return {
        "Wqkv": np.ascontiguousarray(np.asarray(Wqkv)),
        "Wout": np.ascontiguousarray(np.asarray(Wout)),
        "hs": np.asarray(head_scales).reshape(1, 8).copy(),
        "dirs": np.ascontiguousarray(np.asarray(head_dirs).T),
        "cmask": cmask,
    }


def run_l2attn(xT, Wqkv, Wout, head_scales, head_dirs):
    nc = build_l2attn()
    wmap = l2attn_maps(Wqkv, Wout, head_scales, head_dirs)
    in_maps = [dict(wmap, x_in=np.ascontiguousarray(xT[b])) for b in range(8)]
    res = run_bass_kernel_spmd(nc, in_maps, core_ids=list(range(8)))
    return np.stack([r["x_out"] for r in res.results])



def emit_l0attn(ctx, x_in, x_out, W1_d, WoT_d, bo_d, ident_d, ao_dram, mn_dram):
    nc = ctx.nc
    H = 16
    SC = 0.125
    ones_row = ctx.sb("onesrowS", [1, 128], F32)
    ones_bf = ctx.sb("onesbfS", [1, TG], BF16)
    onescol_bf = ctx.sb("onescolS", [128, 1], BF16)
    one_col = ctx.sb("onecolS", [128, 1], F32)
    nqsc = ctx.sb("nqscS", [128, 1], F32)
    two_col = ctx.sb("twocolS", [128, 1], F32)
    identf = ctx.sb("identfS", [128, 128], F32)
    borow = ctx.sb("borowS", [1, D], BF16)

    QN = [ctx.sb(f"qnS{k}", [128, T], BF16) for k in range(NCH)]
    KT = [ctx.sb(f"ktS{k}", [128, T], BF16) for k in range(NCH)]
    V = [ctx.sb(f"vtlS{k}", [128, D], BF16) for k in range(16)]

    @ctx.on("vector")
    def _(v):
        v.memset(ones_row[:, :], 1.0)
        v.memset(ones_bf[:, :], 1.0)
        v.memset(onescol_bf[:, :], 1.0)
        v.memset(one_col[:, :], 1.0)
        v.memset(nqsc[:, :], -SC)
        v.memset(two_col[:, :], 2.0)

    @ctx.on("gpsimd")
    def _(g):
        g.dma("wld", out=identf[:, :], in_=ident_d)
        g.dma("wld", out=borow[:, :], in_=bo_d)

    with ExitStack() as es:
        rstd = es.enter_context(nc.sbuf_tensor("rstdS", [1, T], F32))
        negmr = es.enter_context(nc.sbuf_tensor("negmrS", [1, T], F32))
        rep_rstd = es.enter_context(nc.sbuf_tensor("rep_rstdS", [128, T], BF16))
        rep_negmr = es.enter_context(nc.sbuf_tensor("rep_negmrS", [128, T], BF16))
        xbuf = [es.enter_context(nc.sbuf_tensor(f"xbS{i}", [128, T], F32)) for i in range(2)]
        y = [es.enter_context(nc.sbuf_tensor(f"yS{k}", [128, T], BF16)) for k in range(NCH)]

        ln_rows(ctx, x_in, rstd, negmr, tag="S", xbuf=xbuf)
        replicate_rows(ctx, [(rstd, rep_rstd), (negmr, rep_negmr)], ones_row, tag="S")
        build_y(ctx, x_in, rep_rstd, rep_negmr, y, xbuf)

        with ExitStack() as es2:
            wt = [es2.enter_context(nc.sbuf_tensor(f"wSqk{k}", [128, 2 * D], BF16))
                  for k in range(NCH)]
            load_weight_chunks(ctx, W1_d[:, 0:2 * D], wt)
            wv_done = ctx.val("wld")

            @ctx.on("tensor")
            def _(t):
                t.wait_ge("wld", wv_done)

            def evac_qk(m, g, ps, pv):
                @ctx.on("scalar")
                def _(sx):
                    sx.wait_ge("pe", pv)
                    if m < NCH:
                        sx.activation(out=QN[m][:, g * TG:(g + 1) * TG], in_=ps[:, :],
                                      func=AF.Identity, scale=nqsc[:, 0:1])
                    else:
                        sx.copy(KT[m - NCH][:, g * TG:(g + 1) * TG], ps[:, :])
                return ("sc", ctx.val("sc"))
            proj_1024(ctx, wt, y, evac_qk, psname="qkS", mchunks=16)

        with ExitStack() as es2:
            wt = [es2.enter_context(nc.sbuf_tensor(f"wSv{k}", [128, D], BF16))
                  for k in range(NCH)]
            load_weight_chunks(ctx, W1_d[:, 2 * D:3 * D], wt)
            wv_done = ctx.val("wld")

            @ctx.on("tensor")
            def _(t):
                t.wait_ge("wld", wv_done)

            with nc.psum_tensor("vpsS0", [128, TG], F32) as p0, \
                 nc.psum_tensor("vpsS1", [128, TG], F32) as p1:
                banks = [p0, p1]
                bank_free = [None, None]
                bi = 0
                for tt in range(16):
                    for ng in range(2):
                        ps = banks[bi]
                        gate = bank_free[bi]

                        @ctx.on("tensor")
                        def _(t, tt=tt, ng=ng, ps=ps, gate=gate):
                            if gate is not None:
                                t.wait_ge(gate[0], gate[1])
                            for k in range(NCH):
                                t.matmul(ps[:, :], y[k][:, tt * 128:(tt + 1) * 128],
                                         wt[k][:, ng * TG:(ng + 1) * TG],
                                         start=(k == 0), stop=(k == NCH - 1))
                        pv = ctx.val("pe")

                        @ctx.on("scalar")
                        def _(sx, tt=tt, ng=ng, ps=ps):
                            sx.wait_ge("pe", pv)
                            sx.copy(V[tt][:, ng * TG:(ng + 1) * TG], ps[:, :])
                        bank_free[bi] = ("sc", ctx.val("sc"))
                        bi ^= 1
            barrier(ctx)

    esa = ExitStack()
    mncol = esa.enter_context(nc.sbuf_tensor("mncolS", [128, 16], F32))
    mnsb = esa.enter_context(nc.sbuf_tensor("mnsbS", [16, 128], F32))
    mnrow = esa.enter_context(nc.sbuf_tensor("mnrowS", [1, T], F32))
    mp1 = esa.enter_context(nc.sbuf_tensor("mp1S", [1, T], F32))
    bt = esa.enter_context(nc.sbuf_tensor("btS", [128, T], BF16))
    b2t = esa.enter_context(nc.sbuf_tensor("b2tS", [128, T], BF16))
    gt = esa.enter_context(nc.sbuf_tensor("gtS", [128, T], BF16))
    p2t = esa.enter_context(nc.sbuf_tensor("p2tS", [128, T], F32))
    wti = esa.enter_context(nc.sbuf_tensor("wtiS", [128, T], BF16))
    rsum = esa.enter_context(nc.sbuf_tensor("rsumS", [1, T], F32))
    repn = esa.enter_context(nc.sbuf_tensor("repnS", [64, T], F32))
    aot = [esa.enter_context(nc.sbuf_tensor(f"aotS{i}", [64, T], BF16)) for i in range(2)]
    aot_gate = [None, None]
    vh65 = [esa.enter_context(nc.sbuf_tensor(f"vh65S{i}", [128, 65], BF16)) for i in range(2)]
    vh_gate = [None, None]

    @ctx.on("vector")
    def _(v):
        v.memset(vh65[0][:, 64:65], 1.0)
        v.memset(vh65[1][:, 64:65], 1.0)

    tp_gate = None
    for h in range(H):
        hc, ho = h // 2, (h % 2) * 64
        with nc.psum_tensor(f"apsS{h}", [128, T], F32) as aps, \
             nc.psum_tensor(f"tpsS{h}", [16, 128], F32) as tps:
            red_gate = [ctx.val("dv")]
            for qt in range(16):
                @ctx.on("tensor")
                def _(t, h=h, qt=qt, rg=red_gate[0]):
                    t.wait_ge("dv", rg)
                    for gk in range(NTG):
                        t.matmul(aps[:, gk * TG:(gk + 1) * TG],
                                 QN[hc][ho:ho + 64, qt * 128:(qt + 1) * 128],
                                 KT[hc][ho:ho + 64, gk * TG:(gk + 1) * TG],
                                 start=True, stop=True, tile_position=(ho, 0))
                pv = ctx.val("pe")

                @ctx.on("vector")
                def _(v, h=h, qt=qt):
                    v.wait_ge("pe", pv)
                    v.tensor_reduce(mncol[:, qt:qt + 1], aps[:, :], axis=AX, op=ALU.min)
                red_gate[0] = ctx.val("dv")
            dvmn = ctx.val("dv")

            @ctx.on("tensor")
            def _(t, h=h, tp_gate=tp_gate):
                t.wait_ge("dv", dvmn)
                if tp_gate is not None:
                    t.wait_ge(tp_gate[0], tp_gate[1])
                t.transpose(tps[:, :], mncol[:, 0:16], identf[:, :])
            pvt = ctx.val("pe")

            @ctx.on("scalar")
            def _(sx, h=h):
                sx.wait_ge("pe", pvt)
                sx.copy(mnsb[0:16, :], tps[0:16, :])
            sct = ctx.val("sc")

            @ctx.on("gpsimd")
            def _(g2, h=h):
                g2.wait_ge("sc", sct)
                g2.dma("mnd", out=mn_dram[:, :], in_=mnsb[0:16, :])
                g2.wait_ge("mnd", ctx.cnt["mnd"])
                g2.dma("mnd", out=mnrow[0:1, :],
                       in_=mn_dram.rearrange("a b -> (a b)")[None, :])
            mnd = ctx.val("mnd")

            @ctx.on("vector")
            def _(v, h=h):
                v.wait_ge("mnd", mnd)
                v.tensor_scalar(mp1[0:1, :], mnrow[0:1, :], -1.0, 1.0, ALU.mult, ALU.add)
            dvmp = ctx.val("dv")
            tp_gate = ("dv", dvmp)

        with nc.psum_tensor(f"avpsS{h}", [65, T], F32) as avps:
          with nc.psum_tensor(f"bpsS{h}", [128, T], F32) as bps:
            bp_gate = None
            for kc in range(16):
                @ctx.on("tensor")
                def _(t, h=h, kc=kc, bp_gate=bp_gate):
                    if kc == 0:
                        t.wait_ge("dv", dvmp)
                    if bp_gate is not None:
                        t.wait_ge(bp_gate[0], bp_gate[1])
                    for gq in range(NTG):
                        t.matmul(bps[:, gq * TG:(gq + 1) * TG],
                                 KT[hc][ho:ho + 64, kc * 128:(kc + 1) * 128],
                                 QN[hc][ho:ho + 64, gq * TG:(gq + 1) * TG],
                                 start=True, stop=False, tile_position=(ho, 0))
                        t.matmul(bps[:, gq * TG:(gq + 1) * TG],
                                 ones_row[0:1, 0:128],
                                 mp1[0:1, gq * TG:(gq + 1) * TG],
                                 start=False, stop=True)
                pv = ctx.val("pe")

                @ctx.on("scalar")
                def _(sx, h=h, kc=kc):
                    sx.wait_ge("pe", pv)
                    sx.copy(bt[:, :], bps[:, :])
                bp_gate = ("sc", ctx.val("sc"))

                @ctx.on("vector")
                def _(v, h=h, kc=kc, scv=bp_gate[1]):
                    v.wait_ge("sc", scv)
                    v.tensor_tensor(b2t[:, :], bt[:, :], bt[:, :], ALU.mult)
                dvb2 = ctx.val("dv")

                @ctx.on("vector")
                def _(v, h=h, kc=kc):
                    v.scalar_tensor_tensor(gt[:, :], b2t[:, :], 3.0, bt[:, :],
                                            ALU.add, ALU.mult)
                gpg = ctx.val("dv")

                @ctx.on("scalar")
                def _(sx, h=h, kc=kc):
                    sx.wait_ge("dv", gpg)
                    sx.activation(out=p2t[:, :], in_=gt[:, :], func=AF.Identity,
                                  bias=two_col[:, 0:1])
                scp = ctx.val("sc")

                @ctx.on("vector")
                def _(v, h=h, kc=kc):
                    v.wait_ge("sc", scp)
                    with ctx.nc.allow_low_precision(reason="w is a bf16 matmul operand"):
                        v.reciprocal(wti[:, :], p2t[:, :])
                dvw = ctx.val("dv")

                vbi = kc % 2

                @ctx.on("vector")
                def _(v, h=h, kc=kc, vbi=vbi, gate=vh_gate[vbi]):
                    if gate is not None:
                        v.wait_ge(gate[0], gate[1])
                    v.tensor_copy(vh65[vbi][:, 0:64], V[kc][:, h * 64:(h + 1) * 64])
                dvvh = ctx.val("dv")

                @ctx.on("tensor")
                def _(t, h=h, kc=kc, vbi=vbi):
                    t.wait_ge("dv", dvw)
                    t.wait_ge("dv", dvvh)
                    for gq in range(NTG):
                        t.matmul(avps[0:65, gq * TG:(gq + 1) * TG],
                                 vh65[vbi][:, 0:65],
                                 wti[:, gq * TG:(gq + 1) * TG],
                                 start=(kc == 0), stop=(kc == 15))
                vh_gate[vbi] = ("pe", ctx.val("pe"))
          pvav = ctx.val("pe")

          @ctx.on("vector")
          def _(v, h=h):
              v.wait_ge("pe", pvav)
              v.reciprocal(rsum[0:1, :], avps[64:65, :])
          dvr = ctx.val("dv")

          with nc.psum_tensor(f"rpsS{h}", [64, T], F32) as rps:
              @ctx.on("tensor")
              def _(t, h=h):
                  t.wait_ge("dv", dvr)
                  for gq in range(NTG):
                      t.matmul(rps[:, gq * TG:(gq + 1) * TG], ones_row[0:1, 0:64],
                               rsum[0:1, gq * TG:(gq + 1) * TG], start=True, stop=True)
              pvr = ctx.val("pe")

              @ctx.on("scalar")
              def _(sx, h=h):
                  sx.wait_ge("pe", pvr)
                  sx.copy(repn[0:64, :], rps[:, :])
              scr = ctx.val("sc")

              bi = h % 2

              @ctx.on("vector")
              def _(v, h=h, bi=bi, gate=aot_gate[bi]):
                  v.wait_ge("sc", scr)
                  if gate is not None:
                      v.wait_ge(gate[0], gate[1])
                  v.tensor_tensor(aot[bi][0:64, :], avps[0:64, :], repn[0:64, :],
                                  ALU.mult)
              dvao = ctx.val("dv")

              @ctx.on("gpsimd")
              def _(g2, h=h, bi=bi):
                  g2.wait_ge("dv", dvao)
                  g2.dma("sta2" if bi == 0 else "stb2",
                         out=ao_dram[h * 64:(h + 1) * 64, :], in_=aot[bi][0:64, :])
              aot_gate[bi] = ("sta2" if bi == 0 else "stb2",
                              ctx.val("sta2" if bi == 0 else "stb2"))
    esa.close()
    barrier(ctx, extra=("sta2", "stb2"))

    ao = [ctx.sb(f"aoS{k}", [128, T], BF16) for k in range(NCH)]
    load_weight_chunks(ctx, ao_dram, ao, sem="wld")

    xres = [ctx.sb(f"xresS{i}", [128, TG], F32) for i in range(2)]
    otmp = ctx.sb("otmpS", [128, TG], F32)
    store_gates = []
    with ExitStack() as es:
        wo = [es.enter_context(nc.sbuf_tensor(f"woS{k}", [128, D], BF16))
              for k in range(NCH)]
        load_weight_chunks(ctx, WoT_d, wo)
        wo_done = ctx.val("wld")

        @ctx.on("tensor")
        def _(t):
            t.wait_ge("wld", wo_done)

        with nc.psum_tensor("oSps0", [128, TG], F32) as p0, \
             nc.psum_tensor("oSps1", [128, TG], F32) as p1:
            banks = [p0, p1]
            bank_free = [None, None]
            bi = 0
            for m in range(NCH):
                for g in range(NTG):
                    ps = banks[bi]
                    gate = bank_free[bi]

                    @ctx.on("tensor")
                    def _(t, m=m, g=g, ps=ps, gate=gate):
                        if gate is not None:
                            t.wait_ge(gate[0], gate[1])
                        for k in range(NCH):
                            t.matmul(ps[:, :], wo[k][:, m * 128:(m + 1) * 128],
                                     ao[k][:, g * TG:(g + 1) * TG],
                                     start=(k == 0), stop=False)
                        t.matmul(ps[:, :], borow[0:1, m * 128:(m + 1) * 128],
                                 ones_bf[0:1, :], start=False, stop=True)
                    pv = ctx.val("pe")
                    idx = m * NTG + g

                    @ctx.on("vector")
                    def _(v, ps=ps):
                        v.wait_ge("pe", pv)
                        v.tensor_copy(otmp[:, :], ps[:, :])
                    dvv = ctx.val("dv")
                    dvv2 = residual_evac(ctx, x_in, x_out, m, g, idx, otmp[:, :],
                                         one_col[:, 0:1], xres, store_gates, dvv)
                    bank_free[bi] = ("dv", dvv2)
                    bi ^= 1
    wait_stores(ctx)


def build_l0attn():
    nc = bass.Bass("TRN2", target_bir_lowering=False, debug=False)
    x_in = nc.dram_tensor("x_in", [D, T], F32, kind="ExternalInput").ap()
    W1 = nc.dram_tensor("W1", [D, 3 * D], F32, kind="ExternalInput").ap()
    WoT = nc.dram_tensor("WoT", [D, D], F32, kind="ExternalInput").ap()
    bo = nc.dram_tensor("bo", [1, D], F32, kind="ExternalInput").ap()
    ident = nc.dram_tensor("ident", [128, 128], F32, kind="ExternalInput").ap()
    x_out = nc.dram_tensor("x_out", [D, T], F32, kind="ExternalOutput").ap()
    ao_dram = nc.dram_tensor("aoS_scr", [D, T], BF16).ap()
    mn_dram = nc.dram_tensor("mnS_scr", [16, 128], F32).ap()
    with ExitStack() as stack:
        block = stack.enter_context(nc.Block())
        ctx = Ctx(nc, block, stack)
        emit_l0attn(ctx, x_in, x_out, W1, WoT, bo, ident, ao_dram, mn_dram)
    return nc


def l0attn_maps(w_qkv, w_out, b_out):
    return {
        "W1": np.ascontiguousarray(np.asarray(w_qkv).T),
        "WoT": np.ascontiguousarray(np.asarray(w_out).T),
        "bo": np.asarray(b_out).reshape(1, D).copy(),
        "ident": np.eye(128, dtype=np.float32),
    }


def run_l0attn(xT, w_qkv, w_out, b_out):
    nc = build_l0attn()
    wmap = l0attn_maps(w_qkv, w_out, b_out)
    in_maps = [dict(wmap, x_in=np.ascontiguousarray(xT[b])) for b in range(8)]
    res = run_bass_kernel_spmd(nc, in_maps, core_ids=list(range(8)))
    return np.stack([r["x_out"] for r in res.results])



def emit_l3attn(ctx, x_in, x_out, Wqq_d, Wkk_d, Wv_d, Wout_d, tabs_d,
                masks_d, v_dram, ao_dram):
    nc = ctx.nc
    H = 16
    ones_row = ctx.sb("onesrowW", [1, 128], F32)
    one_col = ctx.sb("onecolW", [128, 1], F32)
    qsc = ctx.sb("qscW", [128, 1], F32)
    negc = ctx.sb("negcW", [128, 1], F32)
    tabs = [ctx.sb(f"tabW{i}", [128, T], BF16) for i in range(4)]
    masks = [ctx.sb(f"maskW{i}", [128, 128], BF16) for i in range(2)]

    QR = [ctx.sb(f"qrW{k}", [128, T], BF16) for k in range(NCH)]
    KH = [ctx.sb(f"khW{k}", [128, T], BF16) for k in range(NCH)]
    KL = [ctx.sb(f"klW{k}", [128, T], BF16) for k in range(NCH)]

    @ctx.on("vector")
    def _(v):
        v.memset(ones_row[:, :], 1.0)
        v.memset(one_col[:, :], 1.0)
        v.memset(qsc[:, :], 0.125)
        v.memset(negc[:, :], -16.0)

    @ctx.on("gpsimd")
    def _(g):
        for i in range(4):
            g.dma("wld", out=tabs[i][:, :], in_=tabs_d[i])
        for i in range(2):
            g.dma("wld", out=masks[i][:, :], in_=masks_d[i])

    with ExitStack() as es:
        rowp2 = es.enter_context(nc.sbuf_tensor("rowsW", [33, T], F32))
        rstd = rowp2[0:1, :]
        negmr = rowp2[32:33, :]
        rep_rstd = es.enter_context(nc.sbuf_tensor("rep_rstdW", [128, T], BF16))
        rep_negmr = es.enter_context(nc.sbuf_tensor("rep_negmrW", [128, T], BF16))
        xbuf = [es.enter_context(nc.sbuf_tensor(f"xbW{i}", [128, T], F32)) for i in range(2)]
        y = [es.enter_context(nc.sbuf_tensor(f"yW{k}", [128, T], BF16)) for k in range(NCH)]

        ln_rows(ctx, x_in, rstd, negmr, tag="W", xbuf=xbuf)
        replicate_rows(ctx, [(rstd, rep_rstd), (negmr, rep_negmr)], ones_row, tag="W")
        build_y(ctx, x_in, rep_rstd, rep_negmr, y, xbuf)

        with ExitStack() as es2:
            wt = [es2.enter_context(nc.sbuf_tensor(f"wWv{k}", [128, D], BF16))
                  for k in range(NCH)]
            v65 = [es2.enter_context(nc.sbuf_tensor(f"v65W{i}", [128, 1040], BF16))
                   for i in range(2)]
            load_weight_chunks(ctx, Wv_d, wt)
            wv_done = ctx.val("wld")

            @ctx.on("vector")
            def _(v):
                v.wait_ge("wld", wv_done)
                for i in range(2):
                    v.memset(v65[i][:, :].rearrange("p (h c) -> p h c", c=65)[:, :, 64:65], 1.0)
            dv65 = ctx.val("dv")

            @ctx.on("tensor")
            def _(t):
                t.wait_ge("wld", wv_done)

            with nc.psum_tensor("vpsW0", [128, TG], F32) as p0, \
                 nc.psum_tensor("vpsW1", [128, TG], F32) as p1:
                banks = [p0, p1]
                bank_free = [None, None]
                v65_gate = [None, None]
                bi = 0
                for tt in range(16):
                    for ng in range(2):
                        ps = banks[bi]
                        gate = bank_free[bi]

                        @ctx.on("tensor")
                        def _(t, tt=tt, ng=ng, ps=ps, gate=gate):
                            if gate is not None:
                                t.wait_ge(gate[0], gate[1])
                            for k in range(NCH):
                                t.matmul(ps[:, :], y[k][:, tt * 128:(tt + 1) * 128],
                                         wt[k][:, ng * TG:(ng + 1) * TG],
                                         start=(k == 0), stop=(k == NCH - 1))
                        pv = ctx.val("pe")
                        vb = tt % 2

                        @ctx.on("scalar")
                        def _(sx, tt=tt, ng=ng, ps=ps, vb=vb, g65=v65_gate[vb]):
                            sx.wait_ge("pe", pv)
                            sx.wait_ge("dv", dv65)
                            if ng == 0 and g65 is not None:
                                sx.wait_ge(g65[0], g65[1])
                            sx.copy(v65[vb][:, :].rearrange("p (h c) -> p h c", c=65)
                                    [:, 8 * ng:8 * ng + 8, 0:64],
                                    ps[:, :].rearrange("p (h c) -> p h c", c=64))
                        bank_free[bi] = ("sc", ctx.val("sc"))
                        bi ^= 1
                        if ng == 1:
                            scv = ctx.val("sc")

                            @ctx.on("gpsimd")
                            def _(g2, tt=tt, vb=vb):
                                g2.wait_ge("sc", scv)
                                g2.dma("sta2" if vb == 0 else "stb2",
                                       out=v_dram[tt * 128:(tt + 1) * 128, :],
                                       in_=v65[vb][:, :])
                            v65_gate[vb] = ("sta2" if vb == 0 else "stb2",
                                            ctx.val("sta2" if vb == 0 else "stb2"))
            barrier(ctx, extra=("sta2", "stb2"))

        for (Wd, outs) in ((Wqq_d, "q"), (Wkk_d, "k")):
          for half in range(2):
            with ExitStack() as es2:
                wt = [es2.enter_context(nc.sbuf_tensor(f"wW{outs}{half}_{k}", [128, D], BF16))
                      for k in range(NCH)]
                raw = es2.enter_context(nc.sbuf_tensor(f"rawW{outs}{half}", [128, T], BF16))
                sw = es2.enter_context(nc.sbuf_tensor(f"swW{outs}{half}", [128, T], BF16))
                tmp = es2.enter_context(nc.sbuf_tensor(f"tmpW{outs}{half}", [128, T], BF16))
                load_weight_chunks(ctx, Wd[:, half * D:(half + 1) * D], wt)
                w_done = ctx.val("wld")

                @ctx.on("tensor")
                def _(t):
                    t.wait_ge("wld", w_done)

                rope_done = [0]

                def evac_qk(m, g, ps, pv, outs=outs, half=half):
                    c, is_sw = half * 4 + m // 2, m % 2
                    dst = sw if is_sw else raw

                    @ctx.on("scalar")
                    def _(sx, ps=ps, dst=dst):
                        sx.wait_ge("pe", pv)
                        sx.wait_ge("dv", rope_done[0])
                        if outs == "q":
                            sx.activation(out=dst[:, g * TG:(g + 1) * TG], in_=ps[:, :],
                                          func=AF.Identity, scale=qsc[:, 0:1])
                        else:
                            sx.copy(dst[:, g * TG:(g + 1) * TG], ps[:, :])
                    gate = ("sc", ctx.val("sc"))
                    if is_sw and g == NTG - 1:
                        scv = ctx.val("sc")

                        @ctx.on("vector")
                        def _(v, c=c, outs=outs):
                            v.wait_ge("sc", scv)
                            if outs == "q":
                                v.tensor_tensor(tmp[:, :], sw[:, :], tabs[1][:, :], ALU.mult)
                                v.tensor_tensor(QR[c][:, :], raw[:, :], tabs[0][:, :], ALU.mult)
                                v.tensor_tensor(QR[c][:, :], QR[c][:, :], tmp[:, :], ALU.add)
                            else:
                                v.tensor_tensor(tmp[:, :], sw[:, :], tabs[1][:, :], ALU.mult)
                                v.tensor_tensor(KH[c][:, :], raw[:, :], tabs[0][:, :], ALU.mult)
                                v.tensor_tensor(KH[c][:, :], KH[c][:, :], tmp[:, :], ALU.add)
                                v.tensor_tensor(tmp[:, :], sw[:, :], tabs[3][:, :], ALU.mult)
                                v.tensor_tensor(KL[c][:, :], raw[:, :], tabs[2][:, :], ALU.mult)
                                v.tensor_tensor(KL[c][:, :], KL[c][:, :], tmp[:, :], ALU.add)
                        rope_done[0] = ctx.val("dv")
                        gate = ("dv", rope_done[0])
                    return gate
                proj_1024(ctx, wt, y, evac_qk, psname=f"qk{outs}{half}W", mchunks=8)

    V65 = [ctx.sb(f"V65W{k}", [128, 1040], BF16) for k in range(16)]
    load_weight_chunks(ctx, v_dram, V65, sem="wld")
    v_done = ctx.val("wld")

    esa = ExitStack()
    wch = [esa.enter_context(nc.sbuf_tensor(f"wchW{i}", [128, 128], BF16)) for i in range(2)]
    rsum = esa.enter_context(nc.sbuf_tensor("rsumW", [1, 128], F32))
    repn = esa.enter_context(nc.sbuf_tensor("repnW", [64, 128], F32))
    aot = [esa.enter_context(nc.sbuf_tensor(f"aotW{i}", [64, T], BF16)) for i in range(2)]
    aot_gate = [None, None]

    with nc.psum_tensor("wpsW0", [128, 128], F32) as w0, \
         nc.psum_tensor("wpsW1", [128, 128], F32) as w1, \
         nc.psum_tensor("avpsW", [65, 128], F32) as avps, \
         nc.psum_tensor("reppsW3", [64, 128], F32) as rps:
        wps = [w0, w1]
        wp_gate = [None, None]
        av_gate = None
        norm_gate = None

        @ctx.on("tensor")
        def _(t):
            t.wait_ge("wld", v_done)

        for h in range(H):
            hc, ho = h // 2, (h % 2) * 64
            bi_ao = h % 2
            for w in range(16):
                kctxs = ([w - 1] if w > 0 else []) + [w]
                nk = len(kctxs)
                for ki, kc in enumerate(kctxs):
                    own = (kc == w)
                    KT_ = KH if own else KL
                    ps = wps[ki % 2]
                    gate = wp_gate[ki % 2]

                    @ctx.on("tensor")
                    def _(t, h=h, w=w, kc=kc, ps=ps, gate=gate, own=own, KT_=KT_):
                        if gate is not None:
                            t.wait_ge(gate[0], gate[1])
                        t.matmul(ps[:, :], KT_[hc][ho:ho + 64, kc * 128:(kc + 1) * 128],
                                 QR[hc][ho:ho + 64, w * 128:(w + 1) * 128],
                                 start=True, stop=True, tile_position=(ho, 0))
                    pv = ctx.val("pe")

                    @ctx.on("scalar")
                    def _(sx, ki=ki, ps=ps):
                        sx.wait_ge("pe", pv)
                        sx.activation(out=wch[ki % 2][:, :], in_=ps[:, :], func=AF.Exp,
                                      bias=negc[:, 0:1])
                    scv = ctx.val("sc")

                    @ctx.on("vector")
                    def _(v, ki=ki, own=own):
                        v.wait_ge("sc", scv)
                        v.tensor_tensor(wch[ki % 2][:, :], wch[ki % 2][:, :],
                                        masks[0 if own else 1][:, :], ALU.mult)
                    dvm = ctx.val("dv")

                    @ctx.on("tensor")
                    def _(t, h=h, kc=kc, ki=ki, nk=nk, av_gate=av_gate):
                        t.wait_ge("dv", dvm)
                        if ki == 0 and av_gate is not None:
                            t.wait_ge(av_gate[0], av_gate[1])
                        t.matmul(avps[:, :], V65[kc][:, h * 65:(h + 1) * 65],
                                 wch[ki % 2][:, :], start=(ki == 0), stop=(ki == nk - 1))
                    wp_gate[ki % 2] = ("pe", ctx.val("pe"))
                pvav = ctx.val("pe")

                @ctx.on("vector")
                def _(v, norm_gate=norm_gate):
                    v.wait_ge("pe", pvav)
                    if norm_gate is not None:
                        v.wait_ge(norm_gate[0], norm_gate[1])
                    v.reciprocal(rsum[0:1, :], avps[64:65, :])
                dvr = ctx.val("dv")

                @ctx.on("tensor")
                def _(t):
                    t.wait_ge("dv", dvr)
                    t.matmul(rps[:, :], ones_row[0:1, 0:64], rsum[0:1, :],
                             start=True, stop=True)
                pvr = ctx.val("pe")

                @ctx.on("scalar")
                def _(sx):
                    sx.wait_ge("pe", pvr)
                    sx.copy(repn[0:64, :], rps[:, :])
                scr = ctx.val("sc")

                @ctx.on("vector")
                def _(v, h=h, w=w, bi=bi_ao, gate=aot_gate[bi_ao]):
                    v.wait_ge("sc", scr)
                    if gate is not None and w == 0:
                        v.wait_ge(gate[0], gate[1])
                    v.tensor_tensor(aot[bi][0:64, w * 128:(w + 1) * 128],
                                    avps[0:64, :], repn[0:64, :], ALU.mult)
                av_gate = ("dv", ctx.val("dv"))
                norm_gate = ("pe", pvr)
            dvao = ctx.val("dv")

            @ctx.on("gpsimd")
            def _(g2, h=h, bi=bi_ao):
                g2.wait_ge("dv", dvao)
                g2.dma("sta3" if bi == 0 else "stb3",
                       out=ao_dram[h * 64:(h + 1) * 64, :], in_=aot[bi][0:64, :])
            aot_gate[bi_ao] = ("sta3" if bi_ao == 0 else "stb3",
                               ctx.val("sta3" if bi_ao == 0 else "stb3"))
    esa.close()
    barrier(ctx, extra=("sta3", "stb3"))

    ao = [ctx.sb(f"aoW{k}", [128, T], BF16) for k in range(NCH)]
    xres = [ctx.sb(f"xresW{i}", [128, TG], F32) for i in range(2)]
    otmp = ctx.sb("otmpW", [128, TG], F32)
    load_weight_chunks(ctx, ao_dram, ao, sem="wld")
    store_gates = []
    with ExitStack() as es:
        wo = [es.enter_context(nc.sbuf_tensor(f"woW{k}", [128, D], BF16))
              for k in range(NCH)]
        load_weight_chunks(ctx, Wout_d, wo)
        wo_done = ctx.val("wld")

        @ctx.on("tensor")
        def _(t):
            t.wait_ge("wld", wo_done)

        def evac_o(m, g, ps, pv):
            idx = m * NTG + g

            @ctx.on("vector")
            def _(v):
                v.wait_ge("pe", pv)
                v.tensor_copy(otmp[:, :], ps[:, :])
            dvv = ctx.val("dv")
            dvv2 = residual_evac(ctx, x_in, x_out, m, g, idx, otmp[:, :],
                                 one_col[:, 0:1], xres, store_gates, dvv)
            return ("dv", dvv2)
        proj_1024(ctx, wo, ao, evac_o, psname="oW")
    wait_stores(ctx)


def build_l3attn():
    nc = bass.Bass("TRN2", target_bir_lowering=False, debug=False)
    x_in = nc.dram_tensor("x_in", [D, T], F32, kind="ExternalInput").ap()
    Wqq = nc.dram_tensor("Wqq", [D, 2 * D], F32, kind="ExternalInput").ap()
    Wkk = nc.dram_tensor("Wkk", [D, 2 * D], F32, kind="ExternalInput").ap()
    Wv = nc.dram_tensor("Wv", [D, D], F32, kind="ExternalInput").ap()
    Wout = nc.dram_tensor("Wout", [D, D], F32, kind="ExternalInput").ap()
    tabs = nc.dram_tensor("tabs", [4, 128, T], F32, kind="ExternalInput").ap()
    masks = nc.dram_tensor("masks", [2, 128, 128], F32, kind="ExternalInput").ap()
    x_out = nc.dram_tensor("x_out", [D, T], F32, kind="ExternalOutput").ap()
    v_dram = nc.dram_tensor("vW_scr", [T, 1040], BF16).ap()
    ao_dram = nc.dram_tensor("aoW_scr", [D, T], BF16).ap()
    with ExitStack() as stack:
        block = stack.enter_context(nc.Block())
        ctx = Ctx(nc, block, stack)
        emit_l3attn(ctx, x_in, x_out, Wqq, Wkk, Wv, Wout, tabs, masks, v_dram, ao_dram)
    return nc


def l3attn_maps(W_qkv, W_out):
    W_qkv = np.asarray(W_qkv)
    dh = 64
    d = np.arange(D)
    swap = (d // dh) * dh + np.where(d % dh < 32, d % dh + 32, d % dh - 32)
    Wq = W_qkv[:, 0:D]
    Wk = W_qkv[:, D:2 * D]
    Wv = W_qkv[:, 2 * D:3 * D]
    def interleave(W):
        Wsw = W[:, swap]
        cols = []
        for c in range(NCH):
            cols.append(W[:, c * 128:(c + 1) * 128])
            cols.append(Wsw[:, c * 128:(c + 1) * 128])
        return np.ascontiguousarray(np.concatenate(cols, axis=1))
    inv_freq = 1.0 / 10000.0 ** (np.arange(0, dh, 2, dtype=np.float64) / dh)
    pos = np.arange(256, dtype=np.float64)
    fr = pos[:, None] * inv_freq[None, :]
    fr = np.concatenate([fr, fr], axis=1)
    cos = np.cos(fr).astype(np.float32)
    sin = np.sin(fr).astype(np.float32)
    sgn = np.where(np.arange(dh) < 32, -1.0, 1.0).astype(np.float32)
    sinp = sin * sgn[None, :]
    tpos = np.tile(np.arange(128), 16)
    def table(tab2d, hi):
        rows = tab2d[(128 if hi else 0) + tpos]
        blk = rows.T
        return np.ascontiguousarray(np.concatenate([blk, blk], axis=0))
    tabs = np.stack([table(cos, True), table(sinp, True),
                     table(cos, False), table(sinp, False)])
    kp = np.arange(128)[:, None]
    qf = np.arange(128)[None, :]
    masks = np.stack([(kp <= qf).astype(np.float32), (kp >= qf).astype(np.float32)])
    return {
        "Wqq": interleave(Wq),
        "Wkk": interleave(Wk),
        "Wv": np.ascontiguousarray(Wv),
        "Wout": np.ascontiguousarray(np.asarray(W_out)),
        "tabs": tabs,
        "masks": masks,
    }


def run_l3attn(xT, W_qkv, W_out):
    nc = build_l3attn()
    wmap = l3attn_maps(W_qkv, W_out)
    in_maps = [dict(wmap, x_in=np.ascontiguousarray(xT[b])) for b in range(8)]
    res = run_bass_kernel_spmd(nc, in_maps, core_ids=list(range(8)))
    return np.stack([r["x_out"] for r in res.results])



CAP = 5120
EXP = 8
BIGF = 1.0e6
NROWT = 40
ZROW = 8 * T
YZROW = 8 * CAP


def emit_moe(ctx, x_in, x_out, Wr_d, W1_d, W3_d, W2_d, Ustrict_d, maskprev_d,
             maskexp_d, ident_d, iotasc_d, iota8_d, dr, gdbg=None):
    nc = ctx.nc
    ones_row = ctx.sb("onesrowM", [1, 128], F32)
    ones_col = ctx.sb("onescolM", [128, 1], F32)
    half = ctx.sb("halfM", [128, 1], F32)
    Ustr = ctx.sb("UstrM", [128, 128], F32)
    maskprev = ctx.sb("mprevM", [8, 1], F32)
    maskexp_rep = ctx.sb("mexpM", [128, EXP], F32)
    identb = ctx.sb("identbM", [128, 128], BF16)
    iota8 = ctx.sb("iota8M", [128, EXP], F32)
    iotasc = ctx.sb("iotascM", [128, 128], I32)

    @ctx.on("vector")
    def _(v):
        v.memset(ones_row[0:1, :], 1.0)
        v.memset(ones_col[:, :], 1.0)
        v.memset(half[:, :], 0.5)

    @ctx.on("gpsimd")
    def _(g):
        g.dma("wld", out=Ustr[:, :], in_=Ustrict_d)
        g.dma("wld", out=maskprev[:, :], in_=maskprev_d)
        g.dma("wld", out=maskexp_rep[:, :], in_=maskexp_d)
        g.dma("wld", out=identb[:, :], in_=ident_d)
        g.dma("wld", out=iota8[:, :], in_=iota8_d)
        g.dma("wld", out=iotasc[:, :], in_=iotasc_d)

    y = [ctx.sb(f"yM{k}", [128, T], BF16) for k in range(NCH)]
    with ExitStack() as es:
        rstd = es.enter_context(nc.sbuf_tensor("rstdM", [1, T], F32))
        negmr = es.enter_context(nc.sbuf_tensor("negmrM", [1, T], F32))
        rep_rstd = es.enter_context(nc.sbuf_tensor("rep_rstdM", [128, T], BF16))
        rep_negmr = es.enter_context(nc.sbuf_tensor("rep_negmrM", [128, T], BF16))
        xbuf = [es.enter_context(nc.sbuf_tensor(f"xbM{i}", [128, T], F32)) for i in range(2)]
        ln_rows(ctx, x_in, rstd, negmr, tag="M", xbuf=xbuf)
        replicate_rows(ctx, [(rstd, rep_rstd), (negmr, rep_negmr)], ones_row, tag="M")
        build_y(ctx, x_in, rep_rstd, rep_negmr, y, xbuf)

    ytsb = [ctx.sb(f"ytsbM{i}", [128, D], BF16) for i in range(2)]
    lgt = [ctx.sb(f"lgtM{i}", [128, EXP], F32) for i in range(16)]
    wr = ctx.sb("wrM", [128, NCH * EXP], BF16)
    load_weight_chunks(ctx, Wr_d, [wr[:, k * EXP:(k + 1) * EXP] for k in range(NCH)])
    wldWr = ctx.val("wld")

    with nc.psum_tensor("tpM0", [128, 128], BF16) as tp0, \
         nc.psum_tensor("tpM1", [128, 128], BF16) as tp1, \
         nc.psum_tensor("lgpsM", [128, EXP], F32) as lgps:
        tps = [tp0, tp1]
        tp_gate = [None, None]
        yt_gate = [None, None]
        lg_gate = None
        for tt in range(16):
            bi = tt % 2
            for c in range(NCH):
                pj = tps[c % 2]

                @ctx.on("tensor")
                def _(t, tt=tt, c=c, pj=pj, gate=tp_gate[c % 2], ygate=yt_gate[bi]):
                    if tt == 0 and c == 0:
                        t.wait_ge("wld", wldWr)
                    if gate is not None:
                        t.wait_ge(gate[0], gate[1])
                    if c < 2 and ygate is not None:
                        t.wait_ge(ygate[0], ygate[1])
                    t.transpose(pj[:, :], y[c][:, tt * 128:(tt + 1) * 128], identb[:, :])
                pvt = ctx.val("pe")

                @ctx.on("scalar")
                def _(sx, tt=tt, bi=bi, c=c, pj=pj):
                    sx.wait_ge("pe", pvt)
                    sx.copy(ytsb[bi][:, c * 128:(c + 1) * 128], pj[:, :])
                tp_gate[c % 2] = ("sc", ctx.val("sc"))

            @ctx.on("tensor")
            def _(t, tt=tt, gate=lg_gate):
                if gate is not None:
                    t.wait_ge(gate[0], gate[1])
                for c in range(NCH):
                    t.matmul(lgps[:, :], y[c][:, tt * 128:(tt + 1) * 128],
                             wr[:, c * EXP:(c + 1) * EXP],
                             start=(c == 0), stop=(c == NCH - 1))
            pvl = ctx.val("pe")

            @ctx.on("vector")
            def _(v, tt=tt):
                v.wait_ge("pe", pvl)
                v.tensor_scalar(lgt[tt][:, :], lgps[:, :], 10000.0, -10000.0,
                                ALU.min, ALU.max)
            lg_gate = ("dv", ctx.val("dv"))

            sc_yt = ctx.val("sc")

            @ctx.on("gpsimd")
            def _(g, tt=tt, bi=bi):
                g.wait_ge("sc", sc_yt)
                g.dma("sta2" if bi == 0 else "stb2",
                      out=dr["ytok"][tt * 128:(tt + 1) * 128, :], in_=ytsb[bi][:, :])
            yt_gate[bi] = ("sta2" if bi == 0 else "stb2",
                           ctx.val("sta2" if bi == 0 else "stb2"))

    zrow = ctx.sb("zrowM", [1, D], BF16)

    @ctx.on("vector")
    def _(v):
        v.memset(zrow[0:1, :], 0.0)
    dvz = ctx.val("dv")

    st_a, st_b = ctx.val("sta2"), ctx.val("stb2")

    @ctx.on("gpsimd")
    def _(g):
        g.wait_ge("sta2", st_a)
        g.wait_ge("stb2", st_b)
        g.wait_ge("dv", dvz)
        g.cc("AllGather", ALU.bypass, replica_groups=[list(range(8))],
             ins=[dr["ytok"]], outs=[dr["ag_y"][0:8 * T, :]])
        g.dma("sta2", out=dr["ag_y"][ZROW:ZROW + 1, :], in_=zrow[0:1, :])
    cc_y = ctx.val("cc")

    oh1 = [ctx.sb(f"oh1M{i}", [128, EXP], F32) for i in range(16)]
    oh2 = [ctx.sb(f"oh2M{i}", [128, EXP], F32) for i in range(16)]
    lpos = [ctx.sb(f"lposM{i}", [128, EXP], F32) for i in range(16)]
    pcols = [ctx.sb(f"pcolM{i}", [128, 2], F32) for i in range(16)]
    rtmp = ctx.sb("rtmpM", [128, EXP], F32)
    m1c = ctx.sb("m1cM", [128, 4], F32)
    base_row = ctx.sb("baserowM", [1, EXP], F32)

    with nc.psum_tensor("cumpsM", [128, EXP], F32) as cps, \
         nc.psum_tensor("totpsM", [1, EXP], F32) as tpsm:
        @ctx.on("vector")
        def _(v):
            v.memset(base_row[0:1, :], 0.0)
        dv0 = ctx.val("dv")
        pe_prev = None
        for tt in range(16):
            @ctx.on("vector")
            def _(v, tt=tt, pe_prev=pe_prev):
                if tt == 0:
                    v.wait_ge("dv", dv0)
                if pe_prev is not None:
                    v.wait_ge(pe_prev[0], pe_prev[1])
                v.reduce_max(m1c[:, 0:1], lgt[tt][:, :], axis=AX)
                v.tensor_scalar(oh1[tt][:, :], lgt[tt][:, :], m1c[:, 0:1], None, ALU.is_ge)
                v.scalar_tensor_tensor(rtmp[:, :], oh1[tt][:, :], -BIGF, lgt[tt][:, :],
                                       ALU.mult, ALU.add)
                v.reduce_max(m1c[:, 1:2], rtmp[:, :], axis=AX)
                v.tensor_scalar(oh2[tt][:, :], rtmp[:, :], m1c[:, 1:2], None, ALU.is_ge)
                v.tensor_scalar(m1c[:, 2:3], m1c[:, 0:1], -1.0, None, ALU.mult)
            dv1 = ctx.val("dv")

            @ctx.on("scalar")
            def _(sx, tt=tt):
                sx.wait_ge("dv", dv1)
                sx.activation(out=m1c[:, 3:4], in_=m1c[:, 1:2], func=AF.Exp,
                              bias=m1c[:, 2:3])
            sc1 = ctx.val("sc")

            @ctx.on("vector")
            def _(v, tt=tt):
                v.wait_ge("sc", sc1)
                v.tensor_scalar(rtmp[:, 0:1], m1c[:, 3:4], 1.0, None, ALU.add)
                v.reciprocal(pcols[tt][:, 0:1], rtmp[:, 0:1])
                v.tensor_tensor(pcols[tt][:, 1:2], pcols[tt][:, 0:1], m1c[:, 3:4], ALU.mult)
                v.tensor_tensor(rtmp[:, :], oh1[tt][:, :], oh2[tt][:, :], ALU.add)
            dvg = ctx.val("dv")

            @ctx.on("tensor")
            def _(t, tt=tt):
                t.wait_ge("dv", dvg)
                t.matmul(cps[:, :], Ustr[:, :], rtmp[:, :], start=True, stop=False)
                t.matmul(cps[:, :], ones_row[0:1, :], base_row[0:1, :],
                         start=False, stop=True)
                t.matmul(tpsm[0:1, :], ones_col[:, 0:1], rtmp[:, :], start=True, stop=True)
            pvc = ctx.val("pe")
            pe_prev = ("pe", pvc)

            @ctx.on("vector")
            def _(v, tt=tt):
                v.wait_ge("pe", pvc)
                v.tensor_copy(lpos[tt][:, :], cps[:, :])
                v.tensor_tensor(base_row[0:1, :], base_row[0:1, :], tpsm[0:1, :], ALU.add)
    dv_route = ctx.val("dv")

    tot_all = ctx.sb("totallM", [8, EXP], F32)
    mybase = ctx.sb("mybaseM", [1, EXP], F32)
    baserep = ctx.sb("baserepM", [128, EXP], F32)

    @ctx.on("gpsimd")
    def _(g):
        g.wait_ge("dv", dv_route)
        g.dma("sta2", out=dr["totals"][:, :], in_=base_row[0:1, :])
        g.wait_ge("sta2", ctx.cnt["sta2"])
        g.cc("AllGather", ALU.bypass, replica_groups=[list(range(8))],
             ins=[dr["totals"]], outs=[dr["ag_totals"]])
        g.wait_ge("cc", ctx.cnt["cc"])
        g.dma("wld", out=tot_all[:, :], in_=dr["ag_totals"])
    wldT = ctx.val("wld")

    with nc.psum_tensor("bpsM", [1, EXP], F32) as bps, \
         nc.psum_tensor("brepM", [128, EXP], F32) as brp:
        @ctx.on("tensor")
        def _(t):
            t.wait_ge("wld", wldT)
            t.matmul(bps[0:1, :], maskprev[:, 0:1], tot_all[:, :], start=True, stop=True)
        pvb = ctx.val("pe")

        @ctx.on("vector")
        def _(v):
            v.wait_ge("pe", pvb)
            v.tensor_copy(mybase[0:1, :], bps[0:1, :])
        dvb = ctx.val("dv")

        @ctx.on("tensor")
        def _(t):
            t.wait_ge("dv", dvb)
            t.matmul(brp[:, :], ones_row[0:1, :], mybase[0:1, :], start=True, stop=True)
        pvb2 = ctx.val("pe")

        @ctx.on("scalar")
        def _(sx):
            sx.wait_ge("pe", pvb2)
            sx.copy(baserep[:, :], brp[:, :])
    scb = ctx.val("sc")

    offs_t = [ctx.sb(f"offsM{i}", [128, EXP], F32) for i in range(16)]
    gidx = [ctx.sb(f"gidxM{i}", [128, 2], I32) for i in range(16)]
    slotm = ctx.sb("slotmM", [128, EXP], F32)
    ktmp2 = ctx.sb("ktmp2M", [128, 2], F32)

    @ctx.on("vector")
    def _(v):
        v.wait_ge("sc", scb)
        for tt in range(16):
            v.tensor_tensor(slotm[:, :], lpos[tt][:, :], baserep[:, :], ALU.add)
            v.tensor_tensor(rtmp[:, :], oh1[tt][:, :], oh2[tt][:, :], ALU.add)
            v.tensor_scalar(offs_t[tt][:, :], slotm[:, :], -BIGF, None, ALU.add)
            v.tensor_tensor(offs_t[tt][:, :], offs_t[tt][:, :], rtmp[:, :], ALU.mult)
            v.tensor_scalar(offs_t[tt][:, :], offs_t[tt][:, :], BIGF, None, ALU.add)
            for k, oh in ((0, oh1[tt]), (1, oh2[tt])):
                v.tensor_tensor(rtmp[:, :], oh[:, :], slotm[:, :], ALU.mult)
                v.reduce_sum(ktmp2[:, k:k + 1], rtmp[:, :], axis=AX)
                v.tensor_tensor(rtmp[:, :], oh[:, :], iota8[:, :], ALU.mult)
                v.reduce_sum(m1c[:, k:k + 1], rtmp[:, :], axis=AX)
                v.tensor_scalar(m1c[:, 2:3], ktmp2[:, k:k + 1], float(CAP), None, ALU.is_lt)
                v.scalar_tensor_tensor(m1c[:, 3:4], m1c[:, k:k + 1], float(CAP),
                                       ktmp2[:, k:k + 1], ALU.mult, ALU.add)
                v.tensor_scalar(m1c[:, 3:4], m1c[:, 3:4], -float(YZROW), None, ALU.add)
                v.tensor_tensor(m1c[:, 3:4], m1c[:, 3:4], m1c[:, 2:3], ALU.mult)
                v.tensor_scalar(m1c[:, 3:4], m1c[:, 3:4], float(YZROW), None, ALU.add)
                v.tensor_copy(gidx[tt][:, k:k + 1], m1c[:, 3:4])
    dvoffs = ctx.val("dv")

    @ctx.on("gpsimd")
    def _(g):
        g.wait_ge("dv", dvoffs)
        for tt in range(16):
            g.dma("sta2", out=dr["offs"][tt * 128:(tt + 1) * 128, :], in_=offs_t[tt][:, :])
        g.wait_ge("sta2", ctx.cnt["sta2"])
        g.cc("AllGather", ALU.bypass, replica_groups=[list(range(8))],
             ins=[dr["offs"]], outs=[dr["ag_offs"]])
        g.wait_ge("cc", ctx.cnt["cc"])
    cc_offs = ctx.val("cc")

    fill = ctx.sb("fillM", [128, 41], I32)
    offcol = [ctx.sb(f"offcolM{i}", [128, EXP], F32) for i in range(2)]
    selc = [ctx.sb(f"selcM{i}", [128, 1], I32) for i in range(2)]
    self32 = ctx.sb("self32M", [128, 1], F32)

    @ctx.on("vector")
    def _(v):
        v.memset(fill[:, :], ZROW)
    dvf = ctx.val("dv")

    @ctx.on("gpsimd")
    def _(g):
        g.wait_ge("dv", dvf)
        g.dma("sta2", out=dr["idxbuf"].rearrange("(p n) o -> p (n o)", p=128),
              in_=fill[:, 0:41])
    prefill_done = ctx.val("sta2")

    bc_reg = [None]

    @ctx.on("gpsimd")
    def _(g):
        bc_reg[0] = g._eng.to_reg(41 * 128 - 1)

    lda0, ldb0 = ctx.val("lda"), ctx.val("ldb")
    sel_done = [0, 0]
    sct_done = [0, 0]
    for tb in range(128):
        bi = tb % 2
        lsem = "lda" if bi == 0 else "ldb"

        @ctx.on("sync")
        def _(sy, tb=tb, bi=bi, lsem=lsem, gate=sel_done[bi]):
            if tb == 0:
                sy.wait_ge("cc", cc_offs)
            if gate:
                sy.wait_ge("dv", gate)
            sy.dma(lsem, out=offcol[bi][:, :],
                   in_=dr["ag_offs"][tb * 128:(tb + 1) * 128, :])
        lv = ctx.val(lsem)

        @ctx.on("vector")
        def _(v, tb=tb, bi=bi, lsem=lsem, lv=lv, gate=sct_done[bi]):
            v.wait_ge(lsem, lv)
            if gate:
                v.wait_ge("scta" if bi == 0 else "sctb", gate)
            v.tensor_tensor(offcol[bi][:, :], offcol[bi][:, :], maskexp_rep[:, :], ALU.mult)
            v.reduce_sum(self32[:, 0:1], offcol[bi][:, :], axis=AX)
            v.tensor_copy(selc[bi][:, 0:1], self32[:, 0:1])
        dvs = ctx.val("dv")
        sel_done[bi] = dvs

        @ctx.on("gpsimd")
        def _(g, tb=tb, bi=bi, dvs=dvs):
            if tb == 0:
                g.wait_ge("sta2", prefill_done)
            g.wait_ge("dv", dvs)
            g.idma("scta" if bi == 0 else "sctb", out=dr["idxbuf"],
                   out_offset=IndirectOffsetOnAxis(ap=selc[bi][:, 0:1], axis=0),
                   in_=iotasc[:, tb:tb + 1], in_offset=None,
                   bounds_check=bc_reg[0], oob_is_err=False)
        sct_done[bi] = ctx.val("scta" if bi == 0 else "sctb")
    sct_all = [("scta", ctx.val("scta")), ("sctb", ctx.val("sctb"))]

    barrier(ctx, extra=("scta", "sctb", "lda", "ldb", "sta2", "stb2"))

    @ctx.on("gpsimd")
    def _(g):
        g.wait_ge("cc", cc_y)
    barrier(ctx)

    idxt = [ctx.sb(f"idxtM{i}", [128, 1], I32) for i in range(NROWT)]

    @ctx.on("sync")
    def _(sy):
        for j in range(NROWT):
            sy.dma("lda", out=idxt[j][:, :], in_=dr["idxbuf"][j * 128:(j + 1) * 128, :])
    idx_done = ctx.val("lda")

    with ExitStack() as es:
        w1 = [es.enter_context(nc.sbuf_tensor(f"w1M{k}", [128, 4 * D], BF16))
              for k in range(NCH)]
        w3 = [es.enter_context(nc.sbuf_tensor(f"w3M{k}", [128, 4 * D], BF16))
              for k in range(NCH)]
        xg = [es.enter_context(nc.sbuf_tensor(f"xgM{i}", [128, D], BF16)) for i in range(2)]
        xeT = [es.enter_context(nc.sbuf_tensor(f"xeTM{k}", [128, TG], BF16))
               for k in range(NCH)]
        s1 = es.enter_context(nc.sbuf_tensor("s1M", [128, TG], BF16))
        hcat = es.enter_context(nc.sbuf_tensor("hcatM", [128, TG], BF16))
        load_weight_chunks(ctx, W1_d, w1)
        load_weight_chunks(ctx, W3_d, w3)
        wld1 = ctx.val("wld")

        with nc.psum_tensor("xtpM0", [128, 128], BF16) as xtp0, \
             nc.psum_tensor("xtpM1", [128, 128], BF16) as xtp1, \
             nc.psum_tensor("h1psM", [128, TG], F32) as h1ps, \
             nc.psum_tensor("h3psM", [128, TG], F32) as h3ps:
            xtps = [xtp0, xtp1]
            xtp_gate = [None, None]
            xg_gate = [None, None]
            hc_gate = None
            for st in range(10):
                xeT_ready = None
                for r4 in range(4):
                    j = st * 4 + r4
                    gbi = r4 % 2

                    gsem = "xgla" if gbi == 0 else "xglb"

                    @ctx.on("gpsimd")
                    def _(g, j=j, gbi=gbi, gsem=gsem, gate=xg_gate[gbi]):
                        if j == 0:
                            g.wait_ge("lda", idx_done)
                            for sn, sv in sct_all:
                                g.wait_ge(sn, sv)
                        if gate is not None:
                            g.wait_ge(gate[0], gate[1])
                        g.idma(gsem, out=xg[gbi][:, :], out_offset=None,
                               in_=dr["ag_y"],
                               in_offset=IndirectOffsetOnAxis(ap=idxt[j][:, 0:1], axis=0))
                    gv = (gsem, ctx.val(gsem))

                    for c in range(NCH):
                        pj = xtps[c % 2]

                        @ctx.on("tensor")
                        def _(t, c=c, pj=pj, gbi=gbi, gv=gv, gate=xtp_gate[c % 2]):
                            t.wait_ge(gv[0], gv[1])
                            if gate is not None:
                                t.wait_ge(gate[0], gate[1])
                            t.transpose(pj[:, :], xg[gbi][:, c * 128:(c + 1) * 128],
                                        identb[:, :])
                        pvt = ctx.val("pe")

                        @ctx.on("scalar")
                        def _(sx, c=c, r4=r4, pj=pj):
                            sx.wait_ge("pe", pvt)
                            sx.copy(xeT[c][:, r4 * 128:(r4 + 1) * 128], pj[:, :])
                        xtp_gate[c % 2] = ("sc", ctx.val("sc"))
                    xg_gate[gbi] = ("pe", ctx.val("pe"))
                xeT_ready = ctx.val("sc")

                if st == 0 and gdbg is not None:
                    @ctx.on("gpsimd")
                    def _(g, xr=ctx.val("sc")):
                        g.wait_ge("sc", xr)
                        g.dma("sta2", out=gdbg["xeT_dbg"][:, :], in_=xeT[0][:, :])

                for hidm in range(32):
                    @ctx.on("tensor")
                    def _(t, st=st, hidm=hidm, hc_gate=hc_gate):
                        if hidm == 0:
                            t.wait_ge("sc", xeT_ready)
                            t.wait_ge("wld", wld1)
                        if hc_gate is not None:
                            t.wait_ge(hc_gate[0], hc_gate[1])
                        for k in range(NCH):
                            t.matmul(h1ps[:, :], w1[k][:, hidm * 128:(hidm + 1) * 128],
                                     xeT[k][:, :], start=(k == 0), stop=(k == NCH - 1))
                        for k in range(NCH):
                            t.matmul(h3ps[:, :], w3[k][:, hidm * 128:(hidm + 1) * 128],
                                     xeT[k][:, :], start=(k == 0), stop=(k == NCH - 1))
                    pvh = ctx.val("pe")

                    @ctx.on("scalar")
                    def _(sx, hidm=hidm):
                        sx.wait_ge("pe", pvh)
                        sx.activation(out=s1[:, :], in_=h1ps[:, :], func=AF.Silu)
                    sch = ctx.val("sc")

                    @ctx.on("vector")
                    def _(v, hidm=hidm):
                        v.wait_ge("sc", sch)
                        v.tensor_tensor(hcat[:, :], s1[:, :], h3ps[:, :], ALU.mult)
                    dvh = ctx.val("dv")

                    @ctx.on("gpsimd")
                    def _(g, st=st, hidm=hidm):
                        g.wait_ge("dv", dvh)
                        g.dma("sta2" if hidm % 2 == 0 else "stb2",
                              out=dr["hbuf"][hidm * 128:(hidm + 1) * 128,
                                             st * TG:(st + 1) * TG],
                              in_=hcat[:, :])
                    hc_gate = ("sta2" if hidm % 2 == 0 else "stb2",
                               ctx.val("sta2" if hidm % 2 == 0 else "stb2"))
    barrier(ctx, extra=("sta2", "stb2", "xgla", "xglb", "wld", "lda", "ldb"))

    with ExitStack() as es:
        w2 = [es.enter_context(nc.sbuf_tensor(f"w2M{k}", [128, D], BF16))
              for k in range(32)]
        hT = [es.enter_context(nc.sbuf_tensor(f"hTM{k}", [128, TG], BF16))
              for k in range(32)]
        yesb = [es.enter_context(nc.sbuf_tensor(f"yesbM{i}", [128, D], BF16))
                for i in range(2)]
        load_weight_chunks(ctx, W2_d, w2)
        wld2 = ctx.val("wld")

        with nc.psum_tensor("yepsM0", [128, TG], F32) as ye0, \
             nc.psum_tensor("yepsM1", [128, TG], F32) as ye1:
            yeps = [ye0, ye1]
            ye_gate = [None, None]
            ye_st = [None, None]
            pe_prev_st = None
            for st in range(10):
                @ctx.on("sync")
                def _(sy, st=st, pe_prev_st=pe_prev_st):
                    if pe_prev_st is not None:
                        sy.wait_ge("pe", pe_prev_st)
                    for k in range(32):
                        sy.dma("ldb", out=hT[k][:, :],
                               in_=dr["hbuf"][k * 128:(k + 1) * 128,
                                              st * TG:(st + 1) * TG])
                hld = ctx.val("ldb")

                for r4 in range(4):
                    rt = st * 4 + r4
                    bi = r4 % 2
                    for ng in range(2):
                        pi = ng
                        ps = yeps[pi]

                        @ctx.on("tensor")
                        def _(t, rt=rt, r4=r4, ng=ng, ps=ps, gate=ye_gate[pi]):
                            if r4 == 0 and ng == 0:
                                t.wait_ge("ldb", hld)
                                t.wait_ge("wld", wld2)
                            if gate is not None:
                                t.wait_ge(gate[0], gate[1])
                            for k in range(32):
                                t.matmul(ps[:, :], hT[k][:, r4 * 128:(r4 + 1) * 128],
                                         w2[k][:, ng * TG:(ng + 1) * TG],
                                         start=(k == 0), stop=(k == 31))
                        pv = ctx.val("pe")

                        @ctx.on("scalar")
                        def _(sx, rt=rt, ng=ng, ps=ps, bi=bi, sgate=ye_st[bi]):
                            sx.wait_ge("pe", pv)
                            if ng == 0 and sgate is not None:
                                sx.wait_ge(sgate[0], sgate[1])
                            sx.copy(yesb[bi][:, ng * TG:(ng + 1) * TG], ps[:, :])
                        ye_gate[pi] = ("sc", ctx.val("sc"))
                    scy = ctx.val("sc")

                    @ctx.on("gpsimd")
                    def _(g, rt=rt, bi=bi):
                        g.wait_ge("sc", scy)
                        g.dma("sta2" if bi == 0 else "stb2",
                              out=dr["ye"][rt * 128:(rt + 1) * 128, :],
                              in_=yesb[bi][:, :])
                    ye_st[bi] = ("sta2" if bi == 0 else "stb2",
                                 ctx.val("sta2" if bi == 0 else "stb2"))
                pe_prev_st = ctx.val("pe")
    barrier(ctx, extra=("wld", "ldb", "sta2", "stb2"))
    st_a, st_b = ctx.val("sta2"), ctx.val("stb2")
    zrow2 = ctx.sb("zrow2M", [1, D], BF16)

    @ctx.on("vector")
    def _(v):
        v.memset(zrow2[0:1, :], 0.0)
    dvz2 = ctx.val("dv")

    @ctx.on("gpsimd")
    def _(g):
        g.wait_ge("sta2", st_a)
        g.wait_ge("stb2", st_b)
        g.wait_ge("dv", dvz2)
        g.cc("AllGather", ALU.bypass, replica_groups=[list(range(8))],
             ins=[dr["ye"]], outs=[dr["ag_ye"][0:8 * CAP, :]])
        g.dma("sta2", out=dr["ag_ye"][YZROW:YZROW + 1, :], in_=zrow2[0:1, :])
        g.wait_ge("cc", ctx.cnt["cc"])
        g.wait_ge("sta2", ctx.cnt["sta2"])
    barrier(ctx)

    ao = [ctx.sb(f"aoM{k}", [128, T], BF16) for k in range(NCH)]
    g0 = [ctx.sb(f"g0M{i}", [128, D], BF16) for i in range(2)]
    g1 = [ctx.sb(f"g1M{i}", [128, D], BF16) for i in range(2)]
    otok = ctx.sb("otokM", [128, D], BF16)

    with nc.psum_tensor("atpM0", [128, 128], BF16) as atp0, \
         nc.psum_tensor("atpM1", [128, 128], BF16) as atp1:
        atps = [atp0, atp1]
        atp_gate = [None, None]
        g_gate = [None, None]
        ot_read = None
        for tt in range(16):
            bi = tt % 2

            gsem = "xga" if bi == 0 else "xgb"

            @ctx.on("gpsimd")
            def _(g, tt=tt, bi=bi, gsem=gsem, gate=g_gate[bi]):
                if gate is not None:
                    g.wait_ge(gate[0], gate[1])
                g.idma(gsem, out=g0[bi][:, :], out_offset=None, in_=dr["ag_ye"],
                       in_offset=IndirectOffsetOnAxis(ap=gidx[tt][:, 0:1], axis=0))
                g.idma(gsem, out=g1[bi][:, :], out_offset=None, in_=dr["ag_ye"],
                       in_offset=IndirectOffsetOnAxis(ap=gidx[tt][:, 1:2], axis=0))
            gv = (gsem, ctx.val(gsem))

            @ctx.on("vector")
            def _(v, tt=tt, bi=bi, gv=gv, ot_read=ot_read):
                v.wait_ge(gv[0], gv[1])
                if ot_read is not None:
                    v.wait_ge(ot_read[0], ot_read[1])
                v.tensor_scalar(otok[:, :], g0[bi][:, :], pcols[tt][:, 0:1], None, ALU.mult)
                v.scalar_tensor_tensor(g1[bi][:, :], g1[bi][:, :], pcols[tt][:, 1:2],
                                       otok[:, :], ALU.mult, ALU.add)
                v.tensor_copy(otok[:, :], g1[bi][:, :])
            dvo = ctx.val("dv")
            g_gate[bi] = ("dv", dvo)

            for c in range(NCH):
                pj = atps[c % 2]

                @ctx.on("tensor")
                def _(t, c=c, pj=pj, dvo=dvo, gate=atp_gate[c % 2]):
                    t.wait_ge("dv", dvo)
                    if gate is not None:
                        t.wait_ge(gate[0], gate[1])
                    t.transpose(pj[:, :], otok[:, c * 128:(c + 1) * 128], identb[:, :])
                pvt = ctx.val("pe")

                @ctx.on("scalar")
                def _(sx, tt=tt, c=c, pj=pj):
                    sx.wait_ge("pe", pvt)
                    sx.copy(ao[c][:, tt * 128:(tt + 1) * 128], pj[:, :])
                atp_gate[c % 2] = ("sc", ctx.val("sc"))
            ot_read = ("pe", ctx.val("pe"))
    barrier(ctx)

    if gdbg is not None:
        dbg_dv = ctx.val("dv")

        @ctx.on("gpsimd")
        def _(g):
            g.wait_ge("dv", dbg_dv)
            for tt in range(16):
                g.dma("sta2", out=gdbg["gidx_dbg"][tt * 128:(tt + 1) * 128, :],
                      in_=gidx[tt][:, :])
                g.dma("sta2", out=gdbg["p_dbg"][tt * 128:(tt + 1) * 128, :],
                      in_=pcols[tt][:, :])
                g.dma("sta2", out=gdbg["offs_dbg"][tt * 128:(tt + 1) * 128, :],
                      in_=offs_t[tt][:, :])
                g.dma("sta2", out=gdbg["oh1_dbg"][tt * 128:(tt + 1) * 128, :],
                      in_=oh1[tt][:, :])
                g.dma("sta2", out=gdbg["lgt_dbg"][tt * 128:(tt + 1) * 128, :],
                      in_=lgt[tt][:, :])
            g.dma("sta2", out=gdbg["idx_dbg"][:, :], in_=dr["idxbuf"])
            g.dma("sta2", out=gdbg["iota_dbg"][:, :], in_=iota8[:, :])
            g.dma("sta2", out=gdbg["h_dbg"][:, :], in_=dr["hbuf"][0:256, 0:TG])
            g.dma("sta2", out=gdbg["tot_dbg"][:, :], in_=tot_all[:, :])
            g.dma("sta2", out=gdbg["ye_dbg"][:, :], in_=dr["ye"][0:256, :])
            g.dma("sta2", out=gdbg["ytok_dbg"][:, :], in_=dr["ytok"])
            g.wait_ge("sta2", ctx.cnt["sta2"])

    xres = [ctx.sb(f"xresM{i}", [128, TG], F32) for i in range(2)]
    store_gates = []
    for m in range(NCH):
        for g in range(NTG):
            idx = m * NTG + g
            dvv = ctx.val("dv")
            residual_evac(ctx, x_in, x_out, m, g, idx, ao[m][:, g * TG:(g + 1) * TG],
                          half[:, 0:1], xres, store_gates, dvv)
    wait_stores(ctx)


def moe_dram(nc, dbg=False):
    return {
        "ytok": nc.dram_tensor("ytokM", [T, D], BF16).ap(),
        "ag_y": nc.dram_tensor("agyM", [8 * T + 1, D], BF16).ap(),
        "totals": nc.dram_tensor("totM", [1, EXP], F32).ap(),
        "ag_totals": nc.dram_tensor("agtotM", [8, EXP], F32).ap(),
        "offs": nc.dram_tensor("offsM", [T, EXP], F32).ap(),
        "ag_offs": nc.dram_tensor("agoffsM", [8 * T, EXP], F32).ap(),
        "idxbuf": nc.dram_tensor("idxbufM", [41 * 128, 1], I32).ap(),
        "hbuf": nc.dram_tensor("hbufM", [4 * D, CAP], BF16).ap(),
        "ye": nc.dram_tensor("yeM", [CAP, D], BF16).ap(),
        "ag_ye": nc.dram_tensor("agyeM", [8 * CAP + 1, D], BF16).ap(),
    }


def build_moe(dbg=False):
    nc = bass.Bass("TRN2", target_bir_lowering=False, debug=False)
    x_in = nc.dram_tensor("x_in", [D, T], F32, kind="ExternalInput").ap()
    Wr = nc.dram_tensor("Wr", [D, EXP], F32, kind="ExternalInput").ap()
    W1 = nc.dram_tensor("W1", [D, 4 * D], F32, kind="ExternalInput").ap()
    W3 = nc.dram_tensor("W3", [D, 4 * D], F32, kind="ExternalInput").ap()
    W2 = nc.dram_tensor("W2", [4 * D, D], F32, kind="ExternalInput").ap()
    Ustr = nc.dram_tensor("Ustr", [128, 128], F32, kind="ExternalInput").ap()
    mprev = nc.dram_tensor("mprev", [8, 1], F32, kind="ExternalInput").ap()
    mexp = nc.dram_tensor("mexp", [128, EXP], F32, kind="ExternalInput").ap()
    ident = nc.dram_tensor("ident", [128, 128], F32, kind="ExternalInput").ap()
    iota8 = nc.dram_tensor("iota8", [128, EXP], F32, kind="ExternalInput").ap()
    iotasc = nc.dram_tensor("iotasc", [128, 128], I32, kind="ExternalInput").ap()
    x_out = nc.dram_tensor("x_out", [D, T], F32, kind="ExternalOutput").ap()
    dr = moe_dram(nc, dbg)
    if dbg:
        gdbg = {
            "gidx_dbg": nc.dram_tensor("gidx_dbg", [T, 2], I32, kind="ExternalOutput").ap(),
            "p_dbg": nc.dram_tensor("p_dbg", [T, 2], F32, kind="ExternalOutput").ap(),
            "idx_dbg": nc.dram_tensor("idx_dbg", [41 * 128, 1], I32, kind="ExternalOutput").ap(),
            "tot_dbg": nc.dram_tensor("tot_dbg", [8, EXP], F32, kind="ExternalOutput").ap(),
            "ye_dbg": nc.dram_tensor("ye_dbg", [256, D], BF16, kind="ExternalOutput").ap(),
            "ytok_dbg": nc.dram_tensor("ytok_dbg", [T, D], BF16, kind="ExternalOutput").ap(),
            "offs_dbg": nc.dram_tensor("offs_dbg", [T, EXP], F32, kind="ExternalOutput").ap(),
            "oh1_dbg": nc.dram_tensor("oh1_dbg", [T, EXP], F32, kind="ExternalOutput").ap(),
            "lgt_dbg": nc.dram_tensor("lgt_dbg", [T, EXP], F32, kind="ExternalOutput").ap(),
            "iota_dbg": nc.dram_tensor("iota_dbg", [128, EXP], F32, kind="ExternalOutput").ap(),
            "h_dbg": nc.dram_tensor("h_dbg", [256, TG], BF16, kind="ExternalOutput").ap(),
            "xeT_dbg": nc.dram_tensor("xeT_dbg", [128, TG], BF16, kind="ExternalOutput").ap(),
        }
    else:
        gdbg = None
    with ExitStack() as stack:
        block = stack.enter_context(nc.Block())
        ctx = Ctx(nc, block, stack)
        emit_moe(ctx, x_in, x_out, Wr, W1, W3, W2, Ustr, mprev, mexp, ident,
                 iotasc, iota8, dr, gdbg=gdbg)
    return nc


def moe_maps(Wr, W1, W3, W2, core):
    return {
        "Wr": np.ascontiguousarray(np.asarray(Wr)),
        "W1": np.ascontiguousarray(np.asarray(W1)[core]),
        "W3": np.ascontiguousarray(np.asarray(W3)[core]),
        "W2": np.ascontiguousarray(np.asarray(W2)[core]),
        "Ustr": np.triu(np.ones((128, 128), np.float32), 1),
        "mprev": (np.arange(8) < core).astype(np.float32).reshape(8, 1),
        "mexp": np.tile((np.arange(EXP) == core).astype(np.float32), (128, 1)),
        "ident": np.eye(128, dtype=np.float32),
        "iota8": np.tile(np.arange(EXP, dtype=np.float32), (128, 1)),
        "iotasc": np.ascontiguousarray(
            (np.arange(128)[:, None] + 128 * np.arange(128)[None, :]).astype(np.int32)),
    }


def run_moe(xT, Wr, W1, W3, W2):
    nc = build_moe()
    in_maps = [dict(moe_maps(Wr, W1, W3, W2, b), x_in=np.ascontiguousarray(xT[b]))
               for b in range(8)]
    res = run_bass_kernel_spmd(nc, in_maps, core_ids=list(range(8)))
    return np.stack([r["x_out"] for r in res.results])



def build_rwkv():
    nc = bass.Bass("TRN2", target_bir_lowering=False, debug=False)
    x_in = nc.dram_tensor("x_in", [D, T], F32, kind="ExternalInput").ap()
    WrT = nc.dram_tensor("WrT", [D, D], F32, kind="ExternalInput").ap()
    WkT = nc.dram_tensor("WkT", [D, D], F32, kind="ExternalInput").ap()
    WvT = nc.dram_tensor("WvT", [D, D], F32, kind="ExternalInput").ap()
    mur = nc.dram_tensor("mur", [128, NCH], F32, kind="ExternalInput").ap()
    muk = nc.dram_tensor("muk", [128, NCH], F32, kind="ExternalInput").ap()
    x_out = nc.dram_tensor("x_out", [D, T], F32, kind="ExternalOutput").ap()
    with ExitStack() as stack:
        block = stack.enter_context(nc.Block())
        ctx = Ctx(nc, block, stack)
        emit_rwkv(ctx, x_in, x_out, WrT, WkT, WvT, mur, muk)
    return nc


def rwkv_maps(Wr, Wk, Wv, mu_r, mu_k):
    return {
        "WrT": np.ascontiguousarray(np.asarray(Wr).T),
        "WkT": np.ascontiguousarray(np.asarray(Wk).T),
        "WvT": np.ascontiguousarray(np.asarray(Wv).T),
        "mur": np.ascontiguousarray(np.asarray(mu_r).reshape(NCH, 128).T),
        "muk": np.ascontiguousarray(np.asarray(mu_k).reshape(NCH, 128).T),
    }


def run_rwkv(xT, Wr, Wk, Wv, mu_r, mu_k):
    nc = build_rwkv()
    wmap = rwkv_maps(Wr, Wk, Wv, mu_r, mu_k)
    in_maps = [dict(wmap, x_in=np.ascontiguousarray(xT[b])) for b in range(8)]
    res = run_bass_kernel_spmd(nc, in_maps, core_ids=list(range(8)))
    return np.stack([r["x_out"] for r in res.results])


_PROGS = {}


def _prog(name, builder):
    if name not in _PROGS:
        _PROGS[name] = builder()
    return _PROGS[name]


def _launch(nc, per_core_maps):
    res = run_bass_kernel_spmd(nc, per_core_maps, core_ids=list(range(8)))
    return [r["x_out"] for r in res.results]


def kernel(**inputs):
    inp = {k: np.asarray(v, dtype=np.float32) for k, v in inputs.items()}
    xT = [np.ascontiguousarray(inp["L0_attn_x"][b].T) for b in range(8)]

    def run(nc, shared_map, per_core=None):
        nonlocal xT
        maps = []
        for b in range(8):
            m = dict(shared_map if per_core is None else per_core[b])
            m["x_in"] = np.ascontiguousarray(xT[b])
            maps.append(m)
        xT = _launch(nc, maps)

    rwkv_nc = _prog("rwkv", build_rwkv)
    moe_nc = _prog("moe", build_moe)

    run(rwkv_nc, rwkv_maps(inp["L0_ffn_W_r"], inp["L0_ffn_W_k"], inp["L0_ffn_W_v"],
                           inp["L0_ffn_mu_r"], inp["L0_ffn_mu_k"]))
    run(_prog("l0attn", build_l0attn),
        l0attn_maps(inp["L0_attn_w_qkv"], inp["L0_attn_w_out"], inp["L0_attn_b_out"]))
    run(rwkv_nc, rwkv_maps(inp["L0_ffn_W_r"], inp["L0_ffn_W_k"], inp["L0_ffn_W_v"],
                           inp["L0_ffn_mu_r"], inp["L0_ffn_mu_k"]))
    run(rwkv_nc, rwkv_maps(inp["L1_ffn_W_r"], inp["L1_ffn_W_k"], inp["L1_ffn_W_v"],
                           inp["L1_ffn_mu_r"], inp["L1_ffn_mu_k"]))
    run(_prog("l1attn", build_l1attn),
        l1attn_maps(inp["L1_attn_W_q"], inp["L1_attn_W_k"], inp["L1_attn_W_v"],
                    inp["L1_attn_b_q"], inp["L1_attn_b_k"], inp["L1_attn_b_v"]))
    run(rwkv_nc, rwkv_maps(inp["L1_ffn_W_r"], inp["L1_ffn_W_k"], inp["L1_ffn_W_v"],
                           inp["L1_ffn_mu_r"], inp["L1_ffn_mu_k"]))
    l2f = l2ffn_maps(inp["L2_ffn_W_fc"], inp["L2_ffn_W_proj"])
    run(_prog("l2ffn", build_l2ffn), l2f)
    run(_prog("l2attn", build_l2attn),
        l2attn_maps(inp["L2_attn_Wqkv"], inp["L2_attn_Wout"],
                    inp["L2_attn_head_scales"], inp["L2_attn_head_dirs"]))
    run(_prog("l2ffn", build_l2ffn), l2f)
    moe_percore = [moe_maps(inp["L3_ffn_Wr"], inp["L3_ffn_W1"], inp["L3_ffn_W3"],
                            inp["L3_ffn_W2"], b) for b in range(8)]
    run(moe_nc, None, per_core=moe_percore)
    run(_prog("l3attn", build_l3attn),
        l3attn_maps(inp["L3_attn_W_qkv"], inp["L3_attn_W_out"]))
    run(moe_nc, None, per_core=moe_percore)

    return np.stack([np.ascontiguousarray(x.T) for x in xT]).astype(np.float32)


# version 02
# speedup vs baseline: 1.1662x; 1.1662x over previous
import numpy as np
from contextlib import ExitStack

import concourse.bass as bass
import concourse.mybir as mybir
from concourse.bass_utils import run_bass_kernel_spmd
from concourse.bass import IndirectOffsetOnAxis

F32 = mybir.dt.float32
BF16 = mybir.dt.bfloat16
I32 = mybir.dt.int32
ALU = mybir.AluOpType
AF = mybir.ActivationFunctionType
AX = mybir.AxisListType.X

D = 1024
T = 2048
NCH = 8
NTG = 4
TG = 512
EPS = 1e-5

ESEM = {"tensor": "pe", "scalar": "sc", "vector": "dv", "gpsimd": "gp", "sync": "sy"}
_NOINC = {"wait_ge", "nop", "sem_inc", "wait_op", "load_library"}


class EW:

    def __init__(self, ctx, eng, ename):
        self._ctx = ctx
        self._eng = eng
        self._ename = ename
        self._esem = ESEM[ename]

    def wait_ge(self, sem, val):
        if isinstance(sem, str):
            sem = self._ctx.sem(sem)
        self._eng.wait_ge(sem, val)

    def wait(self, pairs):
        for name, val in pairs:
            self._eng.wait_ge(self._ctx.sem(name), val)

    def dma(self, semname, *args, **kwargs):
        inst = self._eng.dma_start(*args, **kwargs)
        s, _ = self._ctx.bump(semname, 16)
        inst.then_inc(s, 16)
        return inst

    def idma(self, semname, *args, **kwargs):
        inst = self._eng.indirect_dma_start(*args, **kwargs)
        s, _ = self._ctx.bump(semname, 16)
        inst.then_inc(s, 16)
        return inst

    def cc(self, *args, **kwargs):
        inst = self._eng.collective_compute(*args, **kwargs)
        s, _ = self._ctx.bump("cc", 1)
        inst.then_inc(s, 1)
        return inst

    _SELFSYNC = {"scalar", "vector", "gpsimd"}

    def __getattr__(self, op):
        f = getattr(self._eng, op)
        if op in _NOINC or op.startswith("_") or not callable(f):
            return f
        ctx, esem = self._ctx, self._esem

        def g(*a, **k):
            if self._ename in EW._SELFSYNC and ctx.cnt.get(esem, 0) > 0:
                self._eng.wait_ge(ctx.sem(esem), ctx.cnt[esem])
            inst = f(*a, **k)
            s, _ = ctx.bump(esem, 1)
            try:
                inst.then_inc(s, 1)
            except Exception:
                ctx.cnt[esem] -= 1
            return inst
        return g


class NCProxy:

    def __init__(self, nc, prefix):
        self._nc = nc
        self._prefix = prefix

    def sbuf_tensor(self, name, shape=None, dtype=None, **kw):
        return self._nc.sbuf_tensor(self._prefix + name, shape, dtype, **kw)

    def psum_tensor(self, name, shape=None, dtype=None, **kw):
        return self._nc.psum_tensor(self._prefix + name, shape, dtype, **kw)

    def __getattr__(self, a):
        return getattr(self._nc, a)


class Ctx:
    def __init__(self, nc, block, stack):
        self.raw_nc = nc
        self.nc = nc
        self.block = block
        self.stack = stack
        self.scopes = [stack]
        self.sems = {}
        self.cnt = {}

    def set_prefix(self, prefix):
        self.nc = NCProxy(self.raw_nc, prefix)

    def sem(self, name):
        if name not in self.sems:
            self.sems[name] = self.stack.enter_context(self.nc.semaphore(name))
            self.cnt[name] = 0
        return self.sems[name]

    def bump(self, name, by=1):
        s = self.sem(name)
        self.cnt[name] += by
        return s, by

    def val(self, name):
        self.sem(name)
        return self.cnt[name]

    def sb(self, name, shape, dtype):
        return self.scopes[-1].enter_context(self.nc.sbuf_tensor(name, shape, dtype))

    def on(self, ename):
        def deco(fn):
            @getattr(self.block, ename)
            def _(eng):
                fn(EW(self, eng, ename))
        return deco

    def snapshot(self, names=("pe", "sc", "dv", "gp")):
        return [(n, self.val(n)) for n in names]


def barrier(ctx, extra=()):
    pairs = ctx.snapshot() + [(n, ctx.val(n)) for n in extra]
    for e in ESEM:
        @ctx.on(e)
        def _(eng):
            eng.wait(pairs)



def load_weight_chunks(ctx, dram_ap, tiles, rows=128, sem="wld"):
    @ctx.on("gpsimd")
    def _(g):
        for i, t in enumerate(tiles):
            g.dma(sem, out=t[:, :], in_=dram_ap[i * rows:(i + 1) * rows, :])


def ln_rows(ctx, x_dram, rstd_row, negmr_row, tag="", xbuf=None):
    nc = ctx.nc
    rstd_row = rstd_row[0:1, :]
    negmr_row = negmr_row[0:1, :]
    with ExitStack() as es:
        if xbuf is None:
            xbuf = [es.enter_context(nc.sbuf_tensor(f"lnx{tag}{i}", [128, T], F32)) for i in range(2)]
        sq = es.enter_context(nc.sbuf_tensor(f"lnsq{tag}", [128, T], F32))
        ones = es.enter_context(nc.sbuf_tensor(f"lnones{tag}", [128, 1], F32))
        srow = es.enter_context(nc.sbuf_tensor(f"lnsrow{tag}", [1, T], F32))[0:1, :]
        qrow = es.enter_context(nc.sbuf_tensor(f"lnqrow{tag}", [1, T], F32))[0:1, :]
        epsc = es.enter_context(nc.sbuf_tensor(f"lneps{tag}", [1, 1], F32))
        with nc.psum_tensor(f"ps_s{tag}", [1, T], F32) as ps_s, \
             nc.psum_tensor(f"ps_q{tag}", [1, T], F32) as ps_q:
            _ln_body(ctx, x_dram, rstd_row, negmr_row,
                     xbuf, sq, ones, srow, qrow, epsc, ps_s, ps_q)


def _ln_body(ctx, x_dram, rstd_row, negmr_row,
             xbuf, sq, ones, srow, qrow, epsc, ps_s, ps_q):
    @ctx.on("vector")
    def _(v):
        v.memset(ones[:, :], 1.0)
        v.memset(epsc[:, :], EPS)
    dv0 = ctx.val("dv")

    lda0, ldb0 = ctx.val("lda"), ctx.val("ldb")
    pe0 = ctx.val("pe")
    sc0 = ctx.val("sc")
    sq_done = [sc0 + c + 1 for c in range(NCH)]
    mm_done = [pe0 + 8 * (c + 1) for c in range(NCH)]

    @ctx.on("sync")
    def _(sy):
        for c in range(NCH):
            if c >= 2:
                sy.wait_ge("pe", mm_done[c - 2])
            sy.dma("lda" if c % 2 == 0 else "ldb",
                   out=xbuf[c % 2][:, :], in_=x_dram[c * 128:(c + 1) * 128, :])

    @ctx.on("scalar")
    def _(s):
        for c in range(NCH):
            s.wait_ge("lda" if c % 2 == 0 else "ldb",
                      (lda0 if c % 2 == 0 else ldb0) + 16 * (c // 2 + 1))
            if c >= 1:
                s.wait_ge("pe", mm_done[c - 1])
            s.activation(out=sq[:, :], in_=xbuf[c % 2][:, :], func=AF.Square)

    @ctx.on("tensor")
    def _(t):
        t.wait_ge("dv", dv0)
        for c in range(NCH):
            t.wait_ge("sc", sq_done[c])
            for g in range(NTG):
                t.matmul(ps_s[0:1, g * TG:(g + 1) * TG], ones[:, 0:1],
                         xbuf[c % 2][:, g * TG:(g + 1) * TG],
                         start=(c == 0), stop=(c == NCH - 1))
                t.matmul(ps_q[0:1, g * TG:(g + 1) * TG], ones[:, 0:1],
                         sq[:, g * TG:(g + 1) * TG],
                         start=(c == 0), stop=(c == NCH - 1))
    pe_done = ctx.val("pe")

    @ctx.on("vector")
    def _(v):
        v.wait_ge("pe", pe_done)
        v.tensor_scalar(srow, ps_s[0:1, :], 1.0 / D, None, ALU.mult)
        v.tensor_scalar(qrow, ps_q[0:1, :], 1.0 / D, None, ALU.mult)
        v.tensor_tensor(rstd_row, srow, srow, ALU.mult)
        v.tensor_tensor(qrow, qrow, rstd_row, ALU.subtract)
    dv1 = ctx.val("dv")

    @ctx.on("scalar")
    def _(s):
        s.wait_ge("dv", dv1)
        s.activation(out=rstd_row, in_=qrow, func=AF.Sqrt, bias=epsc[0:1, 0:1])
    sc1 = ctx.val("sc")

    @ctx.on("vector")
    def _(v):
        v.wait_ge("sc", sc1)
        v.reciprocal(rstd_row, rstd_row)
        v.tensor_tensor(negmr_row, srow, rstd_row, ALU.mult)
        v.tensor_scalar(negmr_row, negmr_row, -1.0, None, ALU.mult)

    barrier(ctx, extra=("lda", "ldb"))


def replicate_rows(ctx, rows_and_dsts, ones_row, tag=""):
    nc = ctx.nc
    with nc.psum_tensor(f"repps{tag}", [128, T], F32) as ps, \
         nc.sbuf_tensor(f"repones{tag}", [65, 128], F32) as op:
        @ctx.on("vector")
        def _(v):
            v.memset(op[:, :], 1.0)
        dvo = ctx.val("dv")

        for i, (row, dst) in enumerate(rows_and_dsts):
            sc_gate = ctx.val("sc")
            row = row[0:1, :]
            b_ = getattr(row, "base_partition", 0)
            bp = b_() if callable(b_) else int(b_)

            @ctx.on("tensor")
            def _(t, row=row, bp=bp):
                t.wait_ge("dv", dvo)
                if i > 0:
                    t.wait_ge("sc", sc_gate)
                for g in range(NTG):
                    t.matmul(ps[:, g * TG:(g + 1) * TG], op[bp:bp + 1, 0:128],
                             row[0:1, g * TG:(g + 1) * TG], start=True, stop=True)
            pv = ctx.val("pe")

            @ctx.on("scalar")
            def _(s, dst=dst):
                s.wait_ge("pe", pv)
                s.copy(dst[:, :], ps[:, :])
    barrier(ctx)


def proj_1024(ctx, w_tiles, rhs_tiles, evac, nk=NCH, psname="projps",
              ngroups=NTG, tgw=TG, mchunks=NCH):
    nc = ctx.nc
    with nc.psum_tensor(psname + "0", [128, tgw], F32) as p0, \
         nc.psum_tensor(psname + "1", [128, tgw], F32) as p1:
        banks = [p0, p1]
        bank_free = [None, None]
        bi = 0
        for m in range(mchunks):
            for g in range(ngroups):
                ps = banks[bi]
                gate = bank_free[bi]

                @ctx.on("tensor")
                def _(t, m=m, g=g, ps=ps, gate=gate):
                    if gate is not None:
                        t.wait_ge(gate[0], gate[1])
                    for k in range(nk):
                        t.matmul(ps[:, :], w_tiles[k][:, m * 128:(m + 1) * 128],
                                 rhs_tiles[k][:, g * tgw:(g + 1) * tgw],
                                 start=(k == 0), stop=(k == nk - 1))
                pv = ctx.val("pe")
                bank_free[bi] = evac(m, g, ps, pv)
                bi ^= 1
    barrier(ctx)



def emit_rwkv(ctx, x_in, x_out, WrT, WkT, WvT, mu_r_col, mu_k_col):
    nc = ctx.nc

    wr = [ctx.sb(f"wr{k}", [128, D], BF16) for k in range(NCH)]
    wk = [ctx.sb(f"wk{k}", [128, D], BF16) for k in range(NCH)]
    wv = [ctx.sb(f"wv{k}", [128, D], BF16) for k in range(NCH)]
    murc = ctx.sb("murc", [128, NCH], F32)
    mukc = ctx.sb("mukc", [128, NCH], F32)
    ones_row = ctx.sb("onesrow", [1, 128], F32)
    half = ctx.sb("halfc", [128, 1], F32)
    rstd = ctx.sb("rstd", [1, T], F32)
    negmr = ctx.sb("negmr", [1, T], F32)
    rep_rstd = ctx.sb("rep_rstd", [128, T], BF16)
    rep_negmr = ctx.sb("rep_negmr", [128, T], BF16)

    load_weight_chunks(ctx, WrT, wr)
    load_weight_chunks(ctx, WkT, wk)
    load_weight_chunks(ctx, WvT, wv)

    @ctx.on("gpsimd")
    def _(g):
        g.dma("wld", out=murc[:, :], in_=mu_r_col)
        g.dma("wld", out=mukc[:, :], in_=mu_k_col)

    @ctx.on("vector")
    def _(v):
        v.memset(ones_row[:, :], 1.0)
        v.memset(half[:, :], 0.5)

    ln_rows(ctx, x_in, rstd, negmr)
    replicate_rows(ctx, [(rstd, rep_rstd), (negmr, rep_negmr)], ones_row)

    xr = [ctx.sb(f"xr{k}", [128, T], BF16) for k in range(NCH)]
    xk = [ctx.sb(f"xk{k}", [128, T], BF16) for k in range(NCH)]
    r = [ctx.sb(f"r{k}", [128, T], BF16) for k in range(NCH)]
    ybuf = ctx.sb("ybuf", [128, T], BF16)
    tbuf = ctx.sb("tbuf", [128, T], BF16)
    ktmp = ctx.sb("ktmp", [128, TG], BF16)
    xbuf = [ctx.sb(f"xb{i}", [128, T], F32) for i in range(2)]
    vtmp = ctx.sb("vtmp", [128, TG], F32)
    xres = [ctx.sb(f"xres{i}", [128, TG], F32) for i in range(2)]

    wld_done = ctx.val("wld")

    lda0, ldb0 = ctx.val("lda"), ctx.val("ldb")
    dv0 = ctx.val("dv")
    mix_done = [dv0 + 7 * (c + 1) for c in range(NCH)]

    @ctx.on("sync")
    def _(sy):
        for c in range(NCH):
            if c >= 2:
                sy.wait_ge("dv", mix_done[c - 2])
            sy.dma("lda" if c % 2 == 0 else "ldb",
                   out=xbuf[c % 2][:, :], in_=x_in[c * 128:(c + 1) * 128, :])

    @ctx.on("vector")
    def _(v):
        v.wait_ge("wld", wld_done)
        for c in range(NCH):
            v.wait_ge("lda" if c % 2 == 0 else "ldb",
                      (lda0 if c % 2 == 0 else ldb0) + 16 * (c // 2 + 1))
            v.tensor_tensor(tbuf[:, :], xbuf[c % 2][:, :], rep_rstd[:, :], ALU.mult)
            v.tensor_tensor(ybuf[:, :], tbuf[:, :], rep_negmr[:, :], ALU.add)
            v.tensor_tensor(tbuf[:, 1:T], ybuf[:, 1:T], ybuf[:, 0:T - 1], ALU.subtract)
            v.scalar_tensor_tensor(xr[c][:, 1:T], tbuf[:, 1:T], murc[:, c:c + 1],
                                   ybuf[:, 0:T - 1], ALU.mult, ALU.add)
            v.tensor_scalar(xr[c][:, 0:1], ybuf[:, 0:1], murc[:, c:c + 1], None, ALU.mult)
            v.scalar_tensor_tensor(xk[c][:, 1:T], tbuf[:, 1:T], mukc[:, c:c + 1],
                                   ybuf[:, 0:T - 1], ALU.mult, ALU.add)
            v.tensor_scalar(xk[c][:, 0:1], ybuf[:, 0:1], mukc[:, c:c + 1], None, ALU.mult)
    assert ctx.val("dv") == mix_done[-1], (ctx.val("dv"), mix_done[-1])
    barrier(ctx, extra=("lda", "ldb"))

    def evac_r(m, g, ps, pv):
        @ctx.on("scalar")
        def _(s):
            s.wait_ge("pe", pv)
            s.activation(out=r[m][:, g * TG:(g + 1) * TG], in_=ps[:, :], func=AF.Sigmoid)
        return ("sc", ctx.val("sc"))
    proj_1024(ctx, wr, xr, evac_r, psname="rps")

    kact = xr
    ktmp_read = [0]

    def evac_k(m, g, ps, pv):
        @ctx.on("scalar")
        def _(s):
            s.wait_ge("pe", pv)
            s.wait_ge("dv", ktmp_read[0])
            s.activation(out=ktmp[:, :], in_=ps[:, :], func=AF.Relu)
        sv = ctx.val("sc")

        @ctx.on("vector")
        def _(v):
            v.wait_ge("sc", sv)
            v.tensor_tensor(kact[m][:, g * TG:(g + 1) * TG], ktmp[:, :], ktmp[:, :], ALU.mult)
        ktmp_read[0] = ctx.val("dv")
        return ("dv", ctx.val("dv"))
    proj_1024(ctx, wk, xk, evac_k, psname="kps")

    res_gates = []
    store_gates = []

    def evac_v(m, g, ps, pv):
        idx = m * NTG + g
        rsem = "rla" if idx % 2 == 0 else "rlb"

        ssem = "sta" if idx % 2 == 0 else "stb"

        @ctx.on("sync")
        def _(sy):
            if idx >= 2:
                sy.wait_ge(*store_gates[idx - 2])
            sy.dma(rsem, out=xres[idx % 2][:, :],
                   in_=x_in[m * 128:(m + 1) * 128, g * TG:(g + 1) * TG])
        lv = ctx.val(rsem)

        @ctx.on("vector")
        def _(v):
            v.wait_ge("pe", pv)
            v.wait_ge(rsem, lv)
            v.tensor_tensor(vtmp[:, :], ps[:, :], r[m][:, g * TG:(g + 1) * TG], ALU.mult)
            v.scalar_tensor_tensor(xres[idx % 2][:, :], vtmp[:, :], half[:, 0:1],
                                   xres[idx % 2][:, :], ALU.mult, ALU.add)
        dvv = ctx.val("dv")
        res_gates.append(dvv)

        @ctx.on("gpsimd")
        def _(g2):
            g2.wait_ge("dv", dvv)
            g2.dma(ssem, out=x_out[m * 128:(m + 1) * 128, g * TG:(g + 1) * TG],
                   in_=xres[idx % 2][:, :])
        store_gates.append((ssem, ctx.val(ssem)))
        return ("dv", dvv)
    proj_1024(ctx, wv, kact, evac_v, psname="vps")

    sta_v, stb_v = ctx.val("sta"), ctx.val("stb")

    @ctx.on("gpsimd")
    def _(g):
        g.wait_ge("sta", sta_v)
        g.wait_ge("stb", stb_v)



def build_y(ctx, x_in, rep_rstd, rep_negmr, ytiles, xbuf, per_chunk_extra=None):
    lda0, ldb0 = ctx.val("lda"), ctx.val("ldb")
    dv0 = ctx.val("dv")
    nops = 2
    y_done = [dv0 + nops * (c + 1) for c in range(NCH)]

    @ctx.on("sync")
    def _(sy):
        for c in range(NCH):
            if c >= 2:
                sy.wait_ge("dv", y_done[c - 2])
            sy.dma("lda" if c % 2 == 0 else "ldb",
                   out=xbuf[c % 2][:, :], in_=x_in[c * 128:(c + 1) * 128, :])

    @ctx.on("vector")
    def _(v):
        for c in range(NCH):
            v.wait_ge("lda" if c % 2 == 0 else "ldb",
                      (lda0 if c % 2 == 0 else ldb0) + 16 * (c // 2 + 1))
            v.tensor_tensor(ytiles[c][:, :], xbuf[c % 2][:, :], rep_rstd[:, :], ALU.mult)
            v.tensor_tensor(ytiles[c][:, :], ytiles[c][:, :], rep_negmr[:, :], ALU.add)
    assert ctx.val("dv") == y_done[-1]
    barrier(ctx, extra=("lda", "ldb"))


def residual_evac(ctx, x_in, x_out, m, g, idx, src_ap, scale_col, xres, store_gates, dv_ready):
    rsem = "rla" if idx % 2 == 0 else "rlb"
    ssem = "sta" if idx % 2 == 0 else "stb"

    @ctx.on("sync")
    def _(sy):
        if idx >= 2:
            sy.wait_ge(*store_gates[idx - 2])
        sy.dma(rsem, out=xres[idx % 2][:, :],
               in_=x_in[m * 128:(m + 1) * 128, g * TG:(g + 1) * TG])
    lv = ctx.val(rsem)

    @ctx.on("vector")
    def _(v):
        v.wait_ge("dv", dv_ready)
        v.wait_ge(rsem, lv)
        v.scalar_tensor_tensor(xres[idx % 2][:, :], src_ap, scale_col,
                               xres[idx % 2][:, :], ALU.mult, ALU.add)
    dvv = ctx.val("dv")

    @ctx.on("gpsimd")
    def _(g2):
        g2.wait_ge("dv", dvv)
        g2.dma(ssem, out=x_out[m * 128:(m + 1) * 128, g * TG:(g + 1) * TG],
               in_=xres[idx % 2][:, :])
    store_gates.append((ssem, ctx.val(ssem)))
    return dvv


def wait_stores(ctx):
    sta_v, stb_v = ctx.val("sta"), ctx.val("stb")

    @ctx.on("gpsimd")
    def _(g):
        g.wait_ge("sta", sta_v)
        g.wait_ge("stb", stb_v)



def emit_l2ffn(ctx, x_in, x_out, WfcT_dram, WprojT_dram):
    nc = ctx.nc
    ones_row = ctx.sb("onesrow2", [1, 128], F32)
    tenth = ctx.sb("tenth", [128, 1], F32)
    y = [ctx.sb(f"y2{k}", [128, T], BF16) for k in range(NCH)]

    @ctx.on("vector")
    def _(v):
        v.memset(ones_row[:, :], 1.0)
        v.memset(tenth[:, :], 0.1)

    with ExitStack() as es:
        rstd = es.enter_context(nc.sbuf_tensor("rstd2", [1, T], F32))
        negmr = es.enter_context(nc.sbuf_tensor("negmr2", [1, T], F32))
        rep_rstd = es.enter_context(nc.sbuf_tensor("rep_rstd2", [128, T], BF16))
        rep_negmr = es.enter_context(nc.sbuf_tensor("rep_negmr2", [128, T], BF16))
        xbuf = [es.enter_context(nc.sbuf_tensor(f"xb2{i}", [128, T], F32)) for i in range(2)]
        ln_rows(ctx, x_in, rstd, negmr, tag="f2", xbuf=xbuf)
        replicate_rows(ctx, [(rstd, rep_rstd), (negmr, rep_negmr)], ones_row, tag="f2")
        build_y(ctx, x_in, rep_rstd, rep_negmr, y, xbuf)

    wfc = [ctx.sb(f"wfc{k}", [128, 4 * D], BF16) for k in range(NCH)]
    wproj = [ctx.sb(f"wpj{k}", [128, D], BF16) for k in range(4 * NCH)]
    h = [ctx.sb(f"h{k}", [128, TG], BF16) for k in range(4 * NCH)]
    htmp = ctx.sb("htmp", [128, TG], BF16)
    y2t = ctx.sb("y2t", [128, TG], F32)
    mem = ctx.sb("memt", [128, TG], F32)
    spk = ctx.sb("spkt", [128, TG], F32)
    acc = ctx.sb("acct", [128, TG], F32)
    t1 = ctx.sb("t1t", [128, TG], F32)
    xres = [ctx.sb(f"xres2{i}", [128, TG], F32) for i in range(2)]

    load_weight_chunks(ctx, WfcT_dram, wfc)
    load_weight_chunks(ctx, WprojT_dram, wproj)
    wld_done = ctx.val("wld")

    @ctx.on("tensor")
    def _(t):
        t.wait_ge("wld", wld_done)

    store_gates = []
    htmp_read = [0]
    for g in range(NTG):
        def evac_h(m, _g, ps, pv, g=g):
            @ctx.on("scalar")
            def _(sx):
                sx.wait_ge("pe", pv)
                sx.wait_ge("dv", htmp_read[0])
                sx.activation(out=htmp[:, :], in_=ps[:, :], func=AF.Relu)
            sv = ctx.val("sc")

            @ctx.on("vector")
            def _(v):
                v.wait_ge("sc", sv)
                v.tensor_tensor(h[m][:, :], htmp[:, :], htmp[:, :], ALU.mult)
            htmp_read[0] = ctx.val("dv")
            return ("dv", ctx.val("dv"))
        proj_1024(ctx, wfc, [yy[:, g * TG:(g + 1) * TG] for yy in y], evac_h,
                  psname=f"hps{g}", mchunks=4 * NCH, ngroups=1, tgw=TG)

        def evac_y2(m, _g, ps, pv, g=g):
            idx = g * NCH + m

            @ctx.on("vector")
            def _(v):
                v.wait_ge("pe", pv)
                v.tensor_copy(y2t[:, :], ps[:, :])
                v.tensor_copy(mem[:, :], y2t[:, :])
                v.tensor_scalar(spk[:, :], mem[:, :], 1.0, None, ALU.is_gt)
                v.tensor_copy(acc[:, :], spk[:, :])
                for st in range(1, 5):
                    v.tensor_tensor(t1[:, :], y2t[:, :], spk[:, :], ALU.subtract)
                    v.scalar_tensor_tensor(mem[:, :], mem[:, :], 0.9, t1[:, :],
                                           ALU.mult, ALU.add)
                    v.tensor_scalar(spk[:, :], mem[:, :], 1.0, None, ALU.is_gt)
                    v.tensor_tensor(acc[:, :], acc[:, :], spk[:, :], ALU.add)
            dvv = ctx.val("dv")
            dvv2 = residual_evac(ctx, x_in, x_out, m, g, idx, acc[:, :], tenth[:, 0:1],
                                 xres, store_gates, dvv)
            return ("dv", dvv2)
        proj_1024(ctx, wproj, h, evac_y2, psname=f"y2ps{g}", nk=4 * NCH,
                  mchunks=NCH, ngroups=1, tgw=TG)

    wait_stores(ctx)


def build_l2ffn():
    nc = bass.Bass("TRN2", target_bir_lowering=False, debug=False)
    x_in = nc.dram_tensor("x_in", [D, T], F32, kind="ExternalInput").ap()
    WfcT = nc.dram_tensor("WfcT", [D, 4 * D], F32, kind="ExternalInput").ap()
    WprojT = nc.dram_tensor("WprojT", [4 * D, D], F32, kind="ExternalInput").ap()
    x_out = nc.dram_tensor("x_out", [D, T], F32, kind="ExternalOutput").ap()
    with ExitStack() as stack:
        block = stack.enter_context(nc.Block())
        ctx = Ctx(nc, block, stack)
        emit_l2ffn(ctx, x_in, x_out, WfcT, WprojT)
    return nc


def l2ffn_maps(W_fc, W_proj):
    return {
        "WfcT": np.ascontiguousarray(np.asarray(W_fc).T),
        "WprojT": np.ascontiguousarray(np.asarray(W_proj).T),
    }


def run_l2ffn(xT, W_fc, W_proj):
    nc = build_l2ffn()
    wmap = l2ffn_maps(W_fc, W_proj)
    in_maps = [dict(wmap, x_in=np.ascontiguousarray(xT[b])) for b in range(8)]
    res = run_bass_kernel_spmd(nc, in_maps, core_ids=list(range(8)))
    return np.stack([r["x_out"] for r in res.results])



def emit_l1attn(ctx, x_in, x_out, Wq_d, Wk_d, Wv_d, bq_d, bk_d, bv_d, ident_d):
    nc = ctx.nc
    ones_row = ctx.sb("onesrowA", [1, TG], F32)
    ones_bf = ctx.sb("onesbfA", [1, TG], BF16)
    qscale = ctx.sb("qscaleA", [128, 1], F32)
    ident = ctx.sb("identA", [128, 128], BF16)

    QT = [ctx.sb(f"qt{k}", [128, T], BF16) for k in range(NCH)]
    KT = [ctx.sb(f"kt{k}", [128, T], BF16) for k in range(NCH)]
    V = [ctx.sb(f"vtl{k}", [128, D], BF16) for k in range(16)]

    @ctx.on("vector")
    def _(v):
        v.memset(ones_row[:, :], 1.0)
        v.memset(ones_bf[:, :], 1.0)
        v.memset(qscale[:, :], 1.0 / 32.0)

    @ctx.on("gpsimd")
    def _(g):
        g.dma("wld", out=ident[:, :], in_=ident_d)

    with ExitStack() as es:
        rstd = es.enter_context(nc.sbuf_tensor("rstdA", [1, T], F32))
        negmr = es.enter_context(nc.sbuf_tensor("negmrA", [1, T], F32))
        rep_rstd = es.enter_context(nc.sbuf_tensor("rep_rstdA", [128, T], BF16))
        rep_negmr = es.enter_context(nc.sbuf_tensor("rep_negmrA", [128, T], BF16))
        xbuf = [es.enter_context(nc.sbuf_tensor(f"xbA{i}", [128, T], F32)) for i in range(2)]
        y = [es.enter_context(nc.sbuf_tensor(f"yA{k}", [128, T], BF16)) for k in range(NCH)]
        brow = es.enter_context(nc.sbuf_tensor("browA", [1, D], BF16))

        ln_rows(ctx, x_in, rstd, negmr, tag="A", xbuf=xbuf)
        replicate_rows(ctx, [(rstd, rep_rstd), (negmr, rep_negmr)], ones_row[0:1, 0:128], tag="A")
        build_y(ctx, x_in, rep_rstd, rep_negmr, y, xbuf)

        for wi, (Wd, bd, dst, scale) in enumerate(
                ((Wq_d, bq_d, QT, qscale), (Wk_d, bk_d, KT, None))):
            with ExitStack() as es2:
                wt = [es2.enter_context(nc.sbuf_tensor(f"wA{wi}_{k}", [128, D], BF16))
                      for k in range(NCH)]
                load_weight_chunks(ctx, Wd, wt)

                @ctx.on("gpsimd")
                def _(g, bd=bd):
                    g.dma("wld", out=brow[0:1, :], in_=bd)
                wv_done = ctx.val("wld")

                @ctx.on("tensor")
                def _(t):
                    t.wait_ge("wld", wv_done)

                def evac_qk(m, g, ps, pv, dst=dst, scale=scale):
                    @ctx.on("scalar")
                    def _(sx):
                        sx.wait_ge("pe", pv)
                        if scale is not None:
                            sx.activation(out=dst[m][:, g * TG:(g + 1) * TG], in_=ps[:, :],
                                          func=AF.Identity, scale=scale[:, 0:1])
                        else:
                            sx.copy(dst[m][:, g * TG:(g + 1) * TG], ps[:, :])
                    return ("sc", ctx.val("sc"))

                with nc.psum_tensor(f"qkps0_{wi}", [128, TG], F32) as p0, \
                     nc.psum_tensor(f"qkps1_{wi}", [128, TG], F32) as p1:
                    banks = [p0, p1]
                    bank_free = [None, None]
                    bi = 0
                    for m in range(NCH):
                        for g in range(NTG):
                            ps = banks[bi]
                            gate = bank_free[bi]

                            @ctx.on("tensor")
                            def _(t, m=m, g=g, ps=ps, gate=gate):
                                if gate is not None:
                                    t.wait_ge(gate[0], gate[1])
                                for k in range(NCH):
                                    t.matmul(ps[:, :], wt[k][:, m * 128:(m + 1) * 128],
                                             y[k][:, g * TG:(g + 1) * TG],
                                             start=(k == 0), stop=False)
                                t.matmul(ps[:, :], brow[0:1, m * 128:(m + 1) * 128],
                                         ones_bf[0:1, :], start=False, stop=True)
                            pv = ctx.val("pe")
                            bank_free[bi] = evac_qk(m, g, ps, pv)
                            bi ^= 1
                barrier(ctx)

        with ExitStack() as es2:
            wt = [es2.enter_context(nc.sbuf_tensor(f"wVA{k}", [128, D], BF16))
                  for k in range(NCH)]
            load_weight_chunks(ctx, Wv_d, wt)

            @ctx.on("gpsimd")
            def _(g):
                g.dma("wld", out=brow[0:1, :], in_=bv_d)
            wv_done = ctx.val("wld")

            @ctx.on("tensor")
            def _(t):
                t.wait_ge("wld", wv_done)

            with nc.psum_tensor("vps0", [128, TG], F32) as p0, \
                 nc.psum_tensor("vps1", [128, TG], F32) as p1:
                banks = [p0, p1]
                bank_free = [None, None]
                bi = 0
                for tt in range(16):
                    for ng in range(2):
                        ps = banks[bi]
                        gate = bank_free[bi]

                        @ctx.on("tensor")
                        def _(t, tt=tt, ng=ng, ps=ps, gate=gate):
                            if gate is not None:
                                t.wait_ge(gate[0], gate[1])
                            for k in range(NCH):
                                t.matmul(ps[:, :], y[k][:, tt * 128:(tt + 1) * 128],
                                         wt[k][:, ng * TG:(ng + 1) * TG],
                                         start=(k == 0), stop=False)
                            t.matmul(ps[:, :], ones_bf[0:1, 0:128],
                                     brow[0:1, ng * TG:(ng + 1) * TG],
                                     start=False, stop=True)
                        pv = ctx.val("pe")

                        @ctx.on("scalar")
                        def _(sx, tt=tt, ng=ng, ps=ps):
                            sx.wait_ge("pe", pv)
                            sx.copy(V[tt][:, ng * TG:(ng + 1) * TG], ps[:, :])
                        bank_free[bi] = ("sc", ctx.val("sc"))
                        bi ^= 1
            barrier(ctx)

    ao = [ctx.sb(f"ao{k}", [128, T], BF16) for k in range(NCH)]
    wtile = ctx.sb("wtileA", [128, T], BF16)
    wTt = ctx.sb("wTA", [128, T], BF16)
    mx = ctx.sb("mxA", [128, 1], F32)
    negmx = ctx.sb("negmxA", [128, 1], F32)
    sums = ctx.sb("sumsA", [128, 1], F32)
    rsum = ctx.sb("rsumA", [128, 1], F32)

    with nc.psum_tensor("lgps", [128, T], F32) as lgps, \
         nc.psum_tensor("tpps0", [128, 128], BF16) as tpps0, \
         nc.psum_tensor("tpps1", [128, 128], BF16) as tpps1, \
         nc.psum_tensor("avps", [128, 1024], F32) as avps:
        tpps = [tpps0, tpps1]
        for qt in range(16):
            dv_gate = ctx.val("dv")
            sc_gate = ctx.val("sc")

            @ctx.on("tensor")
            def _(t, qt=qt):
                t.wait_ge("dv", dv_gate)
                t.wait_ge("sc", sc_gate)
                for gk in range(NTG):
                    for kd in range(NCH):
                        t.matmul(lgps[:, gk * TG:(gk + 1) * TG],
                                 QT[kd][:, qt * 128:(qt + 1) * 128],
                                 KT[kd][:, gk * TG:(gk + 1) * TG],
                                 start=(kd == 0), stop=(kd == NCH - 1))
            pv = ctx.val("pe")

            @ctx.on("vector")
            def _(v):
                v.wait_ge("pe", pv)
                v.reduce_max(mx[:, 0:1], lgps[:, :], axis=AX)
                v.tensor_scalar(negmx[:, 0:1], mx[:, 0:1], -1.0, None, ALU.mult)
            dvv = ctx.val("dv")

            @ctx.on("scalar")
            def _(sx):
                sx.wait_ge("dv", dvv)
                sx.activation(out=wtile[:, :], in_=lgps[:, :], func=AF.Exp,
                              bias=negmx[:, 0:1], accum_out=sums[:, 0:1])
            scv = ctx.val("sc")

            @ctx.on("vector")
            def _(v):
                v.wait_ge("sc", scv)
                v.reciprocal(rsum[:, 0:1], sums[:, 0:1])
                v.tensor_scalar(wtile[:, :], wtile[:, :], rsum[:, 0:1], None, ALU.mult)
            dvn = ctx.val("dv")

            tp_evac = [None, None]
            for j in range(16):
                pj = tpps[j % 2]
                gate = tp_evac[j % 2]

                @ctx.on("tensor")
                def _(t, j=j, pj=pj, gate=gate):
                    if j == 0:
                        t.wait_ge("dv", dvn)
                    if gate is not None:
                        t.wait_ge(gate[0], gate[1])
                    t.transpose(pj[:, :], wtile[:, j * 128:(j + 1) * 128], ident[:, :])
                pvj = ctx.val("pe")

                @ctx.on("vector")
                def _(v, j=j, pj=pj):
                    v.wait_ge("pe", pvj)
                    v.tensor_copy(wTt[:, j * 128:(j + 1) * 128], pj[:, :])
                tp_evac[j % 2] = ("dv", ctx.val("dv"))
            dvT = ctx.val("dv")

            @ctx.on("tensor")
            def _(t, qt=qt):
                t.wait_ge("dv", dvT)
                for m in range(NCH):
                    for j in range(16):
                        t.matmul(avps[:, m * 128:(m + 1) * 128],
                                 V[j][:, m * 128:(m + 1) * 128],
                                 wTt[:, j * 128:(j + 1) * 128],
                                 start=(j == 0), stop=(j == 15))
            pva = ctx.val("pe")

            @ctx.on("scalar")
            def _(sx, qt=qt):
                sx.wait_ge("pe", pva)
                for m in range(NCH):
                    sx.copy(ao[m][:, qt * 128:(qt + 1) * 128], avps[:, m * 128:(m + 1) * 128])
    barrier(ctx)

    one_col = ctx.sb("onecolA", [128, 1], F32)
    xres = [ctx.sb(f"xresA{i}", [128, TG], F32) for i in range(2)]

    @ctx.on("vector")
    def _(v):
        v.memset(one_col[:, :], 1.0)

    store_gates = []
    for m in range(NCH):
        for g in range(NTG):
            idx = m * NTG + g
            dvv = ctx.val("dv")
            residual_evac(ctx, x_in, x_out, m, g, idx, ao[m][:, g * TG:(g + 1) * TG],
                          one_col[:, 0:1], xres, store_gates, dvv)
    wait_stores(ctx)


def build_l1attn():
    nc = bass.Bass("TRN2", target_bir_lowering=False, debug=False)
    x_in = nc.dram_tensor("x_in", [D, T], F32, kind="ExternalInput").ap()
    Wq = nc.dram_tensor("Wq", [D, D], F32, kind="ExternalInput").ap()
    Wk = nc.dram_tensor("Wk", [D, D], F32, kind="ExternalInput").ap()
    Wv = nc.dram_tensor("Wv", [D, D], F32, kind="ExternalInput").ap()
    bq = nc.dram_tensor("bq", [1, D], F32, kind="ExternalInput").ap()
    bk = nc.dram_tensor("bk", [1, D], F32, kind="ExternalInput").ap()
    bv = nc.dram_tensor("bv", [1, D], F32, kind="ExternalInput").ap()
    ident = nc.dram_tensor("ident", [128, 128], F32, kind="ExternalInput").ap()
    x_out = nc.dram_tensor("x_out", [D, T], F32, kind="ExternalOutput").ap()
    with ExitStack() as stack:
        block = stack.enter_context(nc.Block())
        ctx = Ctx(nc, block, stack)
        emit_l1attn(ctx, x_in, x_out, Wq, Wk, Wv, bq, bk, bv, ident)
    return nc


def l1attn_maps(Wq, Wk, Wv, bq, bk, bv):
    return {
        "Wq": np.ascontiguousarray(np.asarray(Wq)),
        "Wk": np.ascontiguousarray(np.asarray(Wk)),
        "Wv": np.ascontiguousarray(np.asarray(Wv)),
        "bq": np.asarray(bq).reshape(1, D).copy(),
        "bk": np.asarray(bk).reshape(1, D).copy(),
        "bv": np.asarray(bv).reshape(1, D).copy(),
        "ident": np.eye(128, dtype=np.float32),
    }


def run_l1attn(xT, Wq, Wk, Wv, bq, bk, bv):
    nc = build_l1attn()
    wmap = l1attn_maps(Wq, Wk, Wv, bq, bk, bv)
    in_maps = [dict(wmap, x_in=np.ascontiguousarray(xT[b])) for b in range(8)]
    res = run_bass_kernel_spmd(nc, in_maps, core_ids=list(range(8)))
    return np.stack([r["x_out"] for r in res.results])



def emit_l2attn(ctx, x_in, x_out, Wqkv_d, Wout_d, hs_d, dirs_d, cmask_d, ao_dram):
    nc = ctx.nc
    H = 8
    SC = 1.0 / np.sqrt(128.0)
    ones_row = ctx.sb("onesrowC", [1, 128], F32)
    ones_bf = ctx.sb("onesbfC", [1, TG], BF16)
    onescol_bf = ctx.sb("onescolC", [128, 1], BF16)
    one_col = ctx.sb("onecolC", [128, 1], F32)
    qsc = ctx.sb("qscC", [128, 1], F32)
    dirs = ctx.sb("dirsC", [3, H], BF16)
    hsrow = ctx.sb("hsrowC", [1, H], F32)
    cmask = [ctx.sb(f"cmaskC{j}", [128, TG], BF16) for j in range(4)]

    @ctx.on("vector")
    def _(v):
        v.memset(ones_row[:, :], 1.0)
        v.memset(ones_bf[:, :], 1.0)
        v.memset(onescol_bf[:, :], 1.0)
        v.memset(one_col[:, :], 1.0)
        v.memset(qsc[:, :], SC)

    @ctx.on("gpsimd")
    def _(g):
        g.dma("wld", out=dirs[:, :], in_=dirs_d)
        g.dma("wld", out=hsrow[:, :], in_=hs_d)
        for j in range(4):
            g.dma("wld", out=cmask[j][:, :], in_=cmask_d[j])

    with ExitStack() as esq:
        QT = [esq.enter_context(nc.sbuf_tensor(f"qtC{k}", [128, T], BF16)) for k in range(H)]
        KT = [esq.enter_context(nc.sbuf_tensor(f"ktC{k}", [128, T], BF16)) for k in range(H)]
        V = [esq.enter_context(nc.sbuf_tensor(f"vtlC{k}", [128, D], BF16)) for k in range(16)]

        with ExitStack() as es:
            rstd = es.enter_context(nc.sbuf_tensor("rstdC", [1, T], F32))
            negmr = es.enter_context(nc.sbuf_tensor("negmrC", [1, T], F32))
            rep_rstd = es.enter_context(nc.sbuf_tensor("rep_rstdC", [128, T], BF16))
            rep_negmr = es.enter_context(nc.sbuf_tensor("rep_negmrC", [128, T], BF16))
            xbuf = [es.enter_context(nc.sbuf_tensor(f"xbC{i}", [128, T], F32)) for i in range(2)]
            y = [es.enter_context(nc.sbuf_tensor(f"yC{k}", [128, T], BF16)) for k in range(NCH)]

            ln_rows(ctx, x_in, rstd, negmr, tag="C", xbuf=xbuf)
            replicate_rows(ctx, [(rstd, rep_rstd), (negmr, rep_negmr)], ones_row, tag="C")
            build_y(ctx, x_in, rep_rstd, rep_negmr, y, xbuf)

            with ExitStack() as es2:
                wt = [es2.enter_context(nc.sbuf_tensor(f"wCqk{k}", [128, 2 * D], BF16))
                      for k in range(NCH)]
                load_weight_chunks(ctx, Wqkv_d[:, 0:2 * D], wt)
                wv_done = ctx.val("wld")

                @ctx.on("tensor")
                def _(t):
                    t.wait_ge("wld", wv_done)

                def evac_qk(m, g, ps, pv):
                    @ctx.on("scalar")
                    def _(sx):
                        sx.wait_ge("pe", pv)
                        if m < H:
                            sx.activation(out=QT[m][:, g * TG:(g + 1) * TG], in_=ps[:, :],
                                          func=AF.Identity, scale=qsc[:, 0:1])
                        else:
                            sx.copy(KT[m - H][:, g * TG:(g + 1) * TG], ps[:, :])
                    return ("sc", ctx.val("sc"))
                proj_1024(ctx, wt, y, evac_qk, psname="qkC", mchunks=16)

            with ExitStack() as es2:
                wt = [es2.enter_context(nc.sbuf_tensor(f"wCv{k}", [128, D], BF16))
                      for k in range(NCH)]
                load_weight_chunks(ctx, Wqkv_d[:, 2 * D:3 * D], wt)
                wv_done = ctx.val("wld")

                @ctx.on("tensor")
                def _(t):
                    t.wait_ge("wld", wv_done)

                with nc.psum_tensor("vpsC0", [128, TG], F32) as p0, \
                     nc.psum_tensor("vpsC1", [128, TG], F32) as p1:
                    banks = [p0, p1]
                    bank_free = [None, None]
                    bi = 0
                    for tt in range(16):
                        for ng in range(2):
                            ps = banks[bi]
                            gate = bank_free[bi]

                            @ctx.on("tensor")
                            def _(t, tt=tt, ng=ng, ps=ps, gate=gate):
                                if gate is not None:
                                    t.wait_ge(gate[0], gate[1])
                                for k in range(NCH):
                                    t.matmul(ps[:, :], y[k][:, tt * 128:(tt + 1) * 128],
                                             wt[k][:, ng * TG:(ng + 1) * TG],
                                             start=(k == 0), stop=(k == NCH - 1))
                            pv = ctx.val("pe")

                            @ctx.on("scalar")
                            def _(sx, tt=tt, ng=ng, ps=ps):
                                sx.wait_ge("pe", pv)
                                sx.copy(V[tt][:, ng * TG:(ng + 1) * TG], ps[:, :])
                            bank_free[bi] = ("sc", ctx.val("sc"))
                            bi ^= 1
                barrier(ctx)

        esa = ExitStack()
        qps_r = esa.enter_context(nc.sbuf_tensor("qpsC", [1, T], BF16))
        kp_r = esa.enter_context(nc.sbuf_tensor("kpC", [1, T], BF16))
        negc_r = esa.enter_context(nc.sbuf_tensor("negcC", [1, T], BF16))
        sqt = esa.enter_context(nc.sbuf_tensor("sqtC", [128, T], BF16))
        nq = esa.enter_context(nc.sbuf_tensor("nqC", [1, T], F32))
        nk = esa.enter_context(nc.sbuf_tensor("nkC", [1, T], F32))
        qpf = esa.enter_context(nc.sbuf_tensor("qpfC", [1, T], F32))
        kpf = esa.enter_context(nc.sbuf_tensor("kpfC", [1, T], F32))
        sc2 = esa.enter_context(nc.sbuf_tensor("scal2C", [1, 4], F32))
        crow = esa.enter_context(nc.sbuf_tensor("crowC", [1, T], F32))
        wch = [esa.enter_context(nc.sbuf_tensor(f"wchC{i}", [128, TG], BF16)) for i in range(2)]
        rsum = esa.enter_context(nc.sbuf_tensor("rsumC", [1, TG], F32))
        repn = esa.enter_context(nc.sbuf_tensor("repnC", [128, TG], F32))
        aot = [esa.enter_context(nc.sbuf_tensor(f"aotC{i}", [128, TG], BF16)) for i in range(2)]

        with nc.psum_tensor("wpsC0", [128, TG], F32) as w0, \
             nc.psum_tensor("wpsC1", [128, TG], F32) as w1, \
             nc.psum_tensor("sumpsC", [1, TG], F32) as sps, \
             nc.psum_tensor("avpsC", [128, TG], F32) as avps, \
             nc.psum_tensor("repnpsC", [128, TG], F32) as rps3, \
             nc.psum_tensor("rowpsC", [1, TG], F32) as rps, \
             nc.psum_tensor("rowps2C", [1, TG], F32) as rps2:
            wps = [w0, w1]
            wp_gate = [None, None]
            av_gate = None
            norm_gate = None
            row_gate = None
            aot_gate = [None, None]
            for h in range(H):
                sc_gate = ctx.val("sc")
                dv_gate = ctx.val("dv")
                pe_gate = ctx.val("pe")

                @ctx.on("scalar")
                def _(sx, h=h, row_gate=row_gate):
                    if row_gate is not None:
                        sx.wait_ge(row_gate[0], row_gate[1])
                    sx.activation(out=sqt[:, :], in_=QT[h][:, :], func=AF.Square)
                sc_sq = ctx.val("sc")

                prev_cp = ctx.val("dv")
                for g in range(NTG):
                    @ctx.on("tensor")
                    def _(t, h=h, g=g, prev_cp=prev_cp):
                        if g == 0:
                            t.wait_ge("sc", sc_gate)
                            t.wait_ge("dv", dv_gate)
                        else:
                            t.wait_ge("dv", prev_cp)
                        t.matmul(rps[0:1, :], dirs[0:3, h:h + 1],
                                 QT[h][0:3, g * TG:(g + 1) * TG], start=True, stop=True)
                        t.matmul(rps2[0:1, :], dirs[0:3, h:h + 1],
                                 KT[h][0:3, g * TG:(g + 1) * TG], start=True, stop=True)
                    pv = ctx.val("pe")

                    @ctx.on("vector")
                    def _(v, h=h, g=g):
                        v.wait_ge("pe", pv)
                        v.tensor_copy(qpf[0:1, g * TG:(g + 1) * TG], rps[0:1, :])
                        v.tensor_copy(kpf[0:1, g * TG:(g + 1) * TG], rps2[0:1, :])
                    prev_cp = ctx.val("dv")
                qk_cp = prev_cp
                dv_rows = ctx.val("dv")

                @ctx.on("vector")
                def _(v, h=h):
                    v.wait_ge("dv", dv_rows)
                    v.tensor_copy(kp_r[0:1, :], kpf[0:1, :])
                    v.tensor_scalar(sc2[0:1, 2:3], hsrow[0:1, h:h + 1],
                                    float(np.sqrt(128.0)), None, ALU.mult)
                    v.tensor_scalar(qps_r[0:1, :], qpf[0:1, :], sc2[0:1, 2:3], None, ALU.mult)
                dv1 = ctx.val("dv")

                prev_cp = ctx.val("sc")
                for g in range(NTG):
                    @ctx.on("tensor")
                    def _(t, h=h, g=g, prev_cp=prev_cp):
                        if g == 0:
                            t.wait_ge("sc", sc_sq)
                            t.wait_ge("dv", qk_cp)
                        else:
                            t.wait_ge("sc", prev_cp)
                        t.matmul(rps[0:1, :], onescol_bf[:, 0:1],
                                 sqt[:, g * TG:(g + 1) * TG], start=True, stop=True)
                    pv = ctx.val("pe")

                    @ctx.on("scalar")
                    def _(sx, h=h, g=g):
                        sx.wait_ge("pe", pv)
                        sx.activation(out=nq[0:1, g * TG:(g + 1) * TG], in_=rps[0:1, :],
                                      func=AF.Sqrt)
                    prev_cp = ctx.val("sc")
                sc_nq = ctx.val("sc")

                @ctx.on("scalar")
                def _(sx, h=h):
                    sx.activation(out=sqt[:, :], in_=KT[h][:, :], func=AF.Square)
                sc_sqk = ctx.val("sc")

                prev_cp = ctx.val("sc")
                for g in range(NTG):
                    @ctx.on("tensor")
                    def _(t, h=h, g=g, prev_cp=prev_cp):
                        if g == 0:
                            t.wait_ge("sc", sc_sqk)
                            t.wait_ge("dv", qk_cp)
                        else:
                            t.wait_ge("sc", prev_cp)
                        t.matmul(rps2[0:1, :], onescol_bf[:, 0:1],
                                 sqt[:, g * TG:(g + 1) * TG], start=True, stop=True)
                    pv = ctx.val("pe")

                    @ctx.on("scalar")
                    def _(sx, h=h, g=g):
                        sx.wait_ge("pe", pv)
                        sx.activation(out=nk[0:1, g * TG:(g + 1) * TG], in_=rps2[0:1, :],
                                      func=AF.Sqrt)
                    prev_cp = ctx.val("sc")
                row_gate = ("pe", ctx.val("pe"))
                sc_nk = ctx.val("sc")

                @ctx.on("vector")
                def _(v, h=h):
                    v.wait_ge("sc", sc_nk)
                    v.reduce_max(sc2[0:1, 0:1], nk[0:1, :], axis=AX)
                    v.reduce_max(sc2[0:1, 1:2], kpf[0:1, :], axis=AX,
                                 apply_absolute_value=True)
                    v.tensor_scalar(crow[0:1, :], nq[0:1, :], sc2[0:1, 0:1], None, ALU.mult)
                    v.tensor_scalar(qpf[0:1, :], qps_r[0:1, :], sc2[0:1, 1:2], None, ALU.mult)
                    v.tensor_scalar(kpf[0:1, :], qpf[0:1, :], -1.0, None, ALU.mult)
                    v.tensor_tensor(qpf[0:1, :], qpf[0:1, :], kpf[0:1, :], ALU.max)
                    v.tensor_tensor(crow[0:1, :], crow[0:1, :], qpf[0:1, :], ALU.add)
                    v.tensor_scalar(crow[0:1, :], crow[0:1, :], 1.05, 0.5, ALU.mult, ALU.add)
                    v.tensor_scalar(negc_r[0:1, :], crow[0:1, :], -1.0, None, ALU.mult)
                dv_c = ctx.val("dv")

                for qg in range(NTG):
                    nkc = 4 * (qg + 1)
                    for kc in range(nkc):
                        ps = wps[kc % 2]
                        gate = wp_gate[kc % 2]

                        @ctx.on("tensor")
                        def _(t, h=h, qg=qg, kc=kc, ps=ps, gate=gate):
                            if kc == 0 and qg == 0:
                                t.wait_ge("dv", dv_c)
                            if gate is not None:
                                t.wait_ge(gate[0], gate[1])
                            t.matmul(ps[:, :], KT[h][:, kc * 128:(kc + 1) * 128],
                                     QT[h][:, qg * TG:(qg + 1) * TG], start=True, stop=False)
                            t.matmul(ps[:, :], kp_r[0:1, kc * 128:(kc + 1) * 128],
                                     qps_r[0:1, qg * TG:(qg + 1) * TG], start=False, stop=False)
                            t.matmul(ps[:, :], ones_bf[0:1, 0:128],
                                     negc_r[0:1, qg * TG:(qg + 1) * TG], start=False, stop=True)
                        pv = ctx.val("pe")

                        @ctx.on("scalar")
                        def _(sx, kc=kc, ps=ps):
                            sx.wait_ge("pe", pv)
                            sx.activation(out=wch[kc % 2][:, :], in_=ps[:, :], func=AF.Exp)
                        scv = ctx.val("sc")
                        wgate = ("sc", scv)

                        j = kc - 4 * qg
                        if j >= 0:
                            @ctx.on("vector")
                            def _(v, kc=kc, j=j):
                                v.wait_ge("sc", scv)
                                v.tensor_tensor(wch[kc % 2][:, :], wch[kc % 2][:, :],
                                                cmask[j][:, :], ALU.mult)
                            wgate = ("dv", ctx.val("dv"))

                        @ctx.on("tensor")
                        def _(t, h=h, qg=qg, kc=kc, wgate=wgate, av_gate=av_gate, nkc=nkc):
                            t.wait_ge(wgate[0], wgate[1])
                            if kc == 0 and av_gate is not None:
                                t.wait_ge(av_gate[0], av_gate[1])
                            t.matmul(sps[0:1, :], onescol_bf[:, 0:1], wch[kc % 2][:, :],
                                     start=(kc == 0), stop=(kc == nkc - 1))
                            t.matmul(avps[:, :], V[kc][:, h * 128:(h + 1) * 128],
                                     wch[kc % 2][:, :], start=(kc == 0), stop=(kc == nkc - 1))
                        wp_gate[kc % 2] = ("pe", ctx.val("pe"))
                    pvs = ctx.val("pe")

                    @ctx.on("vector")
                    def _(v, norm_gate=norm_gate):
                        v.wait_ge("pe", pvs)
                        if norm_gate is not None:
                            v.wait_ge(norm_gate[0], norm_gate[1])
                        v.reciprocal(rsum[0:1, :], sps[0:1, :])
                    dvr = ctx.val("dv")

                    @ctx.on("tensor")
                    def _(t):
                        t.wait_ge("dv", dvr)
                        t.matmul(rps3[:, :], ones_row[0:1, 0:128], rsum[0:1, :],
                                 start=True, stop=True)
                    pvr = ctx.val("pe")

                    @ctx.on("scalar")
                    def _(sx):
                        sx.wait_ge("pe", pvr)
                        sx.copy(repn[:, :], rps3[:, :])
                    scr = ctx.val("sc")

                    bi = (h * NTG + qg) % 2

                    @ctx.on("vector")
                    def _(v, bi=bi, gate=aot_gate[bi]):
                        v.wait_ge("sc", scr)
                        if gate is not None:
                            v.wait_ge(gate[0], gate[1])
                        v.tensor_tensor(aot[bi][:, :], avps[:, :], repn[:, :], ALU.mult)
                    av_gate = ("dv", ctx.val("dv"))
                    norm_gate = ("pe", pvr)

                    @ctx.on("gpsimd")
                    def _(g2, h=h, qg=qg, bi=bi, dvv=av_gate[1]):
                        g2.wait_ge("dv", dvv)
                        g2.dma("sta2" if bi == 0 else "stb2",
                               out=ao_dram[h * 128:(h + 1) * 128, qg * TG:(qg + 1) * TG],
                               in_=aot[bi][:, :])
                    aot_gate[bi] = ("sta2" if bi == 0 else "stb2",
                                    ctx.val("sta2" if bi == 0 else "stb2"))
        barrier(ctx, extra=("sta2", "stb2"))
        esa.close()

    ao = [ctx.sb(f"aoC{k}", [128, T], BF16) for k in range(NCH)]
    xres = [ctx.sb(f"xresC{i}", [128, TG], F32) for i in range(2)]
    otmp = ctx.sb("otmpC", [128, TG], F32)
    load_weight_chunks(ctx, ao_dram, ao, sem="wld")
    store_gates = []
    with ExitStack() as es:
        wo = [es.enter_context(nc.sbuf_tensor(f"woC{k}", [128, D], BF16))
              for k in range(NCH)]
        load_weight_chunks(ctx, Wout_d, wo)
        wo_done = ctx.val("wld")

        @ctx.on("tensor")
        def _(t):
            t.wait_ge("wld", wo_done)

        def evac_o(m, g, ps, pv):
            idx = m * NTG + g

            @ctx.on("vector")
            def _(v):
                v.wait_ge("pe", pv)
                v.tensor_copy(otmp[:, :], ps[:, :])
            dvv = ctx.val("dv")
            dvv2 = residual_evac(ctx, x_in, x_out, m, g, idx, otmp[:, :],
                                 one_col[:, 0:1], xres, store_gates, dvv)
            return ("dv", dvv2)
        proj_1024(ctx, wo, ao, evac_o, psname="oC")
    wait_stores(ctx)


def build_l2attn():
    nc = bass.Bass("TRN2", target_bir_lowering=False, debug=False)
    x_in = nc.dram_tensor("x_in", [D, T], F32, kind="ExternalInput").ap()
    Wqkv = nc.dram_tensor("Wqkv", [D, 3 * D], F32, kind="ExternalInput").ap()
    Wout = nc.dram_tensor("Wout", [D, D], F32, kind="ExternalInput").ap()
    hs = nc.dram_tensor("hs", [1, 8], F32, kind="ExternalInput").ap()
    dirs = nc.dram_tensor("dirs", [3, 8], F32, kind="ExternalInput").ap()
    cmask = nc.dram_tensor("cmask", [4, 128, TG], F32, kind="ExternalInput").ap()
    x_out = nc.dram_tensor("x_out", [D, T], F32, kind="ExternalOutput").ap()
    ao_dram = nc.dram_tensor("ao_scr", [D, T], BF16).ap()
    with ExitStack() as stack:
        block = stack.enter_context(nc.Block())
        ctx = Ctx(nc, block, stack)
        emit_l2attn(ctx, x_in, x_out, Wqkv, Wout, hs, dirs, cmask, ao_dram)
    return nc


def l2attn_maps(Wqkv, Wout, head_scales, head_dirs):
    j = np.arange(4)[:, None, None] * 128
    kp = np.arange(128)[None, :, None]
    qf = np.arange(TG)[None, None, :]
    cmask = ((j + kp) <= qf).astype(np.float32)
    return {
        "Wqkv": np.ascontiguousarray(np.asarray(Wqkv)),
        "Wout": np.ascontiguousarray(np.asarray(Wout)),
        "hs": np.asarray(head_scales).reshape(1, 8).copy(),
        "dirs": np.ascontiguousarray(np.asarray(head_dirs).T),
        "cmask": cmask,
    }


def run_l2attn(xT, Wqkv, Wout, head_scales, head_dirs):
    nc = build_l2attn()
    wmap = l2attn_maps(Wqkv, Wout, head_scales, head_dirs)
    in_maps = [dict(wmap, x_in=np.ascontiguousarray(xT[b])) for b in range(8)]
    res = run_bass_kernel_spmd(nc, in_maps, core_ids=list(range(8)))
    return np.stack([r["x_out"] for r in res.results])



def emit_l0attn(ctx, x_in, x_out, W1_d, WoT_d, bo_d, ident_d, ao_dram, mn_dram):
    nc = ctx.nc
    H = 16
    SC = 0.125
    ones_row = ctx.sb("onesrowS", [1, 128], F32)
    ones_bf = ctx.sb("onesbfS", [1, TG], BF16)
    onescol_bf = ctx.sb("onescolS", [128, 1], BF16)
    one_col = ctx.sb("onecolS", [128, 1], F32)
    nqsc = ctx.sb("nqscS", [128, 1], F32)
    two_col = ctx.sb("twocolS", [128, 1], F32)
    identf = ctx.sb("identfS", [128, 128], F32)
    borow = ctx.sb("borowS", [1, D], BF16)

    QN = [ctx.sb(f"qnS{k}", [128, T], BF16) for k in range(NCH)]
    KT = [ctx.sb(f"ktS{k}", [128, T], BF16) for k in range(NCH)]
    V = [ctx.sb(f"vtlS{k}", [128, D], BF16) for k in range(16)]

    @ctx.on("vector")
    def _(v):
        v.memset(ones_row[:, :], 1.0)
        v.memset(ones_bf[:, :], 1.0)
        v.memset(onescol_bf[:, :], 1.0)
        v.memset(one_col[:, :], 1.0)
        v.memset(nqsc[:, :], -SC)
        v.memset(two_col[:, :], 2.0)

    @ctx.on("gpsimd")
    def _(g):
        g.dma("wld", out=identf[:, :], in_=ident_d)
        g.dma("wld", out=borow[:, :], in_=bo_d)

    with ExitStack() as es:
        rstd = es.enter_context(nc.sbuf_tensor("rstdS", [1, T], F32))
        negmr = es.enter_context(nc.sbuf_tensor("negmrS", [1, T], F32))
        rep_rstd = es.enter_context(nc.sbuf_tensor("rep_rstdS", [128, T], BF16))
        rep_negmr = es.enter_context(nc.sbuf_tensor("rep_negmrS", [128, T], BF16))
        xbuf = [es.enter_context(nc.sbuf_tensor(f"xbS{i}", [128, T], F32)) for i in range(2)]
        y = [es.enter_context(nc.sbuf_tensor(f"yS{k}", [128, T], BF16)) for k in range(NCH)]

        ln_rows(ctx, x_in, rstd, negmr, tag="S", xbuf=xbuf)
        replicate_rows(ctx, [(rstd, rep_rstd), (negmr, rep_negmr)], ones_row, tag="S")
        build_y(ctx, x_in, rep_rstd, rep_negmr, y, xbuf)

        with ExitStack() as es2:
            wt = [es2.enter_context(nc.sbuf_tensor(f"wSqk{k}", [128, 2 * D], BF16))
                  for k in range(NCH)]
            load_weight_chunks(ctx, W1_d[:, 0:2 * D], wt)
            wv_done = ctx.val("wld")

            @ctx.on("tensor")
            def _(t):
                t.wait_ge("wld", wv_done)

            def evac_qk(m, g, ps, pv):
                @ctx.on("scalar")
                def _(sx):
                    sx.wait_ge("pe", pv)
                    if m < NCH:
                        sx.activation(out=QN[m][:, g * TG:(g + 1) * TG], in_=ps[:, :],
                                      func=AF.Identity, scale=nqsc[:, 0:1])
                    else:
                        sx.copy(KT[m - NCH][:, g * TG:(g + 1) * TG], ps[:, :])
                return ("sc", ctx.val("sc"))
            proj_1024(ctx, wt, y, evac_qk, psname="qkS", mchunks=16)

        with ExitStack() as es2:
            wt = [es2.enter_context(nc.sbuf_tensor(f"wSv{k}", [128, D], BF16))
                  for k in range(NCH)]
            load_weight_chunks(ctx, W1_d[:, 2 * D:3 * D], wt)
            wv_done = ctx.val("wld")

            @ctx.on("tensor")
            def _(t):
                t.wait_ge("wld", wv_done)

            with nc.psum_tensor("vpsS0", [128, TG], F32) as p0, \
                 nc.psum_tensor("vpsS1", [128, TG], F32) as p1:
                banks = [p0, p1]
                bank_free = [None, None]
                bi = 0
                for tt in range(16):
                    for ng in range(2):
                        ps = banks[bi]
                        gate = bank_free[bi]

                        @ctx.on("tensor")
                        def _(t, tt=tt, ng=ng, ps=ps, gate=gate):
                            if gate is not None:
                                t.wait_ge(gate[0], gate[1])
                            for k in range(NCH):
                                t.matmul(ps[:, :], y[k][:, tt * 128:(tt + 1) * 128],
                                         wt[k][:, ng * TG:(ng + 1) * TG],
                                         start=(k == 0), stop=(k == NCH - 1))
                        pv = ctx.val("pe")

                        @ctx.on("scalar")
                        def _(sx, tt=tt, ng=ng, ps=ps):
                            sx.wait_ge("pe", pv)
                            sx.copy(V[tt][:, ng * TG:(ng + 1) * TG], ps[:, :])
                        bank_free[bi] = ("sc", ctx.val("sc"))
                        bi ^= 1
            barrier(ctx)

    esa = ExitStack()
    mncol = esa.enter_context(nc.sbuf_tensor("mncolS", [128, 16], F32))
    mnsb = esa.enter_context(nc.sbuf_tensor("mnsbS", [16, 128], F32))
    mnrow = esa.enter_context(nc.sbuf_tensor("mnrowS", [1, T], F32))
    mp1 = esa.enter_context(nc.sbuf_tensor("mp1S", [1, T], F32))
    bt = esa.enter_context(nc.sbuf_tensor("btS", [128, T], BF16))
    b2t = esa.enter_context(nc.sbuf_tensor("b2tS", [128, T], BF16))
    gt = esa.enter_context(nc.sbuf_tensor("gtS", [128, T], BF16))
    p2t = esa.enter_context(nc.sbuf_tensor("p2tS", [128, T], F32))
    wti = esa.enter_context(nc.sbuf_tensor("wtiS", [128, T], BF16))
    rsum = esa.enter_context(nc.sbuf_tensor("rsumS", [1, T], F32))
    repn = esa.enter_context(nc.sbuf_tensor("repnS", [64, T], F32))
    aot = [esa.enter_context(nc.sbuf_tensor(f"aotS{i}", [64, T], BF16)) for i in range(2)]
    aot_gate = [None, None]
    vh65 = [esa.enter_context(nc.sbuf_tensor(f"vh65S{i}", [128, 65], BF16)) for i in range(2)]
    vh_gate = [None, None]

    @ctx.on("vector")
    def _(v):
        v.memset(vh65[0][:, 64:65], 1.0)
        v.memset(vh65[1][:, 64:65], 1.0)

    tp_gate = None
    for h in range(H):
        hc, ho = h // 2, (h % 2) * 64
        with nc.psum_tensor(f"apsS{h}", [128, T], F32) as aps, \
             nc.psum_tensor(f"tpsS{h}", [16, 128], F32) as tps:
            red_gate = [ctx.val("dv")]
            for qt in range(16):
                @ctx.on("tensor")
                def _(t, h=h, qt=qt, rg=red_gate[0]):
                    t.wait_ge("dv", rg)
                    for gk in range(NTG):
                        t.matmul(aps[:, gk * TG:(gk + 1) * TG],
                                 QN[hc][ho:ho + 64, qt * 128:(qt + 1) * 128],
                                 KT[hc][ho:ho + 64, gk * TG:(gk + 1) * TG],
                                 start=True, stop=True, tile_position=(ho, 0))
                pv = ctx.val("pe")

                @ctx.on("vector")
                def _(v, h=h, qt=qt):
                    v.wait_ge("pe", pv)
                    v.tensor_reduce(mncol[:, qt:qt + 1], aps[:, :], axis=AX, op=ALU.min)
                red_gate[0] = ctx.val("dv")
            dvmn = ctx.val("dv")

            @ctx.on("tensor")
            def _(t, h=h, tp_gate=tp_gate):
                t.wait_ge("dv", dvmn)
                if tp_gate is not None:
                    t.wait_ge(tp_gate[0], tp_gate[1])
                t.transpose(tps[:, :], mncol[:, 0:16], identf[:, :])
            pvt = ctx.val("pe")

            @ctx.on("scalar")
            def _(sx, h=h):
                sx.wait_ge("pe", pvt)
                sx.copy(mnsb[0:16, :], tps[0:16, :])
            sct = ctx.val("sc")

            @ctx.on("gpsimd")
            def _(g2, h=h):
                g2.wait_ge("sc", sct)
                g2.dma("mnd", out=mn_dram[:, :], in_=mnsb[0:16, :])
                g2.wait_ge("mnd", ctx.cnt["mnd"])
                g2.dma("mnd", out=mnrow[0:1, :],
                       in_=mn_dram.rearrange("a b -> (a b)")[None, :])
            mnd = ctx.val("mnd")

            @ctx.on("vector")
            def _(v, h=h):
                v.wait_ge("mnd", mnd)
                v.tensor_scalar(mp1[0:1, :], mnrow[0:1, :], -1.0, 1.0, ALU.mult, ALU.add)
            dvmp = ctx.val("dv")
            tp_gate = ("dv", dvmp)

        with nc.psum_tensor(f"avpsS{h}", [65, T], F32) as avps:
          with nc.psum_tensor(f"bpsS{h}", [128, T], F32) as bps:
            bp_gate = None
            for kc in range(16):
                @ctx.on("tensor")
                def _(t, h=h, kc=kc, bp_gate=bp_gate):
                    if kc == 0:
                        t.wait_ge("dv", dvmp)
                    if bp_gate is not None:
                        t.wait_ge(bp_gate[0], bp_gate[1])
                    for gq in range(NTG):
                        t.matmul(bps[:, gq * TG:(gq + 1) * TG],
                                 KT[hc][ho:ho + 64, kc * 128:(kc + 1) * 128],
                                 QN[hc][ho:ho + 64, gq * TG:(gq + 1) * TG],
                                 start=True, stop=False, tile_position=(ho, 0))
                        t.matmul(bps[:, gq * TG:(gq + 1) * TG],
                                 ones_row[0:1, 0:128],
                                 mp1[0:1, gq * TG:(gq + 1) * TG],
                                 start=False, stop=True)
                pv = ctx.val("pe")

                @ctx.on("scalar")
                def _(sx, h=h, kc=kc):
                    sx.wait_ge("pe", pv)
                    sx.copy(bt[:, :], bps[:, :])
                bp_gate = ("sc", ctx.val("sc"))

                @ctx.on("vector")
                def _(v, h=h, kc=kc, scv=bp_gate[1]):
                    v.wait_ge("sc", scv)
                    v.tensor_tensor(b2t[:, :], bt[:, :], bt[:, :], ALU.mult)
                dvb2 = ctx.val("dv")

                @ctx.on("vector")
                def _(v, h=h, kc=kc):
                    v.scalar_tensor_tensor(gt[:, :], b2t[:, :], 3.0, bt[:, :],
                                            ALU.add, ALU.mult)
                gpg = ctx.val("dv")

                @ctx.on("scalar")
                def _(sx, h=h, kc=kc):
                    sx.wait_ge("dv", gpg)
                    sx.activation(out=p2t[:, :], in_=gt[:, :], func=AF.Identity,
                                  bias=two_col[:, 0:1])
                scp = ctx.val("sc")

                @ctx.on("vector")
                def _(v, h=h, kc=kc):
                    v.wait_ge("sc", scp)
                    with ctx.nc.allow_low_precision(reason="w is a bf16 matmul operand"):
                        v.reciprocal(wti[:, :], p2t[:, :])
                dvw = ctx.val("dv")

                vbi = kc % 2

                @ctx.on("vector")
                def _(v, h=h, kc=kc, vbi=vbi, gate=vh_gate[vbi]):
                    if gate is not None:
                        v.wait_ge(gate[0], gate[1])
                    v.tensor_copy(vh65[vbi][:, 0:64], V[kc][:, h * 64:(h + 1) * 64])
                dvvh = ctx.val("dv")

                @ctx.on("tensor")
                def _(t, h=h, kc=kc, vbi=vbi):
                    t.wait_ge("dv", dvw)
                    t.wait_ge("dv", dvvh)
                    for gq in range(NTG):
                        t.matmul(avps[0:65, gq * TG:(gq + 1) * TG],
                                 vh65[vbi][:, 0:65],
                                 wti[:, gq * TG:(gq + 1) * TG],
                                 start=(kc == 0), stop=(kc == 15))
                vh_gate[vbi] = ("pe", ctx.val("pe"))
          pvav = ctx.val("pe")

          @ctx.on("vector")
          def _(v, h=h):
              v.wait_ge("pe", pvav)
              v.reciprocal(rsum[0:1, :], avps[64:65, :])
          dvr = ctx.val("dv")

          with nc.psum_tensor(f"rpsS{h}", [64, T], F32) as rps:
              @ctx.on("tensor")
              def _(t, h=h):
                  t.wait_ge("dv", dvr)
                  for gq in range(NTG):
                      t.matmul(rps[:, gq * TG:(gq + 1) * TG], ones_row[0:1, 0:64],
                               rsum[0:1, gq * TG:(gq + 1) * TG], start=True, stop=True)
              pvr = ctx.val("pe")

              @ctx.on("scalar")
              def _(sx, h=h):
                  sx.wait_ge("pe", pvr)
                  sx.copy(repn[0:64, :], rps[:, :])
              scr = ctx.val("sc")

              bi = h % 2

              @ctx.on("vector")
              def _(v, h=h, bi=bi, gate=aot_gate[bi]):
                  v.wait_ge("sc", scr)
                  if gate is not None:
                      v.wait_ge(gate[0], gate[1])
                  v.tensor_tensor(aot[bi][0:64, :], avps[0:64, :], repn[0:64, :],
                                  ALU.mult)
              dvao = ctx.val("dv")

              @ctx.on("gpsimd")
              def _(g2, h=h, bi=bi):
                  g2.wait_ge("dv", dvao)
                  g2.dma("sta2" if bi == 0 else "stb2",
                         out=ao_dram[h * 64:(h + 1) * 64, :], in_=aot[bi][0:64, :])
              aot_gate[bi] = ("sta2" if bi == 0 else "stb2",
                              ctx.val("sta2" if bi == 0 else "stb2"))
    esa.close()
    barrier(ctx, extra=("sta2", "stb2"))

    ao = [ctx.sb(f"aoS{k}", [128, T], BF16) for k in range(NCH)]
    load_weight_chunks(ctx, ao_dram, ao, sem="wld")

    xres = [ctx.sb(f"xresS{i}", [128, TG], F32) for i in range(2)]
    otmp = ctx.sb("otmpS", [128, TG], F32)
    store_gates = []
    with ExitStack() as es:
        wo = [es.enter_context(nc.sbuf_tensor(f"woS{k}", [128, D], BF16))
              for k in range(NCH)]
        load_weight_chunks(ctx, WoT_d, wo)
        wo_done = ctx.val("wld")

        @ctx.on("tensor")
        def _(t):
            t.wait_ge("wld", wo_done)

        with nc.psum_tensor("oSps0", [128, TG], F32) as p0, \
             nc.psum_tensor("oSps1", [128, TG], F32) as p1:
            banks = [p0, p1]
            bank_free = [None, None]
            bi = 0
            for m in range(NCH):
                for g in range(NTG):
                    ps = banks[bi]
                    gate = bank_free[bi]

                    @ctx.on("tensor")
                    def _(t, m=m, g=g, ps=ps, gate=gate):
                        if gate is not None:
                            t.wait_ge(gate[0], gate[1])
                        for k in range(NCH):
                            t.matmul(ps[:, :], wo[k][:, m * 128:(m + 1) * 128],
                                     ao[k][:, g * TG:(g + 1) * TG],
                                     start=(k == 0), stop=False)
                        t.matmul(ps[:, :], borow[0:1, m * 128:(m + 1) * 128],
                                 ones_bf[0:1, :], start=False, stop=True)
                    pv = ctx.val("pe")
                    idx = m * NTG + g

                    @ctx.on("vector")
                    def _(v, ps=ps):
                        v.wait_ge("pe", pv)
                        v.tensor_copy(otmp[:, :], ps[:, :])
                    dvv = ctx.val("dv")
                    dvv2 = residual_evac(ctx, x_in, x_out, m, g, idx, otmp[:, :],
                                         one_col[:, 0:1], xres, store_gates, dvv)
                    bank_free[bi] = ("dv", dvv2)
                    bi ^= 1
    wait_stores(ctx)


def build_l0attn():
    nc = bass.Bass("TRN2", target_bir_lowering=False, debug=False)
    x_in = nc.dram_tensor("x_in", [D, T], F32, kind="ExternalInput").ap()
    W1 = nc.dram_tensor("W1", [D, 3 * D], F32, kind="ExternalInput").ap()
    WoT = nc.dram_tensor("WoT", [D, D], F32, kind="ExternalInput").ap()
    bo = nc.dram_tensor("bo", [1, D], F32, kind="ExternalInput").ap()
    ident = nc.dram_tensor("ident", [128, 128], F32, kind="ExternalInput").ap()
    x_out = nc.dram_tensor("x_out", [D, T], F32, kind="ExternalOutput").ap()
    ao_dram = nc.dram_tensor("aoS_scr", [D, T], BF16).ap()
    mn_dram = nc.dram_tensor("mnS_scr", [16, 128], F32).ap()
    with ExitStack() as stack:
        block = stack.enter_context(nc.Block())
        ctx = Ctx(nc, block, stack)
        emit_l0attn(ctx, x_in, x_out, W1, WoT, bo, ident, ao_dram, mn_dram)
    return nc


def l0attn_maps(w_qkv, w_out, b_out):
    return {
        "W1": np.ascontiguousarray(np.asarray(w_qkv).T),
        "WoT": np.ascontiguousarray(np.asarray(w_out).T),
        "bo": np.asarray(b_out).reshape(1, D).copy(),
        "ident": np.eye(128, dtype=np.float32),
    }


def run_l0attn(xT, w_qkv, w_out, b_out):
    nc = build_l0attn()
    wmap = l0attn_maps(w_qkv, w_out, b_out)
    in_maps = [dict(wmap, x_in=np.ascontiguousarray(xT[b])) for b in range(8)]
    res = run_bass_kernel_spmd(nc, in_maps, core_ids=list(range(8)))
    return np.stack([r["x_out"] for r in res.results])



def emit_l3attn(ctx, x_in, x_out, Wqq_d, Wkk_d, Wv_d, Wout_d, tabs_d,
                masks_d, v_dram, ao_dram):
    nc = ctx.nc
    H = 16
    ones_row = ctx.sb("onesrowW", [1, 128], F32)
    one_col = ctx.sb("onecolW", [128, 1], F32)
    qsc = ctx.sb("qscW", [128, 1], F32)
    negc = ctx.sb("negcW", [128, 1], F32)
    tabs = [ctx.sb(f"tabW{i}", [128, T], BF16) for i in range(4)]
    masks = [ctx.sb(f"maskW{i}", [128, 128], BF16) for i in range(2)]

    QR = [ctx.sb(f"qrW{k}", [128, T], BF16) for k in range(NCH)]
    KH = [ctx.sb(f"khW{k}", [128, T], BF16) for k in range(NCH)]
    KL = [ctx.sb(f"klW{k}", [128, T], BF16) for k in range(NCH)]

    @ctx.on("vector")
    def _(v):
        v.memset(ones_row[:, :], 1.0)
        v.memset(one_col[:, :], 1.0)
        v.memset(qsc[:, :], 0.125)
        v.memset(negc[:, :], -16.0)

    @ctx.on("gpsimd")
    def _(g):
        for i in range(4):
            g.dma("wld", out=tabs[i][:, :], in_=tabs_d[i])
        for i in range(2):
            g.dma("wld", out=masks[i][:, :], in_=masks_d[i])

    with ExitStack() as es:
        rowp2 = es.enter_context(nc.sbuf_tensor("rowsW", [33, T], F32))
        rstd = rowp2[0:1, :]
        negmr = rowp2[32:33, :]
        rep_rstd = es.enter_context(nc.sbuf_tensor("rep_rstdW", [128, T], BF16))
        rep_negmr = es.enter_context(nc.sbuf_tensor("rep_negmrW", [128, T], BF16))
        xbuf = [es.enter_context(nc.sbuf_tensor(f"xbW{i}", [128, T], F32)) for i in range(2)]
        y = [es.enter_context(nc.sbuf_tensor(f"yW{k}", [128, T], BF16)) for k in range(NCH)]

        ln_rows(ctx, x_in, rstd, negmr, tag="W", xbuf=xbuf)
        replicate_rows(ctx, [(rstd, rep_rstd), (negmr, rep_negmr)], ones_row, tag="W")
        build_y(ctx, x_in, rep_rstd, rep_negmr, y, xbuf)

        with ExitStack() as es2:
            wt = [es2.enter_context(nc.sbuf_tensor(f"wWv{k}", [128, D], BF16))
                  for k in range(NCH)]
            v65 = [es2.enter_context(nc.sbuf_tensor(f"v65W{i}", [128, 1040], BF16))
                   for i in range(2)]
            load_weight_chunks(ctx, Wv_d, wt)
            wv_done = ctx.val("wld")

            @ctx.on("vector")
            def _(v):
                v.wait_ge("wld", wv_done)
                for i in range(2):
                    v.memset(v65[i][:, :].rearrange("p (h c) -> p h c", c=65)[:, :, 64:65], 1.0)
            dv65 = ctx.val("dv")

            @ctx.on("tensor")
            def _(t):
                t.wait_ge("wld", wv_done)

            with nc.psum_tensor("vpsW0", [128, TG], F32) as p0, \
                 nc.psum_tensor("vpsW1", [128, TG], F32) as p1:
                banks = [p0, p1]
                bank_free = [None, None]
                v65_gate = [None, None]
                bi = 0
                for tt in range(16):
                    for ng in range(2):
                        ps = banks[bi]
                        gate = bank_free[bi]

                        @ctx.on("tensor")
                        def _(t, tt=tt, ng=ng, ps=ps, gate=gate):
                            if gate is not None:
                                t.wait_ge(gate[0], gate[1])
                            for k in range(NCH):
                                t.matmul(ps[:, :], y[k][:, tt * 128:(tt + 1) * 128],
                                         wt[k][:, ng * TG:(ng + 1) * TG],
                                         start=(k == 0), stop=(k == NCH - 1))
                        pv = ctx.val("pe")
                        vb = tt % 2

                        @ctx.on("scalar")
                        def _(sx, tt=tt, ng=ng, ps=ps, vb=vb, g65=v65_gate[vb]):
                            sx.wait_ge("pe", pv)
                            sx.wait_ge("dv", dv65)
                            if ng == 0 and g65 is not None:
                                sx.wait_ge(g65[0], g65[1])
                            sx.copy(v65[vb][:, :].rearrange("p (h c) -> p h c", c=65)
                                    [:, 8 * ng:8 * ng + 8, 0:64],
                                    ps[:, :].rearrange("p (h c) -> p h c", c=64))
                        bank_free[bi] = ("sc", ctx.val("sc"))
                        bi ^= 1
                        if ng == 1:
                            scv = ctx.val("sc")

                            @ctx.on("gpsimd")
                            def _(g2, tt=tt, vb=vb):
                                g2.wait_ge("sc", scv)
                                g2.dma("sta2" if vb == 0 else "stb2",
                                       out=v_dram[tt * 128:(tt + 1) * 128, :],
                                       in_=v65[vb][:, :])
                            v65_gate[vb] = ("sta2" if vb == 0 else "stb2",
                                            ctx.val("sta2" if vb == 0 else "stb2"))
            barrier(ctx, extra=("sta2", "stb2"))

        for (Wd, outs) in ((Wqq_d, "q"), (Wkk_d, "k")):
          for half in range(2):
            with ExitStack() as es2:
                wt = [es2.enter_context(nc.sbuf_tensor(f"wW{outs}{half}_{k}", [128, D], BF16))
                      for k in range(NCH)]
                raw = es2.enter_context(nc.sbuf_tensor(f"rawW{outs}{half}", [128, T], BF16))
                sw = es2.enter_context(nc.sbuf_tensor(f"swW{outs}{half}", [128, T], BF16))
                tmp = es2.enter_context(nc.sbuf_tensor(f"tmpW{outs}{half}", [128, T], BF16))
                load_weight_chunks(ctx, Wd[:, half * D:(half + 1) * D], wt)
                w_done = ctx.val("wld")

                @ctx.on("tensor")
                def _(t):
                    t.wait_ge("wld", w_done)

                rope_done = [0]

                def evac_qk(m, g, ps, pv, outs=outs, half=half):
                    c, is_sw = half * 4 + m // 2, m % 2
                    dst = sw if is_sw else raw

                    @ctx.on("scalar")
                    def _(sx, ps=ps, dst=dst):
                        sx.wait_ge("pe", pv)
                        sx.wait_ge("dv", rope_done[0])
                        if outs == "q":
                            sx.activation(out=dst[:, g * TG:(g + 1) * TG], in_=ps[:, :],
                                          func=AF.Identity, scale=qsc[:, 0:1])
                        else:
                            sx.copy(dst[:, g * TG:(g + 1) * TG], ps[:, :])
                    gate = ("sc", ctx.val("sc"))
                    if is_sw and g == NTG - 1:
                        scv = ctx.val("sc")

                        @ctx.on("vector")
                        def _(v, c=c, outs=outs):
                            v.wait_ge("sc", scv)
                            if outs == "q":
                                v.tensor_tensor(tmp[:, :], sw[:, :], tabs[1][:, :], ALU.mult)
                                v.tensor_tensor(QR[c][:, :], raw[:, :], tabs[0][:, :], ALU.mult)
                                v.tensor_tensor(QR[c][:, :], QR[c][:, :], tmp[:, :], ALU.add)
                            else:
                                v.tensor_tensor(tmp[:, :], sw[:, :], tabs[1][:, :], ALU.mult)
                                v.tensor_tensor(KH[c][:, :], raw[:, :], tabs[0][:, :], ALU.mult)
                                v.tensor_tensor(KH[c][:, :], KH[c][:, :], tmp[:, :], ALU.add)
                                v.tensor_tensor(tmp[:, :], sw[:, :], tabs[3][:, :], ALU.mult)
                                v.tensor_tensor(KL[c][:, :], raw[:, :], tabs[2][:, :], ALU.mult)
                                v.tensor_tensor(KL[c][:, :], KL[c][:, :], tmp[:, :], ALU.add)
                        rope_done[0] = ctx.val("dv")
                        gate = ("dv", rope_done[0])
                    return gate
                proj_1024(ctx, wt, y, evac_qk, psname=f"qk{outs}{half}W", mchunks=8)

    V65 = [ctx.sb(f"V65W{k}", [128, 1040], BF16) for k in range(16)]
    load_weight_chunks(ctx, v_dram, V65, sem="wld")
    v_done = ctx.val("wld")

    esa = ExitStack()
    wch = [esa.enter_context(nc.sbuf_tensor(f"wchW{i}", [128, 128], BF16)) for i in range(2)]
    rsum = esa.enter_context(nc.sbuf_tensor("rsumW", [1, 128], F32))
    repn = esa.enter_context(nc.sbuf_tensor("repnW", [64, 128], F32))
    aot = [esa.enter_context(nc.sbuf_tensor(f"aotW{i}", [64, T], BF16)) for i in range(2)]
    aot_gate = [None, None]

    with nc.psum_tensor("wpsW0", [128, 128], F32) as w0, \
         nc.psum_tensor("wpsW1", [128, 128], F32) as w1, \
         nc.psum_tensor("avpsW", [65, 128], F32) as avps, \
         nc.psum_tensor("reppsW3", [64, 128], F32) as rps:
        wps = [w0, w1]
        wp_gate = [None, None]
        av_gate = None
        norm_gate = None

        @ctx.on("tensor")
        def _(t):
            t.wait_ge("wld", v_done)

        for h in range(H):
            hc, ho = h // 2, (h % 2) * 64
            bi_ao = h % 2
            for w in range(16):
                kctxs = ([w - 1] if w > 0 else []) + [w]
                nk = len(kctxs)
                for ki, kc in enumerate(kctxs):
                    own = (kc == w)
                    KT_ = KH if own else KL
                    ps = wps[ki % 2]
                    gate = wp_gate[ki % 2]

                    @ctx.on("tensor")
                    def _(t, h=h, w=w, kc=kc, ps=ps, gate=gate, own=own, KT_=KT_):
                        if gate is not None:
                            t.wait_ge(gate[0], gate[1])
                        t.matmul(ps[:, :], KT_[hc][ho:ho + 64, kc * 128:(kc + 1) * 128],
                                 QR[hc][ho:ho + 64, w * 128:(w + 1) * 128],
                                 start=True, stop=True, tile_position=(ho, 0))
                    pv = ctx.val("pe")

                    @ctx.on("scalar")
                    def _(sx, ki=ki, ps=ps):
                        sx.wait_ge("pe", pv)
                        sx.activation(out=wch[ki % 2][:, :], in_=ps[:, :], func=AF.Exp,
                                      bias=negc[:, 0:1])
                    scv = ctx.val("sc")

                    @ctx.on("vector")
                    def _(v, ki=ki, own=own):
                        v.wait_ge("sc", scv)
                        v.tensor_tensor(wch[ki % 2][:, :], wch[ki % 2][:, :],
                                        masks[0 if own else 1][:, :], ALU.mult)
                    dvm = ctx.val("dv")

                    @ctx.on("tensor")
                    def _(t, h=h, kc=kc, ki=ki, nk=nk, av_gate=av_gate):
                        t.wait_ge("dv", dvm)
                        if ki == 0 and av_gate is not None:
                            t.wait_ge(av_gate[0], av_gate[1])
                        t.matmul(avps[:, :], V65[kc][:, h * 65:(h + 1) * 65],
                                 wch[ki % 2][:, :], start=(ki == 0), stop=(ki == nk - 1))
                    wp_gate[ki % 2] = ("pe", ctx.val("pe"))
                pvav = ctx.val("pe")

                @ctx.on("vector")
                def _(v, norm_gate=norm_gate):
                    v.wait_ge("pe", pvav)
                    if norm_gate is not None:
                        v.wait_ge(norm_gate[0], norm_gate[1])
                    v.reciprocal(rsum[0:1, :], avps[64:65, :])
                dvr = ctx.val("dv")

                @ctx.on("tensor")
                def _(t):
                    t.wait_ge("dv", dvr)
                    t.matmul(rps[:, :], ones_row[0:1, 0:64], rsum[0:1, :],
                             start=True, stop=True)
                pvr = ctx.val("pe")

                @ctx.on("scalar")
                def _(sx):
                    sx.wait_ge("pe", pvr)
                    sx.copy(repn[0:64, :], rps[:, :])
                scr = ctx.val("sc")

                @ctx.on("vector")
                def _(v, h=h, w=w, bi=bi_ao, gate=aot_gate[bi_ao]):
                    v.wait_ge("sc", scr)
                    if gate is not None and w == 0:
                        v.wait_ge(gate[0], gate[1])
                    v.tensor_tensor(aot[bi][0:64, w * 128:(w + 1) * 128],
                                    avps[0:64, :], repn[0:64, :], ALU.mult)
                av_gate = ("dv", ctx.val("dv"))
                norm_gate = ("pe", pvr)
            dvao = ctx.val("dv")

            @ctx.on("gpsimd")
            def _(g2, h=h, bi=bi_ao):
                g2.wait_ge("dv", dvao)
                g2.dma("sta3" if bi == 0 else "stb3",
                       out=ao_dram[h * 64:(h + 1) * 64, :], in_=aot[bi][0:64, :])
            aot_gate[bi_ao] = ("sta3" if bi_ao == 0 else "stb3",
                               ctx.val("sta3" if bi_ao == 0 else "stb3"))
    esa.close()
    barrier(ctx, extra=("sta3", "stb3"))

    ao = [ctx.sb(f"aoW{k}", [128, T], BF16) for k in range(NCH)]
    xres = [ctx.sb(f"xresW{i}", [128, TG], F32) for i in range(2)]
    otmp = ctx.sb("otmpW", [128, TG], F32)
    load_weight_chunks(ctx, ao_dram, ao, sem="wld")
    store_gates = []
    with ExitStack() as es:
        wo = [es.enter_context(nc.sbuf_tensor(f"woW{k}", [128, D], BF16))
              for k in range(NCH)]
        load_weight_chunks(ctx, Wout_d, wo)
        wo_done = ctx.val("wld")

        @ctx.on("tensor")
        def _(t):
            t.wait_ge("wld", wo_done)

        def evac_o(m, g, ps, pv):
            idx = m * NTG + g

            @ctx.on("vector")
            def _(v):
                v.wait_ge("pe", pv)
                v.tensor_copy(otmp[:, :], ps[:, :])
            dvv = ctx.val("dv")
            dvv2 = residual_evac(ctx, x_in, x_out, m, g, idx, otmp[:, :],
                                 one_col[:, 0:1], xres, store_gates, dvv)
            return ("dv", dvv2)
        proj_1024(ctx, wo, ao, evac_o, psname="oW")
    wait_stores(ctx)


def build_l3attn():
    nc = bass.Bass("TRN2", target_bir_lowering=False, debug=False)
    x_in = nc.dram_tensor("x_in", [D, T], F32, kind="ExternalInput").ap()
    Wqq = nc.dram_tensor("Wqq", [D, 2 * D], F32, kind="ExternalInput").ap()
    Wkk = nc.dram_tensor("Wkk", [D, 2 * D], F32, kind="ExternalInput").ap()
    Wv = nc.dram_tensor("Wv", [D, D], F32, kind="ExternalInput").ap()
    Wout = nc.dram_tensor("Wout", [D, D], F32, kind="ExternalInput").ap()
    tabs = nc.dram_tensor("tabs", [4, 128, T], F32, kind="ExternalInput").ap()
    masks = nc.dram_tensor("masks", [2, 128, 128], F32, kind="ExternalInput").ap()
    x_out = nc.dram_tensor("x_out", [D, T], F32, kind="ExternalOutput").ap()
    v_dram = nc.dram_tensor("vW_scr", [T, 1040], BF16).ap()
    ao_dram = nc.dram_tensor("aoW_scr", [D, T], BF16).ap()
    with ExitStack() as stack:
        block = stack.enter_context(nc.Block())
        ctx = Ctx(nc, block, stack)
        emit_l3attn(ctx, x_in, x_out, Wqq, Wkk, Wv, Wout, tabs, masks, v_dram, ao_dram)
    return nc


def l3attn_maps(W_qkv, W_out):
    W_qkv = np.asarray(W_qkv)
    dh = 64
    d = np.arange(D)
    swap = (d // dh) * dh + np.where(d % dh < 32, d % dh + 32, d % dh - 32)
    Wq = W_qkv[:, 0:D]
    Wk = W_qkv[:, D:2 * D]
    Wv = W_qkv[:, 2 * D:3 * D]
    def interleave(W):
        Wsw = W[:, swap]
        cols = []
        for c in range(NCH):
            cols.append(W[:, c * 128:(c + 1) * 128])
            cols.append(Wsw[:, c * 128:(c + 1) * 128])
        return np.ascontiguousarray(np.concatenate(cols, axis=1))
    inv_freq = 1.0 / 10000.0 ** (np.arange(0, dh, 2, dtype=np.float64) / dh)
    pos = np.arange(256, dtype=np.float64)
    fr = pos[:, None] * inv_freq[None, :]
    fr = np.concatenate([fr, fr], axis=1)
    cos = np.cos(fr).astype(np.float32)
    sin = np.sin(fr).astype(np.float32)
    sgn = np.where(np.arange(dh) < 32, -1.0, 1.0).astype(np.float32)
    sinp = sin * sgn[None, :]
    tpos = np.tile(np.arange(128), 16)
    def table(tab2d, hi):
        rows = tab2d[(128 if hi else 0) + tpos]
        blk = rows.T
        return np.ascontiguousarray(np.concatenate([blk, blk], axis=0))
    tabs = np.stack([table(cos, True), table(sinp, True),
                     table(cos, False), table(sinp, False)])
    kp = np.arange(128)[:, None]
    qf = np.arange(128)[None, :]
    masks = np.stack([(kp <= qf).astype(np.float32), (kp >= qf).astype(np.float32)])
    return {
        "Wqq": interleave(Wq),
        "Wkk": interleave(Wk),
        "Wv": np.ascontiguousarray(Wv),
        "Wout": np.ascontiguousarray(np.asarray(W_out)),
        "tabs": tabs,
        "masks": masks,
    }


def run_l3attn(xT, W_qkv, W_out):
    nc = build_l3attn()
    wmap = l3attn_maps(W_qkv, W_out)
    in_maps = [dict(wmap, x_in=np.ascontiguousarray(xT[b])) for b in range(8)]
    res = run_bass_kernel_spmd(nc, in_maps, core_ids=list(range(8)))
    return np.stack([r["x_out"] for r in res.results])



CAP = 5120
EXP = 8
BIGF = 1.0e6
NROWT = 40
ZROW = 8 * T
YZROW = 8 * CAP


def emit_moe(ctx, x_in, x_out, Wr_d, W1_d, W3_d, W2_d, Ustrict_d, maskprev_d,
             maskexp_d, ident_d, iotasc_d, iota8_d, dr, gdbg=None):
    nc = ctx.nc
    ones_row = ctx.sb("onesrowM", [1, 128], F32)
    ones_col = ctx.sb("onescolM", [128, 1], F32)
    half = ctx.sb("halfM", [128, 1], F32)
    Ustr = ctx.sb("UstrM", [128, 128], F32)
    maskprev = ctx.sb("mprevM", [8, 1], F32)
    maskexp_rep = ctx.sb("mexpM", [128, EXP], F32)
    identb = ctx.sb("identbM", [128, 128], BF16)
    iota8 = ctx.sb("iota8M", [128, EXP], F32)
    iotasc = ctx.sb("iotascM", [128, 128], I32)

    @ctx.on("vector")
    def _(v):
        v.memset(ones_row[0:1, :], 1.0)
        v.memset(ones_col[:, :], 1.0)
        v.memset(half[:, :], 0.5)

    @ctx.on("gpsimd")
    def _(g):
        g.dma("wld", out=Ustr[:, :], in_=Ustrict_d)
        g.dma("wld", out=maskprev[:, :], in_=maskprev_d)
        g.dma("wld", out=maskexp_rep[:, :], in_=maskexp_d)
        g.dma("wld", out=identb[:, :], in_=ident_d)
        g.dma("wld", out=iota8[:, :], in_=iota8_d)
        g.dma("wld", out=iotasc[:, :], in_=iotasc_d)

    y = [ctx.sb(f"yM{k}", [128, T], BF16) for k in range(NCH)]
    with ExitStack() as es:
        rstd = es.enter_context(nc.sbuf_tensor("rstdM", [1, T], F32))
        negmr = es.enter_context(nc.sbuf_tensor("negmrM", [1, T], F32))
        rep_rstd = es.enter_context(nc.sbuf_tensor("rep_rstdM", [128, T], BF16))
        rep_negmr = es.enter_context(nc.sbuf_tensor("rep_negmrM", [128, T], BF16))
        xbuf = [es.enter_context(nc.sbuf_tensor(f"xbM{i}", [128, T], F32)) for i in range(2)]
        ln_rows(ctx, x_in, rstd, negmr, tag="M", xbuf=xbuf)
        replicate_rows(ctx, [(rstd, rep_rstd), (negmr, rep_negmr)], ones_row, tag="M")
        build_y(ctx, x_in, rep_rstd, rep_negmr, y, xbuf)

    ytsb = [ctx.sb(f"ytsbM{i}", [128, D], BF16) for i in range(2)]
    lgt = [ctx.sb(f"lgtM{i}", [128, EXP], F32) for i in range(16)]
    wr = ctx.sb("wrM", [128, NCH * EXP], BF16)
    load_weight_chunks(ctx, Wr_d, [wr[:, k * EXP:(k + 1) * EXP] for k in range(NCH)])
    wldWr = ctx.val("wld")

    with nc.psum_tensor("tpM0", [128, 128], BF16) as tp0, \
         nc.psum_tensor("tpM1", [128, 128], BF16) as tp1, \
         nc.psum_tensor("lgpsM", [128, EXP], F32) as lgps:
        tps = [tp0, tp1]
        tp_gate = [None, None]
        yt_gate = [None, None]
        lg_gate = None
        for tt in range(16):
            bi = tt % 2
            for c in range(NCH):
                pj = tps[c % 2]

                @ctx.on("tensor")
                def _(t, tt=tt, c=c, pj=pj, gate=tp_gate[c % 2], ygate=yt_gate[bi]):
                    if tt == 0 and c == 0:
                        t.wait_ge("wld", wldWr)
                    if gate is not None:
                        t.wait_ge(gate[0], gate[1])
                    if c < 2 and ygate is not None:
                        t.wait_ge(ygate[0], ygate[1])
                    t.transpose(pj[:, :], y[c][:, tt * 128:(tt + 1) * 128], identb[:, :])
                pvt = ctx.val("pe")

                @ctx.on("scalar")
                def _(sx, tt=tt, bi=bi, c=c, pj=pj):
                    sx.wait_ge("pe", pvt)
                    sx.copy(ytsb[bi][:, c * 128:(c + 1) * 128], pj[:, :])
                tp_gate[c % 2] = ("sc", ctx.val("sc"))

            @ctx.on("tensor")
            def _(t, tt=tt, gate=lg_gate):
                if gate is not None:
                    t.wait_ge(gate[0], gate[1])
                for c in range(NCH):
                    t.matmul(lgps[:, :], y[c][:, tt * 128:(tt + 1) * 128],
                             wr[:, c * EXP:(c + 1) * EXP],
                             start=(c == 0), stop=(c == NCH - 1))
            pvl = ctx.val("pe")

            @ctx.on("vector")
            def _(v, tt=tt):
                v.wait_ge("pe", pvl)
                v.tensor_scalar(lgt[tt][:, :], lgps[:, :], 10000.0, -10000.0,
                                ALU.min, ALU.max)
            lg_gate = ("dv", ctx.val("dv"))

            sc_yt = ctx.val("sc")

            @ctx.on("gpsimd")
            def _(g, tt=tt, bi=bi):
                g.wait_ge("sc", sc_yt)
                g.dma("sta2" if bi == 0 else "stb2",
                      out=dr["ytok"][tt * 128:(tt + 1) * 128, :], in_=ytsb[bi][:, :])
            yt_gate[bi] = ("sta2" if bi == 0 else "stb2",
                           ctx.val("sta2" if bi == 0 else "stb2"))

    zrow = ctx.sb("zrowM", [1, D], BF16)

    @ctx.on("vector")
    def _(v):
        v.memset(zrow[0:1, :], 0.0)
    dvz = ctx.val("dv")

    st_a, st_b = ctx.val("sta2"), ctx.val("stb2")

    @ctx.on("gpsimd")
    def _(g):
        g.wait_ge("sta2", st_a)
        g.wait_ge("stb2", st_b)
        g.wait_ge("dv", dvz)
        g.cc("AllGather", ALU.bypass, replica_groups=[list(range(8))],
             ins=[dr["ytok"]], outs=[dr["ag_y"][0:8 * T, :]])
        g.dma("sta2", out=dr["ag_y"][ZROW:ZROW + 1, :], in_=zrow[0:1, :])
    cc_y = ctx.val("cc")

    oh1 = [ctx.sb(f"oh1M{i}", [128, EXP], F32) for i in range(16)]
    oh2 = [ctx.sb(f"oh2M{i}", [128, EXP], F32) for i in range(16)]
    lpos = [ctx.sb(f"lposM{i}", [128, EXP], F32) for i in range(16)]
    pcols = [ctx.sb(f"pcolM{i}", [128, 2], F32) for i in range(16)]
    rtmp = ctx.sb("rtmpM", [128, EXP], F32)
    m1c = ctx.sb("m1cM", [128, 4], F32)
    base_row = ctx.sb("baserowM", [1, EXP], F32)

    with nc.psum_tensor("cumpsM", [128, EXP], F32) as cps, \
         nc.psum_tensor("totpsM", [1, EXP], F32) as tpsm:
        @ctx.on("vector")
        def _(v):
            v.memset(base_row[0:1, :], 0.0)
        dv0 = ctx.val("dv")
        pe_prev = None
        for tt in range(16):
            @ctx.on("vector")
            def _(v, tt=tt, pe_prev=pe_prev):
                if tt == 0:
                    v.wait_ge("dv", dv0)
                if pe_prev is not None:
                    v.wait_ge(pe_prev[0], pe_prev[1])
                v.reduce_max(m1c[:, 0:1], lgt[tt][:, :], axis=AX)
                v.tensor_scalar(oh1[tt][:, :], lgt[tt][:, :], m1c[:, 0:1], None, ALU.is_ge)
                v.scalar_tensor_tensor(rtmp[:, :], oh1[tt][:, :], -BIGF, lgt[tt][:, :],
                                       ALU.mult, ALU.add)
                v.reduce_max(m1c[:, 1:2], rtmp[:, :], axis=AX)
                v.tensor_scalar(oh2[tt][:, :], rtmp[:, :], m1c[:, 1:2], None, ALU.is_ge)
                v.tensor_scalar(m1c[:, 2:3], m1c[:, 0:1], -1.0, None, ALU.mult)
            dv1 = ctx.val("dv")

            @ctx.on("scalar")
            def _(sx, tt=tt):
                sx.wait_ge("dv", dv1)
                sx.activation(out=m1c[:, 3:4], in_=m1c[:, 1:2], func=AF.Exp,
                              bias=m1c[:, 2:3])
            sc1 = ctx.val("sc")

            @ctx.on("vector")
            def _(v, tt=tt):
                v.wait_ge("sc", sc1)
                v.tensor_scalar(rtmp[:, 0:1], m1c[:, 3:4], 1.0, None, ALU.add)
                v.reciprocal(pcols[tt][:, 0:1], rtmp[:, 0:1])
                v.tensor_tensor(pcols[tt][:, 1:2], pcols[tt][:, 0:1], m1c[:, 3:4], ALU.mult)
                v.tensor_tensor(rtmp[:, :], oh1[tt][:, :], oh2[tt][:, :], ALU.add)
            dvg = ctx.val("dv")

            @ctx.on("tensor")
            def _(t, tt=tt):
                t.wait_ge("dv", dvg)
                t.matmul(cps[:, :], Ustr[:, :], rtmp[:, :], start=True, stop=False)
                t.matmul(cps[:, :], ones_row[0:1, :], base_row[0:1, :],
                         start=False, stop=True)
                t.matmul(tpsm[0:1, :], ones_col[:, 0:1], rtmp[:, :], start=True, stop=True)
            pvc = ctx.val("pe")
            pe_prev = ("pe", pvc)

            @ctx.on("vector")
            def _(v, tt=tt):
                v.wait_ge("pe", pvc)
                v.tensor_copy(lpos[tt][:, :], cps[:, :])
                v.tensor_tensor(base_row[0:1, :], base_row[0:1, :], tpsm[0:1, :], ALU.add)
    dv_route = ctx.val("dv")

    tot_all = ctx.sb("totallM", [8, EXP], F32)
    mybase = ctx.sb("mybaseM", [1, EXP], F32)
    baserep = ctx.sb("baserepM", [128, EXP], F32)

    @ctx.on("gpsimd")
    def _(g):
        g.wait_ge("dv", dv_route)
        g.dma("sta2", out=dr["totals"][:, :], in_=base_row[0:1, :])
        g.wait_ge("sta2", ctx.cnt["sta2"])
        g.cc("AllGather", ALU.bypass, replica_groups=[list(range(8))],
             ins=[dr["totals"]], outs=[dr["ag_totals"]])
        g.wait_ge("cc", ctx.cnt["cc"])
        g.dma("wld", out=tot_all[:, :], in_=dr["ag_totals"])
    wldT = ctx.val("wld")

    with nc.psum_tensor("bpsM", [1, EXP], F32) as bps, \
         nc.psum_tensor("brepM", [128, EXP], F32) as brp:
        @ctx.on("tensor")
        def _(t):
            t.wait_ge("wld", wldT)
            t.matmul(bps[0:1, :], maskprev[:, 0:1], tot_all[:, :], start=True, stop=True)
        pvb = ctx.val("pe")

        @ctx.on("vector")
        def _(v):
            v.wait_ge("pe", pvb)
            v.tensor_copy(mybase[0:1, :], bps[0:1, :])
        dvb = ctx.val("dv")

        @ctx.on("tensor")
        def _(t):
            t.wait_ge("dv", dvb)
            t.matmul(brp[:, :], ones_row[0:1, :], mybase[0:1, :], start=True, stop=True)
        pvb2 = ctx.val("pe")

        @ctx.on("scalar")
        def _(sx):
            sx.wait_ge("pe", pvb2)
            sx.copy(baserep[:, :], brp[:, :])
    scb = ctx.val("sc")

    offs_t = [ctx.sb(f"offsM{i}", [128, EXP], F32) for i in range(16)]
    gidx = [ctx.sb(f"gidxM{i}", [128, 2], I32) for i in range(16)]
    slotm = ctx.sb("slotmM", [128, EXP], F32)
    ktmp2 = ctx.sb("ktmp2M", [128, 2], F32)

    @ctx.on("vector")
    def _(v):
        v.wait_ge("sc", scb)
        for tt in range(16):
            v.tensor_tensor(slotm[:, :], lpos[tt][:, :], baserep[:, :], ALU.add)
            v.tensor_tensor(rtmp[:, :], oh1[tt][:, :], oh2[tt][:, :], ALU.add)
            v.tensor_scalar(offs_t[tt][:, :], slotm[:, :], -BIGF, None, ALU.add)
            v.tensor_tensor(offs_t[tt][:, :], offs_t[tt][:, :], rtmp[:, :], ALU.mult)
            v.tensor_scalar(offs_t[tt][:, :], offs_t[tt][:, :], BIGF, None, ALU.add)
            for k, oh in ((0, oh1[tt]), (1, oh2[tt])):
                v.tensor_tensor(rtmp[:, :], oh[:, :], slotm[:, :], ALU.mult)
                v.reduce_sum(ktmp2[:, k:k + 1], rtmp[:, :], axis=AX)
                v.tensor_tensor(rtmp[:, :], oh[:, :], iota8[:, :], ALU.mult)
                v.reduce_sum(m1c[:, k:k + 1], rtmp[:, :], axis=AX)
                v.tensor_scalar(m1c[:, 2:3], ktmp2[:, k:k + 1], float(CAP), None, ALU.is_lt)
                v.scalar_tensor_tensor(m1c[:, 3:4], m1c[:, k:k + 1], float(CAP),
                                       ktmp2[:, k:k + 1], ALU.mult, ALU.add)
                v.tensor_scalar(m1c[:, 3:4], m1c[:, 3:4], -float(YZROW), None, ALU.add)
                v.tensor_tensor(m1c[:, 3:4], m1c[:, 3:4], m1c[:, 2:3], ALU.mult)
                v.tensor_scalar(m1c[:, 3:4], m1c[:, 3:4], float(YZROW), None, ALU.add)
                v.tensor_copy(gidx[tt][:, k:k + 1], m1c[:, 3:4])
    dvoffs = ctx.val("dv")

    @ctx.on("gpsimd")
    def _(g):
        g.wait_ge("dv", dvoffs)
        for tt in range(16):
            g.dma("sta2", out=dr["offs"][tt * 128:(tt + 1) * 128, :], in_=offs_t[tt][:, :])
        g.wait_ge("sta2", ctx.cnt["sta2"])
        g.cc("AllGather", ALU.bypass, replica_groups=[list(range(8))],
             ins=[dr["offs"]], outs=[dr["ag_offs"]])
        g.wait_ge("cc", ctx.cnt["cc"])
    cc_offs = ctx.val("cc")

    fill = ctx.sb("fillM", [128, 41], I32)
    offcol = [ctx.sb(f"offcolM{i}", [128, EXP], F32) for i in range(2)]
    selc = [ctx.sb(f"selcM{i}", [128, 1], I32) for i in range(2)]
    self32 = ctx.sb("self32M", [128, 1], F32)

    @ctx.on("vector")
    def _(v):
        v.memset(fill[:, :], ZROW)
    dvf = ctx.val("dv")

    @ctx.on("gpsimd")
    def _(g):
        g.wait_ge("dv", dvf)
        g.dma("sta2", out=dr["idxbuf"].rearrange("(p n) o -> p (n o)", p=128),
              in_=fill[:, 0:41])
    prefill_done = ctx.val("sta2")

    bc_reg = [None]

    @ctx.on("gpsimd")
    def _(g):
        bc_reg[0] = g._eng.to_reg(41 * 128 - 1)

    lda0, ldb0 = ctx.val("lda"), ctx.val("ldb")
    sel_done = [0, 0]
    sct_done = [0, 0]
    for tb in range(128):
        bi = tb % 2
        lsem = "lda" if bi == 0 else "ldb"

        @ctx.on("sync")
        def _(sy, tb=tb, bi=bi, lsem=lsem, gate=sel_done[bi]):
            if tb == 0:
                sy.wait_ge("cc", cc_offs)
            if gate:
                sy.wait_ge("dv", gate)
            sy.dma(lsem, out=offcol[bi][:, :],
                   in_=dr["ag_offs"][tb * 128:(tb + 1) * 128, :])
        lv = ctx.val(lsem)

        @ctx.on("vector")
        def _(v, tb=tb, bi=bi, lsem=lsem, lv=lv, gate=sct_done[bi]):
            v.wait_ge(lsem, lv)
            if gate:
                v.wait_ge("scta" if bi == 0 else "sctb", gate)
            v.tensor_tensor(offcol[bi][:, :], offcol[bi][:, :], maskexp_rep[:, :], ALU.mult)
            v.reduce_sum(self32[:, 0:1], offcol[bi][:, :], axis=AX)
            v.tensor_copy(selc[bi][:, 0:1], self32[:, 0:1])
        dvs = ctx.val("dv")
        sel_done[bi] = dvs

        @ctx.on("gpsimd")
        def _(g, tb=tb, bi=bi, dvs=dvs):
            if tb == 0:
                g.wait_ge("sta2", prefill_done)
            g.wait_ge("dv", dvs)
            g.idma("scta" if bi == 0 else "sctb", out=dr["idxbuf"],
                   out_offset=IndirectOffsetOnAxis(ap=selc[bi][:, 0:1], axis=0),
                   in_=iotasc[:, tb:tb + 1], in_offset=None,
                   bounds_check=bc_reg[0], oob_is_err=False)
        sct_done[bi] = ctx.val("scta" if bi == 0 else "sctb")
    sct_all = [("scta", ctx.val("scta")), ("sctb", ctx.val("sctb"))]

    barrier(ctx, extra=("scta", "sctb", "lda", "ldb", "sta2", "stb2"))

    @ctx.on("gpsimd")
    def _(g):
        g.wait_ge("cc", cc_y)
    barrier(ctx)

    idxt = [ctx.sb(f"idxtM{i}", [128, 1], I32) for i in range(NROWT)]

    @ctx.on("sync")
    def _(sy):
        for j in range(NROWT):
            sy.dma("lda", out=idxt[j][:, :], in_=dr["idxbuf"][j * 128:(j + 1) * 128, :])
    idx_done = ctx.val("lda")

    with ExitStack() as es:
        w1 = [es.enter_context(nc.sbuf_tensor(f"w1M{k}", [128, 4 * D], BF16))
              for k in range(NCH)]
        w3 = [es.enter_context(nc.sbuf_tensor(f"w3M{k}", [128, 4 * D], BF16))
              for k in range(NCH)]
        xg = [es.enter_context(nc.sbuf_tensor(f"xgM{i}", [128, D], BF16)) for i in range(2)]
        xeT = [es.enter_context(nc.sbuf_tensor(f"xeTM{k}", [128, TG], BF16))
               for k in range(NCH)]
        s1 = es.enter_context(nc.sbuf_tensor("s1M", [128, TG], BF16))
        hcat = es.enter_context(nc.sbuf_tensor("hcatM", [128, TG], BF16))
        load_weight_chunks(ctx, W1_d, w1)
        load_weight_chunks(ctx, W3_d, w3)
        wld1 = ctx.val("wld")

        with nc.psum_tensor("xtpM0", [128, 128], BF16) as xtp0, \
             nc.psum_tensor("xtpM1", [128, 128], BF16) as xtp1, \
             nc.psum_tensor("h1psM", [128, TG], F32) as h1ps, \
             nc.psum_tensor("h3psM", [128, TG], F32) as h3ps:
            xtps = [xtp0, xtp1]
            xtp_gate = [None, None]
            xg_gate = [None, None]
            hc_gate = None
            for st in range(10):
                xeT_ready = None
                for r4 in range(4):
                    j = st * 4 + r4
                    gbi = r4 % 2

                    gsem = "xgla" if gbi == 0 else "xglb"

                    @ctx.on("gpsimd")
                    def _(g, j=j, gbi=gbi, gsem=gsem, gate=xg_gate[gbi]):
                        if j == 0:
                            g.wait_ge("lda", idx_done)
                            for sn, sv in sct_all:
                                g.wait_ge(sn, sv)
                        if gate is not None:
                            g.wait_ge(gate[0], gate[1])
                        g.idma(gsem, out=xg[gbi][:, :], out_offset=None,
                               in_=dr["ag_y"],
                               in_offset=IndirectOffsetOnAxis(ap=idxt[j][:, 0:1], axis=0))
                    gv = (gsem, ctx.val(gsem))

                    for c in range(NCH):
                        pj = xtps[c % 2]

                        @ctx.on("tensor")
                        def _(t, c=c, pj=pj, gbi=gbi, gv=gv, gate=xtp_gate[c % 2]):
                            t.wait_ge(gv[0], gv[1])
                            if gate is not None:
                                t.wait_ge(gate[0], gate[1])
                            t.transpose(pj[:, :], xg[gbi][:, c * 128:(c + 1) * 128],
                                        identb[:, :])
                        pvt = ctx.val("pe")

                        @ctx.on("scalar")
                        def _(sx, c=c, r4=r4, pj=pj):
                            sx.wait_ge("pe", pvt)
                            sx.copy(xeT[c][:, r4 * 128:(r4 + 1) * 128], pj[:, :])
                        xtp_gate[c % 2] = ("sc", ctx.val("sc"))
                    xg_gate[gbi] = ("pe", ctx.val("pe"))
                xeT_ready = ctx.val("sc")

                if st == 0 and gdbg is not None:
                    @ctx.on("gpsimd")
                    def _(g, xr=ctx.val("sc")):
                        g.wait_ge("sc", xr)
                        g.dma("sta2", out=gdbg["xeT_dbg"][:, :], in_=xeT[0][:, :])

                for hidm in range(32):
                    @ctx.on("tensor")
                    def _(t, st=st, hidm=hidm, hc_gate=hc_gate):
                        if hidm == 0:
                            t.wait_ge("sc", xeT_ready)
                            t.wait_ge("wld", wld1)
                        if hc_gate is not None:
                            t.wait_ge(hc_gate[0], hc_gate[1])
                        for k in range(NCH):
                            t.matmul(h1ps[:, :], w1[k][:, hidm * 128:(hidm + 1) * 128],
                                     xeT[k][:, :], start=(k == 0), stop=(k == NCH - 1))
                        for k in range(NCH):
                            t.matmul(h3ps[:, :], w3[k][:, hidm * 128:(hidm + 1) * 128],
                                     xeT[k][:, :], start=(k == 0), stop=(k == NCH - 1))
                    pvh = ctx.val("pe")

                    @ctx.on("scalar")
                    def _(sx, hidm=hidm):
                        sx.wait_ge("pe", pvh)
                        sx.activation(out=s1[:, :], in_=h1ps[:, :], func=AF.Silu)
                    sch = ctx.val("sc")

                    @ctx.on("vector")
                    def _(v, hidm=hidm):
                        v.wait_ge("sc", sch)
                        v.tensor_tensor(hcat[:, :], s1[:, :], h3ps[:, :], ALU.mult)
                    dvh = ctx.val("dv")

                    @ctx.on("gpsimd")
                    def _(g, st=st, hidm=hidm):
                        g.wait_ge("dv", dvh)
                        g.dma("sta2" if hidm % 2 == 0 else "stb2",
                              out=dr["hbuf"][hidm * 128:(hidm + 1) * 128,
                                             st * TG:(st + 1) * TG],
                              in_=hcat[:, :])
                    hc_gate = ("sta2" if hidm % 2 == 0 else "stb2",
                               ctx.val("sta2" if hidm % 2 == 0 else "stb2"))
    barrier(ctx, extra=("sta2", "stb2", "xgla", "xglb", "wld", "lda", "ldb"))

    with ExitStack() as es:
        w2 = [es.enter_context(nc.sbuf_tensor(f"w2M{k}", [128, D], BF16))
              for k in range(32)]
        hT = [es.enter_context(nc.sbuf_tensor(f"hTM{k}", [128, TG], BF16))
              for k in range(32)]
        yesb = [es.enter_context(nc.sbuf_tensor(f"yesbM{i}", [128, D], BF16))
                for i in range(2)]
        load_weight_chunks(ctx, W2_d, w2)
        wld2 = ctx.val("wld")

        with nc.psum_tensor("yepsM0", [128, TG], F32) as ye0, \
             nc.psum_tensor("yepsM1", [128, TG], F32) as ye1:
            yeps = [ye0, ye1]
            ye_gate = [None, None]
            ye_st = [None, None]
            pe_prev_st = None
            for st in range(10):
                @ctx.on("sync")
                def _(sy, st=st, pe_prev_st=pe_prev_st):
                    if pe_prev_st is not None:
                        sy.wait_ge("pe", pe_prev_st)
                    for k in range(32):
                        sy.dma("ldb", out=hT[k][:, :],
                               in_=dr["hbuf"][k * 128:(k + 1) * 128,
                                              st * TG:(st + 1) * TG])
                hld = ctx.val("ldb")

                for r4 in range(4):
                    rt = st * 4 + r4
                    bi = r4 % 2
                    for ng in range(2):
                        pi = ng
                        ps = yeps[pi]

                        @ctx.on("tensor")
                        def _(t, rt=rt, r4=r4, ng=ng, ps=ps, gate=ye_gate[pi]):
                            if r4 == 0 and ng == 0:
                                t.wait_ge("ldb", hld)
                                t.wait_ge("wld", wld2)
                            if gate is not None:
                                t.wait_ge(gate[0], gate[1])
                            for k in range(32):
                                t.matmul(ps[:, :], hT[k][:, r4 * 128:(r4 + 1) * 128],
                                         w2[k][:, ng * TG:(ng + 1) * TG],
                                         start=(k == 0), stop=(k == 31))
                        pv = ctx.val("pe")

                        @ctx.on("scalar")
                        def _(sx, rt=rt, ng=ng, ps=ps, bi=bi, sgate=ye_st[bi]):
                            sx.wait_ge("pe", pv)
                            if ng == 0 and sgate is not None:
                                sx.wait_ge(sgate[0], sgate[1])
                            sx.copy(yesb[bi][:, ng * TG:(ng + 1) * TG], ps[:, :])
                        ye_gate[pi] = ("sc", ctx.val("sc"))
                    scy = ctx.val("sc")

                    @ctx.on("gpsimd")
                    def _(g, rt=rt, bi=bi):
                        g.wait_ge("sc", scy)
                        g.dma("sta2" if bi == 0 else "stb2",
                              out=dr["ye"][rt * 128:(rt + 1) * 128, :],
                              in_=yesb[bi][:, :])
                    ye_st[bi] = ("sta2" if bi == 0 else "stb2",
                                 ctx.val("sta2" if bi == 0 else "stb2"))
                pe_prev_st = ctx.val("pe")
    barrier(ctx, extra=("wld", "ldb", "sta2", "stb2"))
    st_a, st_b = ctx.val("sta2"), ctx.val("stb2")
    zrow2 = ctx.sb("zrow2M", [1, D], BF16)

    @ctx.on("vector")
    def _(v):
        v.memset(zrow2[0:1, :], 0.0)
    dvz2 = ctx.val("dv")

    @ctx.on("gpsimd")
    def _(g):
        g.wait_ge("sta2", st_a)
        g.wait_ge("stb2", st_b)
        g.wait_ge("dv", dvz2)
        g.cc("AllGather", ALU.bypass, replica_groups=[list(range(8))],
             ins=[dr["ye"]], outs=[dr["ag_ye"][0:8 * CAP, :]])
        g.dma("sta2", out=dr["ag_ye"][YZROW:YZROW + 1, :], in_=zrow2[0:1, :])
        g.wait_ge("cc", ctx.cnt["cc"])
        g.wait_ge("sta2", ctx.cnt["sta2"])
    barrier(ctx)

    ao = [ctx.sb(f"aoM{k}", [128, T], BF16) for k in range(NCH)]
    g0 = [ctx.sb(f"g0M{i}", [128, D], BF16) for i in range(2)]
    g1 = [ctx.sb(f"g1M{i}", [128, D], BF16) for i in range(2)]
    otok = ctx.sb("otokM", [128, D], BF16)

    with nc.psum_tensor("atpM0", [128, 128], BF16) as atp0, \
         nc.psum_tensor("atpM1", [128, 128], BF16) as atp1:
        atps = [atp0, atp1]
        atp_gate = [None, None]
        g_gate = [None, None]
        ot_read = None
        for tt in range(16):
            bi = tt % 2

            gsem = "xga" if bi == 0 else "xgb"

            @ctx.on("gpsimd")
            def _(g, tt=tt, bi=bi, gsem=gsem, gate=g_gate[bi]):
                if gate is not None:
                    g.wait_ge(gate[0], gate[1])
                g.idma(gsem, out=g0[bi][:, :], out_offset=None, in_=dr["ag_ye"],
                       in_offset=IndirectOffsetOnAxis(ap=gidx[tt][:, 0:1], axis=0))
                g.idma(gsem, out=g1[bi][:, :], out_offset=None, in_=dr["ag_ye"],
                       in_offset=IndirectOffsetOnAxis(ap=gidx[tt][:, 1:2], axis=0))
            gv = (gsem, ctx.val(gsem))

            @ctx.on("vector")
            def _(v, tt=tt, bi=bi, gv=gv, ot_read=ot_read):
                v.wait_ge(gv[0], gv[1])
                if ot_read is not None:
                    v.wait_ge(ot_read[0], ot_read[1])
                v.tensor_scalar(otok[:, :], g0[bi][:, :], pcols[tt][:, 0:1], None, ALU.mult)
                v.scalar_tensor_tensor(g1[bi][:, :], g1[bi][:, :], pcols[tt][:, 1:2],
                                       otok[:, :], ALU.mult, ALU.add)
                v.tensor_copy(otok[:, :], g1[bi][:, :])
            dvo = ctx.val("dv")
            g_gate[bi] = ("dv", dvo)

            for c in range(NCH):
                pj = atps[c % 2]

                @ctx.on("tensor")
                def _(t, c=c, pj=pj, dvo=dvo, gate=atp_gate[c % 2]):
                    t.wait_ge("dv", dvo)
                    if gate is not None:
                        t.wait_ge(gate[0], gate[1])
                    t.transpose(pj[:, :], otok[:, c * 128:(c + 1) * 128], identb[:, :])
                pvt = ctx.val("pe")

                @ctx.on("scalar")
                def _(sx, tt=tt, c=c, pj=pj):
                    sx.wait_ge("pe", pvt)
                    sx.copy(ao[c][:, tt * 128:(tt + 1) * 128], pj[:, :])
                atp_gate[c % 2] = ("sc", ctx.val("sc"))
            ot_read = ("pe", ctx.val("pe"))
    barrier(ctx)

    if gdbg is not None:
        dbg_dv = ctx.val("dv")

        @ctx.on("gpsimd")
        def _(g):
            g.wait_ge("dv", dbg_dv)
            for tt in range(16):
                g.dma("sta2", out=gdbg["gidx_dbg"][tt * 128:(tt + 1) * 128, :],
                      in_=gidx[tt][:, :])
                g.dma("sta2", out=gdbg["p_dbg"][tt * 128:(tt + 1) * 128, :],
                      in_=pcols[tt][:, :])
                g.dma("sta2", out=gdbg["offs_dbg"][tt * 128:(tt + 1) * 128, :],
                      in_=offs_t[tt][:, :])
                g.dma("sta2", out=gdbg["oh1_dbg"][tt * 128:(tt + 1) * 128, :],
                      in_=oh1[tt][:, :])
                g.dma("sta2", out=gdbg["lgt_dbg"][tt * 128:(tt + 1) * 128, :],
                      in_=lgt[tt][:, :])
            g.dma("sta2", out=gdbg["idx_dbg"][:, :], in_=dr["idxbuf"])
            g.dma("sta2", out=gdbg["iota_dbg"][:, :], in_=iota8[:, :])
            g.dma("sta2", out=gdbg["h_dbg"][:, :], in_=dr["hbuf"][0:256, 0:TG])
            g.dma("sta2", out=gdbg["tot_dbg"][:, :], in_=tot_all[:, :])
            g.dma("sta2", out=gdbg["ye_dbg"][:, :], in_=dr["ye"][0:256, :])
            g.dma("sta2", out=gdbg["ytok_dbg"][:, :], in_=dr["ytok"])
            g.wait_ge("sta2", ctx.cnt["sta2"])

    xres = [ctx.sb(f"xresM{i}", [128, TG], F32) for i in range(2)]
    store_gates = []
    for m in range(NCH):
        for g in range(NTG):
            idx = m * NTG + g
            dvv = ctx.val("dv")
            residual_evac(ctx, x_in, x_out, m, g, idx, ao[m][:, g * TG:(g + 1) * TG],
                          half[:, 0:1], xres, store_gates, dvv)
    wait_stores(ctx)


def moe_dram(nc, dbg=False):
    return {
        "ytok": nc.dram_tensor("ytokM", [T, D], BF16).ap(),
        "ag_y": nc.dram_tensor("agyM", [8 * T + 1, D], BF16).ap(),
        "totals": nc.dram_tensor("totM", [1, EXP], F32).ap(),
        "ag_totals": nc.dram_tensor("agtotM", [8, EXP], F32).ap(),
        "offs": nc.dram_tensor("offsM", [T, EXP], F32).ap(),
        "ag_offs": nc.dram_tensor("agoffsM", [8 * T, EXP], F32).ap(),
        "idxbuf": nc.dram_tensor("idxbufM", [41 * 128, 1], I32).ap(),
        "hbuf": nc.dram_tensor("hbufM", [4 * D, CAP], BF16).ap(),
        "ye": nc.dram_tensor("yeM", [CAP, D], BF16).ap(),
        "ag_ye": nc.dram_tensor("agyeM", [8 * CAP + 1, D], BF16).ap(),
    }


def build_moe(dbg=False):
    nc = bass.Bass("TRN2", target_bir_lowering=False, debug=False)
    x_in = nc.dram_tensor("x_in", [D, T], F32, kind="ExternalInput").ap()
    Wr = nc.dram_tensor("Wr", [D, EXP], F32, kind="ExternalInput").ap()
    W1 = nc.dram_tensor("W1", [D, 4 * D], F32, kind="ExternalInput").ap()
    W3 = nc.dram_tensor("W3", [D, 4 * D], F32, kind="ExternalInput").ap()
    W2 = nc.dram_tensor("W2", [4 * D, D], F32, kind="ExternalInput").ap()
    Ustr = nc.dram_tensor("Ustr", [128, 128], F32, kind="ExternalInput").ap()
    mprev = nc.dram_tensor("mprev", [8, 1], F32, kind="ExternalInput").ap()
    mexp = nc.dram_tensor("mexp", [128, EXP], F32, kind="ExternalInput").ap()
    ident = nc.dram_tensor("ident", [128, 128], F32, kind="ExternalInput").ap()
    iota8 = nc.dram_tensor("iota8", [128, EXP], F32, kind="ExternalInput").ap()
    iotasc = nc.dram_tensor("iotasc", [128, 128], I32, kind="ExternalInput").ap()
    x_out = nc.dram_tensor("x_out", [D, T], F32, kind="ExternalOutput").ap()
    dr = moe_dram(nc, dbg)
    if dbg:
        gdbg = {
            "gidx_dbg": nc.dram_tensor("gidx_dbg", [T, 2], I32, kind="ExternalOutput").ap(),
            "p_dbg": nc.dram_tensor("p_dbg", [T, 2], F32, kind="ExternalOutput").ap(),
            "idx_dbg": nc.dram_tensor("idx_dbg", [41 * 128, 1], I32, kind="ExternalOutput").ap(),
            "tot_dbg": nc.dram_tensor("tot_dbg", [8, EXP], F32, kind="ExternalOutput").ap(),
            "ye_dbg": nc.dram_tensor("ye_dbg", [256, D], BF16, kind="ExternalOutput").ap(),
            "ytok_dbg": nc.dram_tensor("ytok_dbg", [T, D], BF16, kind="ExternalOutput").ap(),
            "offs_dbg": nc.dram_tensor("offs_dbg", [T, EXP], F32, kind="ExternalOutput").ap(),
            "oh1_dbg": nc.dram_tensor("oh1_dbg", [T, EXP], F32, kind="ExternalOutput").ap(),
            "lgt_dbg": nc.dram_tensor("lgt_dbg", [T, EXP], F32, kind="ExternalOutput").ap(),
            "iota_dbg": nc.dram_tensor("iota_dbg", [128, EXP], F32, kind="ExternalOutput").ap(),
            "h_dbg": nc.dram_tensor("h_dbg", [256, TG], BF16, kind="ExternalOutput").ap(),
            "xeT_dbg": nc.dram_tensor("xeT_dbg", [128, TG], BF16, kind="ExternalOutput").ap(),
        }
    else:
        gdbg = None
    with ExitStack() as stack:
        block = stack.enter_context(nc.Block())
        ctx = Ctx(nc, block, stack)
        emit_moe(ctx, x_in, x_out, Wr, W1, W3, W2, Ustr, mprev, mexp, ident,
                 iotasc, iota8, dr, gdbg=gdbg)
    return nc


def moe_maps(Wr, W1, W3, W2, core):
    return {
        "Wr": np.ascontiguousarray(np.asarray(Wr)),
        "W1": np.ascontiguousarray(np.asarray(W1)[core]),
        "W3": np.ascontiguousarray(np.asarray(W3)[core]),
        "W2": np.ascontiguousarray(np.asarray(W2)[core]),
        "Ustr": np.triu(np.ones((128, 128), np.float32), 1),
        "mprev": (np.arange(8) < core).astype(np.float32).reshape(8, 1),
        "mexp": np.tile((np.arange(EXP) == core).astype(np.float32), (128, 1)),
        "ident": np.eye(128, dtype=np.float32),
        "iota8": np.tile(np.arange(EXP, dtype=np.float32), (128, 1)),
        "iotasc": np.ascontiguousarray(
            (np.arange(128)[:, None] + 128 * np.arange(128)[None, :]).astype(np.int32)),
    }


def run_moe(xT, Wr, W1, W3, W2):
    nc = build_moe()
    in_maps = [dict(moe_maps(Wr, W1, W3, W2, b), x_in=np.ascontiguousarray(xT[b]))
               for b in range(8)]
    res = run_bass_kernel_spmd(nc, in_maps, core_ids=list(range(8)))
    return np.stack([r["x_out"] for r in res.results])



def build_rwkv():
    nc = bass.Bass("TRN2", target_bir_lowering=False, debug=False)
    x_in = nc.dram_tensor("x_in", [D, T], F32, kind="ExternalInput").ap()
    WrT = nc.dram_tensor("WrT", [D, D], F32, kind="ExternalInput").ap()
    WkT = nc.dram_tensor("WkT", [D, D], F32, kind="ExternalInput").ap()
    WvT = nc.dram_tensor("WvT", [D, D], F32, kind="ExternalInput").ap()
    mur = nc.dram_tensor("mur", [128, NCH], F32, kind="ExternalInput").ap()
    muk = nc.dram_tensor("muk", [128, NCH], F32, kind="ExternalInput").ap()
    x_out = nc.dram_tensor("x_out", [D, T], F32, kind="ExternalOutput").ap()
    with ExitStack() as stack:
        block = stack.enter_context(nc.Block())
        ctx = Ctx(nc, block, stack)
        emit_rwkv(ctx, x_in, x_out, WrT, WkT, WvT, mur, muk)
    return nc


def rwkv_maps(Wr, Wk, Wv, mu_r, mu_k):
    return {
        "WrT": np.ascontiguousarray(np.asarray(Wr).T),
        "WkT": np.ascontiguousarray(np.asarray(Wk).T),
        "WvT": np.ascontiguousarray(np.asarray(Wv).T),
        "mur": np.ascontiguousarray(np.asarray(mu_r).reshape(NCH, 128).T),
        "muk": np.ascontiguousarray(np.asarray(mu_k).reshape(NCH, 128).T),
    }


def run_rwkv(xT, Wr, Wk, Wv, mu_r, mu_k):
    nc = build_rwkv()
    wmap = rwkv_maps(Wr, Wk, Wv, mu_r, mu_k)
    in_maps = [dict(wmap, x_in=np.ascontiguousarray(xT[b])) for b in range(8)]
    res = run_bass_kernel_spmd(nc, in_maps, core_ids=list(range(8)))
    return np.stack([r["x_out"] for r in res.results])



def build_fused():
    nc = bass.Bass("TRN2", target_bir_lowering=False, debug=False)
    x0 = nc.dram_tensor("x_in", [D, T], F32, kind="ExternalInput").ap()
    xout = nc.dram_tensor("x_out", [D, T], F32, kind="ExternalOutput").ap()
    xa = nc.dram_tensor("x_ping", [D, T], F32).ap()
    xb = nc.dram_tensor("x_pong", [D, T], F32).ap()

    def din(pref, name, shape):
        return nc.dram_tensor(pref + name, shape, F32, kind="ExternalInput").ap()

    def dini(pref, name, shape):
        return nc.dram_tensor(pref + name, shape, I32, kind="ExternalInput").ap()

    chain = [x0] + [xa, xb] * 6
    chain = chain[:12] + [xout]

    with ExitStack() as stack:
        block = stack.enter_context(nc.Block())
        ctx = Ctx(nc, block, stack)

        def scoped(i, fn):
            ctx.set_prefix(f"s{i}_")
            es = ExitStack()
            ctx.scopes.append(es)
            try:
                fn(f"s{i}_", chain[i], chain[i + 1])
            finally:
                es.close()
                ctx.scopes.pop()

        def mk_rwkv(li):
            def f(pref, xi, xo):
                emit_rwkv(ctx, xi, xo,
                          din(pref, "WrT", [D, D]), din(pref, "WkT", [D, D]),
                          din(pref, "WvT", [D, D]), din(pref, "mur", [128, NCH]),
                          din(pref, "muk", [128, NCH]))
            return f

        def f_l0attn(pref, xi, xo):
            ao_dram = nc.dram_tensor(pref + "ao_scr", [D, T], BF16).ap()
            mn_dram = nc.dram_tensor(pref + "mn_scr", [16, 128], F32).ap()
            emit_l0attn(ctx, xi, xo, din(pref, "W1", [D, 3 * D]),
                        din(pref, "WoT", [D, D]), din(pref, "bo", [1, D]),
                        din(pref, "ident", [128, 128]), ao_dram, mn_dram)

        def f_l1attn(pref, xi, xo):
            emit_l1attn(ctx, xi, xo, din(pref, "Wq", [D, D]), din(pref, "Wk", [D, D]),
                        din(pref, "Wv", [D, D]), din(pref, "bq", [1, D]),
                        din(pref, "bk", [1, D]), din(pref, "bv", [1, D]),
                        din(pref, "ident", [128, 128]))

        def f_l2attn(pref, xi, xo):
            ao_dram = nc.dram_tensor(pref + "ao_scr", [D, T], BF16).ap()
            emit_l2attn(ctx, xi, xo, din(pref, "Wqkv", [D, 3 * D]),
                        din(pref, "Wout", [D, D]), din(pref, "hs", [1, 8]),
                        din(pref, "dirs", [3, 8]), din(pref, "cmask", [4, 128, TG]),
                        ao_dram)

        def f_l2ffn(pref, xi, xo):
            emit_l2ffn(ctx, xi, xo, din(pref, "WfcT", [D, 4 * D]),
                       din(pref, "WprojT", [4 * D, D]))

        def f_l3attn(pref, xi, xo):
            v_dram = nc.dram_tensor(pref + "v_scr", [T, 1040], BF16).ap()
            ao_dram = nc.dram_tensor(pref + "ao_scr", [D, T], BF16).ap()
            emit_l3attn(ctx, xi, xo, din(pref, "Wqq", [D, 2 * D]),
                        din(pref, "Wkk", [D, 2 * D]), din(pref, "Wv", [D, D]),
                        din(pref, "Wout", [D, D]), din(pref, "tabs", [4, 128, T]),
                        din(pref, "masks", [2, 128, 128]), v_dram, ao_dram)

        def f_moe(pref, xi, xo):
            dr = {
                "ytok": nc.dram_tensor(pref + "ytokM", [T, D], BF16).ap(),
                "ag_y": nc.dram_tensor(pref + "agyM", [8 * T + 1, D], BF16).ap(),
                "totals": nc.dram_tensor(pref + "totM", [1, EXP], F32).ap(),
                "ag_totals": nc.dram_tensor(pref + "agtotM", [8, EXP], F32).ap(),
                "offs": nc.dram_tensor(pref + "offsM", [T, EXP], F32).ap(),
                "ag_offs": nc.dram_tensor(pref + "agoffsM", [8 * T, EXP], F32).ap(),
                "idxbuf": nc.dram_tensor(pref + "idxbufM", [41 * 128, 1], I32).ap(),
                "hbuf": nc.dram_tensor(pref + "hbufM", [4 * D, CAP], BF16).ap(),
                "ye": nc.dram_tensor(pref + "yeM", [CAP, D], BF16).ap(),
                "ag_ye": nc.dram_tensor(pref + "agyeM", [8 * CAP + 1, D], BF16).ap(),
            }
            emit_moe(ctx, xi, xo, din(pref, "Wr", [D, EXP]), din(pref, "W1", [D, 4 * D]),
                     din(pref, "W3", [D, 4 * D]), din(pref, "W2", [4 * D, D]),
                     din(pref, "Ustr", [128, 128]), din(pref, "mprev", [8, 1]),
                     din(pref, "mexp", [128, EXP]), din(pref, "ident", [128, 128]),
                     dini(pref, "iotasc", [128, 128]), din(pref, "iota8", [128, EXP]),
                     dr)

        subs = [mk_rwkv(0), f_l0attn, mk_rwkv(0), mk_rwkv(1), f_l1attn, mk_rwkv(1),
                f_l2ffn, f_l2attn, f_l2ffn, f_moe, f_l3attn, f_moe]
        for i, fn in enumerate(subs):
            scoped(i, fn)
    return nc


def fused_maps(inp, core):
    out = {}

    def add(i, m):
        for k, v in m.items():
            out[f"s{i}_{k}"] = v

    r0 = rwkv_maps(inp["L0_ffn_W_r"], inp["L0_ffn_W_k"], inp["L0_ffn_W_v"],
                   inp["L0_ffn_mu_r"], inp["L0_ffn_mu_k"])
    r1 = rwkv_maps(inp["L1_ffn_W_r"], inp["L1_ffn_W_k"], inp["L1_ffn_W_v"],
                   inp["L1_ffn_mu_r"], inp["L1_ffn_mu_k"])
    l2f = l2ffn_maps(inp["L2_ffn_W_fc"], inp["L2_ffn_W_proj"])
    moem = moe_maps(inp["L3_ffn_Wr"], inp["L3_ffn_W1"], inp["L3_ffn_W3"],
                    inp["L3_ffn_W2"], core)
    add(0, r0)
    add(1, l0attn_maps(inp["L0_attn_w_qkv"], inp["L0_attn_w_out"], inp["L0_attn_b_out"]))
    add(2, r0)
    add(3, r1)
    add(4, l1attn_maps(inp["L1_attn_W_q"], inp["L1_attn_W_k"], inp["L1_attn_W_v"],
                       inp["L1_attn_b_q"], inp["L1_attn_b_k"], inp["L1_attn_b_v"]))
    add(5, r1)
    add(6, l2f)
    add(7, l2attn_maps(inp["L2_attn_Wqkv"], inp["L2_attn_Wout"],
                       inp["L2_attn_head_scales"], inp["L2_attn_head_dirs"]))
    add(8, l2f)
    add(9, moem)
    add(10, l3attn_maps(inp["L3_attn_W_qkv"], inp["L3_attn_W_out"]))
    add(11, moem)
    return out


def kernel_fused(**inputs):
    inp = {k: np.asarray(v, dtype=np.float32) for k, v in inputs.items()}
    nc = _prog("fused", build_fused)
    in_maps = []
    for b in range(8):
        m = fused_maps(inp, b)
        m["x_in"] = np.ascontiguousarray(inp["L0_attn_x"][b].T)
        in_maps.append(m)
    res = run_bass_kernel_spmd(nc, in_maps, core_ids=list(range(8)))
    return np.stack([np.ascontiguousarray(r["x_out"].T) for r in res.results]).astype(np.float32)


_PROGS = {}


def _prog(name, builder):
    if name not in _PROGS:
        _PROGS[name] = builder()
    return _PROGS[name]


def _launch(nc, per_core_maps):
    res = run_bass_kernel_spmd(nc, per_core_maps, core_ids=list(range(8)))
    return [r["x_out"] for r in res.results]


def kernel(**inputs):
    return kernel_fused(**inputs)


def kernel_unfused(**inputs):
    inp = {k: np.asarray(v, dtype=np.float32) for k, v in inputs.items()}
    xT = [np.ascontiguousarray(inp["L0_attn_x"][b].T) for b in range(8)]

    def run(nc, shared_map, per_core=None):
        nonlocal xT
        maps = []
        for b in range(8):
            m = dict(shared_map if per_core is None else per_core[b])
            m["x_in"] = np.ascontiguousarray(xT[b])
            maps.append(m)
        xT = _launch(nc, maps)

    rwkv_nc = _prog("rwkv", build_rwkv)
    moe_nc = _prog("moe", build_moe)

    run(rwkv_nc, rwkv_maps(inp["L0_ffn_W_r"], inp["L0_ffn_W_k"], inp["L0_ffn_W_v"],
                           inp["L0_ffn_mu_r"], inp["L0_ffn_mu_k"]))
    run(_prog("l0attn", build_l0attn),
        l0attn_maps(inp["L0_attn_w_qkv"], inp["L0_attn_w_out"], inp["L0_attn_b_out"]))
    run(rwkv_nc, rwkv_maps(inp["L0_ffn_W_r"], inp["L0_ffn_W_k"], inp["L0_ffn_W_v"],
                           inp["L0_ffn_mu_r"], inp["L0_ffn_mu_k"]))
    run(rwkv_nc, rwkv_maps(inp["L1_ffn_W_r"], inp["L1_ffn_W_k"], inp["L1_ffn_W_v"],
                           inp["L1_ffn_mu_r"], inp["L1_ffn_mu_k"]))
    run(_prog("l1attn", build_l1attn),
        l1attn_maps(inp["L1_attn_W_q"], inp["L1_attn_W_k"], inp["L1_attn_W_v"],
                    inp["L1_attn_b_q"], inp["L1_attn_b_k"], inp["L1_attn_b_v"]))
    run(rwkv_nc, rwkv_maps(inp["L1_ffn_W_r"], inp["L1_ffn_W_k"], inp["L1_ffn_W_v"],
                           inp["L1_ffn_mu_r"], inp["L1_ffn_mu_k"]))
    l2f = l2ffn_maps(inp["L2_ffn_W_fc"], inp["L2_ffn_W_proj"])
    run(_prog("l2ffn", build_l2ffn), l2f)
    run(_prog("l2attn", build_l2attn),
        l2attn_maps(inp["L2_attn_Wqkv"], inp["L2_attn_Wout"],
                    inp["L2_attn_head_scales"], inp["L2_attn_head_dirs"]))
    run(_prog("l2ffn", build_l2ffn), l2f)
    moe_percore = [moe_maps(inp["L3_ffn_Wr"], inp["L3_ffn_W1"], inp["L3_ffn_W3"],
                            inp["L3_ffn_W2"], b) for b in range(8)]
    run(moe_nc, None, per_core=moe_percore)
    run(_prog("l3attn", build_l3attn),
        l3attn_maps(inp["L3_attn_W_qkv"], inp["L3_attn_W_out"]))
    run(moe_nc, None, per_core=moe_percore)

    return np.stack([np.ascontiguousarray(x.T) for x in xT]).astype(np.float32)


# version 04 (final)
# speedup vs baseline: 1.6620x; 1.4251x over previous
import numpy as np
from contextlib import ExitStack

import concourse.bass as bass
import concourse.mybir as mybir
from concourse.bass_utils import run_bass_kernel_spmd
from concourse.bass import IndirectOffsetOnAxis

F32 = mybir.dt.float32
BF16 = mybir.dt.bfloat16
I32 = mybir.dt.int32
ALU = mybir.AluOpType
AF = mybir.ActivationFunctionType
AX = mybir.AxisListType.X

D = 1024
T = 2048
NCH = 8
NTG = 4
TG = 512
EPS = 1e-5

ESEM = {"tensor": "pe", "scalar": "sc", "vector": "dv", "gpsimd": "gp", "sync": "sy"}
_NOINC = {"wait_ge", "nop", "sem_inc", "wait_op", "load_library"}


class EW:

    def __init__(self, ctx, eng, ename):
        self._ctx = ctx
        self._eng = eng
        self._ename = ename
        self._esem = ESEM[ename]

    def wait_ge(self, sem, val):
        if isinstance(sem, str):
            sem = self._ctx.sem(sem)
        self._eng.wait_ge(sem, val)

    def wait(self, pairs):
        for name, val in pairs:
            self._eng.wait_ge(self._ctx.sem(name), val)

    def dma(self, semname, *args, **kwargs):
        inst = self._eng.dma_start(*args, **kwargs)
        s, _ = self._ctx.bump(semname, 16)
        inst.then_inc(s, 16)
        return inst

    def idma(self, semname, *args, **kwargs):
        inst = self._eng.indirect_dma_start(*args, **kwargs)
        s, _ = self._ctx.bump(semname, 16)
        inst.then_inc(s, 16)
        return inst

    def cc(self, *args, **kwargs):
        inst = self._eng.collective_compute(*args, **kwargs)
        s, _ = self._ctx.bump("cc", 1)
        inst.then_inc(s, 1)
        return inst

    _SELFSYNC = {"scalar", "vector", "gpsimd"}

    def __getattr__(self, op):
        f = getattr(self._eng, op)
        if op in _NOINC or op.startswith("_") or not callable(f):
            return f
        ctx, esem = self._ctx, self._esem

        def g(*a, **k):
            if self._ename in EW._SELFSYNC and ctx.cnt.get(esem, 0) > 0:
                self._eng.wait_ge(ctx.sem(esem), ctx.cnt[esem])
            inst = f(*a, **k)
            s, _ = ctx.bump(esem, 1)
            try:
                inst.then_inc(s, 1)
            except Exception:
                ctx.cnt[esem] -= 1
            return inst
        return g


class NCProxy:

    def __init__(self, nc, prefix):
        self._nc = nc
        self._prefix = prefix

    def sbuf_tensor(self, name, shape=None, dtype=None, **kw):
        return self._nc.sbuf_tensor(self._prefix + name, shape, dtype, **kw)

    def psum_tensor(self, name, shape=None, dtype=None, **kw):
        return self._nc.psum_tensor(self._prefix + name, shape, dtype, **kw)

    def __getattr__(self, a):
        return getattr(self._nc, a)


class Ctx:
    def __init__(self, nc, block, stack):
        self.raw_nc = nc
        self.nc = nc
        self.block = block
        self.stack = stack
        self.scopes = [stack]
        self.sems = {}
        self.cnt = {}

    def set_prefix(self, prefix):
        self.nc = NCProxy(self.raw_nc, prefix)

    def sem(self, name):
        if name not in self.sems:
            self.sems[name] = self.stack.enter_context(self.nc.semaphore(name))
            self.cnt[name] = 0
        return self.sems[name]

    def bump(self, name, by=1):
        s = self.sem(name)
        self.cnt[name] += by
        return s, by

    def val(self, name):
        self.sem(name)
        return self.cnt[name]

    def sb(self, name, shape, dtype):
        return self.scopes[-1].enter_context(self.nc.sbuf_tensor(name, shape, dtype))

    def on(self, ename):
        def deco(fn):
            @getattr(self.block, ename)
            def _(eng):
                fn(EW(self, eng, ename))
        return deco

    def snapshot(self, names=("pe", "sc", "dv", "gp")):
        return [(n, self.val(n)) for n in names]


def barrier(ctx, extra=()):
    pairs = ctx.snapshot() + [(n, ctx.val(n)) for n in extra]
    for e in ESEM:
        @ctx.on(e)
        def _(eng):
            eng.wait(pairs)



def load_weight_chunks(ctx, dram_ap, tiles, rows=128, sem="wld"):
    @ctx.on("gpsimd")
    def _(g):
        for i, t in enumerate(tiles):
            g.dma(sem, out=t[:, :], in_=dram_ap[i * rows:(i + 1) * rows, :])


def ln_rows(ctx, x_dram, rstd_row, negmr_row, tag="", xbuf=None):
    nc = ctx.nc
    rstd_row = rstd_row[0:1, :]
    negmr_row = negmr_row[0:1, :]
    with ExitStack() as es:
        if xbuf is None:
            xbuf = [es.enter_context(nc.sbuf_tensor(f"lnx{tag}{i}", [128, T], F32)) for i in range(2)]
        sq = es.enter_context(nc.sbuf_tensor(f"lnsq{tag}", [128, T], F32))
        ones = es.enter_context(nc.sbuf_tensor(f"lnones{tag}", [128, 1], F32))
        srow = es.enter_context(nc.sbuf_tensor(f"lnsrow{tag}", [1, T], F32))[0:1, :]
        qrow = es.enter_context(nc.sbuf_tensor(f"lnqrow{tag}", [1, T], F32))[0:1, :]
        epsc = es.enter_context(nc.sbuf_tensor(f"lneps{tag}", [1, 2], F32))
        with nc.psum_tensor(f"ps_s{tag}", [1, T], F32) as ps_s, \
             nc.psum_tensor(f"ps_q{tag}", [1, T], F32) as ps_q:
            _ln_body(ctx, x_dram, rstd_row, negmr_row,
                     xbuf, sq, ones, srow, qrow, epsc, ps_s, ps_q)


def _ln_body(ctx, x_dram, rstd_row, negmr_row,
             xbuf, sq, ones, srow, qrow, epsc, ps_s, ps_q):
    @ctx.on("vector")
    def _(v):
        v.memset(ones[:, :], 1.0)
        v.memset(epsc[0:1, 0:1], EPS)
        v.memset(epsc[0:1, 1:2], -0.5)
    dv0 = ctx.val("dv")

    lda0, ldb0 = ctx.val("lda"), ctx.val("ldb")
    pe0 = ctx.val("pe")
    sc0 = ctx.val("sc")
    sq_done = [sc0 + c + 1 for c in range(NCH)]
    mm_done = [pe0 + 8 * (c + 1) for c in range(NCH)]

    @ctx.on("sync")
    def _(sy):
        for c in range(NCH):
            if c >= 2:
                sy.wait_ge("pe", mm_done[c - 2])
            sy.dma("lda" if c % 2 == 0 else "ldb",
                   out=xbuf[c % 2][:, :], in_=x_dram[c * 128:(c + 1) * 128, :])

    @ctx.on("scalar")
    def _(s):
        for c in range(NCH):
            s.wait_ge("lda" if c % 2 == 0 else "ldb",
                      (lda0 if c % 2 == 0 else ldb0) + 16 * (c // 2 + 1))
            if c >= 1:
                s.wait_ge("pe", mm_done[c - 1])
            s.activation(out=sq[:, :], in_=xbuf[c % 2][:, :], func=AF.Square)

    @ctx.on("tensor")
    def _(t):
        t.wait_ge("dv", dv0)
        for c in range(NCH):
            t.wait_ge("sc", sq_done[c])
            for g in range(NTG):
                t.matmul(ps_s[0:1, g * TG:(g + 1) * TG], ones[:, 0:1],
                         xbuf[c % 2][:, g * TG:(g + 1) * TG],
                         start=(c == 0), stop=(c == NCH - 1))
                t.matmul(ps_q[0:1, g * TG:(g + 1) * TG], ones[:, 0:1],
                         sq[:, g * TG:(g + 1) * TG],
                         start=(c == 0), stop=(c == NCH - 1))
    pe_done = ctx.val("pe")

    @ctx.on("vector")
    def _(v):
        v.wait_ge("pe", pe_done)
        v.tensor_scalar(srow, ps_s[0:1, :], 1.0 / D, None, ALU.mult)
        v.tensor_scalar(qrow, ps_q[0:1, :], 1.0 / D, None, ALU.mult)
        v.tensor_tensor(rstd_row, srow, srow, ALU.mult)
        v.tensor_tensor(qrow, qrow, rstd_row, ALU.subtract)
    dv1 = ctx.val("dv")

    @ctx.on("scalar")
    def _(s):
        s.wait_ge("dv", dv1)
        s.activation(out=rstd_row, in_=qrow, func=AF.Ln, bias=epsc[0:1, 0:1])
        s.activation(out=rstd_row, in_=rstd_row, func=AF.Exp, scale=epsc[0:1, 1:2])
    sc1 = ctx.val("sc")

    @ctx.on("vector")
    def _(v):
        v.wait_ge("sc", sc1)
        v.tensor_tensor(negmr_row, srow, rstd_row, ALU.mult)
        v.tensor_scalar(negmr_row, negmr_row, -1.0, None, ALU.mult)

    barrier(ctx, extra=("lda", "ldb"))


def replicate_rows(ctx, rows_and_dsts, ones_row, tag=""):
    nc = ctx.nc
    with nc.psum_tensor(f"repps{tag}", [128, T], F32) as ps, \
         nc.sbuf_tensor(f"repones{tag}", [65, 128], F32) as op:
        @ctx.on("vector")
        def _(v):
            v.memset(op[:, :], 1.0)
        dvo = ctx.val("dv")

        for i, (row, dst) in enumerate(rows_and_dsts):
            sc_gate = ctx.val("sc")
            row = row[0:1, :]
            b_ = getattr(row, "base_partition", 0)
            bp = b_() if callable(b_) else int(b_)

            @ctx.on("tensor")
            def _(t, row=row, bp=bp):
                t.wait_ge("dv", dvo)
                if i > 0:
                    t.wait_ge("sc", sc_gate)
                for g in range(NTG):
                    t.matmul(ps[:, g * TG:(g + 1) * TG], op[bp:bp + 1, 0:128],
                             row[0:1, g * TG:(g + 1) * TG], start=True, stop=True)
            pv = ctx.val("pe")

            @ctx.on("scalar")
            def _(s, dst=dst):
                s.wait_ge("pe", pv)
                s.copy(dst[:, :], ps[:, :])
    barrier(ctx)


def proj_1024(ctx, w_tiles, rhs_tiles, evac, nk=NCH, psname="projps",
              ngroups=NTG, tgw=TG, mchunks=NCH):
    nc = ctx.nc
    with nc.psum_tensor(psname + "0", [128, tgw], F32) as p0, \
         nc.psum_tensor(psname + "1", [128, tgw], F32) as p1:
        banks = [p0, p1]
        bank_free = [None, None]
        bi = 0
        for m in range(mchunks):
            for g in range(ngroups):
                ps = banks[bi]
                gate = bank_free[bi]

                @ctx.on("tensor")
                def _(t, m=m, g=g, ps=ps, gate=gate):
                    if gate is not None:
                        t.wait_ge(gate[0], gate[1])
                    for k in range(nk):
                        t.matmul(ps[:, :], w_tiles[k][:, m * 128:(m + 1) * 128],
                                 rhs_tiles[k][:, g * tgw:(g + 1) * tgw],
                                 start=(k == 0), stop=(k == nk - 1))
                pv = ctx.val("pe")
                bank_free[bi] = evac(m, g, ps, pv)
                bi ^= 1
    barrier(ctx)



def emit_rwkv(ctx, x_in, x_out, WrT, WkT, WvT, mu_r_col, mu_k_col):
    nc = ctx.nc

    wr = [ctx.sb(f"wr{k}", [128, D], BF16) for k in range(NCH)]
    wk = [ctx.sb(f"wk{k}", [128, D], BF16) for k in range(NCH)]
    wv = [ctx.sb(f"wv{k}", [128, D], BF16) for k in range(NCH)]
    murc = ctx.sb("murc", [128, NCH], F32)
    mukc = ctx.sb("mukc", [128, NCH], F32)
    ones_row = ctx.sb("onesrow", [1, 128], F32)
    half = ctx.sb("halfc", [128, 1], F32)
    rstd = ctx.sb("rstd", [1, T], F32)
    negmr = ctx.sb("negmr", [1, T], F32)
    rep_rstd = ctx.sb("rep_rstd", [128, T], BF16)
    rep_negmr = ctx.sb("rep_negmr", [128, T], BF16)

    load_weight_chunks(ctx, WrT, wr)
    load_weight_chunks(ctx, WkT, wk)
    load_weight_chunks(ctx, WvT, wv)

    @ctx.on("gpsimd")
    def _(g):
        g.dma("wld", out=murc[:, :], in_=mu_r_col)
        g.dma("wld", out=mukc[:, :], in_=mu_k_col)

    @ctx.on("vector")
    def _(v):
        v.memset(ones_row[:, :], 1.0)
        v.memset(half[:, :], 0.5)

    ln_rows(ctx, x_in, rstd, negmr)
    replicate_rows(ctx, [(rstd, rep_rstd), (negmr, rep_negmr)], ones_row)

    xr = [ctx.sb(f"xr{k}", [128, T], BF16) for k in range(NCH)]
    xk = [ctx.sb(f"xk{k}", [128, T], BF16) for k in range(NCH)]
    r = [ctx.sb(f"r{k}", [128, T], BF16) for k in range(NCH)]
    ybuf = ctx.sb("ybuf", [128, T], BF16)
    tbuf = ctx.sb("tbuf", [128, T], BF16)
    ktmp = ctx.sb("ktmp", [128, TG], BF16)
    xbuf = [ctx.sb(f"xb{i}", [128, T], F32) for i in range(2)]
    vtmp = ctx.sb("vtmp", [128, TG], F32)
    xres = [ctx.sb(f"xres{i}", [128, TG], F32) for i in range(2)]

    wld_done = ctx.val("wld")

    lda0, ldb0 = ctx.val("lda"), ctx.val("ldb")
    dv0 = ctx.val("dv")
    mix_done = [dv0 + 7 * (c + 1) for c in range(NCH)]

    @ctx.on("sync")
    def _(sy):
        for c in range(NCH):
            if c >= 2:
                sy.wait_ge("dv", mix_done[c - 2])
            sy.dma("lda" if c % 2 == 0 else "ldb",
                   out=xbuf[c % 2][:, :], in_=x_in[c * 128:(c + 1) * 128, :])

    @ctx.on("vector")
    def _(v):
        v.wait_ge("wld", wld_done)
        for c in range(NCH):
            v.wait_ge("lda" if c % 2 == 0 else "ldb",
                      (lda0 if c % 2 == 0 else ldb0) + 16 * (c // 2 + 1))
            v.tensor_tensor(tbuf[:, :], xbuf[c % 2][:, :], rep_rstd[:, :], ALU.mult)
            v.tensor_tensor(ybuf[:, :], tbuf[:, :], rep_negmr[:, :], ALU.add)
            v.tensor_tensor(tbuf[:, 1:T], ybuf[:, 1:T], ybuf[:, 0:T - 1], ALU.subtract)
            v.scalar_tensor_tensor(xr[c][:, 1:T], tbuf[:, 1:T], murc[:, c:c + 1],
                                   ybuf[:, 0:T - 1], ALU.mult, ALU.add)
            v.tensor_scalar(xr[c][:, 0:1], ybuf[:, 0:1], murc[:, c:c + 1], None, ALU.mult)
            v.scalar_tensor_tensor(xk[c][:, 1:T], tbuf[:, 1:T], mukc[:, c:c + 1],
                                   ybuf[:, 0:T - 1], ALU.mult, ALU.add)
            v.tensor_scalar(xk[c][:, 0:1], ybuf[:, 0:1], mukc[:, c:c + 1], None, ALU.mult)
    assert ctx.val("dv") == mix_done[-1], (ctx.val("dv"), mix_done[-1])
    barrier(ctx, extra=("lda", "ldb"))

    def evac_r(m, g, ps, pv):
        @ctx.on("scalar")
        def _(s):
            s.wait_ge("pe", pv)
            s.activation(out=r[m][:, g * TG:(g + 1) * TG], in_=ps[:, :], func=AF.Sigmoid)
        return ("sc", ctx.val("sc"))
    proj_1024(ctx, wr, xr, evac_r, psname="rps")

    kact = xr
    ktmp_read = [0]

    def evac_k(m, g, ps, pv):
        @ctx.on("scalar")
        def _(s):
            s.wait_ge("pe", pv)
            s.wait_ge("dv", ktmp_read[0])
            s.activation(out=ktmp[:, :], in_=ps[:, :], func=AF.Relu)
        sv = ctx.val("sc")

        @ctx.on("vector")
        def _(v):
            v.wait_ge("sc", sv)
            v.tensor_tensor(kact[m][:, g * TG:(g + 1) * TG], ktmp[:, :], ktmp[:, :], ALU.mult)
        ktmp_read[0] = ctx.val("dv")
        return ("dv", ctx.val("dv"))
    proj_1024(ctx, wk, xk, evac_k, psname="kps")

    res_gates = []
    store_gates = []

    def evac_v(m, g, ps, pv):
        idx = m * NTG + g
        rsem = "rla" if idx % 2 == 0 else "rlb"

        ssem = "sta" if idx % 2 == 0 else "stb"

        @ctx.on("sync")
        def _(sy):
            if idx >= 2:
                sy.wait_ge(*store_gates[idx - 2])
            sy.dma(rsem, out=xres[idx % 2][:, :],
                   in_=x_in[m * 128:(m + 1) * 128, g * TG:(g + 1) * TG])
        lv = ctx.val(rsem)

        @ctx.on("vector")
        def _(v):
            v.wait_ge("pe", pv)
            v.wait_ge(rsem, lv)
            v.tensor_tensor(vtmp[:, :], ps[:, :], r[m][:, g * TG:(g + 1) * TG], ALU.mult)
            v.scalar_tensor_tensor(xres[idx % 2][:, :], vtmp[:, :], half[:, 0:1],
                                   xres[idx % 2][:, :], ALU.mult, ALU.add)
        dvv = ctx.val("dv")
        res_gates.append(dvv)

        @ctx.on("gpsimd")
        def _(g2):
            g2.wait_ge("dv", dvv)
            g2.dma(ssem, out=x_out[m * 128:(m + 1) * 128, g * TG:(g + 1) * TG],
                   in_=xres[idx % 2][:, :])
        store_gates.append((ssem, ctx.val(ssem)))
        return ("dv", dvv)
    proj_1024(ctx, wv, kact, evac_v, psname="vps")

    sta_v, stb_v = ctx.val("sta"), ctx.val("stb")

    @ctx.on("gpsimd")
    def _(g):
        g.wait_ge("sta", sta_v)
        g.wait_ge("stb", stb_v)



def build_y(ctx, x_in, rep_rstd, rep_negmr, ytiles, xbuf, per_chunk_extra=None):
    lda0, ldb0 = ctx.val("lda"), ctx.val("ldb")
    dv0 = ctx.val("dv")
    nops = 2
    y_done = [dv0 + nops * (c + 1) for c in range(NCH)]

    @ctx.on("sync")
    def _(sy):
        for c in range(NCH):
            if c >= 2:
                sy.wait_ge("dv", y_done[c - 2])
            sy.dma("lda" if c % 2 == 0 else "ldb",
                   out=xbuf[c % 2][:, :], in_=x_in[c * 128:(c + 1) * 128, :])

    @ctx.on("vector")
    def _(v):
        for c in range(NCH):
            v.wait_ge("lda" if c % 2 == 0 else "ldb",
                      (lda0 if c % 2 == 0 else ldb0) + 16 * (c // 2 + 1))
            v.tensor_tensor(ytiles[c][:, :], xbuf[c % 2][:, :], rep_rstd[:, :], ALU.mult)
            v.tensor_tensor(ytiles[c][:, :], ytiles[c][:, :], rep_negmr[:, :], ALU.add)
    assert ctx.val("dv") == y_done[-1]
    barrier(ctx, extra=("lda", "ldb"))


def residual_evac(ctx, x_in, x_out, m, g, idx, src_ap, scale_col, xres, store_gates, dv_ready):
    rsem = "rla" if idx % 2 == 0 else "rlb"
    ssem = "sta" if idx % 2 == 0 else "stb"

    @ctx.on("sync")
    def _(sy):
        if idx >= 2:
            sy.wait_ge(*store_gates[idx - 2])
        sy.dma(rsem, out=xres[idx % 2][:, :],
               in_=x_in[m * 128:(m + 1) * 128, g * TG:(g + 1) * TG])
    lv = ctx.val(rsem)

    @ctx.on("vector")
    def _(v):
        v.wait_ge("dv", dv_ready)
        v.wait_ge(rsem, lv)
        v.scalar_tensor_tensor(xres[idx % 2][:, :], src_ap, scale_col,
                               xres[idx % 2][:, :], ALU.mult, ALU.add)
    dvv = ctx.val("dv")

    @ctx.on("gpsimd")
    def _(g2):
        g2.wait_ge("dv", dvv)
        g2.dma(ssem, out=x_out[m * 128:(m + 1) * 128, g * TG:(g + 1) * TG],
               in_=xres[idx % 2][:, :])
    store_gates.append((ssem, ctx.val(ssem)))
    return dvv


def wait_stores(ctx):
    sta_v, stb_v = ctx.val("sta"), ctx.val("stb")

    @ctx.on("gpsimd")
    def _(g):
        g.wait_ge("sta", sta_v)
        g.wait_ge("stb", stb_v)



def emit_l2ffn(ctx, x_in, x_out, WfcT_dram, WprojT_dram):
    nc = ctx.nc
    ones_row = ctx.sb("onesrow2", [1, 128], F32)
    tenth = ctx.sb("tenth", [128, 1], F32)
    y = [ctx.sb(f"y2{k}", [128, T], BF16) for k in range(NCH)]

    @ctx.on("vector")
    def _(v):
        v.memset(ones_row[:, :], 1.0)
        v.memset(tenth[:, :], 0.1)

    with ExitStack() as es:
        rstd = es.enter_context(nc.sbuf_tensor("rstd2", [1, T], F32))
        negmr = es.enter_context(nc.sbuf_tensor("negmr2", [1, T], F32))
        rep_rstd = es.enter_context(nc.sbuf_tensor("rep_rstd2", [128, T], BF16))
        rep_negmr = es.enter_context(nc.sbuf_tensor("rep_negmr2", [128, T], BF16))
        xbuf = [es.enter_context(nc.sbuf_tensor(f"xb2{i}", [128, T], F32)) for i in range(2)]
        ln_rows(ctx, x_in, rstd, negmr, tag="f2", xbuf=xbuf)
        replicate_rows(ctx, [(rstd, rep_rstd), (negmr, rep_negmr)], ones_row, tag="f2")
        build_y(ctx, x_in, rep_rstd, rep_negmr, y, xbuf)

    wfc = [ctx.sb(f"wfc{k}", [128, 4 * D], BF16) for k in range(NCH)]
    wproj = [ctx.sb(f"wpj{k}", [128, D], BF16) for k in range(4 * NCH)]
    h = [ctx.sb(f"h{k}", [128, TG], BF16) for k in range(4 * NCH)]
    htmp = ctx.sb("htmp", [128, TG], BF16)
    y2t = ctx.sb("y2t", [128, TG], F32)
    mem = ctx.sb("memt", [128, TG], F32)
    spk = ctx.sb("spkt", [128, TG], F32)
    acc = ctx.sb("acct", [128, TG], F32)
    t1 = ctx.sb("t1t", [128, TG], F32)
    xres = [ctx.sb(f"xres2{i}", [128, TG], F32) for i in range(2)]

    load_weight_chunks(ctx, WfcT_dram, wfc)
    load_weight_chunks(ctx, WprojT_dram, wproj)
    wld_done = ctx.val("wld")

    @ctx.on("tensor")
    def _(t):
        t.wait_ge("wld", wld_done)

    store_gates = []
    htmp_read = [0]
    for g in range(NTG):
        def evac_h(m, _g, ps, pv, g=g):
            @ctx.on("scalar")
            def _(sx):
                sx.wait_ge("pe", pv)
                sx.wait_ge("dv", htmp_read[0])
                sx.activation(out=htmp[:, :], in_=ps[:, :], func=AF.Relu)
            sv = ctx.val("sc")

            @ctx.on("vector")
            def _(v):
                v.wait_ge("sc", sv)
                v.tensor_tensor(h[m][:, :], htmp[:, :], htmp[:, :], ALU.mult)
            htmp_read[0] = ctx.val("dv")
            return ("dv", ctx.val("dv"))
        proj_1024(ctx, wfc, [yy[:, g * TG:(g + 1) * TG] for yy in y], evac_h,
                  psname=f"hps{g}", mchunks=4 * NCH, ngroups=1, tgw=TG)

        def evac_y2(m, _g, ps, pv, g=g):
            idx = g * NCH + m

            @ctx.on("vector")
            def _(v):
                v.wait_ge("pe", pv)
                v.tensor_copy(y2t[:, :], ps[:, :])
                v.tensor_copy(mem[:, :], y2t[:, :])
                v.tensor_scalar(spk[:, :], mem[:, :], 1.0, None, ALU.is_gt)
                v.tensor_copy(acc[:, :], spk[:, :])
                for st in range(1, 5):
                    v.tensor_tensor(t1[:, :], y2t[:, :], spk[:, :], ALU.subtract)
                    v.scalar_tensor_tensor(mem[:, :], mem[:, :], 0.9, t1[:, :],
                                           ALU.mult, ALU.add)
                    v.tensor_scalar(spk[:, :], mem[:, :], 1.0, None, ALU.is_gt)
                    v.tensor_tensor(acc[:, :], acc[:, :], spk[:, :], ALU.add)
            dvv = ctx.val("dv")
            dvv2 = residual_evac(ctx, x_in, x_out, m, g, idx, acc[:, :], tenth[:, 0:1],
                                 xres, store_gates, dvv)
            return ("dv", dvv2)
        proj_1024(ctx, wproj, h, evac_y2, psname=f"y2ps{g}", nk=4 * NCH,
                  mchunks=NCH, ngroups=1, tgw=TG)

    wait_stores(ctx)


def build_l2ffn():
    nc = bass.Bass("TRN2", target_bir_lowering=False, debug=False)
    x_in = nc.dram_tensor("x_in", [D, T], F32, kind="ExternalInput").ap()
    WfcT = nc.dram_tensor("WfcT", [D, 4 * D], F32, kind="ExternalInput").ap()
    WprojT = nc.dram_tensor("WprojT", [4 * D, D], F32, kind="ExternalInput").ap()
    x_out = nc.dram_tensor("x_out", [D, T], F32, kind="ExternalOutput").ap()
    with ExitStack() as stack:
        block = stack.enter_context(nc.Block())
        ctx = Ctx(nc, block, stack)
        emit_l2ffn(ctx, x_in, x_out, WfcT, WprojT)
    return nc


def l2ffn_maps(W_fc, W_proj):
    return {
        "WfcT": np.ascontiguousarray(np.asarray(W_fc).T),
        "WprojT": np.ascontiguousarray(np.asarray(W_proj).T),
    }


def run_l2ffn(xT, W_fc, W_proj):
    nc = build_l2ffn()
    wmap = l2ffn_maps(W_fc, W_proj)
    in_maps = [dict(wmap, x_in=np.ascontiguousarray(xT[b])) for b in range(8)]
    res = run_bass_kernel_spmd(nc, in_maps, core_ids=list(range(8)))
    return np.stack([r["x_out"] for r in res.results])



def emit_l1attn(ctx, x_in, x_out, Wq_d, Wk_d, Wv_d, bq_d, bk_d, bv_d, ident_d):
    nc = ctx.nc
    ones_row = ctx.sb("onesrowA", [1, TG], F32)
    ones_bf = ctx.sb("onesbfA", [1, TG], BF16)
    qscale = ctx.sb("qscaleA", [128, 1], F32)
    ident = ctx.sb("identA", [128, 128], BF16)

    QT = [ctx.sb(f"qt{k}", [128, T], BF16) for k in range(NCH)]
    KT = [ctx.sb(f"kt{k}", [128, T], BF16) for k in range(NCH)]
    V = [ctx.sb(f"vtl{k}", [128, D], BF16) for k in range(16)]

    @ctx.on("vector")
    def _(v):
        v.memset(ones_row[:, :], 1.0)
        v.memset(ones_bf[:, :], 1.0)
        v.memset(qscale[:, :], 1.0 / 32.0)

    @ctx.on("gpsimd")
    def _(g):
        g.dma("wld", out=ident[:, :], in_=ident_d)

    with ExitStack() as es:
        rstd = es.enter_context(nc.sbuf_tensor("rstdA", [1, T], F32))
        negmr = es.enter_context(nc.sbuf_tensor("negmrA", [1, T], F32))
        rep_rstd = es.enter_context(nc.sbuf_tensor("rep_rstdA", [128, T], BF16))
        rep_negmr = es.enter_context(nc.sbuf_tensor("rep_negmrA", [128, T], BF16))
        xbuf = [es.enter_context(nc.sbuf_tensor(f"xbA{i}", [128, T], F32)) for i in range(2)]
        y = [es.enter_context(nc.sbuf_tensor(f"yA{k}", [128, T], BF16)) for k in range(NCH)]
        brow = es.enter_context(nc.sbuf_tensor("browA", [1, D], BF16))

        ln_rows(ctx, x_in, rstd, negmr, tag="A", xbuf=xbuf)
        replicate_rows(ctx, [(rstd, rep_rstd), (negmr, rep_negmr)], ones_row[0:1, 0:128], tag="A")
        build_y(ctx, x_in, rep_rstd, rep_negmr, y, xbuf)

        for wi, (Wd, bd, dst, scale) in enumerate(
                ((Wq_d, bq_d, QT, qscale), (Wk_d, bk_d, KT, None))):
            with ExitStack() as es2:
                wt = [es2.enter_context(nc.sbuf_tensor(f"wA{wi}_{k}", [128, D], BF16))
                      for k in range(NCH)]
                load_weight_chunks(ctx, Wd, wt)

                @ctx.on("gpsimd")
                def _(g, bd=bd):
                    g.dma("wld", out=brow[0:1, :], in_=bd)
                wv_done = ctx.val("wld")

                @ctx.on("tensor")
                def _(t):
                    t.wait_ge("wld", wv_done)

                def evac_qk(m, g, ps, pv, dst=dst, scale=scale):
                    @ctx.on("scalar")
                    def _(sx):
                        sx.wait_ge("pe", pv)
                        if scale is not None:
                            sx.activation(out=dst[m][:, g * TG:(g + 1) * TG], in_=ps[:, :],
                                          func=AF.Identity, scale=scale[:, 0:1])
                        else:
                            sx.copy(dst[m][:, g * TG:(g + 1) * TG], ps[:, :])
                    return ("sc", ctx.val("sc"))

                with nc.psum_tensor(f"qkps0_{wi}", [128, TG], F32) as p0, \
                     nc.psum_tensor(f"qkps1_{wi}", [128, TG], F32) as p1:
                    banks = [p0, p1]
                    bank_free = [None, None]
                    bi = 0
                    for m in range(NCH):
                        for g in range(NTG):
                            ps = banks[bi]
                            gate = bank_free[bi]

                            @ctx.on("tensor")
                            def _(t, m=m, g=g, ps=ps, gate=gate):
                                if gate is not None:
                                    t.wait_ge(gate[0], gate[1])
                                for k in range(NCH):
                                    t.matmul(ps[:, :], wt[k][:, m * 128:(m + 1) * 128],
                                             y[k][:, g * TG:(g + 1) * TG],
                                             start=(k == 0), stop=False)
                                t.matmul(ps[:, :], brow[0:1, m * 128:(m + 1) * 128],
                                         ones_bf[0:1, :], start=False, stop=True)
                            pv = ctx.val("pe")
                            bank_free[bi] = evac_qk(m, g, ps, pv)
                            bi ^= 1
                barrier(ctx)

        with ExitStack() as es2:
            wt = [es2.enter_context(nc.sbuf_tensor(f"wVA{k}", [128, D], BF16))
                  for k in range(NCH)]
            load_weight_chunks(ctx, Wv_d, wt)

            @ctx.on("gpsimd")
            def _(g):
                g.dma("wld", out=brow[0:1, :], in_=bv_d)
            wv_done = ctx.val("wld")

            @ctx.on("tensor")
            def _(t):
                t.wait_ge("wld", wv_done)

            with nc.psum_tensor("vps0", [128, TG], F32) as p0, \
                 nc.psum_tensor("vps1", [128, TG], F32) as p1:
                banks = [p0, p1]
                bank_free = [None, None]
                bi = 0
                for tt in range(16):
                    for ng in range(2):
                        ps = banks[bi]
                        gate = bank_free[bi]

                        @ctx.on("tensor")
                        def _(t, tt=tt, ng=ng, ps=ps, gate=gate):
                            if gate is not None:
                                t.wait_ge(gate[0], gate[1])
                            for k in range(NCH):
                                t.matmul(ps[:, :], y[k][:, tt * 128:(tt + 1) * 128],
                                         wt[k][:, ng * TG:(ng + 1) * TG],
                                         start=(k == 0), stop=False)
                            t.matmul(ps[:, :], ones_bf[0:1, 0:128],
                                     brow[0:1, ng * TG:(ng + 1) * TG],
                                     start=False, stop=True)
                        pv = ctx.val("pe")

                        @ctx.on("scalar")
                        def _(sx, tt=tt, ng=ng, ps=ps):
                            sx.wait_ge("pe", pv)
                            sx.copy(V[tt][:, ng * TG:(ng + 1) * TG], ps[:, :])
                        bank_free[bi] = ("sc", ctx.val("sc"))
                        bi ^= 1
            barrier(ctx)

    ao = [ctx.sb(f"ao{k}", [128, T], BF16) for k in range(NCH)]
    wtile = ctx.sb("wtileA", [128, T], BF16)
    wTt = ctx.sb("wTA", [128, T], BF16)
    mx = ctx.sb("mxA", [128, 1], F32)
    negmx = ctx.sb("negmxA", [128, 1], F32)
    sums = ctx.sb("sumsA", [128, 1], F32)
    rsum = ctx.sb("rsumA", [128, 1], F32)

    with nc.psum_tensor("lgps", [128, T], F32) as lgps, \
         nc.psum_tensor("tpps0", [128, 128], BF16) as tpps0, \
         nc.psum_tensor("tpps1", [128, 128], BF16) as tpps1, \
         nc.psum_tensor("avps", [128, 1024], F32) as avps:
        tpps = [tpps0, tpps1]
        for qt in range(16):
            dv_gate = ctx.val("dv")
            sc_gate = ctx.val("sc")

            @ctx.on("tensor")
            def _(t, qt=qt):
                t.wait_ge("dv", dv_gate)
                t.wait_ge("sc", sc_gate)
                for gk in range(NTG):
                    for kd in range(NCH):
                        t.matmul(lgps[:, gk * TG:(gk + 1) * TG],
                                 QT[kd][:, qt * 128:(qt + 1) * 128],
                                 KT[kd][:, gk * TG:(gk + 1) * TG],
                                 start=(kd == 0), stop=(kd == NCH - 1))
            pv = ctx.val("pe")

            @ctx.on("vector")
            def _(v):
                v.wait_ge("pe", pv)
                v.reduce_max(mx[:, 0:1], lgps[:, :], axis=AX)
                v.tensor_scalar(negmx[:, 0:1], mx[:, 0:1], -1.0, None, ALU.mult)
            dvv = ctx.val("dv")

            @ctx.on("scalar")
            def _(sx):
                sx.wait_ge("dv", dvv)
                sx.activation(out=wtile[:, :], in_=lgps[:, :], func=AF.Exp,
                              bias=negmx[:, 0:1], accum_out=sums[:, 0:1])
            scv = ctx.val("sc")

            @ctx.on("vector")
            def _(v):
                v.wait_ge("sc", scv)
                v.reciprocal(rsum[:, 0:1], sums[:, 0:1])
                v.tensor_scalar(wtile[:, :], wtile[:, :], rsum[:, 0:1], None, ALU.mult)
            dvn = ctx.val("dv")

            tp_evac = [None, None]
            for j in range(16):
                pj = tpps[j % 2]
                gate = tp_evac[j % 2]

                @ctx.on("tensor")
                def _(t, j=j, pj=pj, gate=gate):
                    if j == 0:
                        t.wait_ge("dv", dvn)
                    if gate is not None:
                        t.wait_ge(gate[0], gate[1])
                    t.transpose(pj[:, :], wtile[:, j * 128:(j + 1) * 128], ident[:, :])
                pvj = ctx.val("pe")

                @ctx.on("vector")
                def _(v, j=j, pj=pj):
                    v.wait_ge("pe", pvj)
                    v.tensor_copy(wTt[:, j * 128:(j + 1) * 128], pj[:, :])
                tp_evac[j % 2] = ("dv", ctx.val("dv"))
            dvT = ctx.val("dv")

            @ctx.on("tensor")
            def _(t, qt=qt):
                t.wait_ge("dv", dvT)
                for m in range(NCH):
                    for j in range(16):
                        t.matmul(avps[:, m * 128:(m + 1) * 128],
                                 V[j][:, m * 128:(m + 1) * 128],
                                 wTt[:, j * 128:(j + 1) * 128],
                                 start=(j == 0), stop=(j == 15))
            pva = ctx.val("pe")

            @ctx.on("scalar")
            def _(sx, qt=qt):
                sx.wait_ge("pe", pva)
                for m in range(NCH):
                    sx.copy(ao[m][:, qt * 128:(qt + 1) * 128], avps[:, m * 128:(m + 1) * 128])
    barrier(ctx)

    one_col = ctx.sb("onecolA", [128, 1], F32)
    xres = [ctx.sb(f"xresA{i}", [128, TG], F32) for i in range(2)]

    @ctx.on("vector")
    def _(v):
        v.memset(one_col[:, :], 1.0)

    store_gates = []
    for m in range(NCH):
        for g in range(NTG):
            idx = m * NTG + g
            dvv = ctx.val("dv")
            residual_evac(ctx, x_in, x_out, m, g, idx, ao[m][:, g * TG:(g + 1) * TG],
                          one_col[:, 0:1], xres, store_gates, dvv)
    wait_stores(ctx)


def build_l1attn():
    nc = bass.Bass("TRN2", target_bir_lowering=False, debug=False)
    x_in = nc.dram_tensor("x_in", [D, T], F32, kind="ExternalInput").ap()
    Wq = nc.dram_tensor("Wq", [D, D], F32, kind="ExternalInput").ap()
    Wk = nc.dram_tensor("Wk", [D, D], F32, kind="ExternalInput").ap()
    Wv = nc.dram_tensor("Wv", [D, D], F32, kind="ExternalInput").ap()
    bq = nc.dram_tensor("bq", [1, D], F32, kind="ExternalInput").ap()
    bk = nc.dram_tensor("bk", [1, D], F32, kind="ExternalInput").ap()
    bv = nc.dram_tensor("bv", [1, D], F32, kind="ExternalInput").ap()
    ident = nc.dram_tensor("ident", [128, 128], F32, kind="ExternalInput").ap()
    x_out = nc.dram_tensor("x_out", [D, T], F32, kind="ExternalOutput").ap()
    with ExitStack() as stack:
        block = stack.enter_context(nc.Block())
        ctx = Ctx(nc, block, stack)
        emit_l1attn(ctx, x_in, x_out, Wq, Wk, Wv, bq, bk, bv, ident)
    return nc


def l1attn_maps(Wq, Wk, Wv, bq, bk, bv):
    return {
        "Wq": np.ascontiguousarray(np.asarray(Wq)),
        "Wk": np.ascontiguousarray(np.asarray(Wk)),
        "Wv": np.ascontiguousarray(np.asarray(Wv)),
        "bq": np.asarray(bq).reshape(1, D).copy(),
        "bk": np.asarray(bk).reshape(1, D).copy(),
        "bv": np.asarray(bv).reshape(1, D).copy(),
        "ident": np.eye(128, dtype=np.float32),
    }


def run_l1attn(xT, Wq, Wk, Wv, bq, bk, bv):
    nc = build_l1attn()
    wmap = l1attn_maps(Wq, Wk, Wv, bq, bk, bv)
    in_maps = [dict(wmap, x_in=np.ascontiguousarray(xT[b])) for b in range(8)]
    res = run_bass_kernel_spmd(nc, in_maps, core_ids=list(range(8)))
    return np.stack([r["x_out"] for r in res.results])



def emit_l2attn(ctx, x_in, x_out, Wqkv_d, Wout_d, hs_d, dirs_d, cmask_d, ao_dram):
    nc = ctx.nc
    H = 8
    SC = 1.0 / np.sqrt(128.0)
    ones_row = ctx.sb("onesrowC", [1, 128], F32)
    ones_bf = ctx.sb("onesbfC", [1, TG], BF16)
    onescol_bf = ctx.sb("onescolC", [128, 1], BF16)
    one_col = ctx.sb("onecolC", [128, 1], F32)
    qsc = ctx.sb("qscC", [128, 1], F32)
    dirs = ctx.sb("dirsC", [3, H], BF16)
    hsrow = ctx.sb("hsrowC", [1, H], F32)
    cmask = [ctx.sb(f"cmaskC{j}", [128, TG], BF16) for j in range(4)]

    @ctx.on("vector")
    def _(v):
        v.memset(ones_row[:, :], 1.0)
        v.memset(ones_bf[:, :], 1.0)
        v.memset(onescol_bf[:, :], 1.0)
        v.memset(one_col[:, :], 1.0)
        v.memset(qsc[:, :], SC)

    @ctx.on("gpsimd")
    def _(g):
        g.dma("wld", out=dirs[:, :], in_=dirs_d)
        g.dma("wld", out=hsrow[:, :], in_=hs_d)
        for j in range(4):
            g.dma("wld", out=cmask[j][:, :], in_=cmask_d[j])

    with ExitStack() as esq:
        QT = [esq.enter_context(nc.sbuf_tensor(f"qtC{k}", [128, T], BF16)) for k in range(H)]
        KT = [esq.enter_context(nc.sbuf_tensor(f"ktC{k}", [128, T], BF16)) for k in range(H)]
        V = [esq.enter_context(nc.sbuf_tensor(f"vtlC{k}", [128, D], BF16)) for k in range(16)]

        with ExitStack() as es:
            rstd = es.enter_context(nc.sbuf_tensor("rstdC", [1, T], F32))
            negmr = es.enter_context(nc.sbuf_tensor("negmrC", [1, T], F32))
            rep_rstd = es.enter_context(nc.sbuf_tensor("rep_rstdC", [128, T], BF16))
            rep_negmr = es.enter_context(nc.sbuf_tensor("rep_negmrC", [128, T], BF16))
            xbuf = [es.enter_context(nc.sbuf_tensor(f"xbC{i}", [128, T], F32)) for i in range(2)]
            y = [es.enter_context(nc.sbuf_tensor(f"yC{k}", [128, T], BF16)) for k in range(NCH)]

            ln_rows(ctx, x_in, rstd, negmr, tag="C", xbuf=xbuf)
            replicate_rows(ctx, [(rstd, rep_rstd), (negmr, rep_negmr)], ones_row, tag="C")
            build_y(ctx, x_in, rep_rstd, rep_negmr, y, xbuf)

            with ExitStack() as es2:
                wt = [es2.enter_context(nc.sbuf_tensor(f"wCqk{k}", [128, 2 * D], BF16))
                      for k in range(NCH)]
                load_weight_chunks(ctx, Wqkv_d[:, 0:2 * D], wt)
                wv_done = ctx.val("wld")

                @ctx.on("tensor")
                def _(t):
                    t.wait_ge("wld", wv_done)

                def evac_qk(m, g, ps, pv):
                    @ctx.on("scalar")
                    def _(sx):
                        sx.wait_ge("pe", pv)
                        if m < H:
                            sx.activation(out=QT[m][:, g * TG:(g + 1) * TG], in_=ps[:, :],
                                          func=AF.Identity, scale=qsc[:, 0:1])
                        else:
                            sx.copy(KT[m - H][:, g * TG:(g + 1) * TG], ps[:, :])
                    return ("sc", ctx.val("sc"))
                proj_1024(ctx, wt, y, evac_qk, psname="qkC", mchunks=16)

            with ExitStack() as es2:
                wt = [es2.enter_context(nc.sbuf_tensor(f"wCv{k}", [128, D], BF16))
                      for k in range(NCH)]
                load_weight_chunks(ctx, Wqkv_d[:, 2 * D:3 * D], wt)
                wv_done = ctx.val("wld")

                @ctx.on("tensor")
                def _(t):
                    t.wait_ge("wld", wv_done)

                with nc.psum_tensor("vpsC0", [128, TG], F32) as p0, \
                     nc.psum_tensor("vpsC1", [128, TG], F32) as p1:
                    banks = [p0, p1]
                    bank_free = [None, None]
                    bi = 0
                    for tt in range(16):
                        for ng in range(2):
                            ps = banks[bi]
                            gate = bank_free[bi]

                            @ctx.on("tensor")
                            def _(t, tt=tt, ng=ng, ps=ps, gate=gate):
                                if gate is not None:
                                    t.wait_ge(gate[0], gate[1])
                                for k in range(NCH):
                                    t.matmul(ps[:, :], y[k][:, tt * 128:(tt + 1) * 128],
                                             wt[k][:, ng * TG:(ng + 1) * TG],
                                             start=(k == 0), stop=(k == NCH - 1))
                            pv = ctx.val("pe")

                            @ctx.on("scalar")
                            def _(sx, tt=tt, ng=ng, ps=ps):
                                sx.wait_ge("pe", pv)
                                sx.copy(V[tt][:, ng * TG:(ng + 1) * TG], ps[:, :])
                            bank_free[bi] = ("sc", ctx.val("sc"))
                            bi ^= 1
                barrier(ctx)

        esa = ExitStack()
        qps_r = esa.enter_context(nc.sbuf_tensor("qpsC", [1, T], BF16))
        kp_r = esa.enter_context(nc.sbuf_tensor("kpC", [1, T], BF16))
        negc_r = esa.enter_context(nc.sbuf_tensor("negcC", [1, T], BF16))
        sqt = esa.enter_context(nc.sbuf_tensor("sqtC", [128, T], BF16))
        nq = esa.enter_context(nc.sbuf_tensor("nqC", [1, T], F32))
        nk = esa.enter_context(nc.sbuf_tensor("nkC", [1, T], F32))
        qpf = esa.enter_context(nc.sbuf_tensor("qpfC", [1, T], F32))
        kpf = esa.enter_context(nc.sbuf_tensor("kpfC", [1, T], F32))
        sc2 = esa.enter_context(nc.sbuf_tensor("scal2C", [1, 4], F32))
        crow = esa.enter_context(nc.sbuf_tensor("crowC", [1, T], F32))
        wch = [esa.enter_context(nc.sbuf_tensor(f"wchC{i}", [128, TG], BF16)) for i in range(2)]
        rsum = esa.enter_context(nc.sbuf_tensor("rsumC", [1, TG], F32))
        repn = esa.enter_context(nc.sbuf_tensor("repnC", [128, TG], F32))
        aot = [esa.enter_context(nc.sbuf_tensor(f"aotC{i}", [128, TG], BF16)) for i in range(2)]

        with nc.psum_tensor("wpsC0", [128, TG], F32) as w0, \
             nc.psum_tensor("wpsC1", [128, TG], F32) as w1, \
             nc.psum_tensor("sumpsC", [1, TG], F32) as sps, \
             nc.psum_tensor("avpsC", [128, TG], F32) as avps, \
             nc.psum_tensor("repnpsC", [128, TG], F32) as rps3, \
             nc.psum_tensor("rowpsC", [1, TG], F32) as rps, \
             nc.psum_tensor("rowps2C", [1, TG], F32) as rps2:
            wps = [w0, w1]
            wp_gate = [None, None]
            av_gate = None
            norm_gate = None
            row_gate = None
            aot_gate = [None, None]
            for h in range(H):
                sc_gate = ctx.val("sc")
                dv_gate = ctx.val("dv")
                pe_gate = ctx.val("pe")

                @ctx.on("scalar")
                def _(sx, h=h, row_gate=row_gate):
                    if row_gate is not None:
                        sx.wait_ge(row_gate[0], row_gate[1])
                    sx.activation(out=sqt[:, :], in_=QT[h][:, :], func=AF.Square)
                sc_sq = ctx.val("sc")

                prev_cp = ctx.val("dv")
                for g in range(NTG):
                    @ctx.on("tensor")
                    def _(t, h=h, g=g, prev_cp=prev_cp):
                        if g == 0:
                            t.wait_ge("sc", sc_gate)
                            t.wait_ge("dv", dv_gate)
                        else:
                            t.wait_ge("dv", prev_cp)
                        t.matmul(rps[0:1, :], dirs[0:3, h:h + 1],
                                 QT[h][0:3, g * TG:(g + 1) * TG], start=True, stop=True)
                        t.matmul(rps2[0:1, :], dirs[0:3, h:h + 1],
                                 KT[h][0:3, g * TG:(g + 1) * TG], start=True, stop=True)
                    pv = ctx.val("pe")

                    @ctx.on("vector")
                    def _(v, h=h, g=g):
                        v.wait_ge("pe", pv)
                        v.tensor_copy(qpf[0:1, g * TG:(g + 1) * TG], rps[0:1, :])
                        v.tensor_copy(kpf[0:1, g * TG:(g + 1) * TG], rps2[0:1, :])
                    prev_cp = ctx.val("dv")
                qk_cp = prev_cp
                dv_rows = ctx.val("dv")

                @ctx.on("vector")
                def _(v, h=h):
                    v.wait_ge("dv", dv_rows)
                    v.tensor_copy(kp_r[0:1, :], kpf[0:1, :])
                    v.tensor_scalar(sc2[0:1, 2:3], hsrow[0:1, h:h + 1],
                                    float(np.sqrt(128.0)), None, ALU.mult)
                    v.tensor_scalar(qps_r[0:1, :], qpf[0:1, :], sc2[0:1, 2:3], None, ALU.mult)
                dv1 = ctx.val("dv")

                prev_cp = ctx.val("sc")
                for g in range(NTG):
                    @ctx.on("tensor")
                    def _(t, h=h, g=g, prev_cp=prev_cp):
                        if g == 0:
                            t.wait_ge("sc", sc_sq)
                            t.wait_ge("dv", qk_cp)
                        else:
                            t.wait_ge("sc", prev_cp)
                        t.matmul(rps[0:1, :], onescol_bf[:, 0:1],
                                 sqt[:, g * TG:(g + 1) * TG], start=True, stop=True)
                    pv = ctx.val("pe")

                    @ctx.on("scalar")
                    def _(sx, h=h, g=g):
                        sx.wait_ge("pe", pv)
                        sx.activation(out=nq[0:1, g * TG:(g + 1) * TG], in_=rps[0:1, :],
                                      func=AF.Sqrt)
                    prev_cp = ctx.val("sc")
                sc_nq = ctx.val("sc")

                @ctx.on("scalar")
                def _(sx, h=h):
                    sx.activation(out=sqt[:, :], in_=KT[h][:, :], func=AF.Square)
                sc_sqk = ctx.val("sc")

                prev_cp = ctx.val("sc")
                for g in range(NTG):
                    @ctx.on("tensor")
                    def _(t, h=h, g=g, prev_cp=prev_cp):
                        if g == 0:
                            t.wait_ge("sc", sc_sqk)
                            t.wait_ge("dv", qk_cp)
                        else:
                            t.wait_ge("sc", prev_cp)
                        t.matmul(rps2[0:1, :], onescol_bf[:, 0:1],
                                 sqt[:, g * TG:(g + 1) * TG], start=True, stop=True)
                    pv = ctx.val("pe")

                    @ctx.on("scalar")
                    def _(sx, h=h, g=g):
                        sx.wait_ge("pe", pv)
                        sx.activation(out=nk[0:1, g * TG:(g + 1) * TG], in_=rps2[0:1, :],
                                      func=AF.Sqrt)
                    prev_cp = ctx.val("sc")
                row_gate = ("pe", ctx.val("pe"))
                sc_nk = ctx.val("sc")

                @ctx.on("vector")
                def _(v, h=h):
                    v.wait_ge("sc", sc_nk)
                    v.reduce_max(sc2[0:1, 0:1], nk[0:1, :], axis=AX)
                    v.reduce_max(sc2[0:1, 1:2], kpf[0:1, :], axis=AX,
                                 apply_absolute_value=True)
                    v.tensor_scalar(crow[0:1, :], nq[0:1, :], sc2[0:1, 0:1], None, ALU.mult)
                    v.tensor_scalar(qpf[0:1, :], qps_r[0:1, :], sc2[0:1, 1:2], None, ALU.mult)
                    v.tensor_scalar(kpf[0:1, :], qpf[0:1, :], -1.0, None, ALU.mult)
                    v.tensor_tensor(qpf[0:1, :], qpf[0:1, :], kpf[0:1, :], ALU.max)
                    v.tensor_tensor(crow[0:1, :], crow[0:1, :], qpf[0:1, :], ALU.add)
                    v.tensor_scalar(crow[0:1, :], crow[0:1, :], 1.05, 0.5, ALU.mult, ALU.add)
                    v.tensor_scalar(negc_r[0:1, :], crow[0:1, :], -1.0, None, ALU.mult)
                dv_c = ctx.val("dv")

                for qg in range(NTG):
                    nkc = 4 * (qg + 1)
                    for kc in range(nkc):
                        ps = wps[kc % 2]
                        gate = wp_gate[kc % 2]

                        @ctx.on("tensor")
                        def _(t, h=h, qg=qg, kc=kc, ps=ps, gate=gate):
                            if kc == 0 and qg == 0:
                                t.wait_ge("dv", dv_c)
                            if gate is not None:
                                t.wait_ge(gate[0], gate[1])
                            t.matmul(ps[:, :], KT[h][:, kc * 128:(kc + 1) * 128],
                                     QT[h][:, qg * TG:(qg + 1) * TG], start=True, stop=False)
                            t.matmul(ps[:, :], kp_r[0:1, kc * 128:(kc + 1) * 128],
                                     qps_r[0:1, qg * TG:(qg + 1) * TG], start=False, stop=False)
                            t.matmul(ps[:, :], ones_bf[0:1, 0:128],
                                     negc_r[0:1, qg * TG:(qg + 1) * TG], start=False, stop=True)
                        pv = ctx.val("pe")

                        @ctx.on("scalar")
                        def _(sx, kc=kc, ps=ps):
                            sx.wait_ge("pe", pv)
                            sx.activation(out=wch[kc % 2][:, :], in_=ps[:, :], func=AF.Exp)
                        scv = ctx.val("sc")
                        wgate = ("sc", scv)

                        j = kc - 4 * qg
                        if j >= 0:
                            @ctx.on("vector")
                            def _(v, kc=kc, j=j):
                                v.wait_ge("sc", scv)
                                v.tensor_tensor(wch[kc % 2][:, :], wch[kc % 2][:, :],
                                                cmask[j][:, :], ALU.mult)
                            wgate = ("dv", ctx.val("dv"))

                        @ctx.on("tensor")
                        def _(t, h=h, qg=qg, kc=kc, wgate=wgate, av_gate=av_gate, nkc=nkc):
                            t.wait_ge(wgate[0], wgate[1])
                            if kc == 0 and av_gate is not None:
                                t.wait_ge(av_gate[0], av_gate[1])
                            t.matmul(sps[0:1, :], onescol_bf[:, 0:1], wch[kc % 2][:, :],
                                     start=(kc == 0), stop=(kc == nkc - 1))
                            t.matmul(avps[:, :], V[kc][:, h * 128:(h + 1) * 128],
                                     wch[kc % 2][:, :], start=(kc == 0), stop=(kc == nkc - 1))
                        wp_gate[kc % 2] = ("pe", ctx.val("pe"))
                    pvs = ctx.val("pe")

                    @ctx.on("vector")
                    def _(v, norm_gate=norm_gate):
                        v.wait_ge("pe", pvs)
                        if norm_gate is not None:
                            v.wait_ge(norm_gate[0], norm_gate[1])
                        v.reciprocal(rsum[0:1, :], sps[0:1, :])
                    dvr = ctx.val("dv")

                    @ctx.on("tensor")
                    def _(t):
                        t.wait_ge("dv", dvr)
                        t.matmul(rps3[:, :], ones_row[0:1, 0:128], rsum[0:1, :],
                                 start=True, stop=True)
                    pvr = ctx.val("pe")

                    @ctx.on("scalar")
                    def _(sx):
                        sx.wait_ge("pe", pvr)
                        sx.copy(repn[:, :], rps3[:, :])
                    scr = ctx.val("sc")

                    bi = (h * NTG + qg) % 2

                    @ctx.on("vector")
                    def _(v, bi=bi, gate=aot_gate[bi]):
                        v.wait_ge("sc", scr)
                        if gate is not None:
                            v.wait_ge(gate[0], gate[1])
                        v.tensor_tensor(aot[bi][:, :], avps[:, :], repn[:, :], ALU.mult)
                    av_gate = ("dv", ctx.val("dv"))
                    norm_gate = ("pe", pvr)

                    @ctx.on("gpsimd")
                    def _(g2, h=h, qg=qg, bi=bi, dvv=av_gate[1]):
                        g2.wait_ge("dv", dvv)
                        g2.dma("sta2" if bi == 0 else "stb2",
                               out=ao_dram[h * 128:(h + 1) * 128, qg * TG:(qg + 1) * TG],
                               in_=aot[bi][:, :])
                    aot_gate[bi] = ("sta2" if bi == 0 else "stb2",
                                    ctx.val("sta2" if bi == 0 else "stb2"))
        barrier(ctx, extra=("sta2", "stb2"))
        esa.close()

    ao = [ctx.sb(f"aoC{k}", [128, T], BF16) for k in range(NCH)]
    xres = [ctx.sb(f"xresC{i}", [128, TG], F32) for i in range(2)]
    otmp = ctx.sb("otmpC", [128, TG], F32)
    load_weight_chunks(ctx, ao_dram, ao, sem="wld")
    store_gates = []
    with ExitStack() as es:
        wo = [es.enter_context(nc.sbuf_tensor(f"woC{k}", [128, D], BF16))
              for k in range(NCH)]
        load_weight_chunks(ctx, Wout_d, wo)
        wo_done = ctx.val("wld")

        @ctx.on("tensor")
        def _(t):
            t.wait_ge("wld", wo_done)

        def evac_o(m, g, ps, pv):
            idx = m * NTG + g

            @ctx.on("vector")
            def _(v):
                v.wait_ge("pe", pv)
                v.tensor_copy(otmp[:, :], ps[:, :])
            dvv = ctx.val("dv")
            dvv2 = residual_evac(ctx, x_in, x_out, m, g, idx, otmp[:, :],
                                 one_col[:, 0:1], xres, store_gates, dvv)
            return ("dv", dvv2)
        proj_1024(ctx, wo, ao, evac_o, psname="oC")
    wait_stores(ctx)


def build_l2attn():
    nc = bass.Bass("TRN2", target_bir_lowering=False, debug=False)
    x_in = nc.dram_tensor("x_in", [D, T], F32, kind="ExternalInput").ap()
    Wqkv = nc.dram_tensor("Wqkv", [D, 3 * D], F32, kind="ExternalInput").ap()
    Wout = nc.dram_tensor("Wout", [D, D], F32, kind="ExternalInput").ap()
    hs = nc.dram_tensor("hs", [1, 8], F32, kind="ExternalInput").ap()
    dirs = nc.dram_tensor("dirs", [3, 8], F32, kind="ExternalInput").ap()
    cmask = nc.dram_tensor("cmask", [4, 128, TG], F32, kind="ExternalInput").ap()
    x_out = nc.dram_tensor("x_out", [D, T], F32, kind="ExternalOutput").ap()
    ao_dram = nc.dram_tensor("ao_scr", [D, T], BF16).ap()
    with ExitStack() as stack:
        block = stack.enter_context(nc.Block())
        ctx = Ctx(nc, block, stack)
        emit_l2attn(ctx, x_in, x_out, Wqkv, Wout, hs, dirs, cmask, ao_dram)
    return nc


def l2attn_maps(Wqkv, Wout, head_scales, head_dirs):
    j = np.arange(4)[:, None, None] * 128
    kp = np.arange(128)[None, :, None]
    qf = np.arange(TG)[None, None, :]
    cmask = ((j + kp) <= qf).astype(np.float32)
    return {
        "Wqkv": np.ascontiguousarray(np.asarray(Wqkv)),
        "Wout": np.ascontiguousarray(np.asarray(Wout)),
        "hs": np.asarray(head_scales).reshape(1, 8).copy(),
        "dirs": np.ascontiguousarray(np.asarray(head_dirs).T),
        "cmask": cmask,
    }


def run_l2attn(xT, Wqkv, Wout, head_scales, head_dirs):
    nc = build_l2attn()
    wmap = l2attn_maps(Wqkv, Wout, head_scales, head_dirs)
    in_maps = [dict(wmap, x_in=np.ascontiguousarray(xT[b])) for b in range(8)]
    res = run_bass_kernel_spmd(nc, in_maps, core_ids=list(range(8)))
    return np.stack([r["x_out"] for r in res.results])



def emit_l0attn(ctx, x_in, x_out, W1_d, WoT_d, bo_d, ident_d, ao_dram, mn_dram):
    nc = ctx.nc
    H = 16
    SC = 0.125
    ones_row = ctx.sb("onesrowS", [1, 128], F32)
    ones_bf = ctx.sb("onesbfS", [1, TG], BF16)
    onescol_bf = ctx.sb("onescolS", [128, 1], BF16)
    one_col = ctx.sb("onecolS", [128, 1], F32)
    nqsc = ctx.sb("nqscS", [128, 1], F32)
    two_col = ctx.sb("twocolS", [128, 1], F32)
    identf = ctx.sb("identfS", [128, 128], F32)
    borow = ctx.sb("borowS", [1, D], BF16)

    QN = [ctx.sb(f"qnS{k}", [128, T], BF16) for k in range(NCH)]
    KT = [ctx.sb(f"ktS{k}", [128, T], BF16) for k in range(NCH)]
    V = [ctx.sb(f"vtlS{k}", [128, 1040], BF16) for k in range(16)]

    @ctx.on("vector")
    def _(v):
        v.memset(ones_row[:, :], 1.0)
        v.memset(ones_bf[:, :], 1.0)
        v.memset(onescol_bf[:, :], 1.0)
        v.memset(one_col[:, :], 1.0)
        v.memset(nqsc[:, :], -SC)
        v.memset(two_col[:, :], 2.0)

    @ctx.on("gpsimd")
    def _(g):
        g.dma("wld", out=identf[:, :], in_=ident_d)
        g.dma("wld", out=borow[:, :], in_=bo_d)

    with ExitStack() as es:
        rstd = es.enter_context(nc.sbuf_tensor("rstdS", [1, T], F32))
        negmr = es.enter_context(nc.sbuf_tensor("negmrS", [1, T], F32))
        rep_rstd = es.enter_context(nc.sbuf_tensor("rep_rstdS", [128, T], BF16))
        rep_negmr = es.enter_context(nc.sbuf_tensor("rep_negmrS", [128, T], BF16))
        xbuf = [es.enter_context(nc.sbuf_tensor(f"xbS{i}", [128, T], F32)) for i in range(2)]
        y = [es.enter_context(nc.sbuf_tensor(f"yS{k}", [128, T], BF16)) for k in range(NCH)]

        ln_rows(ctx, x_in, rstd, negmr, tag="S", xbuf=xbuf)
        replicate_rows(ctx, [(rstd, rep_rstd), (negmr, rep_negmr)], ones_row, tag="S")
        build_y(ctx, x_in, rep_rstd, rep_negmr, y, xbuf)

        with ExitStack() as es2:
            wt = [es2.enter_context(nc.sbuf_tensor(f"wSqk{k}", [128, 2 * D], BF16))
                  for k in range(NCH)]
            load_weight_chunks(ctx, W1_d[:, 0:2 * D], wt)
            wv_done = ctx.val("wld")

            @ctx.on("tensor")
            def _(t):
                t.wait_ge("wld", wv_done)

            def evac_qk(m, g, ps, pv):
                @ctx.on("scalar")
                def _(sx):
                    sx.wait_ge("pe", pv)
                    if m < NCH:
                        sx.activation(out=QN[m][:, g * TG:(g + 1) * TG], in_=ps[:, :],
                                      func=AF.Identity, scale=nqsc[:, 0:1])
                    else:
                        sx.copy(KT[m - NCH][:, g * TG:(g + 1) * TG], ps[:, :])
                return ("sc", ctx.val("sc"))
            proj_1024(ctx, wt, y, evac_qk, psname="qkS", mchunks=16)

        with ExitStack() as es2:
            wt = [es2.enter_context(nc.sbuf_tensor(f"wSv{k}", [128, D], BF16))
                  for k in range(NCH)]
            load_weight_chunks(ctx, W1_d[:, 2 * D:3 * D], wt)
            wv_done = ctx.val("wld")

            @ctx.on("vector")
            def _(v):
                v.wait_ge("wld", wv_done)
                for i in range(16):
                    v.memset(V[i][:, :].rearrange("p (h c) -> p h c", c=65)[:, :, 64:65], 1.0)
            dv65 = ctx.val("dv")

            @ctx.on("tensor")
            def _(t):
                t.wait_ge("wld", wv_done)

            with nc.psum_tensor("vpsS0", [128, TG], F32) as p0, \
                 nc.psum_tensor("vpsS1", [128, TG], F32) as p1:
                banks = [p0, p1]
                bank_free = [None, None]
                bi = 0
                for tt in range(16):
                    for ng in range(2):
                        ps = banks[bi]
                        gate = bank_free[bi]

                        @ctx.on("tensor")
                        def _(t, tt=tt, ng=ng, ps=ps, gate=gate):
                            if gate is not None:
                                t.wait_ge(gate[0], gate[1])
                            for k in range(NCH):
                                t.matmul(ps[:, :], y[k][:, tt * 128:(tt + 1) * 128],
                                         wt[k][:, ng * TG:(ng + 1) * TG],
                                         start=(k == 0), stop=(k == NCH - 1))
                        pv = ctx.val("pe")

                        @ctx.on("scalar")
                        def _(sx, tt=tt, ng=ng, ps=ps):
                            sx.wait_ge("pe", pv)
                            sx.wait_ge("dv", dv65)
                            sx.copy(V[tt][:, :].rearrange("p (h c) -> p h c", c=65)
                                    [:, 8 * ng:8 * ng + 8, 0:64],
                                    ps[:, :].rearrange("p (h c) -> p h c", c=64))
                        bank_free[bi] = ("sc", ctx.val("sc"))
                        bi ^= 1
            barrier(ctx)

    HQ = 1024

    esa = ExitStack()
    mncol = esa.enter_context(nc.sbuf_tensor("mncolS", [128, 16], F32))
    mtmp = esa.enter_context(nc.sbuf_tensor("mtmpS", [128, 2], F32))
    mnsb = esa.enter_context(nc.sbuf_tensor("mnsbS", [16, 128], F32))
    mnrow = esa.enter_context(nc.sbuf_tensor("mnrowS", [1, T], F32))
    mp1 = esa.enter_context(nc.sbuf_tensor("mp1S", [1, T], F32))
    bt = [esa.enter_context(nc.sbuf_tensor(f"btS{i}", [128, HQ], BF16)) for i in range(4)]
    b2t = [esa.enter_context(nc.sbuf_tensor(f"b2tS{i}", [128, HQ], BF16)) for i in range(4)]
    gt = [esa.enter_context(nc.sbuf_tensor(f"gtS{i}", [128, HQ], BF16)) for i in range(4)]
    lt = [esa.enter_context(nc.sbuf_tensor(f"ltS{i}", [128, HQ], F32)) for i in range(4)]
    wti = [esa.enter_context(nc.sbuf_tensor(f"wtiS{i}", [128, HQ], BF16)) for i in range(4)]
    negone = esa.enter_context(nc.sbuf_tensor("negoneS", [128, 1], F32))
    nhalf = esa.enter_context(nc.sbuf_tensor("nhalfS", [1, 2], F32))

    @ctx.on("vector")
    def _(v):
        v.memset(negone[:, :], -1.0)
        v.memset(nhalf[0:1, :], -1.0)
    rsum = esa.enter_context(nc.sbuf_tensor("rsumS", [1, HQ], F32))
    repn = esa.enter_context(nc.sbuf_tensor("repnS", [64, HQ], F32))
    aot = [esa.enter_context(nc.sbuf_tensor(f"aotS{i}", [64, T], BF16)) for i in range(2)]
    aot_gate = [None, None]

    tp_gate = None
    half_gates = []
    for h in range(H):
        hc, ho = h // 2, (h % 2) * 64
        with nc.psum_tensor(f"apsS{h}_0", [128, T], F32) as a0, \
             nc.psum_tensor(f"apsS{h}_1", [128, T], F32) as a1:
            apss = [a0, a1]
            red_gate = [None, None]
            for qt in range(16):
                aps = apss[qt % 2]

                @ctx.on("tensor")
                def _(t, h=h, qt=qt, aps=aps, gate=red_gate[qt % 2],
                      hg=list(half_gates)):
                    if qt < 2:
                        for sn, sv in hg:
                            t.wait_ge(sn, sv)
                    if gate is not None:
                        t.wait_ge(gate[0], gate[1])
                    for gk in range(NTG):
                        t.matmul(aps[:, gk * TG:(gk + 1) * TG],
                                 QN[hc][ho:ho + 64, qt * 128:(qt + 1) * 128],
                                 KT[hc][ho:ho + 64, gk * TG:(gk + 1) * TG],
                                 start=True, stop=True, tile_position=(ho, 0))
                pv = ctx.val("pe")

                @ctx.on("vector")
                def _(v, h=h, qt=qt, aps=aps):
                    v.wait_ge("pe", pv)
                    v.tensor_reduce(mncol[:, qt:qt + 1], aps[:, :], axis=AX, op=ALU.min)
                red_gate[qt % 2] = ("dv", ctx.val("dv"))
            dvmn = ctx.val("dv")

        with nc.psum_tensor(f"tpsS{h}", [16, 128], F32) as tps:
            @ctx.on("tensor")
            def _(t, h=h, tp_gate=tp_gate):
                t.wait_ge("dv", dvmn)
                if tp_gate is not None:
                    t.wait_ge(tp_gate[0], tp_gate[1])
                t.transpose(tps[:, :], mncol[:, 0:16], identf[:, :])
            pvt = ctx.val("pe")

            @ctx.on("scalar")
            def _(sx, h=h):
                sx.wait_ge("pe", pvt)
                sx.copy(mnsb[0:16, :], tps[0:16, :])
            sct = ctx.val("sc")

            @ctx.on("gpsimd")
            def _(g2, h=h):
                g2.wait_ge("sc", sct)
                g2.dma("mnd", out=mn_dram[:, :], in_=mnsb[0:16, :])
                g2.wait_ge("mnd", ctx.cnt["mnd"])
                g2.dma("mnd", out=mnrow[0:1, :],
                       in_=mn_dram.rearrange("a b -> (a b)")[None, :])
            mnd = ctx.val("mnd")

            @ctx.on("vector")
            def _(v, h=h):
                v.wait_ge("mnd", mnd)
                v.tensor_scalar(mp1[0:1, :], mnrow[0:1, :], -1.0, 1.0, ALU.mult, ALU.add)
            dvmp = ctx.val("dv")
            tp_gate = ("dv", dvmp)

        for qh in range(2):
            q0 = qh * HQ
            with nc.psum_tensor(f"bpsS{h}_{qh}0", [128, HQ], F32) as bp0, \
                 nc.psum_tensor(f"bpsS{h}_{qh}1", [128, HQ], F32) as bp1, \
                 nc.psum_tensor(f"avpsS{h}_{qh}", [65, HQ], F32) as avps, \
                 nc.psum_tensor(f"rpsS{h}_{qh}", [64, HQ], F32) as rps:
                bpss = [bp0, bp1]
                mm_done = [None] * 16
                cp_done = [None] * 16
                pl_done = [None] * 16
                le_done = [None] * 16
                av_done = [None] * 16
                for st_ in range(20):
                    kc = st_
                    if kc < 16:
                        bps = bpss[kc % 2]

                        @ctx.on("tensor")
                        def _(t, h=h, kc=kc, q0=q0, bps=bps,
                              gate=cp_done[kc - 2] if kc >= 2 else None,
                              hg=list(half_gates)):
                            if kc <= 1:
                                t.wait_ge("dv", dvmp)
                                for sn, sv in hg:
                                    t.wait_ge(sn, sv)
                            if gate is not None:
                                t.wait_ge(gate[0], gate[1])
                            for gq in range(2):
                                t.matmul(bps[:, gq * TG:(gq + 1) * TG],
                                         KT[hc][ho:ho + 64, kc * 128:(kc + 1) * 128],
                                         QN[hc][ho:ho + 64, q0 + gq * TG:q0 + (gq + 1) * TG],
                                         start=True, stop=False, tile_position=(ho, 0))
                            for gq in range(2):
                                t.matmul(bps[:, gq * TG:(gq + 1) * TG],
                                         ones_row[0:1, 0:128],
                                         mp1[0:1, q0 + gq * TG:q0 + (gq + 1) * TG],
                                         start=False, stop=True)
                        mm_done[kc] = ("pe", ctx.val("pe"))
                    kc = st_ - 1
                    if 0 <= kc < 16:
                        @ctx.on("scalar")
                        def _(sx, kc=kc, gate=mm_done[kc],
                              wgate=pl_done[kc - 4] if kc >= 4 else None):
                            sx.wait_ge(gate[0], gate[1])
                            if wgate is not None:
                                sx.wait_ge(wgate[0], wgate[1])
                            sx.copy(bt[kc % 4][:, :], bpss[kc % 2][:, :])
                        cp_done[kc] = ("sc", ctx.val("sc"))
                    kc = st_ - 2
                    if 0 <= kc < 16:
                        @ctx.on("vector")
                        def _(v, kc=kc, gate=cp_done[kc],
                              wgate=le_done[kc - 4] if kc >= 4 else None):
                            v.wait_ge(gate[0], gate[1])
                            if wgate is not None:
                                v.wait_ge(wgate[0], wgate[1])
                            v.tensor_tensor(b2t[kc % 4][:, :], bt[kc % 4][:, :],
                                            bt[kc % 4][:, :], ALU.mult)
                            v.scalar_tensor_tensor(gt[kc % 4][:, :], b2t[kc % 4][:, :], 3.0,
                                                   bt[kc % 4][:, :], ALU.add, ALU.mult)
                        pl_done[kc] = ("dv", ctx.val("dv"))
                    kc = st_ - 3
                    if 0 <= kc < 16:
                        @ctx.on("scalar")
                        def _(sx, kc=kc, gate=pl_done[kc],
                              wgate=av_done[kc - 4] if kc >= 4 else None):
                            sx.wait_ge(gate[0], gate[1])
                            if wgate is not None:
                                sx.wait_ge(wgate[0], wgate[1])
                            sx.activation(out=lt[kc % 4][:, :], in_=gt[kc % 4][:, :],
                                          func=AF.Ln, bias=two_col[:, 0:1])
                            sx.activation(out=wti[kc % 4][:, :], in_=lt[kc % 4][:, :],
                                          func=AF.Exp, scale=negone[:, 0:1])
                        le_done[kc] = ("sc", ctx.val("sc"))
                    kc = st_ - 4
                    if 0 <= kc < 16:
                        @ctx.on("tensor")
                        def _(t, h=h, kc=kc, gate=le_done[kc]):
                            t.wait_ge(gate[0], gate[1])
                            for gq in range(2):
                                t.matmul(avps[0:65, gq * TG:(gq + 1) * TG],
                                         V[kc][:, h * 65:(h + 1) * 65],
                                         wti[kc % 4][:, gq * TG:(gq + 1) * TG],
                                         start=(kc == 0), stop=(kc == 15))
                        av_done[kc] = ("pe", ctx.val("pe"))
                pvav = ctx.val("pe")

                @ctx.on("scalar")
                def _(sx, h=h, qh=qh):
                    sx.wait_ge("pe", pvav)
                    sx.activation(out=rsum[0:1, :], in_=avps[64:65, :], func=AF.Ln)
                    sx.activation(out=rsum[0:1, :], in_=rsum[0:1, :], func=AF.Exp,
                                  scale=nhalf[0:1, 0:1])
                dvr = ctx.val("sc")

                @ctx.on("tensor")
                def _(t, h=h, qh=qh):
                    t.wait_ge("sc", dvr)
                    for gq in range(2):
                        t.matmul(rps[:, gq * TG:(gq + 1) * TG], ones_row[0:1, 0:64],
                                 rsum[0:1, gq * TG:(gq + 1) * TG], start=True, stop=True)
                pvr = ctx.val("pe")

                @ctx.on("scalar")
                def _(sx, h=h, qh=qh):
                    sx.wait_ge("pe", pvr)
                    sx.copy(repn[0:64, :], rps[:, :])
                scr = ctx.val("sc")

                bi = h % 2

                @ctx.on("vector")
                def _(v, h=h, qh=qh, q0=q0, bi=bi, gate=aot_gate[bi]):
                    v.wait_ge("sc", scr)
                    if qh == 0 and gate is not None:
                        v.wait_ge(gate[0], gate[1])
                    v.tensor_tensor(aot[bi][0:64, q0:q0 + HQ], avps[0:64, :],
                                    repn[0:64, :], ALU.mult)
                half_gates = [("dv", ctx.val("dv")), ("sc", scr)]
        dvao = ctx.val("dv")
        bi = h % 2

        @ctx.on("gpsimd")
        def _(g2, h=h, bi=bi):
            g2.wait_ge("dv", dvao)
            g2.dma("sta2" if bi == 0 else "stb2",
                   out=ao_dram[h * 64:(h + 1) * 64, :], in_=aot[bi][0:64, :])
        aot_gate[bi] = ("sta2" if bi == 0 else "stb2",
                        ctx.val("sta2" if bi == 0 else "stb2"))
    esa.close()
    barrier(ctx, extra=("sta2", "stb2"))

    ao = [ctx.sb(f"aoS{k}", [128, T], BF16) for k in range(NCH)]
    load_weight_chunks(ctx, ao_dram, ao, sem="wld")

    xres = [ctx.sb(f"xresS{i}", [128, TG], F32) for i in range(2)]
    otmp = ctx.sb("otmpS", [128, TG], F32)
    store_gates = []
    with ExitStack() as es:
        wo = [es.enter_context(nc.sbuf_tensor(f"woS{k}", [128, D], BF16))
              for k in range(NCH)]
        load_weight_chunks(ctx, WoT_d, wo)
        wo_done = ctx.val("wld")

        @ctx.on("tensor")
        def _(t):
            t.wait_ge("wld", wo_done)

        with nc.psum_tensor("oSps0", [128, TG], F32) as p0, \
             nc.psum_tensor("oSps1", [128, TG], F32) as p1:
            banks = [p0, p1]
            bank_free = [None, None]
            bi = 0
            for m in range(NCH):
                for g in range(NTG):
                    ps = banks[bi]
                    gate = bank_free[bi]

                    @ctx.on("tensor")
                    def _(t, m=m, g=g, ps=ps, gate=gate):
                        if gate is not None:
                            t.wait_ge(gate[0], gate[1])
                        for k in range(NCH):
                            t.matmul(ps[:, :], wo[k][:, m * 128:(m + 1) * 128],
                                     ao[k][:, g * TG:(g + 1) * TG],
                                     start=(k == 0), stop=False)
                        t.matmul(ps[:, :], borow[0:1, m * 128:(m + 1) * 128],
                                 ones_bf[0:1, :], start=False, stop=True)
                    pv = ctx.val("pe")
                    idx = m * NTG + g

                    @ctx.on("vector")
                    def _(v, ps=ps):
                        v.wait_ge("pe", pv)
                        v.tensor_copy(otmp[:, :], ps[:, :])
                    dvv = ctx.val("dv")
                    dvv2 = residual_evac(ctx, x_in, x_out, m, g, idx, otmp[:, :],
                                         one_col[:, 0:1], xres, store_gates, dvv)
                    bank_free[bi] = ("dv", dvv2)
                    bi ^= 1
    wait_stores(ctx)


def build_l0attn():
    nc = bass.Bass("TRN2", target_bir_lowering=False, debug=False)
    x_in = nc.dram_tensor("x_in", [D, T], F32, kind="ExternalInput").ap()
    W1 = nc.dram_tensor("W1", [D, 3 * D], F32, kind="ExternalInput").ap()
    WoT = nc.dram_tensor("WoT", [D, D], F32, kind="ExternalInput").ap()
    bo = nc.dram_tensor("bo", [1, D], F32, kind="ExternalInput").ap()
    ident = nc.dram_tensor("ident", [128, 128], F32, kind="ExternalInput").ap()
    x_out = nc.dram_tensor("x_out", [D, T], F32, kind="ExternalOutput").ap()
    ao_dram = nc.dram_tensor("aoS_scr", [D, T], BF16).ap()
    mn_dram = nc.dram_tensor("mnS_scr", [16, 128], F32).ap()
    with ExitStack() as stack:
        block = stack.enter_context(nc.Block())
        ctx = Ctx(nc, block, stack)
        emit_l0attn(ctx, x_in, x_out, W1, WoT, bo, ident, ao_dram, mn_dram)
    return nc


def l0attn_maps(w_qkv, w_out, b_out):
    return {
        "W1": np.ascontiguousarray(np.asarray(w_qkv).T),
        "WoT": np.ascontiguousarray(np.asarray(w_out).T),
        "bo": np.asarray(b_out).reshape(1, D).copy(),
        "ident": np.eye(128, dtype=np.float32),
    }


def run_l0attn(xT, w_qkv, w_out, b_out):
    nc = build_l0attn()
    wmap = l0attn_maps(w_qkv, w_out, b_out)
    in_maps = [dict(wmap, x_in=np.ascontiguousarray(xT[b])) for b in range(8)]
    res = run_bass_kernel_spmd(nc, in_maps, core_ids=list(range(8)))
    return np.stack([r["x_out"] for r in res.results])



def emit_l3attn(ctx, x_in, x_out, Wqq_d, Wkk_d, Wv_d, Wout_d, tabs_d,
                masks_d, v_dram, ao_dram):
    nc = ctx.nc
    H = 16
    ones_row = ctx.sb("onesrowW", [1, 128], F32)
    one_col = ctx.sb("onecolW", [128, 1], F32)
    qsc = ctx.sb("qscW", [128, 1], F32)
    negc = ctx.sb("negcW", [128, 1], F32)
    tabs = [ctx.sb(f"tabW{i}", [128, T], BF16) for i in range(4)]
    masks = [ctx.sb(f"maskW{i}", [128, 128], BF16) for i in range(2)]

    QR = [ctx.sb(f"qrW{k}", [128, T], BF16) for k in range(NCH)]
    KH = [ctx.sb(f"khW{k}", [128, T], BF16) for k in range(NCH)]
    KL = [ctx.sb(f"klW{k}", [128, T], BF16) for k in range(NCH)]

    @ctx.on("vector")
    def _(v):
        v.memset(ones_row[:, :], 1.0)
        v.memset(one_col[:, :], 1.0)
        v.memset(qsc[:, :], 0.125)
        v.memset(negc[:, :], -16.0)

    @ctx.on("gpsimd")
    def _(g):
        for i in range(4):
            g.dma("wld", out=tabs[i][:, :], in_=tabs_d[i])
        for i in range(2):
            g.dma("wld", out=masks[i][:, :], in_=masks_d[i])

    with ExitStack() as es:
        rowp2 = es.enter_context(nc.sbuf_tensor("rowsW", [33, T], F32))
        rstd = rowp2[0:1, :]
        negmr = rowp2[32:33, :]
        rep_rstd = es.enter_context(nc.sbuf_tensor("rep_rstdW", [128, T], BF16))
        rep_negmr = es.enter_context(nc.sbuf_tensor("rep_negmrW", [128, T], BF16))
        xbuf = [es.enter_context(nc.sbuf_tensor(f"xbW{i}", [128, T], F32)) for i in range(2)]
        y = [es.enter_context(nc.sbuf_tensor(f"yW{k}", [128, T], BF16)) for k in range(NCH)]

        ln_rows(ctx, x_in, rstd, negmr, tag="W", xbuf=xbuf)
        replicate_rows(ctx, [(rstd, rep_rstd), (negmr, rep_negmr)], ones_row, tag="W")
        build_y(ctx, x_in, rep_rstd, rep_negmr, y, xbuf)

        with ExitStack() as es2:
            wt = [es2.enter_context(nc.sbuf_tensor(f"wWv{k}", [128, D], BF16))
                  for k in range(NCH)]
            v65 = [es2.enter_context(nc.sbuf_tensor(f"v65W{i}", [128, 1040], BF16))
                   for i in range(2)]
            load_weight_chunks(ctx, Wv_d, wt)
            wv_done = ctx.val("wld")

            @ctx.on("vector")
            def _(v):
                v.wait_ge("wld", wv_done)
                for i in range(2):
                    v.memset(v65[i][:, :].rearrange("p (h c) -> p h c", c=65)[:, :, 64:65], 1.0)
            dv65 = ctx.val("dv")

            @ctx.on("tensor")
            def _(t):
                t.wait_ge("wld", wv_done)

            with nc.psum_tensor("vpsW0", [128, TG], F32) as p0, \
                 nc.psum_tensor("vpsW1", [128, TG], F32) as p1:
                banks = [p0, p1]
                bank_free = [None, None]
                v65_gate = [None, None]
                bi = 0
                for tt in range(16):
                    for ng in range(2):
                        ps = banks[bi]
                        gate = bank_free[bi]

                        @ctx.on("tensor")
                        def _(t, tt=tt, ng=ng, ps=ps, gate=gate):
                            if gate is not None:
                                t.wait_ge(gate[0], gate[1])
                            for k in range(NCH):
                                t.matmul(ps[:, :], y[k][:, tt * 128:(tt + 1) * 128],
                                         wt[k][:, ng * TG:(ng + 1) * TG],
                                         start=(k == 0), stop=(k == NCH - 1))
                        pv = ctx.val("pe")
                        vb = tt % 2

                        @ctx.on("scalar")
                        def _(sx, tt=tt, ng=ng, ps=ps, vb=vb, g65=v65_gate[vb]):
                            sx.wait_ge("pe", pv)
                            sx.wait_ge("dv", dv65)
                            if ng == 0 and g65 is not None:
                                sx.wait_ge(g65[0], g65[1])
                            sx.copy(v65[vb][:, :].rearrange("p (h c) -> p h c", c=65)
                                    [:, 8 * ng:8 * ng + 8, 0:64],
                                    ps[:, :].rearrange("p (h c) -> p h c", c=64))
                        bank_free[bi] = ("sc", ctx.val("sc"))
                        bi ^= 1
                        if ng == 1:
                            scv = ctx.val("sc")

                            @ctx.on("gpsimd")
                            def _(g2, tt=tt, vb=vb):
                                g2.wait_ge("sc", scv)
                                g2.dma("sta2" if vb == 0 else "stb2",
                                       out=v_dram[tt * 128:(tt + 1) * 128, :],
                                       in_=v65[vb][:, :])
                            v65_gate[vb] = ("sta2" if vb == 0 else "stb2",
                                            ctx.val("sta2" if vb == 0 else "stb2"))
            barrier(ctx, extra=("sta2", "stb2"))

        for (Wd, outs) in ((Wqq_d, "q"), (Wkk_d, "k")):
          for half in range(2):
            with ExitStack() as es2:
                wt = [es2.enter_context(nc.sbuf_tensor(f"wW{outs}{half}_{k}", [128, D], BF16))
                      for k in range(NCH)]
                raw = es2.enter_context(nc.sbuf_tensor(f"rawW{outs}{half}", [128, T], BF16))
                sw = es2.enter_context(nc.sbuf_tensor(f"swW{outs}{half}", [128, T], BF16))
                tmp = es2.enter_context(nc.sbuf_tensor(f"tmpW{outs}{half}", [128, T], BF16))
                load_weight_chunks(ctx, Wd[:, half * D:(half + 1) * D], wt)
                w_done = ctx.val("wld")

                @ctx.on("tensor")
                def _(t):
                    t.wait_ge("wld", w_done)

                rope_done = [0]

                def evac_qk(m, g, ps, pv, outs=outs, half=half):
                    c, is_sw = half * 4 + m // 2, m % 2
                    dst = sw if is_sw else raw

                    @ctx.on("scalar")
                    def _(sx, ps=ps, dst=dst):
                        sx.wait_ge("pe", pv)
                        sx.wait_ge("dv", rope_done[0])
                        if outs == "q":
                            sx.activation(out=dst[:, g * TG:(g + 1) * TG], in_=ps[:, :],
                                          func=AF.Identity, scale=qsc[:, 0:1])
                        else:
                            sx.copy(dst[:, g * TG:(g + 1) * TG], ps[:, :])
                    gate = ("sc", ctx.val("sc"))
                    if is_sw and g == NTG - 1:
                        scv = ctx.val("sc")

                        @ctx.on("vector")
                        def _(v, c=c, outs=outs):
                            v.wait_ge("sc", scv)
                            if outs == "q":
                                v.tensor_tensor(tmp[:, :], sw[:, :], tabs[1][:, :], ALU.mult)
                                v.tensor_tensor(QR[c][:, :], raw[:, :], tabs[0][:, :], ALU.mult)
                                v.tensor_tensor(QR[c][:, :], QR[c][:, :], tmp[:, :], ALU.add)
                            else:
                                v.tensor_tensor(tmp[:, :], sw[:, :], tabs[1][:, :], ALU.mult)
                                v.tensor_tensor(KH[c][:, :], raw[:, :], tabs[0][:, :], ALU.mult)
                                v.tensor_tensor(KH[c][:, :], KH[c][:, :], tmp[:, :], ALU.add)
                                v.tensor_tensor(tmp[:, :], sw[:, :], tabs[3][:, :], ALU.mult)
                                v.tensor_tensor(KL[c][:, :], raw[:, :], tabs[2][:, :], ALU.mult)
                                v.tensor_tensor(KL[c][:, :], KL[c][:, :], tmp[:, :], ALU.add)
                        rope_done[0] = ctx.val("dv")
                        gate = ("dv", rope_done[0])
                    return gate
                proj_1024(ctx, wt, y, evac_qk, psname=f"qk{outs}{half}W", mchunks=8)

    V65 = [ctx.sb(f"V65W{k}", [128, 1040], BF16) for k in range(16)]
    load_weight_chunks(ctx, v_dram, V65, sem="wld")
    v_done = ctx.val("wld")

    esa = ExitStack()
    wch = [esa.enter_context(nc.sbuf_tensor(f"wchW{i}", [128, 128], BF16)) for i in range(2)]
    rsum = esa.enter_context(nc.sbuf_tensor("rsumW", [1, 128], F32))
    repn = esa.enter_context(nc.sbuf_tensor("repnW", [64, 128], F32))
    aot = [esa.enter_context(nc.sbuf_tensor(f"aotW{i}", [64, T], BF16)) for i in range(2)]
    aot_gate = [None, None]

    with nc.psum_tensor("wpsW0", [128, 128], F32) as w0, \
         nc.psum_tensor("wpsW1", [128, 128], F32) as w1, \
         nc.psum_tensor("avpsW", [65, 128], F32) as avps, \
         nc.psum_tensor("reppsW3", [64, 128], F32) as rps:
        wps = [w0, w1]
        wp_gate = [None, None]
        av_gate = None
        norm_gate = None

        @ctx.on("tensor")
        def _(t):
            t.wait_ge("wld", v_done)

        for h in range(H):
            hc, ho = h // 2, (h % 2) * 64
            bi_ao = h % 2
            for w in range(16):
                kctxs = ([w - 1] if w > 0 else []) + [w]
                nk = len(kctxs)
                for ki, kc in enumerate(kctxs):
                    own = (kc == w)
                    KT_ = KH if own else KL
                    ps = wps[ki % 2]
                    gate = wp_gate[ki % 2]

                    @ctx.on("tensor")
                    def _(t, h=h, w=w, kc=kc, ps=ps, gate=gate, own=own, KT_=KT_):
                        if gate is not None:
                            t.wait_ge(gate[0], gate[1])
                        t.matmul(ps[:, :], KT_[hc][ho:ho + 64, kc * 128:(kc + 1) * 128],
                                 QR[hc][ho:ho + 64, w * 128:(w + 1) * 128],
                                 start=True, stop=True, tile_position=(ho, 0))
                    pv = ctx.val("pe")

                    @ctx.on("scalar")
                    def _(sx, ki=ki, ps=ps):
                        sx.wait_ge("pe", pv)
                        sx.activation(out=wch[ki % 2][:, :], in_=ps[:, :], func=AF.Exp,
                                      bias=negc[:, 0:1])
                    scv = ctx.val("sc")

                    @ctx.on("vector")
                    def _(v, ki=ki, own=own):
                        v.wait_ge("sc", scv)
                        v.tensor_tensor(wch[ki % 2][:, :], wch[ki % 2][:, :],
                                        masks[0 if own else 1][:, :], ALU.mult)
                    dvm = ctx.val("dv")

                    @ctx.on("tensor")
                    def _(t, h=h, kc=kc, ki=ki, nk=nk, av_gate=av_gate):
                        t.wait_ge("dv", dvm)
                        if ki == 0 and av_gate is not None:
                            t.wait_ge(av_gate[0], av_gate[1])
                        t.matmul(avps[:, :], V65[kc][:, h * 65:(h + 1) * 65],
                                 wch[ki % 2][:, :], start=(ki == 0), stop=(ki == nk - 1))
                    wp_gate[ki % 2] = ("pe", ctx.val("pe"))
                pvav = ctx.val("pe")

                @ctx.on("vector")
                def _(v, norm_gate=norm_gate):
                    v.wait_ge("pe", pvav)
                    if norm_gate is not None:
                        v.wait_ge(norm_gate[0], norm_gate[1])
                    v.reciprocal(rsum[0:1, :], avps[64:65, :])
                dvr = ctx.val("dv")

                @ctx.on("tensor")
                def _(t):
                    t.wait_ge("dv", dvr)
                    t.matmul(rps[:, :], ones_row[0:1, 0:64], rsum[0:1, :],
                             start=True, stop=True)
                pvr = ctx.val("pe")

                @ctx.on("scalar")
                def _(sx):
                    sx.wait_ge("pe", pvr)
                    sx.copy(repn[0:64, :], rps[:, :])
                scr = ctx.val("sc")

                @ctx.on("vector")
                def _(v, h=h, w=w, bi=bi_ao, gate=aot_gate[bi_ao]):
                    v.wait_ge("sc", scr)
                    if gate is not None and w == 0:
                        v.wait_ge(gate[0], gate[1])
                    v.tensor_tensor(aot[bi][0:64, w * 128:(w + 1) * 128],
                                    avps[0:64, :], repn[0:64, :], ALU.mult)
                av_gate = ("dv", ctx.val("dv"))
                norm_gate = ("pe", pvr)
            dvao = ctx.val("dv")

            @ctx.on("gpsimd")
            def _(g2, h=h, bi=bi_ao):
                g2.wait_ge("dv", dvao)
                g2.dma("sta3" if bi == 0 else "stb3",
                       out=ao_dram[h * 64:(h + 1) * 64, :], in_=aot[bi][0:64, :])
            aot_gate[bi_ao] = ("sta3" if bi_ao == 0 else "stb3",
                               ctx.val("sta3" if bi_ao == 0 else "stb3"))
    esa.close()
    barrier(ctx, extra=("sta3", "stb3"))

    ao = [ctx.sb(f"aoW{k}", [128, T], BF16) for k in range(NCH)]
    xres = [ctx.sb(f"xresW{i}", [128, TG], F32) for i in range(2)]
    otmp = ctx.sb("otmpW", [128, TG], F32)
    load_weight_chunks(ctx, ao_dram, ao, sem="wld")
    store_gates = []
    with ExitStack() as es:
        wo = [es.enter_context(nc.sbuf_tensor(f"woW{k}", [128, D], BF16))
              for k in range(NCH)]
        load_weight_chunks(ctx, Wout_d, wo)
        wo_done = ctx.val("wld")

        @ctx.on("tensor")
        def _(t):
            t.wait_ge("wld", wo_done)

        def evac_o(m, g, ps, pv):
            idx = m * NTG + g

            @ctx.on("vector")
            def _(v):
                v.wait_ge("pe", pv)
                v.tensor_copy(otmp[:, :], ps[:, :])
            dvv = ctx.val("dv")
            dvv2 = residual_evac(ctx, x_in, x_out, m, g, idx, otmp[:, :],
                                 one_col[:, 0:1], xres, store_gates, dvv)
            return ("dv", dvv2)
        proj_1024(ctx, wo, ao, evac_o, psname="oW")
    wait_stores(ctx)


def build_l3attn():
    nc = bass.Bass("TRN2", target_bir_lowering=False, debug=False)
    x_in = nc.dram_tensor("x_in", [D, T], F32, kind="ExternalInput").ap()
    Wqq = nc.dram_tensor("Wqq", [D, 2 * D], F32, kind="ExternalInput").ap()
    Wkk = nc.dram_tensor("Wkk", [D, 2 * D], F32, kind="ExternalInput").ap()
    Wv = nc.dram_tensor("Wv", [D, D], F32, kind="ExternalInput").ap()
    Wout = nc.dram_tensor("Wout", [D, D], F32, kind="ExternalInput").ap()
    tabs = nc.dram_tensor("tabs", [4, 128, T], F32, kind="ExternalInput").ap()
    masks = nc.dram_tensor("masks", [2, 128, 128], F32, kind="ExternalInput").ap()
    x_out = nc.dram_tensor("x_out", [D, T], F32, kind="ExternalOutput").ap()
    v_dram = nc.dram_tensor("vW_scr", [T, 1040], BF16).ap()
    ao_dram = nc.dram_tensor("aoW_scr", [D, T], BF16).ap()
    with ExitStack() as stack:
        block = stack.enter_context(nc.Block())
        ctx = Ctx(nc, block, stack)
        emit_l3attn(ctx, x_in, x_out, Wqq, Wkk, Wv, Wout, tabs, masks, v_dram, ao_dram)
    return nc


def l3attn_maps(W_qkv, W_out):
    W_qkv = np.asarray(W_qkv)
    dh = 64
    d = np.arange(D)
    swap = (d // dh) * dh + np.where(d % dh < 32, d % dh + 32, d % dh - 32)
    Wq = W_qkv[:, 0:D]
    Wk = W_qkv[:, D:2 * D]
    Wv = W_qkv[:, 2 * D:3 * D]
    def interleave(W):
        Wsw = W[:, swap]
        cols = []
        for c in range(NCH):
            cols.append(W[:, c * 128:(c + 1) * 128])
            cols.append(Wsw[:, c * 128:(c + 1) * 128])
        return np.ascontiguousarray(np.concatenate(cols, axis=1))
    inv_freq = 1.0 / 10000.0 ** (np.arange(0, dh, 2, dtype=np.float64) / dh)
    pos = np.arange(256, dtype=np.float64)
    fr = pos[:, None] * inv_freq[None, :]
    fr = np.concatenate([fr, fr], axis=1)
    cos = np.cos(fr).astype(np.float32)
    sin = np.sin(fr).astype(np.float32)
    sgn = np.where(np.arange(dh) < 32, -1.0, 1.0).astype(np.float32)
    sinp = sin * sgn[None, :]
    tpos = np.tile(np.arange(128), 16)
    def table(tab2d, hi):
        rows = tab2d[(128 if hi else 0) + tpos]
        blk = rows.T
        return np.ascontiguousarray(np.concatenate([blk, blk], axis=0))
    tabs = np.stack([table(cos, True), table(sinp, True),
                     table(cos, False), table(sinp, False)])
    kp = np.arange(128)[:, None]
    qf = np.arange(128)[None, :]
    masks = np.stack([(kp <= qf).astype(np.float32), (kp >= qf).astype(np.float32)])
    return {
        "Wqq": interleave(Wq),
        "Wkk": interleave(Wk),
        "Wv": np.ascontiguousarray(Wv),
        "Wout": np.ascontiguousarray(np.asarray(W_out)),
        "tabs": tabs,
        "masks": masks,
    }


def run_l3attn(xT, W_qkv, W_out):
    nc = build_l3attn()
    wmap = l3attn_maps(W_qkv, W_out)
    in_maps = [dict(wmap, x_in=np.ascontiguousarray(xT[b])) for b in range(8)]
    res = run_bass_kernel_spmd(nc, in_maps, core_ids=list(range(8)))
    return np.stack([r["x_out"] for r in res.results])



CAP = 5120
EXP = 8
BIGF = 1.0e6
NROWT = 40
ZROW = 8 * T
YZROW = 8 * CAP


def emit_moe(ctx, x_in, x_out, Wr_d, W1_d, W3_d, W2_d, Ustrict_d, maskprev_d,
             maskexp_d, ident_d, iotasc_d, iota8_d, dr, gdbg=None):
    nc = ctx.nc
    ones_row = ctx.sb("onesrowM", [1, 128], F32)
    ones_col = ctx.sb("onescolM", [128, 1], F32)
    half = ctx.sb("halfM", [128, 1], F32)
    Ustr = ctx.sb("UstrM", [128, 128], F32)
    maskprev = ctx.sb("mprevM", [8, 1], F32)
    maskexp_rep = ctx.sb("mexpM", [128, EXP], F32)
    identb = ctx.sb("identbM", [128, 128], BF16)
    iota8 = ctx.sb("iota8M", [128, EXP], F32)
    iotasc = ctx.sb("iotascM", [128, 128], I32)

    @ctx.on("vector")
    def _(v):
        v.memset(ones_row[0:1, :], 1.0)
        v.memset(ones_col[:, :], 1.0)
        v.memset(half[:, :], 0.5)

    @ctx.on("gpsimd")
    def _(g):
        g.dma("wld", out=Ustr[:, :], in_=Ustrict_d)
        g.dma("wld", out=maskprev[:, :], in_=maskprev_d)
        g.dma("wld", out=maskexp_rep[:, :], in_=maskexp_d)
        g.dma("wld", out=identb[:, :], in_=ident_d)
        g.dma("wld", out=iota8[:, :], in_=iota8_d)
        g.dma("wld", out=iotasc[:, :], in_=iotasc_d)

    y = [ctx.sb(f"yM{k}", [128, T], BF16) for k in range(NCH)]
    with ExitStack() as es:
        rstd = es.enter_context(nc.sbuf_tensor("rstdM", [1, T], F32))
        negmr = es.enter_context(nc.sbuf_tensor("negmrM", [1, T], F32))
        rep_rstd = es.enter_context(nc.sbuf_tensor("rep_rstdM", [128, T], BF16))
        rep_negmr = es.enter_context(nc.sbuf_tensor("rep_negmrM", [128, T], BF16))
        xbuf = [es.enter_context(nc.sbuf_tensor(f"xbM{i}", [128, T], F32)) for i in range(2)]
        ln_rows(ctx, x_in, rstd, negmr, tag="M", xbuf=xbuf)
        replicate_rows(ctx, [(rstd, rep_rstd), (negmr, rep_negmr)], ones_row, tag="M")
        build_y(ctx, x_in, rep_rstd, rep_negmr, y, xbuf)

    ytsb = [ctx.sb(f"ytsbM{i}", [128, D], BF16) for i in range(2)]
    lgt = [ctx.sb(f"lgtM{i}", [128, EXP], F32) for i in range(16)]
    wr = ctx.sb("wrM", [128, NCH * EXP], BF16)
    load_weight_chunks(ctx, Wr_d, [wr[:, k * EXP:(k + 1) * EXP] for k in range(NCH)])
    wldWr = ctx.val("wld")

    with nc.psum_tensor("tpM0", [128, 128], BF16) as tp0, \
         nc.psum_tensor("tpM1", [128, 128], BF16) as tp1, \
         nc.psum_tensor("lgpsM", [128, EXP], F32) as lgps:
        tps = [tp0, tp1]
        tp_gate = [None, None]
        yt_gate = [None, None]
        lg_gate = None
        for tt in range(16):
            bi = tt % 2
            for c in range(NCH):
                pj = tps[c % 2]

                @ctx.on("tensor")
                def _(t, tt=tt, c=c, pj=pj, gate=tp_gate[c % 2], ygate=yt_gate[bi]):
                    if tt == 0 and c == 0:
                        t.wait_ge("wld", wldWr)
                    if gate is not None:
                        t.wait_ge(gate[0], gate[1])
                    if c < 2 and ygate is not None:
                        t.wait_ge(ygate[0], ygate[1])
                    t.transpose(pj[:, :], y[c][:, tt * 128:(tt + 1) * 128], identb[:, :])
                pvt = ctx.val("pe")

                @ctx.on("scalar")
                def _(sx, tt=tt, bi=bi, c=c, pj=pj):
                    sx.wait_ge("pe", pvt)
                    sx.copy(ytsb[bi][:, c * 128:(c + 1) * 128], pj[:, :])
                tp_gate[c % 2] = ("sc", ctx.val("sc"))

            @ctx.on("tensor")
            def _(t, tt=tt, gate=lg_gate):
                if gate is not None:
                    t.wait_ge(gate[0], gate[1])
                for c in range(NCH):
                    t.matmul(lgps[:, :], y[c][:, tt * 128:(tt + 1) * 128],
                             wr[:, c * EXP:(c + 1) * EXP],
                             start=(c == 0), stop=(c == NCH - 1))
            pvl = ctx.val("pe")

            @ctx.on("vector")
            def _(v, tt=tt):
                v.wait_ge("pe", pvl)
                v.tensor_scalar(lgt[tt][:, :], lgps[:, :], 10000.0, -10000.0,
                                ALU.min, ALU.max)
            lg_gate = ("dv", ctx.val("dv"))

            sc_yt = ctx.val("sc")

            @ctx.on("gpsimd")
            def _(g, tt=tt, bi=bi):
                g.wait_ge("sc", sc_yt)
                g.dma("sta2" if bi == 0 else "stb2",
                      out=dr["ytok"][tt * 128:(tt + 1) * 128, :], in_=ytsb[bi][:, :])
            yt_gate[bi] = ("sta2" if bi == 0 else "stb2",
                           ctx.val("sta2" if bi == 0 else "stb2"))

    zrow = ctx.sb("zrowM", [1, D], BF16)

    @ctx.on("vector")
    def _(v):
        v.memset(zrow[0:1, :], 0.0)
    dvz = ctx.val("dv")

    st_a, st_b = ctx.val("sta2"), ctx.val("stb2")

    @ctx.on("gpsimd")
    def _(g):
        g.wait_ge("sta2", st_a)
        g.wait_ge("stb2", st_b)
        g.wait_ge("dv", dvz)
        g.cc("AllGather", ALU.bypass, replica_groups=[list(range(8))],
             ins=[dr["ytok"]], outs=[dr["ag_y"][0:8 * T, :]])
        g.dma("sta2", out=dr["ag_y"][ZROW:ZROW + 1, :], in_=zrow[0:1, :])
    cc_y = ctx.val("cc")

    oh1 = [ctx.sb(f"oh1M{i}", [128, EXP], F32) for i in range(16)]
    oh2 = [ctx.sb(f"oh2M{i}", [128, EXP], F32) for i in range(16)]
    lpos = [ctx.sb(f"lposM{i}", [128, EXP], F32) for i in range(16)]
    pcols = [ctx.sb(f"pcolM{i}", [128, 2], F32) for i in range(16)]
    rtmp = ctx.sb("rtmpM", [128, EXP], F32)
    m1c = ctx.sb("m1cM", [128, 4], F32)
    base_row = ctx.sb("baserowM", [1, EXP], F32)

    with nc.psum_tensor("cumpsM", [128, EXP], F32) as cps, \
         nc.psum_tensor("totpsM", [1, EXP], F32) as tpsm:
        @ctx.on("vector")
        def _(v):
            v.memset(base_row[0:1, :], 0.0)
        dv0 = ctx.val("dv")
        pe_prev = None
        for tt in range(16):
            @ctx.on("vector")
            def _(v, tt=tt, pe_prev=pe_prev):
                if tt == 0:
                    v.wait_ge("dv", dv0)
                if pe_prev is not None:
                    v.wait_ge(pe_prev[0], pe_prev[1])
                v.reduce_max(m1c[:, 0:1], lgt[tt][:, :], axis=AX)
                v.tensor_scalar(oh1[tt][:, :], lgt[tt][:, :], m1c[:, 0:1], None, ALU.is_ge)
                v.scalar_tensor_tensor(rtmp[:, :], oh1[tt][:, :], -BIGF, lgt[tt][:, :],
                                       ALU.mult, ALU.add)
                v.reduce_max(m1c[:, 1:2], rtmp[:, :], axis=AX)
                v.tensor_scalar(oh2[tt][:, :], rtmp[:, :], m1c[:, 1:2], None, ALU.is_ge)
                v.tensor_scalar(m1c[:, 2:3], m1c[:, 0:1], -1.0, None, ALU.mult)
            dv1 = ctx.val("dv")

            @ctx.on("scalar")
            def _(sx, tt=tt):
                sx.wait_ge("dv", dv1)
                sx.activation(out=m1c[:, 3:4], in_=m1c[:, 1:2], func=AF.Exp,
                              bias=m1c[:, 2:3])
            sc1 = ctx.val("sc")

            @ctx.on("vector")
            def _(v, tt=tt):
                v.wait_ge("sc", sc1)
                v.tensor_scalar(rtmp[:, 0:1], m1c[:, 3:4], 1.0, None, ALU.add)
                v.reciprocal(pcols[tt][:, 0:1], rtmp[:, 0:1])
                v.tensor_tensor(pcols[tt][:, 1:2], pcols[tt][:, 0:1], m1c[:, 3:4], ALU.mult)
                v.tensor_tensor(rtmp[:, :], oh1[tt][:, :], oh2[tt][:, :], ALU.add)
            dvg = ctx.val("dv")

            @ctx.on("tensor")
            def _(t, tt=tt):
                t.wait_ge("dv", dvg)
                t.matmul(cps[:, :], Ustr[:, :], rtmp[:, :], start=True, stop=False)
                t.matmul(cps[:, :], ones_row[0:1, :], base_row[0:1, :],
                         start=False, stop=True)
                t.matmul(tpsm[0:1, :], ones_col[:, 0:1], rtmp[:, :], start=True, stop=True)
            pvc = ctx.val("pe")
            pe_prev = ("pe", pvc)

            @ctx.on("vector")
            def _(v, tt=tt):
                v.wait_ge("pe", pvc)
                v.tensor_copy(lpos[tt][:, :], cps[:, :])
                v.tensor_tensor(base_row[0:1, :], base_row[0:1, :], tpsm[0:1, :], ALU.add)
    dv_route = ctx.val("dv")

    tot_all = ctx.sb("totallM", [8, EXP], F32)
    mybase = ctx.sb("mybaseM", [1, EXP], F32)
    baserep = ctx.sb("baserepM", [128, EXP], F32)

    @ctx.on("gpsimd")
    def _(g):
        g.wait_ge("dv", dv_route)
        g.dma("sta2", out=dr["totals"][:, :], in_=base_row[0:1, :])
        g.wait_ge("sta2", ctx.cnt["sta2"])
        g.cc("AllGather", ALU.bypass, replica_groups=[list(range(8))],
             ins=[dr["totals"]], outs=[dr["ag_totals"]])
        g.wait_ge("cc", ctx.cnt["cc"])
        g.dma("wld", out=tot_all[:, :], in_=dr["ag_totals"])
    wldT = ctx.val("wld")

    with nc.psum_tensor("bpsM", [1, EXP], F32) as bps, \
         nc.psum_tensor("brepM", [128, EXP], F32) as brp:
        @ctx.on("tensor")
        def _(t):
            t.wait_ge("wld", wldT)
            t.matmul(bps[0:1, :], maskprev[:, 0:1], tot_all[:, :], start=True, stop=True)
        pvb = ctx.val("pe")

        @ctx.on("vector")
        def _(v):
            v.wait_ge("pe", pvb)
            v.tensor_copy(mybase[0:1, :], bps[0:1, :])
        dvb = ctx.val("dv")

        @ctx.on("tensor")
        def _(t):
            t.wait_ge("dv", dvb)
            t.matmul(brp[:, :], ones_row[0:1, :], mybase[0:1, :], start=True, stop=True)
        pvb2 = ctx.val("pe")

        @ctx.on("scalar")
        def _(sx):
            sx.wait_ge("pe", pvb2)
            sx.copy(baserep[:, :], brp[:, :])
    scb = ctx.val("sc")

    offs_t = [ctx.sb(f"offsM{i}", [128, EXP], F32) for i in range(16)]
    gidx = [ctx.sb(f"gidxM{i}", [128, 2], I32) for i in range(16)]
    slotm = ctx.sb("slotmM", [128, EXP], F32)
    ktmp2 = ctx.sb("ktmp2M", [128, 2], F32)

    @ctx.on("vector")
    def _(v):
        v.wait_ge("sc", scb)
        for tt in range(16):
            v.tensor_tensor(slotm[:, :], lpos[tt][:, :], baserep[:, :], ALU.add)
            v.tensor_tensor(rtmp[:, :], oh1[tt][:, :], oh2[tt][:, :], ALU.add)
            v.tensor_scalar(offs_t[tt][:, :], slotm[:, :], -BIGF, None, ALU.add)
            v.tensor_tensor(offs_t[tt][:, :], offs_t[tt][:, :], rtmp[:, :], ALU.mult)
            v.tensor_scalar(offs_t[tt][:, :], offs_t[tt][:, :], BIGF, None, ALU.add)
            for k, oh in ((0, oh1[tt]), (1, oh2[tt])):
                v.tensor_tensor(rtmp[:, :], oh[:, :], slotm[:, :], ALU.mult)
                v.reduce_sum(ktmp2[:, k:k + 1], rtmp[:, :], axis=AX)
                v.tensor_tensor(rtmp[:, :], oh[:, :], iota8[:, :], ALU.mult)
                v.reduce_sum(m1c[:, k:k + 1], rtmp[:, :], axis=AX)
                v.tensor_scalar(m1c[:, 2:3], ktmp2[:, k:k + 1], float(CAP), None, ALU.is_lt)
                v.scalar_tensor_tensor(m1c[:, 3:4], m1c[:, k:k + 1], float(CAP),
                                       ktmp2[:, k:k + 1], ALU.mult, ALU.add)
                v.tensor_scalar(m1c[:, 3:4], m1c[:, 3:4], -float(YZROW), None, ALU.add)
                v.tensor_tensor(m1c[:, 3:4], m1c[:, 3:4], m1c[:, 2:3], ALU.mult)
                v.tensor_scalar(m1c[:, 3:4], m1c[:, 3:4], float(YZROW), None, ALU.add)
                v.tensor_copy(gidx[tt][:, k:k + 1], m1c[:, 3:4])
    dvoffs = ctx.val("dv")

    @ctx.on("gpsimd")
    def _(g):
        g.wait_ge("dv", dvoffs)
        for tt in range(16):
            g.dma("sta2", out=dr["offs"][tt * 128:(tt + 1) * 128, :], in_=offs_t[tt][:, :])
        g.wait_ge("sta2", ctx.cnt["sta2"])
        g.cc("AllGather", ALU.bypass, replica_groups=[list(range(8))],
             ins=[dr["offs"]], outs=[dr["ag_offs"]])
        g.wait_ge("cc", ctx.cnt["cc"])
    cc_offs = ctx.val("cc")

    fill = ctx.sb("fillM", [128, 41], I32)
    offcol = [ctx.sb(f"offcolM{i}", [128, EXP], F32) for i in range(2)]
    selc = [ctx.sb(f"selcM{i}", [128, 1], I32) for i in range(2)]
    self32 = ctx.sb("self32M", [128, 1], F32)

    @ctx.on("vector")
    def _(v):
        v.memset(fill[:, :], ZROW)
    dvf = ctx.val("dv")

    @ctx.on("gpsimd")
    def _(g):
        g.wait_ge("dv", dvf)
        g.dma("sta2", out=dr["idxbuf"].rearrange("(p n) o -> p (n o)", p=128),
              in_=fill[:, 0:41])
    prefill_done = ctx.val("sta2")

    bc_reg = [None]
    bcy_reg = [None]
    bcye_reg = [None]

    @ctx.on("gpsimd")
    def _(g):
        bc_reg[0] = g._eng.to_reg(41 * 128 - 1)
        bcy_reg[0] = g._eng.to_reg(8 * T)
        bcye_reg[0] = g._eng.to_reg(8 * CAP)

    lda0, ldb0 = ctx.val("lda"), ctx.val("ldb")
    sel_done = [0, 0]
    sct_done = [0, 0]
    for tb in range(128):
        bi = tb % 2
        lsem = "lda" if bi == 0 else "ldb"

        @ctx.on("sync")
        def _(sy, tb=tb, bi=bi, lsem=lsem, gate=sel_done[bi]):
            if tb == 0:
                sy.wait_ge("cc", cc_offs)
            if gate:
                sy.wait_ge("dv", gate)
            sy.dma(lsem, out=offcol[bi][:, :],
                   in_=dr["ag_offs"][tb * 128:(tb + 1) * 128, :])
        lv = ctx.val(lsem)

        @ctx.on("vector")
        def _(v, tb=tb, bi=bi, lsem=lsem, lv=lv, gate=sct_done[bi]):
            v.wait_ge(lsem, lv)
            if gate:
                v.wait_ge("scta" if bi == 0 else "sctb", gate)
            v.tensor_tensor(offcol[bi][:, :], offcol[bi][:, :], maskexp_rep[:, :], ALU.mult)
            v.reduce_sum(self32[:, 0:1], offcol[bi][:, :], axis=AX)
            v.tensor_copy(selc[bi][:, 0:1], self32[:, 0:1])
        dvs = ctx.val("dv")
        sel_done[bi] = dvs

        @ctx.on("gpsimd")
        def _(g, tb=tb, bi=bi, dvs=dvs):
            if tb == 0:
                g.wait_ge("sta2", prefill_done)
            g.wait_ge("dv", dvs)
            g.idma("scta" if bi == 0 else "sctb", out=dr["idxbuf"],
                   out_offset=IndirectOffsetOnAxis(ap=selc[bi][:, 0:1], axis=0),
                   in_=iotasc[:, tb:tb + 1], in_offset=None,
                   bounds_check=bc_reg[0], oob_is_err=False)
        sct_done[bi] = ctx.val("scta" if bi == 0 else "sctb")
    sct_all = [("scta", ctx.val("scta")), ("sctb", ctx.val("sctb"))]

    barrier(ctx, extra=("scta", "sctb", "lda", "ldb", "sta2", "stb2"))

    @ctx.on("gpsimd")
    def _(g):
        g.wait_ge("cc", cc_y)
    barrier(ctx)

    idxt = [ctx.sb(f"idxtM{i}", [128, 1], I32) for i in range(NROWT)]

    @ctx.on("sync")
    def _(sy):
        for j in range(NROWT):
            sy.dma("lda", out=idxt[j][:, :], in_=dr["idxbuf"][j * 128:(j + 1) * 128, :])
    idx_done = ctx.val("lda")

    with ExitStack() as es:
        w1 = [es.enter_context(nc.sbuf_tensor(f"w1M{k}", [128, 4 * D], BF16))
              for k in range(NCH)]
        w3 = [es.enter_context(nc.sbuf_tensor(f"w3M{k}", [128, 4 * D], BF16))
              for k in range(NCH)]
        xg = [es.enter_context(nc.sbuf_tensor(f"xgM{i}", [128, D], BF16)) for i in range(2)]
        xeT = [es.enter_context(nc.sbuf_tensor(f"xeTM{k}", [128, TG], BF16))
               for k in range(NCH)]
        s1b = [es.enter_context(nc.sbuf_tensor(f"s1M{i}", [128, TG], BF16))
               for i in range(4)]
        hcb = [es.enter_context(nc.sbuf_tensor(f"hcatM{i}", [128, TG], BF16))
               for i in range(4)]
        load_weight_chunks(ctx, W1_d, w1)
        load_weight_chunks(ctx, W3_d, w3)
        wld1 = ctx.val("wld")

        with nc.psum_tensor("xtpM0", [128, 128], BF16) as xtp0, \
             nc.psum_tensor("xtpM1", [128, 128], BF16) as xtp1, \
             nc.psum_tensor("h1psM0", [128, TG], F32) as h1a, \
             nc.psum_tensor("h3psM0", [128, TG], F32) as h3a, \
             nc.psum_tensor("h1psM1", [128, TG], F32) as h1b, \
             nc.psum_tensor("h3psM1", [128, TG], F32) as h3b:
            xtps = [xtp0, xtp1]
            h1p = [h1a, h1b]
            h3p = [h3a, h3b]
            xtp_gate = [None, None]
            xg_gate = [None, None]
            mmd = {}
            sld = {}
            ttd = {}
            dmd = {}
            for kq in range(-4, 0):
                ttd[kq] = None
                dmd[kq] = None
            gi = 0
            for st in range(10):
                xeT_ready = None
                for r4 in range(4):
                    j = st * 4 + r4
                    gbi = r4 % 2

                    gsem = "xgla" if gbi == 0 else "xglb"

                    @ctx.on("gpsimd")
                    def _(g, j=j, gbi=gbi, gsem=gsem, gate=xg_gate[gbi]):
                        if j == 0:
                            g.wait_ge("lda", idx_done)
                            for sn, sv in sct_all:
                                g.wait_ge(sn, sv)
                        if gate is not None:
                            g.wait_ge(gate[0], gate[1])
                        g.idma(gsem, out=xg[gbi][:, :], out_offset=None,
                               in_=dr["ag_y"],
                               in_offset=IndirectOffsetOnAxis(ap=idxt[j][:, 0:1], axis=0),
                               bounds_check=bcy_reg[0], oob_is_err=False)
                    gv = (gsem, ctx.val(gsem))

                    for c in range(NCH):
                        pj = xtps[c % 2]

                        @ctx.on("tensor")
                        def _(t, c=c, pj=pj, gbi=gbi, gv=gv, gate=xtp_gate[c % 2]):
                            t.wait_ge(gv[0], gv[1])
                            if gate is not None:
                                t.wait_ge(gate[0], gate[1])
                            t.transpose(pj[:, :], xg[gbi][:, c * 128:(c + 1) * 128],
                                        identb[:, :])
                        pvt = ctx.val("pe")

                        @ctx.on("scalar")
                        def _(sx, c=c, r4=r4, pj=pj):
                            sx.wait_ge("pe", pvt)
                            sx.copy(xeT[c][:, r4 * 128:(r4 + 1) * 128], pj[:, :])
                        xtp_gate[c % 2] = ("sc", ctx.val("sc"))
                    xg_gate[gbi] = ("pe", ctx.val("pe"))
                xeT_ready = ctx.val("sc")

                if st == 0 and gdbg is not None:
                    @ctx.on("gpsimd")
                    def _(g, xr=ctx.val("sc")):
                        g.wait_ge("sc", xr)
                        g.dma("sta2", out=gdbg["xeT_dbg"][:, :], in_=xeT[0][:, :])

                for st_ in range(35):
                    k = st_ - 3
                    if 0 <= k < 32:
                        g0 = gi + k
                        hsem = ("sta2", "stb2", "scta", "sctb")[g0 % 4]

                        @ctx.on("sync")
                        def _(sy, st=st, k=k, hsem=hsem, gate=ttd[g0]):
                            sy.wait_ge(gate[0], gate[1])
                            sy.dma(hsem,
                                   out=dr["hbuf"][k * 128:(k + 1) * 128,
                                                  st * TG:(st + 1) * TG],
                                   in_=hcb[k % 4][:, :])
                        dmd[g0] = (hsem, ctx.val(hsem))
                    k = st_ - 2
                    if 0 <= k < 32:
                        g0 = gi + k

                        @ctx.on("vector")
                        def _(v, k=k, gate=sld[g0], wg=dmd[g0 - 4]):
                            v.wait_ge(gate[0], gate[1])
                            if wg is not None:
                                v.wait_ge(wg[0], wg[1])
                            v.tensor_tensor(hcb[k % 4][:, :], s1b[k % 4][:, :],
                                            h3p[k % 2][:, :], ALU.mult)
                        ttd[g0] = ("dv", ctx.val("dv"))
                    k = st_ - 1
                    if 0 <= k < 32:
                        g0 = gi + k

                        @ctx.on("scalar")
                        def _(sx, k=k, gate=mmd[g0], wg=ttd[g0 - 4]):
                            sx.wait_ge(gate[0], gate[1])
                            if wg is not None:
                                sx.wait_ge(wg[0], wg[1])
                            sx.activation(out=s1b[k % 4][:, :],
                                          in_=h1p[k % 2][:, :], func=AF.Silu)
                        sld[g0] = ("sc", ctx.val("sc"))
                    k = st_
                    if k < 32:
                        g0 = gi + k

                        @ctx.on("tensor")
                        def _(t, k=k, xr=xeT_ready, tg=ttd[g0 - 2]):
                            if k == 0:
                                t.wait_ge("sc", xr)
                                t.wait_ge("wld", wld1)
                            if tg is not None:
                                t.wait_ge(tg[0], tg[1])
                            for kk in range(NCH):
                                t.matmul(h1p[k % 2][:, :],
                                         w1[kk][:, k * 128:(k + 1) * 128],
                                         xeT[kk][:, :], start=(kk == 0),
                                         stop=(kk == NCH - 1))
                            for kk in range(NCH):
                                t.matmul(h3p[k % 2][:, :],
                                         w3[kk][:, k * 128:(k + 1) * 128],
                                         xeT[kk][:, :], start=(kk == 0),
                                         stop=(kk == NCH - 1))
                        mmd[g0] = ("pe", ctx.val("pe"))
                gi += 32
    barrier(ctx, extra=("sta2", "stb2", "scta", "sctb", "xgla", "xglb",
                        "wld", "lda", "ldb"))

    with ExitStack() as es:
        w2 = [es.enter_context(nc.sbuf_tensor(f"w2M{k}", [128, D], BF16))
              for k in range(32)]
        hT = [es.enter_context(nc.sbuf_tensor(f"hTM{k}", [128, TG], BF16))
              for k in range(32)]
        yesb = [es.enter_context(nc.sbuf_tensor(f"yesbM{i}", [128, D], BF16))
                for i in range(2)]
        load_weight_chunks(ctx, W2_d, w2)
        wld2 = ctx.val("wld")

        with nc.psum_tensor("yepsM0", [128, TG], F32) as ye0, \
             nc.psum_tensor("yepsM1", [128, TG], F32) as ye1:
            yeps = [ye0, ye1]
            ye_gate = [None, None]
            ye_st = [None, None]
            pe_prev_st = None
            for st in range(10):
                @ctx.on("sync")
                def _(sy, st=st, pe_prev_st=pe_prev_st):
                    if pe_prev_st is not None:
                        sy.wait_ge("pe", pe_prev_st)
                    for k in range(32):
                        sy.dma("ldb", out=hT[k][:, :],
                               in_=dr["hbuf"][k * 128:(k + 1) * 128,
                                              st * TG:(st + 1) * TG])
                hld = ctx.val("ldb")

                for r4 in range(4):
                    rt = st * 4 + r4
                    bi = r4 % 2
                    for ng in range(2):
                        pi = ng
                        ps = yeps[pi]

                        @ctx.on("tensor")
                        def _(t, rt=rt, r4=r4, ng=ng, ps=ps, gate=ye_gate[pi]):
                            if r4 == 0 and ng == 0:
                                t.wait_ge("ldb", hld)
                                t.wait_ge("wld", wld2)
                            if gate is not None:
                                t.wait_ge(gate[0], gate[1])
                            for k in range(32):
                                t.matmul(ps[:, :], hT[k][:, r4 * 128:(r4 + 1) * 128],
                                         w2[k][:, ng * TG:(ng + 1) * TG],
                                         start=(k == 0), stop=(k == 31))
                        pv = ctx.val("pe")

                        @ctx.on("scalar")
                        def _(sx, rt=rt, ng=ng, ps=ps, bi=bi, sgate=ye_st[bi]):
                            sx.wait_ge("pe", pv)
                            if ng == 0 and sgate is not None:
                                sx.wait_ge(sgate[0], sgate[1])
                            sx.copy(yesb[bi][:, ng * TG:(ng + 1) * TG], ps[:, :])
                        ye_gate[pi] = ("sc", ctx.val("sc"))
                    scy = ctx.val("sc")

                    @ctx.on("gpsimd")
                    def _(g, rt=rt, bi=bi):
                        g.wait_ge("sc", scy)
                        g.dma("sta2" if bi == 0 else "stb2",
                              out=dr["ye"][rt * 128:(rt + 1) * 128, :],
                              in_=yesb[bi][:, :])
                    ye_st[bi] = ("sta2" if bi == 0 else "stb2",
                                 ctx.val("sta2" if bi == 0 else "stb2"))
                pe_prev_st = ctx.val("pe")
    barrier(ctx, extra=("wld", "ldb", "sta2", "stb2"))
    st_a, st_b = ctx.val("sta2"), ctx.val("stb2")
    zrow2 = ctx.sb("zrow2M", [1, D], BF16)

    @ctx.on("vector")
    def _(v):
        v.memset(zrow2[0:1, :], 0.0)
    dvz2 = ctx.val("dv")

    @ctx.on("gpsimd")
    def _(g):
        g.wait_ge("sta2", st_a)
        g.wait_ge("stb2", st_b)
        g.wait_ge("dv", dvz2)
        g.cc("AllGather", ALU.bypass, replica_groups=[list(range(8))],
             ins=[dr["ye"]], outs=[dr["ag_ye"][0:8 * CAP, :]])
        g.dma("sta2", out=dr["ag_ye"][YZROW:YZROW + 1, :], in_=zrow2[0:1, :])
        g.wait_ge("cc", ctx.cnt["cc"])
        g.wait_ge("sta2", ctx.cnt["sta2"])
    barrier(ctx)

    ao = [ctx.sb(f"aoM{k}", [128, T], BF16) for k in range(NCH)]
    g0 = [ctx.sb(f"g0M{i}", [128, D], BF16) for i in range(2)]
    g1 = [ctx.sb(f"g1M{i}", [128, D], BF16) for i in range(2)]
    otok = ctx.sb("otokM", [128, D], BF16)

    with nc.psum_tensor("atpM0", [128, 128], BF16) as atp0, \
         nc.psum_tensor("atpM1", [128, 128], BF16) as atp1:
        atps = [atp0, atp1]
        atp_gate = [None, None]
        g_gate = [None, None]
        ot_read = None
        for tt in range(16):
            bi = tt % 2

            gsem = "xga" if bi == 0 else "xgb"

            @ctx.on("gpsimd")
            def _(g, tt=tt, bi=bi, gsem=gsem, gate=g_gate[bi]):
                if gate is not None:
                    g.wait_ge(gate[0], gate[1])
                g.idma(gsem, out=g0[bi][:, :], out_offset=None, in_=dr["ag_ye"],
                       in_offset=IndirectOffsetOnAxis(ap=gidx[tt][:, 0:1], axis=0),
                       bounds_check=bcye_reg[0], oob_is_err=False)
                g.idma(gsem, out=g1[bi][:, :], out_offset=None, in_=dr["ag_ye"],
                       in_offset=IndirectOffsetOnAxis(ap=gidx[tt][:, 1:2], axis=0),
                       bounds_check=bcye_reg[0], oob_is_err=False)
            gv = (gsem, ctx.val(gsem))

            @ctx.on("vector")
            def _(v, tt=tt, bi=bi, gv=gv, ot_read=ot_read):
                v.wait_ge(gv[0], gv[1])
                if ot_read is not None:
                    v.wait_ge(ot_read[0], ot_read[1])
                v.tensor_scalar(otok[:, :], g0[bi][:, :], pcols[tt][:, 0:1], None, ALU.mult)
                v.scalar_tensor_tensor(g1[bi][:, :], g1[bi][:, :], pcols[tt][:, 1:2],
                                       otok[:, :], ALU.mult, ALU.add)
                v.tensor_copy(otok[:, :], g1[bi][:, :])
            dvo = ctx.val("dv")
            g_gate[bi] = ("dv", dvo)

            for c in range(NCH):
                pj = atps[c % 2]

                @ctx.on("tensor")
                def _(t, c=c, pj=pj, dvo=dvo, gate=atp_gate[c % 2]):
                    t.wait_ge("dv", dvo)
                    if gate is not None:
                        t.wait_ge(gate[0], gate[1])
                    t.transpose(pj[:, :], otok[:, c * 128:(c + 1) * 128], identb[:, :])
                pvt = ctx.val("pe")

                @ctx.on("scalar")
                def _(sx, tt=tt, c=c, pj=pj):
                    sx.wait_ge("pe", pvt)
                    sx.copy(ao[c][:, tt * 128:(tt + 1) * 128], pj[:, :])
                atp_gate[c % 2] = ("sc", ctx.val("sc"))
            ot_read = ("pe", ctx.val("pe"))
    barrier(ctx)

    if gdbg is not None:
        dbg_dv = ctx.val("dv")

        @ctx.on("gpsimd")
        def _(g):
            g.wait_ge("dv", dbg_dv)
            for tt in range(16):
                g.dma("sta2", out=gdbg["gidx_dbg"][tt * 128:(tt + 1) * 128, :],
                      in_=gidx[tt][:, :])
                g.dma("sta2", out=gdbg["p_dbg"][tt * 128:(tt + 1) * 128, :],
                      in_=pcols[tt][:, :])
                g.dma("sta2", out=gdbg["offs_dbg"][tt * 128:(tt + 1) * 128, :],
                      in_=offs_t[tt][:, :])
                g.dma("sta2", out=gdbg["oh1_dbg"][tt * 128:(tt + 1) * 128, :],
                      in_=oh1[tt][:, :])
                g.dma("sta2", out=gdbg["lgt_dbg"][tt * 128:(tt + 1) * 128, :],
                      in_=lgt[tt][:, :])
            g.dma("sta2", out=gdbg["idx_dbg"][:, :], in_=dr["idxbuf"])
            g.dma("sta2", out=gdbg["iota_dbg"][:, :], in_=iota8[:, :])
            g.dma("sta2", out=gdbg["h_dbg"][:, :], in_=dr["hbuf"][0:256, 0:TG])
            g.dma("sta2", out=gdbg["tot_dbg"][:, :], in_=tot_all[:, :])
            g.dma("sta2", out=gdbg["ye_dbg"][:, :], in_=dr["ye"][0:256, :])
            g.dma("sta2", out=gdbg["ytok_dbg"][:, :], in_=dr["ytok"])
            g.wait_ge("sta2", ctx.cnt["sta2"])

    xres = [ctx.sb(f"xresM{i}", [128, TG], F32) for i in range(2)]
    store_gates = []
    for m in range(NCH):
        for g in range(NTG):
            idx = m * NTG + g
            dvv = ctx.val("dv")
            residual_evac(ctx, x_in, x_out, m, g, idx, ao[m][:, g * TG:(g + 1) * TG],
                          half[:, 0:1], xres, store_gates, dvv)
    wait_stores(ctx)


def moe_dram(nc, dbg=False):
    return {
        "ytok": nc.dram_tensor("ytokM", [T, D], BF16).ap(),
        "ag_y": nc.dram_tensor("agyM", [8 * T + 1, D], BF16).ap(),
        "totals": nc.dram_tensor("totM", [1, EXP], F32).ap(),
        "ag_totals": nc.dram_tensor("agtotM", [8, EXP], F32).ap(),
        "offs": nc.dram_tensor("offsM", [T, EXP], F32).ap(),
        "ag_offs": nc.dram_tensor("agoffsM", [8 * T, EXP], F32).ap(),
        "idxbuf": nc.dram_tensor("idxbufM", [41 * 128, 1], I32).ap(),
        "hbuf": nc.dram_tensor("hbufM", [4 * D, CAP], BF16).ap(),
        "ye": nc.dram_tensor("yeM", [CAP, D], BF16).ap(),
        "ag_ye": nc.dram_tensor("agyeM", [8 * CAP + 1, D], BF16).ap(),
    }


def build_moe(dbg=False):
    nc = bass.Bass("TRN2", target_bir_lowering=False, debug=False)
    x_in = nc.dram_tensor("x_in", [D, T], F32, kind="ExternalInput").ap()
    Wr = nc.dram_tensor("Wr", [D, EXP], F32, kind="ExternalInput").ap()
    W1 = nc.dram_tensor("W1", [D, 4 * D], F32, kind="ExternalInput").ap()
    W3 = nc.dram_tensor("W3", [D, 4 * D], F32, kind="ExternalInput").ap()
    W2 = nc.dram_tensor("W2", [4 * D, D], F32, kind="ExternalInput").ap()
    Ustr = nc.dram_tensor("Ustr", [128, 128], F32, kind="ExternalInput").ap()
    mprev = nc.dram_tensor("mprev", [8, 1], F32, kind="ExternalInput").ap()
    mexp = nc.dram_tensor("mexp", [128, EXP], F32, kind="ExternalInput").ap()
    ident = nc.dram_tensor("ident", [128, 128], F32, kind="ExternalInput").ap()
    iota8 = nc.dram_tensor("iota8", [128, EXP], F32, kind="ExternalInput").ap()
    iotasc = nc.dram_tensor("iotasc", [128, 128], I32, kind="ExternalInput").ap()
    x_out = nc.dram_tensor("x_out", [D, T], F32, kind="ExternalOutput").ap()
    dr = moe_dram(nc, dbg)
    if dbg:
        gdbg = {
            "gidx_dbg": nc.dram_tensor("gidx_dbg", [T, 2], I32, kind="ExternalOutput").ap(),
            "p_dbg": nc.dram_tensor("p_dbg", [T, 2], F32, kind="ExternalOutput").ap(),
            "idx_dbg": nc.dram_tensor("idx_dbg", [41 * 128, 1], I32, kind="ExternalOutput").ap(),
            "tot_dbg": nc.dram_tensor("tot_dbg", [8, EXP], F32, kind="ExternalOutput").ap(),
            "ye_dbg": nc.dram_tensor("ye_dbg", [256, D], BF16, kind="ExternalOutput").ap(),
            "ytok_dbg": nc.dram_tensor("ytok_dbg", [T, D], BF16, kind="ExternalOutput").ap(),
            "offs_dbg": nc.dram_tensor("offs_dbg", [T, EXP], F32, kind="ExternalOutput").ap(),
            "oh1_dbg": nc.dram_tensor("oh1_dbg", [T, EXP], F32, kind="ExternalOutput").ap(),
            "lgt_dbg": nc.dram_tensor("lgt_dbg", [T, EXP], F32, kind="ExternalOutput").ap(),
            "iota_dbg": nc.dram_tensor("iota_dbg", [128, EXP], F32, kind="ExternalOutput").ap(),
            "h_dbg": nc.dram_tensor("h_dbg", [256, TG], BF16, kind="ExternalOutput").ap(),
            "xeT_dbg": nc.dram_tensor("xeT_dbg", [128, TG], BF16, kind="ExternalOutput").ap(),
        }
    else:
        gdbg = None
    with ExitStack() as stack:
        block = stack.enter_context(nc.Block())
        ctx = Ctx(nc, block, stack)
        emit_moe(ctx, x_in, x_out, Wr, W1, W3, W2, Ustr, mprev, mexp, ident,
                 iotasc, iota8, dr, gdbg=gdbg)
    return nc


def moe_maps(Wr, W1, W3, W2, core):
    return {
        "Wr": np.ascontiguousarray(np.asarray(Wr)),
        "W1": np.ascontiguousarray(np.asarray(W1)[core]),
        "W3": np.ascontiguousarray(np.asarray(W3)[core]),
        "W2": np.ascontiguousarray(np.asarray(W2)[core]),
        "Ustr": np.triu(np.ones((128, 128), np.float32), 1),
        "mprev": (np.arange(8) < core).astype(np.float32).reshape(8, 1),
        "mexp": np.tile((np.arange(EXP) == core).astype(np.float32), (128, 1)),
        "ident": np.eye(128, dtype=np.float32),
        "iota8": np.tile(np.arange(EXP, dtype=np.float32), (128, 1)),
        "iotasc": np.ascontiguousarray(
            (np.arange(128)[:, None] + 128 * np.arange(128)[None, :]).astype(np.int32)),
    }


def run_moe(xT, Wr, W1, W3, W2):
    nc = build_moe()
    in_maps = [dict(moe_maps(Wr, W1, W3, W2, b), x_in=np.ascontiguousarray(xT[b]))
               for b in range(8)]
    res = run_bass_kernel_spmd(nc, in_maps, core_ids=list(range(8)))
    return np.stack([r["x_out"] for r in res.results])



def build_rwkv():
    nc = bass.Bass("TRN2", target_bir_lowering=False, debug=False)
    x_in = nc.dram_tensor("x_in", [D, T], F32, kind="ExternalInput").ap()
    WrT = nc.dram_tensor("WrT", [D, D], F32, kind="ExternalInput").ap()
    WkT = nc.dram_tensor("WkT", [D, D], F32, kind="ExternalInput").ap()
    WvT = nc.dram_tensor("WvT", [D, D], F32, kind="ExternalInput").ap()
    mur = nc.dram_tensor("mur", [128, NCH], F32, kind="ExternalInput").ap()
    muk = nc.dram_tensor("muk", [128, NCH], F32, kind="ExternalInput").ap()
    x_out = nc.dram_tensor("x_out", [D, T], F32, kind="ExternalOutput").ap()
    with ExitStack() as stack:
        block = stack.enter_context(nc.Block())
        ctx = Ctx(nc, block, stack)
        emit_rwkv(ctx, x_in, x_out, WrT, WkT, WvT, mur, muk)
    return nc


def rwkv_maps(Wr, Wk, Wv, mu_r, mu_k):
    return {
        "WrT": np.ascontiguousarray(np.asarray(Wr).T),
        "WkT": np.ascontiguousarray(np.asarray(Wk).T),
        "WvT": np.ascontiguousarray(np.asarray(Wv).T),
        "mur": np.ascontiguousarray(np.asarray(mu_r).reshape(NCH, 128).T),
        "muk": np.ascontiguousarray(np.asarray(mu_k).reshape(NCH, 128).T),
    }


def run_rwkv(xT, Wr, Wk, Wv, mu_r, mu_k):
    nc = build_rwkv()
    wmap = rwkv_maps(Wr, Wk, Wv, mu_r, mu_k)
    in_maps = [dict(wmap, x_in=np.ascontiguousarray(xT[b])) for b in range(8)]
    res = run_bass_kernel_spmd(nc, in_maps, core_ids=list(range(8)))
    return np.stack([r["x_out"] for r in res.results])



def build_fused():
    nc = bass.Bass("TRN2", target_bir_lowering=False, debug=False)
    x0 = nc.dram_tensor("x_in", [D, T], F32, kind="ExternalInput").ap()
    xout = nc.dram_tensor("x_out", [D, T], F32, kind="ExternalOutput").ap()
    xa = nc.dram_tensor("x_ping", [D, T], F32).ap()
    xb = nc.dram_tensor("x_pong", [D, T], F32).ap()

    def din(pref, name, shape):
        return nc.dram_tensor(pref + name, shape, F32, kind="ExternalInput").ap()

    def dini(pref, name, shape):
        return nc.dram_tensor(pref + name, shape, I32, kind="ExternalInput").ap()

    chain = [x0] + [xa, xb] * 6
    chain = chain[:12] + [xout]

    with ExitStack() as stack:
        block = stack.enter_context(nc.Block())
        ctx = Ctx(nc, block, stack)

        def scoped(i, fn):
            ctx.set_prefix(f"s{i}_")
            es = ExitStack()
            ctx.scopes.append(es)
            try:
                fn(f"s{i}_", chain[i], chain[i + 1])
            finally:
                es.close()
                ctx.scopes.pop()

        def mk_rwkv(li):
            def f(pref, xi, xo):
                emit_rwkv(ctx, xi, xo,
                          din(pref, "WrT", [D, D]), din(pref, "WkT", [D, D]),
                          din(pref, "WvT", [D, D]), din(pref, "mur", [128, NCH]),
                          din(pref, "muk", [128, NCH]))
            return f

        def f_l0attn(pref, xi, xo):
            ao_dram = nc.dram_tensor(pref + "ao_scr", [D, T], BF16).ap()
            mn_dram = nc.dram_tensor(pref + "mn_scr", [16, 128], F32).ap()
            emit_l0attn(ctx, xi, xo, din(pref, "W1", [D, 3 * D]),
                        din(pref, "WoT", [D, D]), din(pref, "bo", [1, D]),
                        din(pref, "ident", [128, 128]), ao_dram, mn_dram)

        def f_l1attn(pref, xi, xo):
            emit_l1attn(ctx, xi, xo, din(pref, "Wq", [D, D]), din(pref, "Wk", [D, D]),
                        din(pref, "Wv", [D, D]), din(pref, "bq", [1, D]),
                        din(pref, "bk", [1, D]), din(pref, "bv", [1, D]),
                        din(pref, "ident", [128, 128]))

        def f_l2attn(pref, xi, xo):
            ao_dram = nc.dram_tensor(pref + "ao_scr", [D, T], BF16).ap()
            emit_l2attn(ctx, xi, xo, din(pref, "Wqkv", [D, 3 * D]),
                        din(pref, "Wout", [D, D]), din(pref, "hs", [1, 8]),
                        din(pref, "dirs", [3, 8]), din(pref, "cmask", [4, 128, TG]),
                        ao_dram)

        def f_l2ffn(pref, xi, xo):
            emit_l2ffn(ctx, xi, xo, din(pref, "WfcT", [D, 4 * D]),
                       din(pref, "WprojT", [4 * D, D]))

        def f_l3attn(pref, xi, xo):
            v_dram = nc.dram_tensor(pref + "v_scr", [T, 1040], BF16).ap()
            ao_dram = nc.dram_tensor(pref + "ao_scr", [D, T], BF16).ap()
            emit_l3attn(ctx, xi, xo, din(pref, "Wqq", [D, 2 * D]),
                        din(pref, "Wkk", [D, 2 * D]), din(pref, "Wv", [D, D]),
                        din(pref, "Wout", [D, D]), din(pref, "tabs", [4, 128, T]),
                        din(pref, "masks", [2, 128, 128]), v_dram, ao_dram)

        def f_moe(pref, xi, xo):
            dr = {
                "ytok": nc.dram_tensor(pref + "ytokM", [T, D], BF16).ap(),
                "ag_y": nc.dram_tensor(pref + "agyM", [8 * T + 1, D], BF16).ap(),
                "totals": nc.dram_tensor(pref + "totM", [1, EXP], F32).ap(),
                "ag_totals": nc.dram_tensor(pref + "agtotM", [8, EXP], F32).ap(),
                "offs": nc.dram_tensor(pref + "offsM", [T, EXP], F32).ap(),
                "ag_offs": nc.dram_tensor(pref + "agoffsM", [8 * T, EXP], F32).ap(),
                "idxbuf": nc.dram_tensor(pref + "idxbufM", [41 * 128, 1], I32).ap(),
                "hbuf": nc.dram_tensor(pref + "hbufM", [4 * D, CAP], BF16).ap(),
                "ye": nc.dram_tensor(pref + "yeM", [CAP, D], BF16).ap(),
                "ag_ye": nc.dram_tensor(pref + "agyeM", [8 * CAP + 1, D], BF16).ap(),
            }
            emit_moe(ctx, xi, xo, din(pref, "Wr", [D, EXP]), din(pref, "W1", [D, 4 * D]),
                     din(pref, "W3", [D, 4 * D]), din(pref, "W2", [4 * D, D]),
                     din(pref, "Ustr", [128, 128]), din(pref, "mprev", [8, 1]),
                     din(pref, "mexp", [128, EXP]), din(pref, "ident", [128, 128]),
                     dini(pref, "iotasc", [128, 128]), din(pref, "iota8", [128, EXP]),
                     dr)

        subs = [mk_rwkv(0), f_l0attn, mk_rwkv(0), mk_rwkv(1), f_l1attn, mk_rwkv(1),
                f_l2ffn, f_l2attn, f_l2ffn, f_moe, f_l3attn, f_moe]
        for i, fn in enumerate(subs):
            scoped(i, fn)
    return nc


def fused_maps(inp, core):
    out = {}

    def add(i, m):
        for k, v in m.items():
            out[f"s{i}_{k}"] = v

    r0 = rwkv_maps(inp["L0_ffn_W_r"], inp["L0_ffn_W_k"], inp["L0_ffn_W_v"],
                   inp["L0_ffn_mu_r"], inp["L0_ffn_mu_k"])
    r1 = rwkv_maps(inp["L1_ffn_W_r"], inp["L1_ffn_W_k"], inp["L1_ffn_W_v"],
                   inp["L1_ffn_mu_r"], inp["L1_ffn_mu_k"])
    l2f = l2ffn_maps(inp["L2_ffn_W_fc"], inp["L2_ffn_W_proj"])
    moem = moe_maps(inp["L3_ffn_Wr"], inp["L3_ffn_W1"], inp["L3_ffn_W3"],
                    inp["L3_ffn_W2"], core)
    add(0, r0)
    add(1, l0attn_maps(inp["L0_attn_w_qkv"], inp["L0_attn_w_out"], inp["L0_attn_b_out"]))
    add(2, r0)
    add(3, r1)
    add(4, l1attn_maps(inp["L1_attn_W_q"], inp["L1_attn_W_k"], inp["L1_attn_W_v"],
                       inp["L1_attn_b_q"], inp["L1_attn_b_k"], inp["L1_attn_b_v"]))
    add(5, r1)
    add(6, l2f)
    add(7, l2attn_maps(inp["L2_attn_Wqkv"], inp["L2_attn_Wout"],
                       inp["L2_attn_head_scales"], inp["L2_attn_head_dirs"]))
    add(8, l2f)
    add(9, moem)
    add(10, l3attn_maps(inp["L3_attn_W_qkv"], inp["L3_attn_W_out"]))
    add(11, moem)
    return out


def kernel_fused(**inputs):
    inp = {k: np.asarray(v, dtype=np.float32) for k, v in inputs.items()}
    nc = _prog("fused", build_fused)
    in_maps = []
    for b in range(8):
        m = fused_maps(inp, b)
        m["x_in"] = np.ascontiguousarray(inp["L0_attn_x"][b].T)
        in_maps.append(m)
    res = run_bass_kernel_spmd(nc, in_maps, core_ids=list(range(8)))
    return np.stack([np.ascontiguousarray(r["x_out"].T) for r in res.results]).astype(np.float32)


_PROGS = {}


def _prog(name, builder):
    if name not in _PROGS:
        _PROGS[name] = builder()
    return _PROGS[name]


def _launch(nc, per_core_maps):
    res = run_bass_kernel_spmd(nc, per_core_maps, core_ids=list(range(8)))
    return [r["x_out"] for r in res.results]


def kernel(**inputs):
    return kernel_fused(**inputs)


def kernel_unfused(**inputs):
    inp = {k: np.asarray(v, dtype=np.float32) for k, v in inputs.items()}
    xT = [np.ascontiguousarray(inp["L0_attn_x"][b].T) for b in range(8)]

    def run(nc, shared_map, per_core=None):
        nonlocal xT
        maps = []
        for b in range(8):
            m = dict(shared_map if per_core is None else per_core[b])
            m["x_in"] = np.ascontiguousarray(xT[b])
            maps.append(m)
        xT = _launch(nc, maps)

    rwkv_nc = _prog("rwkv", build_rwkv)
    moe_nc = _prog("moe", build_moe)

    run(rwkv_nc, rwkv_maps(inp["L0_ffn_W_r"], inp["L0_ffn_W_k"], inp["L0_ffn_W_v"],
                           inp["L0_ffn_mu_r"], inp["L0_ffn_mu_k"]))
    run(_prog("l0attn", build_l0attn),
        l0attn_maps(inp["L0_attn_w_qkv"], inp["L0_attn_w_out"], inp["L0_attn_b_out"]))
    run(rwkv_nc, rwkv_maps(inp["L0_ffn_W_r"], inp["L0_ffn_W_k"], inp["L0_ffn_W_v"],
                           inp["L0_ffn_mu_r"], inp["L0_ffn_mu_k"]))
    run(rwkv_nc, rwkv_maps(inp["L1_ffn_W_r"], inp["L1_ffn_W_k"], inp["L1_ffn_W_v"],
                           inp["L1_ffn_mu_r"], inp["L1_ffn_mu_k"]))
    run(_prog("l1attn", build_l1attn),
        l1attn_maps(inp["L1_attn_W_q"], inp["L1_attn_W_k"], inp["L1_attn_W_v"],
                    inp["L1_attn_b_q"], inp["L1_attn_b_k"], inp["L1_attn_b_v"]))
    run(rwkv_nc, rwkv_maps(inp["L1_ffn_W_r"], inp["L1_ffn_W_k"], inp["L1_ffn_W_v"],
                           inp["L1_ffn_mu_r"], inp["L1_ffn_mu_k"]))
    l2f = l2ffn_maps(inp["L2_ffn_W_fc"], inp["L2_ffn_W_proj"])
    run(_prog("l2ffn", build_l2ffn), l2f)
    run(_prog("l2attn", build_l2attn),
        l2attn_maps(inp["L2_attn_Wqkv"], inp["L2_attn_Wout"],
                    inp["L2_attn_head_scales"], inp["L2_attn_head_dirs"]))
    run(_prog("l2ffn", build_l2ffn), l2f)
    moe_percore = [moe_maps(inp["L3_ffn_Wr"], inp["L3_ffn_W1"], inp["L3_ffn_W3"],
                            inp["L3_ffn_W2"], b) for b in range(8)]
    run(moe_nc, None, per_core=moe_percore)
    run(_prog("l3attn", build_l3attn),
        l3attn_maps(inp["L3_attn_W_qkv"], inp["L3_attn_W_out"]))
    run(moe_nc, None, per_core=moe_percore)

    return np.stack([np.ascontiguousarray(x.T) for x in xT]).astype(np.float32)
